# version 21
# baseline (speedup 1.0000x reference)
import numpy as np
from contextlib import ExitStack

import concourse.bass as bass
import concourse.mybir as mybir
from concourse.bass_utils import run_bass_kernel_spmd

F32 = mybir.dt.float32
BF16 = mybir.dt.bfloat16
U32 = mybir.dt.uint32
AF = mybir.ActivationFunctionType
ALU = mybir.AluOpType
AX = mybir.AxisListType
IOA = bass.IndirectOffsetOnAxis

PASS_TOK = 512
NPASS = 16
NSLOT = NPASS * PASS_TOK

ENGS = ("sync", "tensor", "vector", "scalar", "gpsimd")
NDQ = 16
NGQ = 24

NB = 2
TT = 16
T = 2048
D = 1024
NC = 8
NH = 16
DK = 64
NE = 8
HID = 4096
EPS = 1e-5


class Buf:
    __slots__ = ("w", "rs")

    def __init__(self):
        self.w = None
        self.rs = []


PARANOID = False


class Sched:
    def __init__(self):
        self.items = []
        self.per_eng = {e: [] for e in ENGS}
        self.last_per_eng = {e: None for e in ENGS}
        self.ndma = 0
        self.dma_ring = [None] * NDQ
        self.ngma = 0
        self.gma_ring = [None] * NGQ
        self.nsma = 0
        self.sma_ring = [None] * 4

    def _mk(self, eng, fn, reads, writes, dma):
        deps = set()
        for b in reads:
            if b.w is not None:
                deps.add(b.w)
        for b in writes:
            if b.w is not None:
                deps.add(b.w)
            for r in b.rs:
                deps.add(r)
        if PARANOID:
            for e2 in ENGS:
                if e2 != eng and self.last_per_eng[e2] is not None:
                    deps.add(self.last_per_eng[e2])
        eid = len(self.items)
        it = dict(eid=eid, eng=eng, fn=fn, dma=dma)
        if dma and eng == "sync":
            q = self.ndma % NDQ
            it["sem"] = "dq%d" % q
            it["val"] = 16 * (self.ndma // NDQ + 1)
            if self.dma_ring[q] is not None:
                deps.add(self.dma_ring[q])
            self.dma_ring[q] = eid
            self.ndma += 1
        elif dma and eng == "scalar":
            q = self.nsma % 4
            it["sem"] = "sq%d" % q
            it["val"] = 16 * (self.nsma // 4 + 1)
            if self.sma_ring[q] is not None:
                deps.add(self.sma_ring[q])
            self.sma_ring[q] = eid
            self.nsma += 1
        elif dma:
            q = self.ngma % NGQ
            it["sem"] = "gq%d" % q
            it["val"] = 16 * (self.ngma // NGQ + 1)
            if self.gma_ring[q] is not None:
                deps.add(self.gma_ring[q])
            self.gma_ring[q] = eid
            self.ngma += 1
        it["deps"] = sorted(deps)
        self.items.append(it)
        self.per_eng[eng].append(it)
        self.last_per_eng[eng] = eid
        for b in writes:
            b.w = eid
            b.rs = []
        for b in reads:
            b.rs.append(eid)
        return eid

    def op(self, eng, fn, reads=(), writes=()):
        return self._mk(eng, fn, tuple(reads), tuple(writes), False)

    def dma(self, out_ap, in_ap, reads=(), writes=(), slow=False):
        def fn(E, o=out_ap, i=in_ap, s=slow):
            if s:
                return E.dma_start(out=o, in_=i, allow_slow_non_contiguous=True)
            return E.dma_start(out=o, in_=i)
        return self._mk("sync", fn, tuple(reads), tuple(writes), True)

    def gma(self, fn, reads=(), writes=()):
        return self._mk("gpsimd", fn, tuple(reads), tuple(writes), True)

    def sdma(self, out_ap, in_ap, reads=(), writes=()):
        def fn(E, o=out_ap, i=in_ap):
            return E.dma_start(out=o, in_=i)
        return self._mk("scalar", fn, tuple(reads), tuple(writes), True)

    def finalize(self, nc, sems):
        demanded = set()
        for it in self.items:
            for d in it["deps"]:
                demanded.add(d)
        for e in ENGS:
            c = 0
            for it in self.per_eng[e]:
                if it["dma"]:
                    it["inc"] = 16
                elif it["eid"] in demanded:
                    c += 1
                    it["val"] = c
                    it["sem"] = e
                    it["inc"] = 1
                else:
                    it["val"] = None
                    it["sem"] = None
                    it["inc"] = 0

        items_all = self.items

        def mk_runner(engname):
            eng_items = self.per_eng[engname]

            def go(E):
                hi = {}
                for it in eng_items:
                    for d in it["deps"]:
                        p = items_all[d]
                        if engname == "tensor" and p["eng"] == "tensor" and not p["dma"]:
                            continue
                        v = p["val"]
                        if v is not None and hi.get(p["sem"], 0) < v:
                            E.wait_ge(sems[p["sem"]], v)
                            hi[p["sem"]] = v
                    inst = it["fn"](E)
                    if it["inc"]:
                        inst.then_inc(sems[it["sem"]], it["inc"])
            return go

        with nc.Block() as block:
            block.sync(mk_runner("sync"))
            block.tensor(mk_runner("tensor"))
            block.vector(mk_runner("vector"))
            block.scalar(mk_runner("scalar"))
            block.gpsimd(mk_runner("gpsimd"))


class Builder:
    def __init__(self, dump=()):
        self.dump = set(dump)
        nc = bass.Bass("TRN2", target_bir_lowering=False, debug=False)
        self.nc = nc
        self.S = Sched()
        self.ctx = ExitStack()
        self._alloc_dram()
        self._alloc_sbuf()
        self._alloc_psum()
        self.sems = {e: self.ctx.enter_context(nc.semaphore("sem_" + e)) for e in ENGS}
        for q in range(NDQ):
            self.sems["dq%d" % q] = self.ctx.enter_context(nc.semaphore("dq%d" % q))
        for q in range(NGQ):
            self.sems["gq%d" % q] = self.ctx.enter_context(nc.semaphore("gq%d" % q))
        for q in range(4):
            self.sems["sq%d" % q] = self.ctx.enter_context(nc.semaphore("sq%d" % q))
        self._mk_bufs()

    def _alloc_dram(self):
        nc = self.nc
        din = lambda n, s: nc.dram_tensor(n, s, F32, kind="ExternalInput").ap()
        self.x = din("x", [NB, T, D])
        self.wq_d = din("wq", [D, DK]); self.bq_d = din("bq", [DK])
        self.wk_d = din("wk", [D, DK]); self.bk_d = din("bk", [DK])
        self.wv_d = din("wv", [D, DK]); self.bv_d = din("bv", [DK])
        self.wo_d = din("wo", [DK, D]); self.bo_d = din("bo", [D])
        self.ln1w_d = din("ln1w", [D]); self.ln1b_d = din("ln1b", [D])
        self.wqkv_d = din("wqkv", [D, 3 * D])
        self.wproj_d = din("wproj", [D, D])
        self.ln2w_d = din("ln2w", [D]); self.ln2b_d = din("ln2b", [D])
        self.gatew_d = din("gatew", [D, NE]); self.gateb_d = din("gateb", [NE])
        self.w1_d = din("w1", [NE, D, HID])
        self.w2_d = din("w2", [NE, HID, D])
        self.out = nc.dram_tensor("out", [NB, T, D], F32, kind="ExternalOutput").ap()
        self.s_spill = nc.dram_tensor("s_spill", [NB, T, D], F32, kind="Internal").ap()
        self.wqkv_bf = nc.dram_tensor("wqkv_bf", [D, 3 * D], BF16, kind="Internal").ap()
        self.wproj_bf = nc.dram_tensor("wproj_bf", [D, D], BF16, kind="Internal").ap()
        ik = lambda n: "ExternalOutput" if n in self.dump else "Internal"
        self.w1bf = nc.dram_tensor("w1bf", [NE * 4 * 128, 8 * 1024], BF16,
                                   kind="Internal").ap()
        self.w2bf = nc.dram_tensor("w2bf", [NE * 4 * 128, 8 * 1024], BF16,
                                   kind="Internal").ap()
        self.u_buf = nc.dram_tensor("u_buf", [NB * T, D], BF16, kind=ik("u_buf")).ap()
        self.vit_buf = nc.dram_tensor("vit_buf", [T, D], F32, kind="Internal").ap()
        self.y_buf = nc.dram_tensor("y_buf", [NB * T + 128, D], F32,
                                    kind=ik("y_buf")).ap()
        self.gidx_buf = nc.dram_tensor("gidx_buf", [NSLOT], U32, kind=ik("gidx_buf")).ap()
        self.iscr = nc.dram_tensor("iscr", [NB * TT * NE], F32, kind="Internal").ap()
        self.sidx_buf = nc.dram_tensor("sidx_buf", [NSLOT], U32, kind=ik("sidx_buf")).ap()
        self.ident_dram = nc.inline_tensor(np.eye(8, dtype=np.float32), name="identc")
        import ml_dtypes
        self.identb_dram = nc.inline_tensor(
            np.eye(128).astype(ml_dtypes.bfloat16), name="identbc")
        lt = (np.arange(128)[:, None] <= np.arange(128)[None, :]).astype(np.float32)
        self.lt_dram = nc.inline_tensor(lt, name="ltmaskc")
        thr16 = np.tile(PASS_TOK * np.arange(16, dtype=np.float32)[None, :], (NE, 1))
        self.thr16_dram = nc.inline_tensor(thr16, name="thr16c")
        thrp = np.tile(PASS_TOK * np.arange(NPASS, dtype=np.float32)[None, :], (NE, 1))
        self.thrp_dram = nc.inline_tensor(thrp, name="thrpc")
        evals = np.arange(NE, dtype=np.float32)[:, None]
        self.evals_dram = nc.inline_tensor(evals, name="evalsc")
        self.dmp = {}
        for name in self.dump:
            if name in ("u_buf", "y_buf", "gidx_buf", "sidx_buf"):
                continue
            shp = {"xn": [NB, T, D], "xn2": [NB, T, D], "s": [NB, T, D],
                   "lgt": [NB, T, 2 * NE], "gin": [NB, T, D]}[name]
            self.dmp[name] = nc.dram_tensor("dbg_" + name, shp, F32, kind="ExternalOutput").ap()

    def _alloc_sbuf(self):
        nc = self.nc
        E_ = self.ctx.enter_context
        sb = lambda n, s, d: E_(nc.sbuf_tensor(n, s, d))
        self.xn_tok = sb("xn_tok", [128, TT, D], F32)
        self.ln_fm = sb("ln_fm", [128, NC, T], BF16)
        self.bigA = sb("bigA", [128, 16384], BF16)
        self.bigB = sb("bigB", [128, 16384], BF16)
        self.hg = sb("hg", [128, 4096], BF16)
        self.misc = sb("misc", [128, 4096], F32)
        self.stage = sb("stage", [128, 2048], F32)
        self.pool = sb("pool", [128, 3, 1024], F32)
        self.identf = sb("identf", [8, 8], F32)
        self.identb = sb("identb", [128, 128], BF16)
        self.ltmask = sb("ltmask", [128, 128], BF16)
        self.onesb = sb("onesb", [128, 128], BF16)
        self.gam1 = sb("gam1", [128, NC], F32)
        self.bet1 = sb("bet1", [128, NC], F32)
        self.gam2 = sb("gam2", [128, NC], F32)
        self.bet2 = sb("bet2", [128, NC], F32)
        self.bq8 = sb("bq8", [64, 2], F32)
        self.bkc = sb("bkc", [64, 1], F32)
        self.bvc = sb("bvc", [64, 1], F32)
        self.gw_st = sb("gw_st", [128, NC, NE], F32)
        self.gwg_hi = sb("gwg_hi", [128, NC, NE], BF16)
        self.gwg_lo = sb("gwg_lo", [128, NC, NE], BF16)
        self.gw_bf = sb("gw_bf", [128, NC, NE], BF16)
        self.b2bf = sb("b2bf", [128, NC], BF16)
        self.gb_row = sb("gb_row", [1, NE], F32)
        self.gbadd = sb("gbadd", [NE, 1], F32)
        self.masks = sb("masksb", [128, NB, TT, NE], F32)
        self.lgt_tok = sb("lgt_tok", [128, TT, NE], F32)
        self.statA = sb("statA", [128, TT, 4], F32)
        self.ntcol = sb("ntcol", [128, 1], F32)
        self.maxsc = sb("maxsc", [1, 2], F32)
        self.reccol = sb("reccol", [128, TT], F32)
        self.epsc = sb("epsc", [128, 1], F32)
        self.idxs = sb("idxs", [128, 40], U32)
        self.miscu = self.misc.bitcast(U32)
        self.ltf = self.misc[:, 0:128]
        self.csum = self.misc[0:1, 128:384]
        self.csum32 = self.misc[0:32, 384:392]
        self.carry32 = self.misc[0:32, 392:400]
        self.irow = self.misc[0:8, 400:464]
        self.icol = self.misc[0:8, 464:472]
        self.srow = self.misc[0:1, 472:504]
        self.soffr = self.misc[0:1, 504:512]
        self.thr16 = self.misc[0:8, 512:528]
        self.thrp = self.misc[0:8, 528:528 + NPASS]
        self.evalsf = self.misc[0:8, 552:553]
        self.idxf = self.misc[:, 560:584]
        self.pcol = self.misc[:, 584:585]
        self.posf = self.misc[:, 585:587]
        self.tid32 = self.miscu[:, 588:588 + NB * TT]
        self.iniz = self.miscu[:, 0:NSLOT // 128]
        self.onescol = self.ltf[:, 127:128]
        self.onesrow = self.ltf[0:1, 0:128]
        self.miscb = self.misc.bitcast(BF16)
        self.poolb = self.pool.bitcast(BF16)
        self.hgf = self.hg.bitcast(F32)
        self.stageb = self.stage.bitcast(BF16)
        self.lnfm_f32 = self.ln_fm.bitcast(F32)
        rem = nc.sbuf_bytes_remaining
        assert rem >= 0, f"SBUF over budget by {-rem} bytes"

    def _alloc_psum(self):
        nc = self.nc
        E_ = self.ctx.enter_context
        self.pL = E_(nc.psum_tensor("pL", [128, 2048], F32))
        self.pM = E_(nc.psum_tensor("pM", [128, 1024], F32))
        self.pT1 = E_(nc.psum_tensor("pT1", [128, 512], F32))
        self.pT2 = E_(nc.psum_tensor("pT2", [128, 512], F32))
        self.pT1b = self.pT1.bitcast(BF16)
        self.pT2b = self.pT2.bitcast(BF16)
        self.pMb = self.pM.bitcast(BF16)

    def _mk_bufs(self):
        self.b_ident = Buf(); self.b_lt = Buf(); self.b_ones = Buf()
        self.b_params = Buf(); self.b_gw = Buf()
        self.b_pool = [Buf() for _ in range(3)]
        self.b_stage = [Buf(), Buf()]
        self.b_hg = [Buf() for _ in range(4)]
        self.b_misc = [Buf() for _ in range(4)]
        self.b_xn = [Buf() for _ in range(TT)]
        self.b_lnfm = [Buf() for _ in range(TT)]
        self.b_bigA = [Buf() for _ in range(8)]
        self.b_bigB = [Buf() for _ in range(8)]
        self.b_stat = [Buf(), Buf()]
        self.b_masks = Buf()
        self.b_pL = [Buf(), Buf(), Buf(), Buf()]
        self.b_pM = [Buf(), Buf()]
        self.b_pT = [Buf(), Buf()]
        self.b_sspill = Buf()
        self.b_wbf = Buf()
        self.b_reccol = Buf()
        self.b_ptile = [Buf(), Buf(), Buf()]
        self.b_stsum = Buf()
        self.b_stsq = Buf()
        self.b_ust = [Buf(), Buf()]
        self.b_wcvt = Buf()
        self.b_ubuf = Buf()
        self.b_vit = Buf()
        self.b_ybuf = Buf()
        self.b_gidxb = Buf()
        self.b_sidxb = Buf()
        self.b_idx = Buf()
        self.b_icons = Buf()
        self.b_utok = [Buf(), Buf()]
        self.b_ufm = [Buf(), Buf()]
        self.b_yst = [Buf(), Buf()]
        self.b_gsb = [Buf(), Buf()]
        self.b_widx = Buf()

    def pLh(self, i):
        return [self.b_pL[2 * i], self.b_pL[2 * i + 1]]

    def bA(self, lo, hi):
        return [self.b_bigA[i] for i in range(lo // 2048, (hi + 2047) // 2048)]

    def bB(self, lo, hi):
        return [self.b_bigB[i] for i in range(lo // 2048, (hi + 2047) // 2048)]

    def op(self, eng, fn, reads=(), writes=()):
        return self.S.op(eng, fn, reads, writes)

    def dma(self, out_ap, in_ap, reads=(), writes=(), slow=False):
        return self.S.dma(out_ap, in_ap, reads, writes, slow=slow)

    def mm(self, mms, reads=(), writes=()):
        def fn(E, mms=mms):
            n = len(mms)
            last = None
            for i, (o, l, r) in enumerate(mms):
                last = E.matmul(o, l, r, start=(i == 0), stop=(i == n - 1))
            return last
        return self.op("tensor", fn, reads, writes)

    def mmx(self, mms, reads=(), writes=()):
        def fn(E, mms=mms):
            last = None
            for (o, l, r, st, sp) in mms:
                last = E.matmul(o, l, r, start=st, stop=sp)
            return last
        return self.op("tensor", fn, reads, writes)

    def trans(self, trs, reads=(), writes=(), bf=True):
        ident = self.identb if bf else self.identf

        def fn(E, trs=trs, ident=ident):
            last = None
            for item in trs:
                (o, i), bp = item[:2], (item[2] if len(item) > 2 else 0)
                p = i.shape[0]
                last = E.transpose(o, i, ident[bp:bp + p, bp:bp + p])
            return last
        return self.op("tensor", fn, tuple(reads) + (self.b_ident,), writes)

    def act(self, out, in_, func, reads=(), writes=(), **kw):
        def fn(E, o=out, i=in_, f=func, kw=kw):
            return E.activation(o, i, f, **kw)
        return self.op("scalar", fn, reads, writes)

    def vop(self, name, *args, reads=(), writes=(), **kw):
        def fn(E, n=name, a=args, kw=kw):
            return getattr(E, n)(*a, **kw)
        return self.op("vector", fn, reads, writes)

    def gop(self, name, *args, reads=(), writes=(), **kw):
        def fn(E, n=name, a=args, kw=kw):
            return getattr(E, n)(*a, **kw)
        return self.op("gpsimd", fn, reads, writes)

    def init_consts(self):
        pb = self.b_params
        self.dma(self.identf[:], self.ident_dram.ap(), writes=[self.b_ident])
        self.dma(self.identb[:], self.identb_dram.ap(), writes=[self.b_ident])
        self.dma(self.pool[:, 2, :][:, 0:128], self.lt_dram.ap()[:, 0:128],
                 writes=[self.b_pool[2]])
        self.vop("tensor_copy", self.ltmask[:], self.pool[:, 2, 0:128],
                 reads=[self.b_pool[2]], writes=[self.b_lt])
        self.vop("memset", self.onesb[:], 1.0, writes=[self.b_ones])
        self.vop("memset", self.epsc[:], EPS, writes=[self.b_ones])
        rr = lambda ap: ap.rearrange("(c p) -> p c", p=128)
        cc = lambda ap: ap.rearrange("(p one) -> p one", one=1)
        self.dma(self.gam1[:], rr(self.ln1w_d), writes=[pb], slow=True)
        self.dma(self.bet1[:], rr(self.ln1b_d), writes=[pb], slow=True)
        self.dma(self.gam2[:], rr(self.ln2w_d), writes=[pb], slow=True)
        self.dma(self.bet2[:], rr(self.ln2b_d), writes=[pb], slow=True)
        self.dma(self.bq8[:, 1:2], cc(self.bq_d), writes=[pb])
        self.dma(self.bkc[:], cc(self.bk_d), writes=[pb])
        self.dma(self.bvc[:], cc(self.bv_d), writes=[pb])
        self.dma(self.gb_row[:], self.gateb_d.rearrange("(one d) -> one d", one=1), writes=[pb])
        self.vop("tensor_scalar", self.bq8[:, 0:1], self.bq8[:, 1:2], 0.125, None, ALU.mult,
                 reads=[pb], writes=[pb])
        gwr = self.gatew_d.rearrange("(c p) e -> p c e", p=128)
        self.dma(self.gw_st[:], gwr, writes=[self.b_gw])
        for c in range(NC):
            self.vop("tensor_scalar", self.gw_st[:, c, :], self.gw_st[:, c, :],
                     self.gam2[:, c:c + 1], None, ALU.mult,
                     reads=[self.b_gw, pb], writes=[self.b_gw])
        self.vop("tensor_copy", self.gwg_hi[:], self.gw_st[:],
                 reads=[self.b_gw], writes=[self.b_gw])
        self.vop("tensor_sub", self.gwg_lo[:], self.gw_st[:], self.gwg_hi[:],
                 reads=[self.b_gw], writes=[self.b_gw])
        self.dma(self.gw_st[:], gwr, reads=[self.b_gw], writes=[self.b_gw])
        self.vop("tensor_copy", self.gw_bf[:], self.gw_st[:],
                 reads=[self.b_gw], writes=[self.b_gw])
        self.vop("tensor_copy", self.b2bf[:], self.bet2[:], reads=[pb], writes=[pb])
        mms = [(self.pT1[0:1, 0:NE], self.b2bf[:, c:c + 1], self.gw_bf[:, c, :])
               for c in range(NC)]
        self.mm(mms, reads=[self.b_gw, pb], writes=[self.b_pT[0]])
        self.vop("tensor_add", self.misc[0:1, 0:NE], self.pT1[0:1, 0:NE], self.gb_row[:],
                 reads=[self.b_pT[0], pb], writes=[self.b_misc[0]])
        self.vop("tensor_copy", self.miscb[0:1, 2048:2048 + NE], self.misc[0:1, 0:NE],
                 reads=[self.b_misc[0]], writes=[self.b_misc[1]])
        self.trans([(self.pT2b[0:NE, 0:1], self.miscb[0:1, 2048:2048 + NE])],
                   reads=(self.b_misc[1],), writes=[self.b_pT[1]])
        self.vop("tensor_copy", self.gbadd[:], self.pT2b[0:NE, 0:1],
                 reads=[self.b_pT[1]], writes=[pb])
        iniz2 = self.miscu[0:NSLOT // 128, 0:128]
        self.gop("iota", iniz2, [[0, 128]], base=0,
                 channel_multiplier=0, reads=[], writes=[self.b_idx, self.b_misc[0]])
        self.dma(self.gidx_buf.rearrange("(c p) -> c p", p=128), iniz2,
                 reads=[self.b_idx, self.b_misc[0]], writes=[self.b_gidxb])
        self.gop("iota", iniz2, [[1, 128]], base=NB * T,
                 channel_multiplier=0, reads=[self.b_gidxb],
                 writes=[self.b_idx, self.b_misc[0]])
        self.dma(self.sidx_buf.rearrange("(c p) -> c p", p=128), iniz2,
                 reads=[self.b_idx, self.b_misc[0]], writes=[self.b_sidxb])

    def convert_qkv_weights(self):
        def cast_dma(out_ap, in_ap, reads, writes):
            def fn(E, o=out_ap, i=in_ap):
                return E.dma_start(out=o, in_=i, max_dma_last_dim=8192)
            return self.S.gma(fn, reads, writes)
        cast_dma(self.wqkv_bf, self.wqkv_d, [self.b_wbf], [])
        cast_dma(self.wproj_bf, self.wproj_d, [self.b_wbf], [])

    def convert_moe_weights(self):
        def cast_dma(out_ap, in_ap, reads, writes):
            def fn(E, o=out_ap, i=in_ap):
                return E.dma_start(out=o, in_=i, max_dma_last_dim=8192)
            return self.S.gma(fn, reads, writes)
        w1v = self.w1bf.rearrange("(s p) (c n) -> s p c n", p=128, n=1024)
        w2v = self.w2bf.rearrange("(s p) (c n) -> s p c n", p=128, n=1024)
        for e in range(NE):
            for hb in range(4):
                s = e * 4 + hb
                cast_dma(w1v[s],
                         self.w1_d[e, :, hb * 1024:(hb + 1) * 1024].rearrange(
                             "(c p) n -> p c n", p=128),
                         [self.b_wcvt], [])
                cast_dma(w2v[s],
                         self.w2_d[e, hb * 1024:(hb + 1) * 1024, :].rearrange(
                             "(c p) n -> p c n", p=128),
                         [self.b_wcvt], [])

    def load_attn_weights(self):
        w_bufs = self.bB(10240, 12800)
        for i, w in enumerate((self.wq_d, self.wk_d, self.wv_d)):
            sl = i % 2
            self.dma(self.stage[:, sl * 1024:sl * 1024 + 512].rearrange(
                         "p (c k) -> p c k", k=DK),
                     w.rearrange("(c p) k -> p c k", p=128),
                     reads=[self.b_stage[sl]], writes=[self.b_stage[sl]])
            self.gop("tensor_copy", self.bigB[:, 10240 + i * 512:10240 + (i + 1) * 512],
                     self.stage[:, sl * 1024:sl * 1024 + 512],
                     reads=[self.b_stage[sl]], writes=w_bufs)
        self.dma(self.stage[0:64, 1024:2048], self.wo_d,
                 reads=[self.b_stage[1]], writes=[self.b_stage[1]])
        self.gop("tensor_copy", self.bigB[0:64, 11776:12800], self.stage[0:64, 1024:2048],
                 reads=[self.b_stage[1]], writes=w_bufs)

    def wq_sb(self, i, c):
        off = 10240 + i * 512 + c * 64
        return self.bigB[:, off:off + 64]

    def wo_sb(self, half):
        return self.bigB[0:64, 11776 + half * 512:11776 + (half + 1) * 512]

    def phase_A_pass1(self, b):
        for t in range(TT):
            xt = self.xn_tok[:, t, :]
            bx = self.b_xn[t]
            self.dma(xt, self.x[b, t * 128:(t + 1) * 128, :], reads=[bx], writes=[bx])
            self._stats_collect(t, xt, bx)
        self._stats_finalize()

    def phase_A_pass2(self, b):
        QT = self.bigB[0:65, 0:2048]
        KT = self.bigB[0:65, 2048:4096]
        VT = self.bigB[:, 4096:5136].rearrange("p (t k) -> p t k", k=65)
        bQ, bK = self.bB(0, 2048), self.bB(2048, 4096)
        bV = self.bB(4096, 5456)
        plB = self.pL.bitcast(BF16)
        self.vop("memset", self.bigB[64:128, 0:2048], 0.0, reads=bQ, writes=bQ)
        self.vop("memset", self.bigB[64:128, 2048:4096], 0.0, reads=bK, writes=bK)
        self.vop("memset", KT[64:65, :], 1.0, reads=bK, writes=bK)
        self.vop("memset", VT[:, :, 64:65], 1.0, reads=bV, writes=bV)
        self.vop("memset", self.bigB[:, 5136:5200], 0.0,
                 reads=self.bB(5136, 5200), writes=self.bB(5136, 5200))
        pend = None
        for t in range(TT + 1):
            if t < TT:
                pp = t % 2
                self.vop("tensor_scalar", self.xn_tok[:, t, :], self.xn_tok[:, t, :],
                         self.statA[:, t, 2:3], self.statA[:, t, 3:4],
                         ALU.mult, ALU.subtract,
                         reads=[self.b_stsum], writes=[self.b_xn[t]])
                xc = self.hg[:, pp * 1024:(pp + 1) * 1024]
                self.vop("tensor_copy", xc, self.xn_tok[:, t, :],
                         reads=[self.b_xn[t]], writes=[self.b_hg[pp]])
                pt = (self.pT1b, self.pT2b)[pp]
                self.trans([(pt[:, c * 128:(c + 1) * 128], xc[:, c * 128:(c + 1) * 128])
                            for c in range(NC)],
                           reads=(self.b_hg[pp],), writes=[self.b_pT[pp]])
                if "xn" in self.dmp:
                    self.dma(self.dmp["xn"][b, t * 128:(t + 1) * 128, :],
                             self.xn_tok[:, t, :], reads=[self.b_xn[t]], writes=[])
            if pend is not None:
                tp_ = pend
                pp_ = tp_ % 2
                pt_ = (self.pT1b, self.pT2b)[pp_]
                fm = self.hg[:, (2 + pp_) * 1024:(3 + pp_) * 1024]
                self.vop("tensor_copy", fm, pt_[:],
                         reads=[self.b_pT[pp_]], writes=[self.b_hg[2 + pp_]])
                ps = self.pM[:, pp_ * 512:(pp_ + 1) * 512]
                bps = self.b_pM[pp_]
                mms = []
                for i in range(3):
                    for c in range(NC):
                        mms.append((ps[0:64, i * 128:i * 128 + 128],
                                    self.wq_sb(i, c), fm[:, c * 128:(c + 1) * 128]))
                def fnmm(E, mms=mms):
                    last = None
                    for gi in range(3):
                        for ci in range(NC):
                            o, l, r = mms[gi * NC + ci]
                            last = E.matmul(o, l, r, start=(ci == 0), stop=(ci == NC - 1))
                    return last
                self.op("tensor", fnmm,
                        reads=[self.b_hg[2 + pp_]] + self.bB(10240, 12800), writes=[bps])
                self.act(QT[0:64, tp_ * 128:(tp_ + 1) * 128], ps[0:64, 0:128], AF.Identity,
                         scale=0.125, bias=self.bq8[:, 0:1],
                         reads=[bps, self.b_params], writes=bQ)
                self.act(KT[0:64, tp_ * 128:(tp_ + 1) * 128], ps[0:64, 128:256], AF.Identity,
                         bias=self.bkc[:],
                         reads=[bps, self.b_params], writes=bK)
                vt = self.bigB[0:64, 5200 + pp_ * 128:5200 + (pp_ + 1) * 128]
                self.act(vt, ps[0:64, 256:384], AF.Identity, bias=self.bvc[:],
                         reads=[bps, self.b_params], writes=bV)
                vslot = plB[:, pp_ * 2048:pp_ * 2048 + 64]
                self.trans([(vslot, vt)], reads=tuple(bV),
                           writes=[self.b_pL[2 * pp_]])
                self.vop("tensor_copy", VT[:, tp_, 0:64], vslot,
                         reads=[self.b_pL[2 * pp_]], writes=bV)
            pend = t if t < TT else None

    def _stats_collect(self, t, src_ap, bsrc):
        self.vop("reduce_sum", self.statA[:, t, 0:1], src_ap, axis=AX.X,
                 reads=[bsrc], writes=[self.b_stsum])
        self.act(self.pool[:, 2, :], src_ap, AF.Square,
                 accum_out=self.statA[:, t, 1:2],
                 reads=[bsrc], writes=[self.b_pool[2], self.b_stsq])

    def _stats_finalize(self):
        c = lambda i: self.statA[:, :, i:i + 1]
        bb = [self.b_stsum, self.b_stsq]
        self.vop("tensor_scalar", c(0), c(0), 1.0 / D, None, ALU.mult,
                 reads=bb, writes=[self.b_stsum])
        self.vop("tensor_scalar", c(1), c(1), 1.0 / D, None, ALU.mult,
                 reads=bb, writes=[self.b_stsq])
        self.vop("tensor_mul", c(2), c(0), c(0), reads=bb, writes=[self.b_stsum])
        self.vop("tensor_sub", c(1), c(1), c(2), reads=bb, writes=[self.b_stsq])
        self.act(c(3), c(1), AF.Ln, bias=self.epsc[:],
                 reads=bb + [self.b_ones], writes=[self.b_stsum])
        self.act(c(2), c(3), AF.Exp, scale=-0.5, reads=bb, writes=[self.b_stsum])
        self.vop("tensor_mul", c(3), c(0), c(2), reads=bb, writes=[self.b_stsum])

    def phase_B(self, b):
        QT = self.bigB[0:65, 0:2048]
        KT = self.bigB[0:65, 2048:4096]
        VT = self.bigB[:, 4096:5136].rearrange("p (t k) -> p t k", k=65)
        ZF = self.bigB[0:64, 6144:8192]
        QSQ = self.bigB[0:64, 8192:10240]
        bQ, bK = self.bB(0, 2048), self.bB(2048, 4096)
        bV, bZ, bQs = self.bB(4096, 5456), self.bB(6144, 8192), self.bB(8192, 10240)
        row = lambda s, lo, hi: self.poolb[0:1, s, lo:hi]
        for (tile, bt) in ((QT, bQ), (KT, bK)):
            self.vop("tensor_mul", QSQ[:], tile[0:64, :], tile[0:64, :],
                     reads=bt, writes=bQs)
            mms = [(self.pL[0:1, j * 512:(j + 1) * 512], self.onesb[0:64, 0:1],
                    QSQ[:, j * 512:(j + 1) * 512]) for j in range(4)]
            def fnmm(E, mms=mms):
                last = None
                for (o, l, r) in mms:
                    last = E.matmul(o, l, r, start=True, stop=True)
                return last
            self.op("tensor", fnmm, reads=bQs + [self.b_ones],
                    writes=self.pLh(0) + self.pLh(1))
            self.vop("tensor_copy", row(2, 0, 2048), self.pL[0:1, 0:2048],
                     reads=self.pLh(0) + self.pLh(1), writes=[self.b_pool[2]])
            if tile is QT:
                self.act(row(1, 0, 2048), row(2, 0, 2048), AF.Sqrt,
                         reads=[self.b_pool[2]], writes=[self.b_pool[1]])
        self.vop("reduce_max", self.maxsc[0:1, 0:1], row(2, 0, 2048), axis=AX.X,
                 reads=[self.b_pool[2]], writes=[self.b_params])
        self.act(self.maxsc[0:1, 1:2], self.maxsc[0:1, 0:1], AF.Sqrt,
                 reads=[self.b_params], writes=[self.b_params])
        self.vop("tensor_scalar", row(0, 0, 2048), row(1, 0, 2048),
                 self.maxsc[0:1, 1:2], -1.0, ALU.mult, ALU.mult,
                 reads=[self.b_pool[1], self.b_params], writes=[self.b_pool[0]])
        self.dma(QT[64:65, :], row(0, 0, 2048), reads=[self.b_pool[0]], writes=bQ)
        for qb in range(2):
            kmax = (qb + 1) * 8
            zb = [self.b_pM[0], self.b_pM[1]]

            def emit_z(kc, kmax=kmax, zb=zb, bV=bV):
                pt = self.miscb[:, 4096 + (kc % 2) * 1024:4096 + (kc % 2 + 1) * 1024]
                vwide = self.bigB[:, 4096 + kc * 65:4096 + kc * 65 + 128]
                self.mmx([(self.pM[:, j * 512:(j + 1) * 512], vwide,
                           pt[:, j * 512:(j + 1) * 512],
                           kc == 0, kc == kmax - 1) for j in range(2)],
                         reads=bV + self.bB(4096, 5456) + [self.b_ptile[kc % 2]],
                         writes=zb)

            for kc in range(kmax):
                sp = self.pL[:, (kc % 2) * 1024:(kc % 2 + 1) * 1024]
                bsp = self.pLh(kc % 2)
                QT8 = self.bigB[0:128, 0:2048]
                KT8 = self.bigB[0:128, 2048:4096]
                self.mmx([(sp[:, j * 512:(j + 1) * 512],
                           KT8[:, kc * 128:(kc + 1) * 128],
                           QT8[:, qb * 1024 + j * 512:qb * 1024 + (j + 1) * 512],
                           True, True) for j in range(2)],
                         reads=bQ + bK, writes=bsp)
                pt = self.miscb[:, 4096 + (kc % 2) * 1024:4096 + (kc % 2 + 1) * 1024]
                bpt = self.b_ptile[kc % 2]
                d = kc * 128 - qb * 1024
                if d > 0:
                    self.gop("memset", pt[:, 0:d], 0.0, reads=[bpt], writes=[bpt])
                lo = max(d, 0)
                self.act(pt[:, lo:1024], sp[:, lo:1024], AF.Exp,
                         reads=bsp, writes=[bpt])
                if d >= 0:
                    self.vop("tensor_mul", pt[:, d:d + 128], pt[:, d:d + 128], self.ltmask[:],
                             reads=[bpt, self.b_lt], writes=[bpt])
                if kc > 0:
                    emit_z(kc - 1)
            emit_z(kmax - 1)
            self.vop("reciprocal", self.misc[64:65, 3072:4096],
                     self.pM[64:65, 0:1024], reads=zb, writes=[self.b_misc[3]])
            self.dma(self.misc[0:1, 3072:4096], self.misc[64:65, 3072:4096],
                     reads=[self.b_misc[3]], writes=[self.b_misc[3]])
            self.trans([(self.pT1[:, tq:tq + 1],
                         self.misc[0:1, 3072 + tq * 128:3072 + (tq + 1) * 128])
                        for tq in range(8)],
                       reads=(self.b_misc[3],), writes=[self.b_pT[0]], bf=False)
            self.vop("tensor_copy", self.reccol[:, qb * 8:(qb + 1) * 8],
                     self.pT1[:, 0:8], reads=[self.b_pT[0]], writes=[self.b_reccol])
            self.vop("tensor_copy", ZF[:, qb * 1024:(qb + 1) * 1024], self.pM[0:64, 0:1024],
                     reads=zb, writes=bZ)
        self.dma(self.pool[0:1, 2, :], self.bo_d.rearrange("(one d) -> one d", one=1),
                 reads=[self.b_pool[2]], writes=[self.b_pool[2]])
        self.vop("tensor_copy", self.poolb[0:1, 0, 0:1024], self.pool[0:1, 2, :],
                 reads=[self.b_pool[2]], writes=[self.b_pool[0]])
        self.vop("tensor_sub", self.poolb[0:1, 1, 0:1024], self.pool[0:1, 2, :],
                 self.poolb[0:1, 0, 0:1024],
                 reads=[self.b_pool[2], self.b_pool[0]], writes=[self.b_pool[1]])
        mms = []
        for j in range(2):
            for hl in range(2):
                src = self.poolb[0:1, hl, j * 512:(j + 1) * 512]
                mms.append((self.pL[:, j * 512:(j + 1) * 512],
                            self.onesb[0:1, 0:128], src))
        def fnb(E, mms=mms):
            last = None
            for i, (o, l, r) in enumerate(mms):
                last = E.matmul(o, l, r, start=(i % 2 == 0), stop=(i % 2 == 1))
            return last
        self.op("tensor", fnb,
                reads=[self.b_pool[0], self.b_pool[1], self.b_ones],
                writes=self.pLh(0))
        self.act(self.misc[:, 3072:4096], self.pL[:, 0:1024], AF.Copy,
                 reads=self.pLh(0), writes=[self.b_misc[3]])
        for t in range(TT):
            pp = t % 2
            stok = self.misc[:, pp * 1024:(pp + 1) * 1024]
            bst = self.b_misc[pp]
            for half in range(2):
                ps = self.pL[:, (t * 2 + half) % 2 * 1024:((t * 2 + half) % 2) * 1024 + 512]
                bps = self.b_pL[2 * ((t * 2 + half) % 2)]
                self.mm([(ps, ZF[:, t * 128:(t + 1) * 128], self.wo_sb(half))],
                        reads=bZ + self.bB(10240, 12800), writes=[bps])
                self.op("vector",
                        lambda E, o=stok[:, half * 512:(half + 1) * 512], i0=ps,
                               sc=self.reccol[:, t:t + 1],
                               i1=self.misc[:, 3072 + half * 512:3072 + (half + 1) * 512]:
                        E.scalar_tensor_tensor(o, i0, sc, i1, ALU.mult, ALU.add),
                        reads=[bps, self.b_reccol, self.b_misc[3]], writes=[bst])
            self.dma(self.s_spill[b, t * 128:(t + 1) * 128, :], stok,
                     reads=[bst, self.b_sspill], writes=[])
            if "s" in self.dmp:
                self.dma(self.dmp["s"][b, t * 128:(t + 1) * 128, :], stok,
                         reads=[bst], writes=[])

    def _attn_norms(self, QT, KT, QSQ, bQ, bK, bQs):
        W = QT.shape[1]
        nj = W // 512
        row = lambda s: self.poolb[0:1, s, 0:W]
        for (tile, bt, dst_s) in ((QT, bQ, 1), (KT, bK, 2)):
            self.vop("tensor_mul", QSQ[:, 0:W], tile[0:64, :], tile[0:64, :],
                     reads=bt, writes=bQs)
            def fnmm(E, QSQ=QSQ, nj=nj):
                last = None
                for j in range(nj):
                    last = E.matmul(self.pL[0:1, j * 512:(j + 1) * 512],
                                    self.onesb[0:64, 0:1],
                                    QSQ[:, j * 512:(j + 1) * 512],
                                    start=True, stop=True)
                return last
            self.op("tensor", fnmm, reads=tuple(bQs) + (self.b_ones,),
                    writes=self.pLh(0) + self.pLh(1))
            self.vop("tensor_copy", row(2), self.pL[0:1, 0:W],
                     reads=self.pLh(0) + self.pLh(1), writes=[self.b_pool[2]])
            if dst_s == 1:
                self.act(row(1), row(2), AF.Sqrt,
                         reads=[self.b_pool[2]], writes=[self.b_pool[1]])
        self.vop("reduce_max", self.maxsc[0:1, 0:1], row(2), axis=AX.X,
                 reads=[self.b_pool[2]], writes=[self.b_params])
        self.act(self.maxsc[0:1, 1:2], self.maxsc[0:1, 0:1], AF.Sqrt,
                 reads=[self.b_params], writes=[self.b_params])
        self.vop("tensor_scalar", row(0), row(1),
                 self.maxsc[0:1, 1:2], -1.0, ALU.mult, ALU.mult,
                 reads=[self.b_pool[1], self.b_params], writes=[self.b_pool[0]])
        self.dma(QT[64:65, :], row(0), reads=[self.b_pool[0]], writes=bQ)

    def phase_C(self, b):
        for t in range(TT):
            self._stats_collect(t, self.xn_tok[:, t, :], self.b_xn[t])
        self._stats_finalize()
        pend = None
        for t in range(TT + 1):
            if t < TT:
                pp = t % 2
                xt = self.xn_tok[:, t, :]
                u = self.pool[:, pp, :]
                self.vop("tensor_scalar", u, xt,
                         self.statA[:, t, 2:3], self.statA[:, t, 3:4],
                         ALU.mult, ALU.subtract,
                         reads=[self.b_xn[t], self.b_stsum], writes=[self.b_pool[pp]])
                xc = self.hg[:, pp * 1024:(pp + 1) * 1024]
                eng = "vop" if b == 0 else "gop"
                getattr(self, eng)("tensor_copy", xc, u, reads=[self.b_pool[pp]],
                                   writes=[self.b_hg[pp]])
                pt = (self.pT1b, self.pT2b)[pp]
                self.trans([(pt[:, c * 128:(c + 1) * 128], xc[:, c * 128:(c + 1) * 128])
                            for c in range(NC)],
                           reads=(self.b_hg[pp],), writes=[self.b_pT[pp]])
            if pend is not None:
                tp_ = pend
                pp_ = tp_ % 2
                pt_ = (self.pT1b, self.pT2b)[pp_]
                for c in range(NC):
                    dst = self.ln_fm[:, c, tp_ * 128:(tp_ + 1) * 128]
                    src = pt_[:, c * 128:(c + 1) * 128]
                    if c % 2 == 0:
                        self.act(dst, src, AF.Identity,
                                 scale=self.gam1[:, c:c + 1],
                                 bias=self.bet1[:, c:c + 1],
                                 reads=[self.b_pT[pp_], self.b_params],
                                 writes=[self.b_lnfm[tp_]])
                    else:
                        self.vop("tensor_scalar", dst, src,
                                 self.gam1[:, c:c + 1], self.bet1[:, c:c + 1],
                                 ALU.mult, ALU.add,
                                 reads=[self.b_pT[pp_], self.b_params],
                                 writes=[self.b_lnfm[tp_]])
            pend = t if t < TT else None
        QTg = lambda i: self.bigB[0:65, i * 2048:(i + 1) * 2048]
        KTg = lambda i: self.bigB[0:65, 4096 + i * 2048:4096 + (i + 1) * 2048]
        VG = self.bigB[:, 8192:10272].rearrange("p (h t k) -> p h t k", h=2, k=65)
        VTW = self.bigB[:, 10272:10784]
        QSQ = self.bigB[0:64, 10784:12832]
        qkvsl = lambda i: self.bigB[:, 12832 + i * 1024:12832 + (i + 1) * 1024].rearrange(
            "p (c k) -> p c k", k=128)
        bQg = lambda i: self.bB(i * 2048, (i + 1) * 2048)
        bKg = lambda i: self.bB(4096 + i * 2048, 4096 + (i + 1) * 2048)
        bVG = self.bB(8192, 10272)
        bVTW = self.bB(10272, 10784)
        bQs = self.bB(10784, 12832)
        bSL = self.bB(12832, 15904)
        self.vop("memset", self.bigB[64:128, 0:4096], 0.0,
                 reads=bQg(0) + bQg(1), writes=bQg(0) + bQg(1))
        self.vop("memset", self.bigB[64:128, 4096:8192], 0.0,
                 reads=bKg(0) + bKg(1), writes=bKg(0) + bKg(1))
        for i in range(2):
            self.vop("memset", KTg(i)[64:65, :], 1.0, reads=bKg(i), writes=bKg(i))
            self.vop("memset", VG[:, i, :, 64:65], 1.0, reads=bVG, writes=bVG)
        wqkv_r = self.wqkv_bf.rearrange("(c p) n -> p c n", p=128)
        lnfm_all = list(self.b_lnfm)
        for g2 in range(8):
            for i in range(3):
                cs = i * 1024 + g2 * 128
                self.dma(qkvsl(i)[:], wqkv_r[:, :, cs:cs + 128],
                         reads=bSL, writes=bSL + [self.b_wbf])
            for i in (0, 1, "norms", 2):
                if i == "norms":
                    for h01 in range(2):
                        self._attn_norms(QTg(h01), KTg(h01), QSQ,
                                         bQg(h01), bKg(h01), bQs)
                    continue
                for cg in range(4):
                    pp = (i * 4 + cg) % 2
                    ps = (self.pT1, self.pT2)[pp]
                    bps = self.b_pT[pp]
                    self.mm([(ps[:], qkvsl(i)[:, c, :],
                              self.ln_fm[:, c, cg * 512:(cg + 1) * 512])
                             for c in range(NC)],
                            reads=bSL + lnfm_all, writes=[bps])
                    if i == 0:
                        for h01 in range(2):
                            self.act(QTg(h01)[0:64, cg * 512:(cg + 1) * 512],
                                     ps[h01 * 64:(h01 + 1) * 64, :], AF.Copy, scale=0.125,
                                     reads=[bps], writes=bQg(h01))
                    elif i == 1:
                        for h01 in range(2):
                            self.vop("tensor_copy",
                                     KTg(h01)[0:64, cg * 512:(cg + 1) * 512],
                                     ps[h01 * 64:(h01 + 1) * 64, :],
                                     reads=[bps], writes=bKg(h01))
                    else:
                        self.vop("tensor_copy", VTW[:], ps[:],
                                 reads=[bps], writes=bVTW)
                        for h01 in range(2):
                            for t4 in range(4):
                                tg = cg * 4 + t4
                                po = (self.pT1b, self.pT2b)[1 - pp]
                                self.trans([(po[:, 960:1024],
                                             VTW[h01 * 64:(h01 + 1) * 64,
                                                 t4 * 128:(t4 + 1) * 128],
                                             h01 * 64)],
                                           reads=tuple(bVTW), writes=[self.b_pT[1 - pp]])
                                self.vop("tensor_copy", VG[:, h01, tg, 0:64],
                                         po[:, 960:1024],
                                         reads=[self.b_pT[1 - pp]], writes=bVG)
            pending = [None]

            def make_evac(h01, qb, par, ztile, zb, g2=g2):
                def evA():
                    rsl = (h01 + qb) % 2
                    lnrow = self.misc[64:65, rsl * 1024:(rsl + 1) * 1024]
                    if par == 0:
                        self.vop("reciprocal", lnrow, self.pM[64:65, 0:1024],
                                 reads=zb, writes=[self.b_misc[rsl]])
                    else:
                        self.vop("reciprocal", lnrow[:, 0:512],
                                 self.pT1[64:65, 0:512],
                                 reads=zb, writes=[self.b_misc[rsl]])
                        self.vop("reciprocal", lnrow[:, 512:1024],
                                 self.pT2[64:65, 0:512],
                                 reads=zb, writes=[self.b_misc[rsl]])
                    recrow = self.poolb[64:65, 2, rsl * 1024:(rsl + 1) * 1024]
                    eng = "vop" if b == 0 else "gop"
                    getattr(self, eng)("tensor_copy", recrow, lnrow,
                                       reads=[self.b_misc[rsl]],
                                       writes=[self.b_pool[2]])

                def ev():
                    rsl = (h01 + qb) % 2
                    recrow = self.poolb[64:65, 2, rsl * 1024:(rsl + 1) * 1024]
                    def fnb(E, recrow=recrow, par=par):
                        last = None
                        for j in range(2):
                            odst = self.pL[0:64, par * 1024 + j * 512:
                                           par * 1024 + (j + 1) * 512]
                            last = E.matmul(
                                odst, self.onesb[64:65, 0:64],
                                recrow[:, j * 512:(j + 1) * 512],
                                start=True, stop=True)
                        return last
                    self.op("tensor", fnb,
                            reads=[self.b_pool[2], self.b_ones],
                            writes=self.pLh(par))
                    rb = self.misc[0:64, rsl * 1024:(rsl + 1) * 1024]
                    self.vop("tensor_copy", rb,
                             self.pL[0:64, par * 1024:par * 1024 + 1024],
                             reads=self.pLh(par), writes=[self.b_misc[rsl]])
                    ydst = self.bigA[h01 * 64:(h01 + 1) * 64,
                                     g2 * 2048 + qb * 1024:g2 * 2048 + (qb + 1) * 1024]
                    if par == 0:
                        self.vop("tensor_mul", ydst, self.pM[0:64, 0:1024], rb,
                                 reads=zb + [self.b_misc[rsl]],
                                 writes=[self.b_bigA[g2]])
                    else:
                        self.vop("tensor_mul", ydst[:, 0:512], self.pT1[0:64, 0:512],
                                 rb[:, 0:512],
                                 reads=zb + [self.b_misc[rsl]],
                                 writes=[self.b_bigA[g2]])
                        self.vop("tensor_mul", ydst[:, 512:1024], self.pT2[0:64, 0:512],
                                 rb[:, 512:1024],
                                 reads=zb + [self.b_misc[rsl]],
                                 writes=[self.b_bigA[g2]])
                return evA, ev

            for li, (h01, qb) in enumerate(((0, 0), (0, 1), (1, 0), (1, 1))):
                QT = self.bigB[0:128, h01 * 2048:(h01 + 1) * 2048]
                KT = self.bigB[0:128, 4096 + h01 * 2048:4096 + (h01 + 1) * 2048]
                bQ, bK = bQg(h01), bKg(h01)
                par = li % 2
                if par == 0:
                    ztile = lambda j: self.pM[:, j * 512:(j + 1) * 512]
                    zb = [self.b_pM[0], self.b_pM[1]]
                else:
                    ztile = lambda j: (self.pT1, self.pT2)[j][:, 0:512]
                    zb = [self.b_pT[0], self.b_pT[1]]

                def emit_z(kc, zb=zb, h01=h01, ztile=ztile):
                    pt = self.miscb[:, 4096 + (kc % 3) * 1024:4096 + (kc % 3 + 1) * 1024]
                    vwide = self.bigB[:, 8192 + h01 * 1040 + kc * 65:
                                      8192 + h01 * 1040 + kc * 65 + 128]
                    self.mmx([(ztile(j),
                               vwide, pt[:, j * 512:(j + 1) * 512],
                               kc == 0, kc == 15) for j in range(2)],
                             reads=bVG + self.bB(8192, 10784) + [self.b_ptile[kc % 3]],
                             writes=zb)

                for kc in range(16):
                    pt = self.miscb[:, 4096 + (kc % 3) * 1024:4096 + (kc % 3 + 1) * 1024]
                    bpt = self.b_ptile[kc % 3]
                    wpt = [bpt, self.b_misc[3]] if kc % 3 == 2 else [bpt]
                    for j in range(2):
                        q = (2 * kc + j) % 4
                        sp = self.pL[:, q * 512:(q + 1) * 512]
                        self.mmx([(sp,
                                   KT[:, kc * 128:(kc + 1) * 128],
                                   QT[:, qb * 1024 + j * 512:qb * 1024 + (j + 1) * 512],
                                   True, True)],
                                 reads=bQ + bK, writes=[self.b_pL[q]])
                        self.act(pt[:, j * 512:(j + 1) * 512], sp, AF.Exp,
                                 reads=[self.b_pL[q]], writes=wpt)
                    if kc > 1:
                        emit_z(kc - 2)
                    if kc == 1 and pending[0] is not None:
                        pending[0][0]()
                    if kc == 3 and pending[0] is not None:
                        pending[0][1]()
                        pending[0] = None
                emit_z(14)
                emit_z(15)
                ev = make_evac(h01, qb, par, ztile, zb)
                if li < 3:
                    pending[0] = ev
                else:
                    ev[0]()
                    ev[1]()
        wps = self.bigB[:, 0:8192].rearrange("p (c n) -> p c n", n=1024)
        bWP = self.bB(0, 8192)
        self.dma(wps[:], self.wproj_bf.rearrange("(c p) n -> p c n", p=128),
                 reads=bWP, writes=bWP + [self.b_wbf])
        yall = list(self.b_bigA)
        for t in range(TT):
            for half in range(2):
                pp = (t * 2 + half) % 2
                ps = (self.pT1, self.pT2)[pp]
                bps = self.b_pT[pp]
                self.mm([(ps[:], self.bigA[:, c * 2048 + t * 128:c * 2048 + (t + 1) * 128],
                          wps[:, c, half * 512:(half + 1) * 512])
                         for c in range(NC)],
                        reads=yall + bWP, writes=[bps])
                self.vop("tensor_add", self.xn_tok[:, t, half * 512:(half + 1) * 512],
                         self.xn_tok[:, t, half * 512:(half + 1) * 512], ps[:],
                         reads=[bps, self.b_xn[t]], writes=[self.b_xn[t]])
            if "xn2" in self.dmp:
                self.dma(self.dmp["xn2"][b, t * 128:(t + 1) * 128, :],
                         self.xn_tok[:, t, :], reads=[self.b_xn[t]], writes=[])

    def phase_D(self, b):
        for t in range(TT):
            self._stats_collect(t, self.xn_tok[:, t, :], self.b_xn[t])
        self._stats_finalize()
        for t in range(TT):
            pp = t % 2
            xt = self.xn_tok[:, t, :]
            u = self.pool[:, pp, :]
            self.vop("tensor_scalar", u, xt,
                     self.statA[:, t, 2:3], self.statA[:, t, 3:4],
                     ALU.mult, ALU.subtract,
                     reads=[self.b_xn[t], self.b_stsum], writes=[self.b_pool[pp]])
            uhi = self.hg[:, 0:1024]
            ulo = self.hg[:, 1024:2048]
            self.gop("tensor_copy", uhi, u, reads=[self.b_pool[pp]], writes=[self.b_hg[0]])
            self.gop("tensor_sub", ulo, u, uhi,
                     reads=[self.b_pool[pp], self.b_hg[0]], writes=[self.b_hg[1]])
            self.trans([(self.pT1b[:, c * 128:(c + 1) * 128], uhi[:, c * 128:(c + 1) * 128])
                        for c in range(NC)],
                       reads=(self.b_hg[0],), writes=[self.b_pT[0]])
            self.trans([(self.pT2b[:, c * 128:(c + 1) * 128], ulo[:, c * 128:(c + 1) * 128])
                        for c in range(NC)],
                       reads=(self.b_hg[1],), writes=[self.b_pT[1]])
            for c in range(NC):
                dst = self.ln_fm[:, c, t * 128:(t + 1) * 128]
                src = self.pT1b[:, c * 128:(c + 1) * 128]
                if c % 2 == 0:
                    self.act(dst, src, AF.Identity,
                             scale=self.gam2[:, c:c + 1], bias=self.bet2[:, c:c + 1],
                             reads=[self.b_pT[0], self.b_params],
                             writes=[self.b_lnfm[t]])
                else:
                    self.vop("tensor_scalar", dst, src,
                             self.gam2[:, c:c + 1], self.bet2[:, c:c + 1],
                             ALU.mult, ALU.add,
                             reads=[self.b_pT[0], self.b_params],
                             writes=[self.b_lnfm[t]])
            uhf = self.hg[:, 2048:3072]
            ulf = self.hg[:, 3072:4096]
            self.vop("tensor_copy", uhf, self.pT1b[:, 0:1024],
                     reads=[self.b_pT[0]], writes=[self.b_hg[2]])
            self.act(ulf, self.pT2b[:, 0:1024], AF.Copy,
                     reads=[self.b_pT[1]], writes=[self.b_hg[3]])
            X = (t % 2) * 1024 + ((t // 2) % 8) * 128
            lg = self.pL[0:NE, X:X + 128]
            blg = self.b_pL[X // 512]
            mms = []
            for c in range(NC):
                mms.append((lg, self.gwg_hi[:, c, :], uhf[:, c * 128:(c + 1) * 128]))
                mms.append((lg, self.gwg_lo[:, c, :], uhf[:, c * 128:(c + 1) * 128]))
                mms.append((lg, self.gwg_hi[:, c, :], ulf[:, c * 128:(c + 1) * 128]))
            self.mm(mms, reads=[self.b_hg[2], self.b_hg[3], self.b_gw], writes=[blg])
            lgt_sl = self.stage[0:NE, 0:128]
            self.act(lgt_sl, lg, AF.Identity, bias=self.gbadd[:],
                     reads=[blg, self.b_params], writes=[self.b_stage[0]])
            po = (self.pT1, self.pT2)[t % 2]
            self.trans([(po[:, 0:NE], lgt_sl)],
                       reads=(self.b_stage[0],), writes=[self.b_pT[t % 2]], bf=False)
            self.vop("tensor_copy", self.lgt_tok[:, t, :], po[:, 0:NE],
                     reads=[self.b_pT[t % 2]], writes=[self.b_masks])
            self.vop("reduce_max", self.statA[:, t, 0:1], self.lgt_tok[:, t, :], axis=AX.X,
                     reads=[self.b_masks], writes=[self.b_stsq])
            self.vop("tensor_scalar", self.masks[:, b, t, :], self.lgt_tok[:, t, :],
                     self.statA[:, t, 0:1], None, ALU.is_ge,
                     reads=[self.b_masks, self.b_stsq], writes=[self.b_masks])
            self.trans([(self.pT1b[:, c * 128:(c + 1) * 128],
                         self.ln_fm[:, c, t * 128:(t + 1) * 128])
                        for c in range(NC)],
                       reads=(self.b_lnfm[t],), writes=[self.b_pT[0]])
            us = self.stageb[:, 2048 + (t % 2) * 1024:3072 + (t % 2) * 1024]
            self.vop("tensor_copy", us, self.pT1b[:, 0:1024],
                     reads=[self.b_pT[0], self.b_ust[t % 2]],
                     writes=[self.b_ust[t % 2], self.b_stage[1]])
            self.dma(self.u_buf[b * T + t * 128:b * T + (t + 1) * 128, :], us,
                     reads=[self.b_ust[t % 2], self.b_stage[1], self.b_ubuf],
                     writes=[])
            if "lgt" in self.dmp:
                self.dma(self.dmp["lgt"][b, t * 128:(t + 1) * 128, 0:NE],
                         self.lgt_tok[:, t, :], reads=[self.b_masks], writes=[])
                self.dma(self.dmp["lgt"][b, t * 128:(t + 1) * 128, NE:2 * NE],
                         self.masks[:, b, t, :], reads=[self.b_masks], writes=[])

    def index_build(self):
        bi = self.b_idx
        ic = self.b_icons
        NT = NB * TT
        allm = [self.b_misc[0], self.b_misc[1], self.b_misc[2], self.b_misc[3]]
        self.dma(self.ltf, self.lt_dram.ap(), reads=[], writes=[bi] + allm)
        self.dma(self.thr16, self.thr16_dram.ap(), reads=[], writes=[bi])
        self.dma(self.thrp, self.thrp_dram.ap(), reads=[], writes=[bi])
        self.dma(self.evalsf, self.evals_dram.ap(), reads=[], writes=[bi])
        self.gop("iota", self.tid32, [[128, NB * TT]], base=0,
                 channel_multiplier=1, reads=[], writes=[bi])
        self.gop("iota", self.pcol, [[0, 1]], base=0, channel_multiplier=1,
                 allow_small_or_imprecise_dtypes=True, reads=[], writes=[bi])
        ntv = self.misc[:, 640:640 + NB * TT].rearrange("p (a b) -> p a b", b=TT)
        self.vop("memset", ntv, 1.0, reads=[bi], writes=[bi])
        for e in range(NE):
            self.vop("tensor_mul", self.masks[:, :, :, e], self.masks[:, :, :, e],
                     ntv, reads=[bi, self.b_masks], writes=[self.b_masks])
            if e < NE - 1:
                self.vop("tensor_sub", ntv, ntv, self.masks[:, :, :, e],
                         reads=[self.b_masks, bi], writes=[bi])
        mms = [(self.pT1[0:1, bt * NE:(bt + 1) * NE], self.onescol,
                self.masks[:, bt // TT, bt % TT, :], True, True)
               for bt in range(NT)]
        self.mmx(mms, reads=[self.b_masks, bi], writes=[self.b_pT[0]])
        self.vop("tensor_copy", self.csum, self.pT1[0:1, 0:NT * NE],
                 reads=[self.b_pT[0]], writes=[bi])
        self.dma(self.iscr.rearrange("(one n) -> one n", one=1), self.csum,
                 reads=[bi], writes=[bi])
        self.dma(self.csum32, self.iscr.rearrange("(a b) -> a b", b=NE),
                 reads=[bi], writes=[bi])
        self.mm([(self.pT2[0:NT, 0:NE], self.ltf[0:NT, 0:NT], self.csum32)],
                reads=[bi, ic], writes=[self.b_pT[1]])
        self.vop("tensor_sub", self.carry32, self.pT2[0:NT, 0:NE], self.csum32,
                 reads=[self.b_pT[1], bi], writes=[bi])
        self.mm([(self.pT2[0:1, 128:128 + NE], self.ltf[0:NT, 127:128],
                  self.csum32)],
                reads=[bi], writes=[self.b_pT[1]])
        self.vop("tensor_copy", self.srow[0:1, 0:NE], self.pT2[0:1, 128:128 + NE],
                 reads=[self.b_pT[1]], writes=[bi])
        self.dma(self.iscr.rearrange("(a b) -> a b", b=NE), self.carry32,
                 reads=[bi], writes=[bi])
        self.dma(self.csum, self.iscr.rearrange("(one n) -> one n", one=1),
                 reads=[bi], writes=[bi])
        self.trans([(self.pT1[0:NE, 0:1], self.srow[0:1, 0:NE])], reads=(bi,),
                   writes=[self.b_pT[0]], bf=False)
        self.vop("tensor_copy", self.icol[:, 0:1], self.pT1[0:NE, 0:1],
                 reads=[self.b_pT[0]], writes=[bi])
        self.vop("tensor_scalar", self.irow[:, 0:16], self.thr16, self.icol[:, 0:1],
                 None, ALU.is_lt, reads=[bi, ic], writes=[bi])
        self.vop("reduce_sum", self.icol[:, 1:2], self.irow[:, 0:16], axis=AX.X,
                 reads=[bi], writes=[bi])
        self.vop("tensor_scalar", self.icol[:, 1:2], self.icol[:, 1:2],
                 float(PASS_TOK), None, ALU.mult, reads=[bi], writes=[bi])
        self.mm([(self.pT1[0:NE, 4:5], self.ltf[0:NE, 0:NE], self.icol[:, 1:2])],
                reads=[bi, ic], writes=[self.b_pT[0]])
        self.vop("tensor_sub", self.icol[:, 2:3], self.pT1[0:NE, 4:5], self.icol[:, 1:2],
                 reads=[self.b_pT[0], bi], writes=[bi])
        self.vop("tensor_add", self.icol[:, 3:4], self.icol[:, 2:3], self.icol[:, 1:2],
                 reads=[bi], writes=[bi])
        self.trans([(self.pT2[0:1, 8:16], self.icol[:, 2:3])], reads=(bi,),
                   writes=[self.b_pT[1]], bf=False)
        self.vop("tensor_copy", self.soffr, self.pT2[0:1, 8:16],
                 reads=[self.b_pT[1]], writes=[bi])
        self.vop("tensor_scalar", self.irow[:, 16:16 + NPASS], self.thrp,
                 self.icol[:, 2:3], None, ALU.is_ge, reads=[bi, ic], writes=[bi])
        self.vop("tensor_scalar", self.irow[:, 40:40 + NPASS], self.thrp,
                 self.icol[:, 3:4], None, ALU.is_lt, reads=[bi, ic], writes=[bi])
        self.vop("tensor_mul", self.irow[:, 16:16 + NPASS], self.irow[:, 16:16 + NPASS],
                 self.irow[:, 40:40 + NPASS], reads=[bi], writes=[bi])
        self.mm([(self.pT1[0:1, 16:16 + NPASS], self.evalsf,
                  self.irow[:, 16:16 + NPASS])],
                reads=[bi, ic], writes=[self.b_pT[0]])
        self.vop("tensor_scalar", self.srow[0:1, 8:8 + NPASS],
                 self.pT1[0:1, 16:16 + NPASS], 512.0, None, ALU.mult,
                 reads=[self.b_pT[0]], writes=[bi])
        self.mm([(self.pM[0:128, 0:NPASS], self.onesrow,
                  self.srow[0:1, 8:8 + NPASS])],
                reads=[bi, ic], writes=[self.b_pM[0]])
        self.vop("tensor_scalar", self.idxf[:, 0:NPASS], self.pM[0:128, 0:NPASS], self.pcol,
                 None, ALU.add, reads=[self.b_pM[0], ic], writes=[bi])
        self.vop("tensor_copy", self.idxs[:, 0:NPASS], self.idxf[:, 0:NPASS],
                 reads=[bi], writes=[self.b_widx])
        self._e_wload(0, 0)
        self._e_wload(0, 1)
        b_it = [Buf(), Buf()]
        b_ip = [self.b_pM[1], self.b_pT[0]]
        for bt in range(NT):
            sl = bt % 2
            m_bt = self.masks[:, bt // TT, bt % TT, :]
            ps = (self.pM[:, 512:512 + NE], self.pT1[:, 96:96 + NE])[sl]
            mms = [(ps, self.ltf, m_bt, True, False),
                   (ps, self.onesrow,
                    self.csum[0:1, bt * NE:(bt + 1) * NE], False, False),
                   (ps, self.onesrow, self.soffr, False, True)]
            self.mmx(mms, reads=[self.b_masks, bi, ic], writes=[b_ip[sl]])
            scr = self.idxf[:, sl * NE:(sl + 1) * NE]
            self.vop("tensor_sub", scr, ps, m_bt,
                     reads=[b_ip[sl], self.b_masks, self.b_widx, b_it[sl]],
                     writes=[b_it[sl]])
            self.vop("tensor_mul", scr, scr, m_bt,
                     reads=[b_it[sl], self.b_masks], writes=[b_it[sl]])
            self.vop("reduce_sum", self.posf[:, sl:sl + 1], scr, axis=AX.X,
                     reads=[b_it[sl]], writes=[b_it[sl]])
            self.vop("tensor_copy", self.idxs[:, 16 + sl:17 + sl],
                     self.posf[:, sl:sl + 1], reads=[b_it[sl]], writes=[b_it[sl]])
            for dst, bdst in ((self.gidx_buf.rearrange("(a b) -> a b", b=1),
                               self.b_gidxb),
                              (self.sidx_buf.rearrange("(a b) -> a b", b=1),
                               self.b_sidxb)):
                def fn(E, o=dst, ix=self.idxs[:, 16 + sl:17 + sl],
                       i=self.tid32[:, bt:bt + 1]):
                    return E.indirect_dma_start(
                        out=o, out_offset=IOA(ap=ix, axis=0), in_=i, in_offset=None)
                self.S.gma(fn, reads=[b_it[sl], ic, bdst], writes=[])
        self.vop("memset", self.misc[0:1, 0:1], 0.0,
                 reads=[], writes=[bi, b_it[0], b_it[1]] + allm)

    def _utok(self):
        return self.ln_fm[:, 0:2, :].rearrange("p a n -> p (a n)")

    def _ufm(self):
        return self.ln_fm[:, 2:4, :].rearrange("p a (b n) -> p (a b) n", n=PASS_TOK)

    def _yacc(self):
        return self.lnfm_f32[:, 4:8, :]

    def _gsl(self, s):
        return self.idxs[:, 18 + 4 * s:22 + 4 * s]

    def _ssl(self, s):
        return self.idxs[:, 26 + 4 * s:30 + 4 * s]

    def _e_gload(self, ps):
        sl = ps % 2
        c0 = ps * PASS_TOK
        for c in range(4):
            if not self._gidx_claimed:
                self._gidx_claimed = True
                wg, rg = [self.b_gidxb], []
                ws, rs_ = [self.b_sidxb], []
            else:
                wg, rg = [], [self.b_gidxb]
                ws, rs_ = [], [self.b_sidxb]
            self.dma(self._gsl(sl)[:, c:c + 1],
                     self.gidx_buf[c0 + c * 128:c0 + (c + 1) * 128].rearrange(
                         "(p one) -> p one", one=1),
                     reads=[self.b_gsb[sl]] + rg, writes=[self.b_gsb[sl]] + wg)
            self.dma(self._ssl(sl)[:, c:c + 1],
                     self.sidx_buf[c0 + c * 128:c0 + (c + 1) * 128].rearrange(
                         "(p one) -> p one", one=1),
                     reads=[self.b_gsb[sl]] + rs_, writes=[self.b_gsb[sl]] + ws)

    def _e_ugather(self, ps):
        sl = ps % 2
        ut = self._utok()
        for c in range(4):
            dst = ut[:, c * 1024:(c + 1) * 1024]
            def fn(E, o=dst, ix=self._gsl(sl)[:, c:c + 1]):
                return E.indirect_dma_start(out=o, out_offset=None, in_=self.u_buf,
                                            in_offset=IOA(ap=ix, axis=0))
            rd = [self.b_gsb[sl], self.b_utok[0]]
            if ps < 1:
                rd += list(self.b_lnfm)
            if not self._ubuf_claimed:
                self._ubuf_claimed = True
                self.S.gma(fn, reads=rd, writes=[self.b_utok[0], self.b_ubuf])
            else:
                self.S.gma(fn, reads=rd + [self.b_ubuf],
                           writes=[self.b_utok[0]])

    def _e_wload(self, ps, hb):
        wsl = hb % 2
        eo = hb * 128 * 8192
        for (big, bufs, src) in ((self.bigA, self.bA, self.w1bf),
                                 (self.bigB, self.bB, self.w2bf)):
            for h2 in range(2):
                dst = big[:, wsl * 8192 + h2 * 4096:wsl * 8192 + (h2 + 1) * 4096]
                def fn(E, o=dst, s=src, ix=self.idxs[:, ps:ps + 1],
                       eo=eo + h2 * 4096):
                    return E.indirect_dma_start(out=o, out_offset=None, in_=s,
                                                in_offset=IOA(ap=ix, axis=0),
                                                element_offset=eo)
                bl = bufs(wsl * 8192, (wsl + 1) * 8192)
                if not self._wcvt_claimed:
                    self._wcvt_claimed = True
                    self.S.gma(fn, reads=[self.b_widx] + bl,
                               writes=bl + [self.b_wcvt])
                else:
                    self.S.gma(fn, reads=[self.b_widx, self.b_wcvt] + bl,
                               writes=bl)

    def phase_E_sparse(self):
        W1q = lambda s: self.bigA[:, s * 8192:(s + 1) * 8192].rearrange(
            "p (c n) -> p c n", n=1024)
        W2q = lambda s: self.bigB[:, s * 8192:(s + 1) * 8192].rearrange(
            "p (c n) -> p c n", n=1024)
        bW1 = lambda s: self.bA(s * 8192, (s + 1) * 8192)
        bW2 = lambda s: self.bB(s * 8192, (s + 1) * 8192)
        hgv = self.hg.rearrange("p (c n) -> p c n", n=PASS_TOK)
        ya = self._yacc()
        self._e_gload(0)
        self._e_ugather(0)
        for ps in range(NPASS):
            sl = ps % 2
            if ps + 1 < NPASS:
                self._e_gload(ps + 1)
            ut = self._utok()
            fence = list(self.b_lnfm) if ps < 1 else []
            uf = self._ufm()
            trs0, trs1, trs2 = [], [], []
            for c2 in range(4):
                for fc in range(8):
                    src_t = ut[:, c2 * 1024 + fc * 128:c2 * 1024 + (fc + 1) * 128]
                    if c2 < 2:
                        trs0.append((self.pMb[:, (c2 * 8 + fc) * 128:
                                              (c2 * 8 + fc + 1) * 128], src_t))
                    elif c2 == 2:
                        trs1.append((self.pT1b[:, fc * 128:(fc + 1) * 128], src_t))
                    else:
                        trs2.append((self.pT2b[:, fc * 128:(fc + 1) * 128], src_t))
            self.trans(trs0, reads=(self.b_utok[0],),
                       writes=[self.b_pM[0], self.b_pM[1]])
            self.trans(trs1, reads=(self.b_utok[0],), writes=[self.b_pT[0]])
            self.trans(trs2, reads=(self.b_utok[0],), writes=[self.b_pT[1]])
            self.vop("tensor_copy", uf[:, :, 0:128],
                     self.pMb[:, 0:1024].rearrange("p (c n) -> p c n", n=128),
                     reads=[self.b_pM[0], self.b_pM[1], self.b_ufm[0]] + fence,
                     writes=[self.b_ufm[0]])
            self.vop("tensor_copy", uf[:, :, 128:256],
                     self.pMb[:, 1024:2048].rearrange("p (c n) -> p c n", n=128),
                     reads=[self.b_pM[0], self.b_pM[1], self.b_ufm[0]],
                     writes=[self.b_ufm[0]])
            self.vop("tensor_copy", uf[:, :, 256:384],
                     self.pT1b[:, 0:1024].rearrange("p (c n) -> p c n", n=128),
                     reads=[self.b_pT[0], self.b_ufm[0]], writes=[self.b_ufm[0]])
            self.vop("tensor_copy", uf[:, :, 384:512],
                     self.pT2b[:, 0:1024].rearrange("p (c n) -> p c n", n=128),
                     reads=[self.b_pT[1], self.b_ufm[0]], writes=[self.b_ufm[0]])
            for hb in range(4):
                wsl = hb % 2
                nps, nhb = (ps, hb + 1) if hb < 3 else (ps + 1, 0)
                if nps < NPASS and not (nps == 0 and nhb <= 1):
                    self._e_wload(nps, nhb)
                if hb == 2 and ps + 1 < NPASS:
                    self._e_ugather(ps + 1)
                bh = list(self.b_hg)
                for rnd in range(4):
                    pr = (rnd % 2) * 1024
                    mms = []
                    for hcq in range(2):
                        hc = rnd * 2 + hcq
                        for fc in range(NC):
                            mms.append((self.pL[:, pr + hcq * 512:
                                                pr + (hcq + 1) * 512],
                                        W1q(wsl)[:, fc, hc * 128:(hc + 1) * 128],
                                        uf[:, fc, :], fc == 0, fc == NC - 1))
                    self.mmx(mms, reads=bW1(wsl) + [self.b_ufm[0]],
                             writes=self.pLh(rnd % 2))
                    self.act(self.hg[:, rnd * 1024:(rnd + 1) * 1024],
                             self.pL[:, pr:pr + 1024],
                             AF.Gelu_apprx_tanh, reads=self.pLh(rnd % 2),
                             writes=[self.b_hg[rnd]])
                for half in range(2):
                    mms = []
                    for t2 in range(4):
                        if half == 0:
                            out = (self.pM[:, 0:512], self.pM[:, 512:1024],
                                   self.pT1[:, 0:512], self.pT2[:, 0:512])[t2]
                        else:
                            out = self.pL[:, t2 * 512:(t2 + 1) * 512]
                        for hc in range(NC):
                            mms.append((out, hgv[:, hc, t2 * 128:(t2 + 1) * 128],
                                        W2q(wsl)[:, hc, half * 512:(half + 1) * 512],
                                        hc == 0, hc == NC - 1))
                    wr = ([self.b_pM[0], self.b_pM[1], self.b_pT[0], self.b_pT[1]]
                          if half == 0 else self.pLh(0) + self.pLh(1))
                    self.mmx(mms, reads=bW2(wsl) + bh, writes=wr)
                    op = "tensor_copy" if hb == 0 else "tensor_add"
                    for t2 in range(4):
                        if half == 0:
                            pin = (self.pM[:, 0:512], self.pM[:, 512:1024],
                                   self.pT1[:, 0:512], self.pT2[:, 0:512])[t2]
                            bp = (self.b_pM[0], self.b_pM[1],
                                  self.b_pT[0], self.b_pT[1])[t2]
                        else:
                            pin = self.pL[:, t2 * 512:(t2 + 1) * 512]
                            bp = self.b_pL[t2]
                        dst = ya[:, t2, half * 512:(half + 1) * 512]
                        args = (dst, pin) if hb == 0 else (dst, dst, pin)
                        rdy = [bp, self.b_yst[0]] + (fence if hb == 0 else [])
                        self.vop(op, *args, reads=rdy, writes=[self.b_yst[0]])
            for c in range(4):
                def fn(E, i=ya[:, c, :], ix=self._ssl(sl)[:, c:c + 1]):
                    return E.indirect_dma_start(out=self.y_buf,
                                                out_offset=IOA(ap=ix, axis=0),
                                                in_=i, in_offset=None)
                self.S.gma(fn, reads=[self.b_yst[0], self.b_gsb[sl]],
                           writes=[self.b_ybuf])

    def phase_F(self, b):
        bigA32 = self.bigA.bitcast(F32)
        bigB32 = self.bigB.bitcast(F32)
        for t in range(TT):
            i4 = t % 4
            xs = bigA32[:, (4 + i4) * 1024:(5 + i4) * 1024]
            bxs = self.b_bigA[4 + i4]
            ygt = bigA32[:, i4 * 1024:(i4 + 1) * 1024]
            byg = [self.b_bigA[i4]]
            ss = bigB32[:, i4 * 1024:(i4 + 1) * 1024]
            bss = self.b_bigB[i4]
            self.dma(xs, self.x[b, t * 128:(t + 1) * 128, :],
                     reads=[bxs], writes=[bxs])
            if not self._ybuf_claimed:
                self._ybuf_claimed = True
                wy, ry = [self.b_ybuf], []
            else:
                wy, ry = [], [self.b_ybuf]
            self.dma(ygt, self.y_buf[b * T + t * 128:b * T + (t + 1) * 128, :],
                     reads=byg + ry, writes=byg + wy)
            if not self._sspill_claimed:
                self._sspill_claimed = True
                wsx, rsx = [self.b_sspill], []
            else:
                wsx, rsx = [], [self.b_sspill]
            self.dma(ss, self.s_spill[b, t * 128:(t + 1) * 128, :],
                     reads=[bss] + rsx, writes=[bss] + wsx)
            if b == 0:
                vs = bigB32[:, (4 + i4) * 1024:(5 + i4) * 1024]
                bvs = self.b_bigB[4 + i4]
                if not self._vit_claimed:
                    self._vit_claimed = True
                    wv, rv = [self.b_vit], []
                else:
                    wv, rv = [], [self.b_vit]
                self.dma(vs, self.vit_buf[t * 128:(t + 1) * 128, :],
                         reads=[bvs] + rv, writes=[bvs] + wv)
                vread = [bvs]
            else:
                vs = self.xn_tok[:, t, :]
                vread = [self.b_xn[t]]
            self.vop("tensor_add", ygt, ygt, vs, reads=byg + vread, writes=byg)
            if "gin" in self.dmp:
                self.dma(self.dmp["gin"][b, t * 128:(t + 1) * 128, :], ygt,
                         reads=byg, writes=[])
            gg = self.lnfm_f32[:, t % 4, :]
            bgg = self.b_lnfm[t % 4]
            wex = ([self.b_utok[0], self.b_ufm[0], self.b_yst[0]]
                   if (b == 0 and t < 4) else [])
            self.act(gg, ygt, AF.Silu, reads=byg, writes=[bgg] + wex)
            self.vop("tensor_mul", gg, gg, ss, reads=[bgg, bss], writes=[bgg])
            self.gop("tensor_add", xs, xs, gg, reads=[bxs, bgg], writes=[bxs])
            self.S.sdma(self.out[b, t * 128:(t + 1) * 128, :], xs,
                        reads=[bxs], writes=[self.b_out])

    def build(self):
        self.b_out = Buf()
        self._wcvt_claimed = False
        self._ubuf_claimed = False
        self._gidx_claimed = False
        self._ybuf_claimed = False
        self._sspill_claimed = False
        self._vit_claimed = False
        self.init_consts()
        self.convert_qkv_weights()
        for b in range(NB):
            self.phase_A_pass1(b)
            self.load_attn_weights()
            self.phase_A_pass2(b)
            self.phase_B(b)
            if b == 0:
                self.convert_moe_weights()
            self.phase_C(b)
            self.phase_D(b)
            if b == 0:
                for t in range(TT):
                    self.dma(self.vit_buf[t * 128:(t + 1) * 128, :],
                             self.xn_tok[:, t, :],
                             reads=[self.b_xn[t], self.b_vit], writes=[])
        self.index_build()
        self.phase_E_sparse()
        for b in range(NB):
            self.phase_F(b)
        it_deps = [e for e in self.S.dma_ring if e is not None]
        it_deps += [e for e in self.S.gma_ring if e is not None]
        eid = self.S._mk("sync", lambda E: None, (self.b_out,), (), False)
        self.S.items[eid]["deps"] = sorted(set(self.S.items[eid]["deps"]) | set(it_deps))
        self.S.finalize(self.nc, self.sems)
        return self.nc


NAME_MAP = {
    "L0_attn_x": "x",
    "L0_attn_wq": "wq", "L0_attn_bq": "bq",
    "L0_attn_wk": "wk", "L0_attn_bk": "bk",
    "L0_attn_wv": "wv", "L0_attn_bv": "bv",
    "L0_attn_wo": "wo", "L0_attn_bo": "bo",
    "L0_ffn_ln1_w": "ln1w", "L0_ffn_ln1_b": "ln1b",
    "L0_ffn_w_qkv": "wqkv", "L0_ffn_w_proj": "wproj",
    "L0_ffn_ln2_w": "ln2w", "L0_ffn_ln2_b": "ln2b",
    "L0_ffn_gate_w": "gatew", "L0_ffn_gate_b": "gateb",
    "L0_ffn_w1": "w1", "L0_ffn_w2": "w2",
}


def build_program(dump=()):
    bld = Builder(dump=dump)
    nc = bld.build()
    return nc


def kernel(**inputs):
    arrs = {NAME_MAP[k]: np.ascontiguousarray(np.asarray(v, dtype=np.float32))
            for k, v in inputs.items()}
    x = arrs.pop("x")
    nc = build_program()
    in_maps = []
    for c in range(8):
        m = dict(arrs)
        m["x"] = np.ascontiguousarray(x[c * NB:(c + 1) * NB])
        in_maps.append(m)
    res = run_bass_kernel_spmd(nc, in_maps, core_ids=list(range(8)))
    return np.concatenate([res.results[c]["out"] for c in range(8)], axis=0)


if __name__ == "__main__":
    pass


# version 22
# speedup vs baseline: 1.0398x; 1.0398x over previous
import numpy as np
from contextlib import ExitStack

import concourse.bass as bass
import concourse.mybir as mybir
from concourse.bass_utils import run_bass_kernel_spmd

F32 = mybir.dt.float32
BF16 = mybir.dt.bfloat16
U32 = mybir.dt.uint32
AF = mybir.ActivationFunctionType
ALU = mybir.AluOpType
AX = mybir.AxisListType
IOA = bass.IndirectOffsetOnAxis

PASS_TOK = 512
NPASS = 16
NSLOT = NPASS * PASS_TOK

ENGS = ("sync", "tensor", "vector", "scalar", "gpsimd")
NDQ = 16
NGQ = 24

NB = 2
TT = 16
T = 2048
D = 1024
NC = 8
NH = 16
DK = 64
NE = 8
HID = 4096
EPS = 1e-5


class Buf:
    __slots__ = ("w", "rs")

    def __init__(self):
        self.w = None
        self.rs = []


PARANOID = False


class Sched:
    def __init__(self):
        self.items = []
        self.per_eng = {e: [] for e in ENGS}
        self.last_per_eng = {e: None for e in ENGS}
        self.ndma = 0
        self.dma_ring = [None] * NDQ
        self.ngma = 0
        self.gma_ring = [None] * NGQ
        self.nsma = 0
        self.sma_ring = [None] * 4

    def _mk(self, eng, fn, reads, writes, dma):
        deps = set()
        for b in reads:
            if b.w is not None:
                deps.add(b.w)
        for b in writes:
            if b.w is not None:
                deps.add(b.w)
            for r in b.rs:
                deps.add(r)
        if PARANOID:
            for e2 in ENGS:
                if e2 != eng and self.last_per_eng[e2] is not None:
                    deps.add(self.last_per_eng[e2])
        eid = len(self.items)
        it = dict(eid=eid, eng=eng, fn=fn, dma=dma)
        if dma and eng == "sync":
            q = self.ndma % NDQ
            it["sem"] = "dq%d" % q
            it["val"] = 16 * (self.ndma // NDQ + 1)
            if self.dma_ring[q] is not None:
                deps.add(self.dma_ring[q])
            self.dma_ring[q] = eid
            self.ndma += 1
        elif dma and eng == "scalar":
            q = self.nsma % 4
            it["sem"] = "sq%d" % q
            it["val"] = 16 * (self.nsma // 4 + 1)
            if self.sma_ring[q] is not None:
                deps.add(self.sma_ring[q])
            self.sma_ring[q] = eid
            self.nsma += 1
        elif dma:
            q = self.ngma % NGQ
            it["sem"] = "gq%d" % q
            it["val"] = 16 * (self.ngma // NGQ + 1)
            if self.gma_ring[q] is not None:
                deps.add(self.gma_ring[q])
            self.gma_ring[q] = eid
            self.ngma += 1
        it["deps"] = sorted(deps)
        self.items.append(it)
        self.per_eng[eng].append(it)
        self.last_per_eng[eng] = eid
        for b in writes:
            b.w = eid
            b.rs = []
        for b in reads:
            b.rs.append(eid)
        return eid

    def op(self, eng, fn, reads=(), writes=()):
        return self._mk(eng, fn, tuple(reads), tuple(writes), False)

    def dma(self, out_ap, in_ap, reads=(), writes=(), slow=False):
        def fn(E, o=out_ap, i=in_ap, s=slow):
            if s:
                return E.dma_start(out=o, in_=i, allow_slow_non_contiguous=True)
            return E.dma_start(out=o, in_=i)
        return self._mk("sync", fn, tuple(reads), tuple(writes), True)

    def gma(self, fn, reads=(), writes=()):
        return self._mk("gpsimd", fn, tuple(reads), tuple(writes), True)

    def sdma(self, out_ap, in_ap, reads=(), writes=()):
        def fn(E, o=out_ap, i=in_ap):
            return E.dma_start(out=o, in_=i)
        return self._mk("scalar", fn, tuple(reads), tuple(writes), True)

    def finalize(self, nc, sems):
        demanded = set()
        for it in self.items:
            for d in it["deps"]:
                demanded.add(d)
        for e in ENGS:
            c = 0
            for it in self.per_eng[e]:
                if it["dma"]:
                    it["inc"] = 16
                elif it["eid"] in demanded:
                    c += 1
                    it["val"] = c
                    it["sem"] = e
                    it["inc"] = 1
                else:
                    it["val"] = None
                    it["sem"] = None
                    it["inc"] = 0

        items_all = self.items

        def mk_runner(engname):
            eng_items = self.per_eng[engname]

            def go(E):
                hi = {}
                for it in eng_items:
                    for d in it["deps"]:
                        p = items_all[d]
                        if engname == "tensor" and p["eng"] == "tensor" and not p["dma"]:
                            continue
                        v = p["val"]
                        if v is not None and hi.get(p["sem"], 0) < v:
                            E.wait_ge(sems[p["sem"]], v)
                            hi[p["sem"]] = v
                    inst = it["fn"](E)
                    if it["inc"]:
                        inst.then_inc(sems[it["sem"]], it["inc"])
            return go

        with nc.Block() as block:
            block.sync(mk_runner("sync"))
            block.tensor(mk_runner("tensor"))
            block.vector(mk_runner("vector"))
            block.scalar(mk_runner("scalar"))
            block.gpsimd(mk_runner("gpsimd"))


class Builder:
    def __init__(self, dump=()):
        self.dump = set(dump)
        nc = bass.Bass("TRN2", target_bir_lowering=False, debug=False)
        self.nc = nc
        self.S = Sched()
        self.ctx = ExitStack()
        self._alloc_dram()
        self._alloc_sbuf()
        self._alloc_psum()
        self.sems = {e: self.ctx.enter_context(nc.semaphore("sem_" + e)) for e in ENGS}
        for q in range(NDQ):
            self.sems["dq%d" % q] = self.ctx.enter_context(nc.semaphore("dq%d" % q))
        for q in range(NGQ):
            self.sems["gq%d" % q] = self.ctx.enter_context(nc.semaphore("gq%d" % q))
        for q in range(4):
            self.sems["sq%d" % q] = self.ctx.enter_context(nc.semaphore("sq%d" % q))
        self._mk_bufs()

    def _alloc_dram(self):
        nc = self.nc
        din = lambda n, s: nc.dram_tensor(n, s, F32, kind="ExternalInput").ap()
        self.x = din("x", [NB, T, D])
        self.wq_d = din("wq", [D, DK]); self.bq_d = din("bq", [DK])
        self.wk_d = din("wk", [D, DK]); self.bk_d = din("bk", [DK])
        self.wv_d = din("wv", [D, DK]); self.bv_d = din("bv", [DK])
        self.wo_d = din("wo", [DK, D]); self.bo_d = din("bo", [D])
        self.ln1w_d = din("ln1w", [D]); self.ln1b_d = din("ln1b", [D])
        self.wqkv_d = din("wqkv", [D, 3 * D])
        self.wproj_d = din("wproj", [D, D])
        self.ln2w_d = din("ln2w", [D]); self.ln2b_d = din("ln2b", [D])
        self.gatew_d = din("gatew", [D, NE]); self.gateb_d = din("gateb", [NE])
        self.w1_d = din("w1", [NE, D, HID])
        self.w2_d = din("w2", [NE, HID, D])
        self.out = nc.dram_tensor("out", [NB, T, D], F32, kind="ExternalOutput").ap()
        self.s_spill = nc.dram_tensor("s_spill", [NB, T, D], F32, kind="Internal").ap()
        self.wqkv_bf = nc.dram_tensor("wqkv_bf", [D, 3 * D], BF16, kind="Internal").ap()
        self.wproj_bf = nc.dram_tensor("wproj_bf", [D, D], BF16, kind="Internal").ap()
        ik = lambda n: "ExternalOutput" if n in self.dump else "Internal"
        self.w1bf = nc.dram_tensor("w1bf", [NE * 4 * 128, 8 * 1024], BF16,
                                   kind="Internal").ap()
        self.w2bf = nc.dram_tensor("w2bf", [NE * 4 * 128, 8 * 1024], BF16,
                                   kind="Internal").ap()
        self.u_buf = nc.dram_tensor("u_buf", [NB * T, D], BF16, kind=ik("u_buf")).ap()
        self.vit_buf = nc.dram_tensor("vit_buf", [T, D], F32, kind="Internal").ap()
        self.y_buf = nc.dram_tensor("y_buf", [NB * T + 128, D], F32,
                                    kind=ik("y_buf")).ap()
        self.gidx_buf = nc.dram_tensor("gidx_buf", [NSLOT], U32, kind=ik("gidx_buf")).ap()
        self.iscr = nc.dram_tensor("iscr", [NB * TT * NE], F32, kind="Internal").ap()
        self.sidx_buf = nc.dram_tensor("sidx_buf", [NSLOT], U32, kind=ik("sidx_buf")).ap()
        self.ident_dram = nc.inline_tensor(np.eye(8, dtype=np.float32), name="identc")
        import ml_dtypes
        self.identb_dram = nc.inline_tensor(
            np.eye(128).astype(ml_dtypes.bfloat16), name="identbc")
        lt = (np.arange(128)[:, None] <= np.arange(128)[None, :]).astype(np.float32)
        self.lt_dram = nc.inline_tensor(lt, name="ltmaskc")
        thr16 = np.tile(PASS_TOK * np.arange(16, dtype=np.float32)[None, :], (NE, 1))
        self.thr16_dram = nc.inline_tensor(thr16, name="thr16c")
        thrp = np.tile(PASS_TOK * np.arange(NPASS, dtype=np.float32)[None, :], (NE, 1))
        self.thrp_dram = nc.inline_tensor(thrp, name="thrpc")
        evals = np.arange(NE, dtype=np.float32)[:, None]
        self.evals_dram = nc.inline_tensor(evals, name="evalsc")
        self.dmp = {}
        for name in self.dump:
            if name in ("u_buf", "y_buf", "gidx_buf", "sidx_buf"):
                continue
            shp = {"xn": [NB, T, D], "xn2": [NB, T, D], "s": [NB, T, D],
                   "lgt": [NB, T, 2 * NE], "gin": [NB, T, D]}[name]
            self.dmp[name] = nc.dram_tensor("dbg_" + name, shp, F32, kind="ExternalOutput").ap()

    def _alloc_sbuf(self):
        nc = self.nc
        E_ = self.ctx.enter_context
        sb = lambda n, s, d: E_(nc.sbuf_tensor(n, s, d))
        self.xn_tok = sb("xn_tok", [128, TT, D], F32)
        self.ln_fm = sb("ln_fm", [128, NC, T], BF16)
        self.bigA = sb("bigA", [128, 16384], BF16)
        self.bigB = sb("bigB", [128, 16384], BF16)
        self.hg = sb("hg", [128, 4096], BF16)
        self.misc = sb("misc", [128, 4096], F32)
        self.stage = sb("stage", [128, 2048], F32)
        self.pool = sb("pool", [128, 3, 1024], F32)
        self.identf = sb("identf", [8, 8], F32)
        self.identb = sb("identb", [128, 128], BF16)
        self.ltmask = sb("ltmask", [128, 128], BF16)
        self.onesb = sb("onesb", [128, 128], BF16)
        self.gam1 = sb("gam1", [128, NC], F32)
        self.bet1 = sb("bet1", [128, NC], F32)
        self.gam2 = sb("gam2", [128, NC], F32)
        self.bet2 = sb("bet2", [128, NC], F32)
        self.bq8 = sb("bq8", [64, 2], F32)
        self.bkc = sb("bkc", [64, 1], F32)
        self.bvc = sb("bvc", [64, 1], F32)
        self.gw_st = sb("gw_st", [128, NC, NE], F32)
        self.gwg_hi = sb("gwg_hi", [128, NC, NE], BF16)
        self.gwg_lo = sb("gwg_lo", [128, NC, NE], BF16)
        self.gw_bf = sb("gw_bf", [128, NC, NE], BF16)
        self.b2bf = sb("b2bf", [128, NC], BF16)
        self.gb_row = sb("gb_row", [1, NE], F32)
        self.gbadd = sb("gbadd", [NE, 1], F32)
        self.masks = sb("masksb", [128, NB, TT, NE], F32)
        self.lgt_tok = sb("lgt_tok", [128, TT, NE], F32)
        self.statA = sb("statA", [128, TT, 4], F32)
        self.ntcol = sb("ntcol", [128, 1], F32)
        self.maxsc = sb("maxsc", [1, 2], F32)
        self.reccol = sb("reccol", [128, TT], F32)
        self.epsc = sb("epsc", [128, 1], F32)
        self.idxs = sb("idxs", [128, 40], U32)
        self.miscu = self.misc.bitcast(U32)
        self.ltf = self.misc[:, 0:128]
        self.csum = self.misc[0:1, 128:384]
        self.csum32 = self.misc[0:32, 384:392]
        self.carry32 = self.misc[0:32, 392:400]
        self.irow = self.misc[0:8, 400:464]
        self.icol = self.misc[0:8, 464:472]
        self.srow = self.misc[0:1, 472:504]
        self.soffr = self.misc[0:1, 504:512]
        self.thr16 = self.misc[0:8, 512:528]
        self.thrp = self.misc[0:8, 528:528 + NPASS]
        self.evalsf = self.misc[0:8, 552:553]
        self.idxf = self.misc[:, 560:584]
        self.pcol = self.misc[:, 584:585]
        self.posf = self.misc[:, 585:587]
        self.tid32 = self.miscu[:, 588:588 + NB * TT]
        self.iniz = self.miscu[:, 0:NSLOT // 128]
        self.onescol = self.ltf[:, 127:128]
        self.onesrow = self.ltf[0:1, 0:128]
        self.miscb = self.misc.bitcast(BF16)
        self.poolb = self.pool.bitcast(BF16)
        self.hgf = self.hg.bitcast(F32)
        self.stageb = self.stage.bitcast(BF16)
        self.lnfm_f32 = self.ln_fm.bitcast(F32)
        rem = nc.sbuf_bytes_remaining
        assert rem >= 0, f"SBUF over budget by {-rem} bytes"

    def _alloc_psum(self):
        nc = self.nc
        E_ = self.ctx.enter_context
        self.pL = E_(nc.psum_tensor("pL", [128, 2048], F32))
        self.pM = E_(nc.psum_tensor("pM", [128, 1024], F32))
        self.pT1 = E_(nc.psum_tensor("pT1", [128, 512], F32))
        self.pT2 = E_(nc.psum_tensor("pT2", [128, 512], F32))
        self.pT1b = self.pT1.bitcast(BF16)
        self.pT2b = self.pT2.bitcast(BF16)
        self.pMb = self.pM.bitcast(BF16)

    def _mk_bufs(self):
        self.b_ident = Buf(); self.b_lt = Buf(); self.b_ones = Buf()
        self.b_params = Buf(); self.b_gw = Buf()
        self.b_pool = [Buf() for _ in range(3)]
        self.b_stage = [Buf(), Buf()]
        self.b_hg = [Buf() for _ in range(4)]
        self.b_misc = [Buf() for _ in range(4)]
        self.b_xn = [Buf() for _ in range(TT)]
        self.b_lnfm = [Buf() for _ in range(TT)]
        self.b_bigA = [Buf() for _ in range(8)]
        self.b_bigB = [Buf() for _ in range(8)]
        self.b_stat = [Buf(), Buf()]
        self.b_masks = Buf()
        self.b_pL = [Buf(), Buf(), Buf(), Buf()]
        self.b_pM = [Buf(), Buf()]
        self.b_pT = [Buf(), Buf()]
        self.b_sspill = Buf()
        self.b_wbf = Buf()
        self.b_reccol = Buf()
        self.b_ptile = [Buf(), Buf(), Buf()]
        self.b_stsum = Buf()
        self.b_stsq = Buf()
        self.b_ust = [Buf(), Buf()]
        self.b_wcvt = Buf()
        self.b_ubuf = Buf()
        self.b_vit = Buf()
        self.b_ybuf = Buf()
        self.b_gidxb = Buf()
        self.b_sidxb = Buf()
        self.b_idx = Buf()
        self.b_icons = Buf()
        self.b_utok = [Buf(), Buf()]
        self.b_ufm = [Buf(), Buf()]
        self.b_yst = [Buf(), Buf()]
        self.b_gsb = [Buf(), Buf()]
        self.b_widx = Buf()

    def pLh(self, i):
        return [self.b_pL[2 * i], self.b_pL[2 * i + 1]]

    def bA(self, lo, hi):
        return [self.b_bigA[i] for i in range(lo // 2048, (hi + 2047) // 2048)]

    def bB(self, lo, hi):
        return [self.b_bigB[i] for i in range(lo // 2048, (hi + 2047) // 2048)]

    def op(self, eng, fn, reads=(), writes=()):
        return self.S.op(eng, fn, reads, writes)

    def dma(self, out_ap, in_ap, reads=(), writes=(), slow=False):
        return self.S.dma(out_ap, in_ap, reads, writes, slow=slow)

    def mm(self, mms, reads=(), writes=()):
        def fn(E, mms=mms):
            n = len(mms)
            last = None
            for i, (o, l, r) in enumerate(mms):
                last = E.matmul(o, l, r, start=(i == 0), stop=(i == n - 1))
            return last
        return self.op("tensor", fn, reads, writes)

    def mmx(self, mms, reads=(), writes=()):
        def fn(E, mms=mms):
            last = None
            for (o, l, r, st, sp) in mms:
                last = E.matmul(o, l, r, start=st, stop=sp)
            return last
        return self.op("tensor", fn, reads, writes)

    def trans(self, trs, reads=(), writes=(), bf=True):
        ident = self.identb if bf else self.identf

        def fn(E, trs=trs, ident=ident):
            last = None
            for item in trs:
                (o, i), bp = item[:2], (item[2] if len(item) > 2 else 0)
                p = i.shape[0]
                last = E.transpose(o, i, ident[bp:bp + p, bp:bp + p])
            return last
        return self.op("tensor", fn, tuple(reads) + (self.b_ident,), writes)

    def act(self, out, in_, func, reads=(), writes=(), **kw):
        def fn(E, o=out, i=in_, f=func, kw=kw):
            return E.activation(o, i, f, **kw)
        return self.op("scalar", fn, reads, writes)

    def vop(self, name, *args, reads=(), writes=(), **kw):
        def fn(E, n=name, a=args, kw=kw):
            return getattr(E, n)(*a, **kw)
        return self.op("vector", fn, reads, writes)

    def gop(self, name, *args, reads=(), writes=(), **kw):
        def fn(E, n=name, a=args, kw=kw):
            return getattr(E, n)(*a, **kw)
        return self.op("gpsimd", fn, reads, writes)

    def init_consts(self):
        pb = self.b_params
        self.dma(self.identf[:], self.ident_dram.ap(), writes=[self.b_ident])
        self.dma(self.identb[:], self.identb_dram.ap(), writes=[self.b_ident])
        self.dma(self.pool[:, 2, :][:, 0:128], self.lt_dram.ap()[:, 0:128],
                 writes=[self.b_pool[2]])
        self.vop("tensor_copy", self.ltmask[:], self.pool[:, 2, 0:128],
                 reads=[self.b_pool[2]], writes=[self.b_lt])
        self.vop("memset", self.onesb[:], 1.0, writes=[self.b_ones])
        self.vop("memset", self.epsc[:], EPS, writes=[self.b_ones])
        rr = lambda ap: ap.rearrange("(c p) -> p c", p=128)
        cc = lambda ap: ap.rearrange("(p one) -> p one", one=1)
        self.dma(self.gam1[:], rr(self.ln1w_d), writes=[pb], slow=True)
        self.dma(self.bet1[:], rr(self.ln1b_d), writes=[pb], slow=True)
        self.dma(self.gam2[:], rr(self.ln2w_d), writes=[pb], slow=True)
        self.dma(self.bet2[:], rr(self.ln2b_d), writes=[pb], slow=True)
        self.dma(self.bq8[:, 1:2], cc(self.bq_d), writes=[pb])
        self.dma(self.bkc[:], cc(self.bk_d), writes=[pb])
        self.dma(self.bvc[:], cc(self.bv_d), writes=[pb])
        self.dma(self.gb_row[:], self.gateb_d.rearrange("(one d) -> one d", one=1), writes=[pb])
        self.vop("tensor_scalar", self.bq8[:, 0:1], self.bq8[:, 1:2], 0.125, None, ALU.mult,
                 reads=[pb], writes=[pb])
        gwr = self.gatew_d.rearrange("(c p) e -> p c e", p=128)
        self.dma(self.gw_st[:], gwr, writes=[self.b_gw])
        for c in range(NC):
            self.vop("tensor_scalar", self.gw_st[:, c, :], self.gw_st[:, c, :],
                     self.gam2[:, c:c + 1], None, ALU.mult,
                     reads=[self.b_gw, pb], writes=[self.b_gw])
        self.vop("tensor_copy", self.gwg_hi[:], self.gw_st[:],
                 reads=[self.b_gw], writes=[self.b_gw])
        self.vop("tensor_sub", self.gwg_lo[:], self.gw_st[:], self.gwg_hi[:],
                 reads=[self.b_gw], writes=[self.b_gw])
        self.dma(self.gw_st[:], gwr, reads=[self.b_gw], writes=[self.b_gw])
        self.vop("tensor_copy", self.gw_bf[:], self.gw_st[:],
                 reads=[self.b_gw], writes=[self.b_gw])
        self.vop("tensor_copy", self.b2bf[:], self.bet2[:], reads=[pb], writes=[pb])
        mms = [(self.pT1[0:1, 0:NE], self.b2bf[:, c:c + 1], self.gw_bf[:, c, :])
               for c in range(NC)]
        self.mm(mms, reads=[self.b_gw, pb], writes=[self.b_pT[0]])
        self.vop("tensor_add", self.misc[0:1, 0:NE], self.pT1[0:1, 0:NE], self.gb_row[:],
                 reads=[self.b_pT[0], pb], writes=[self.b_misc[0]])
        self.vop("tensor_copy", self.miscb[0:1, 2048:2048 + NE], self.misc[0:1, 0:NE],
                 reads=[self.b_misc[0]], writes=[self.b_misc[1]])
        self.trans([(self.pT2b[0:NE, 0:1], self.miscb[0:1, 2048:2048 + NE])],
                   reads=(self.b_misc[1],), writes=[self.b_pT[1]])
        self.vop("tensor_copy", self.gbadd[:], self.pT2b[0:NE, 0:1],
                 reads=[self.b_pT[1]], writes=[pb])
        iniz2 = self.miscu[0:NSLOT // 128, 0:128]
        self.gop("iota", iniz2, [[0, 128]], base=0,
                 channel_multiplier=0, reads=[], writes=[self.b_idx, self.b_misc[0]])
        self.dma(self.gidx_buf.rearrange("(c p) -> c p", p=128), iniz2,
                 reads=[self.b_idx, self.b_misc[0]], writes=[self.b_gidxb])
        self.gop("iota", iniz2, [[1, 128]], base=NB * T,
                 channel_multiplier=0, reads=[self.b_gidxb],
                 writes=[self.b_idx, self.b_misc[0]])
        self.dma(self.sidx_buf.rearrange("(c p) -> c p", p=128), iniz2,
                 reads=[self.b_idx, self.b_misc[0]], writes=[self.b_sidxb])

    def convert_qkv_weights(self):
        def cast_dma(out_ap, in_ap, reads, writes):
            def fn(E, o=out_ap, i=in_ap):
                return E.dma_start(out=o, in_=i, max_dma_last_dim=8192)
            return self.S.gma(fn, reads, writes)
        cast_dma(self.wqkv_bf, self.wqkv_d, [self.b_wbf], [])
        cast_dma(self.wproj_bf, self.wproj_d, [self.b_wbf], [])

    def convert_moe_weights(self):
        def cast_dma(out_ap, in_ap, reads, writes):
            def fn(E, o=out_ap, i=in_ap):
                return E.dma_start(out=o, in_=i, max_dma_last_dim=8192)
            return self.S.gma(fn, reads, writes)
        w1v = self.w1bf.rearrange("(s p) (c n) -> s p c n", p=128, n=1024)
        w2v = self.w2bf.rearrange("(s p) (c n) -> s p c n", p=128, n=1024)
        for e in range(NE):
            for hb in range(4):
                s = e * 4 + hb
                cast_dma(w1v[s],
                         self.w1_d[e, :, hb * 1024:(hb + 1) * 1024].rearrange(
                             "(c p) n -> p c n", p=128),
                         [self.b_wcvt], [])
                cast_dma(w2v[s],
                         self.w2_d[e, hb * 1024:(hb + 1) * 1024, :].rearrange(
                             "(c p) n -> p c n", p=128),
                         [self.b_wcvt], [])

    def load_attn_weights(self):
        w_bufs = self.bB(10240, 12800)
        for i, w in enumerate((self.wq_d, self.wk_d, self.wv_d)):
            sl = i % 2
            self.dma(self.stage[:, sl * 1024:sl * 1024 + 512].rearrange(
                         "p (c k) -> p c k", k=DK),
                     w.rearrange("(c p) k -> p c k", p=128),
                     reads=[self.b_stage[sl]], writes=[self.b_stage[sl]])
            self.gop("tensor_copy", self.bigB[:, 10240 + i * 512:10240 + (i + 1) * 512],
                     self.stage[:, sl * 1024:sl * 1024 + 512],
                     reads=[self.b_stage[sl]], writes=w_bufs)
        self.dma(self.stage[0:64, 1024:2048], self.wo_d,
                 reads=[self.b_stage[1]], writes=[self.b_stage[1]])
        self.gop("tensor_copy", self.bigB[0:64, 11776:12800], self.stage[0:64, 1024:2048],
                 reads=[self.b_stage[1]], writes=w_bufs)

    def wq_sb(self, i, c):
        off = 10240 + i * 512 + c * 64
        return self.bigB[:, off:off + 64]

    def wo_sb(self, half):
        return self.bigB[0:64, 11776 + half * 512:11776 + (half + 1) * 512]

    def phase_A_pass1(self, b):
        for t in range(TT):
            xt = self.xn_tok[:, t, :]
            bx = self.b_xn[t]
            self.dma(xt, self.x[b, t * 128:(t + 1) * 128, :], reads=[bx], writes=[bx])
            self._stats_collect(t, xt, bx)
        self._stats_finalize()

    def phase_A_pass2(self, b):
        QT = self.bigB[0:65, 0:2048]
        KT = self.bigB[0:65, 2048:4096]
        VT = self.bigB[:, 4096:5136].rearrange("p (t k) -> p t k", k=65)
        bQ, bK = self.bB(0, 2048), self.bB(2048, 4096)
        bV = self.bB(4096, 5456)
        plB = self.pL.bitcast(BF16)
        self.vop("memset", self.bigB[64:128, 0:2048], 0.0, reads=bQ, writes=bQ)
        self.vop("memset", self.bigB[64:128, 2048:4096], 0.0, reads=bK, writes=bK)
        self.vop("memset", KT[64:65, :], 1.0, reads=bK, writes=bK)
        self.vop("memset", VT[:, :, 64:65], 1.0, reads=bV, writes=bV)
        self.vop("memset", self.bigB[:, 5136:5200], 0.0,
                 reads=self.bB(5136, 5200), writes=self.bB(5136, 5200))
        pend = None
        for t in range(TT + 1):
            if t < TT:
                pp = t % 2
                self.vop("tensor_scalar", self.xn_tok[:, t, :], self.xn_tok[:, t, :],
                         self.statA[:, t, 2:3], self.statA[:, t, 3:4],
                         ALU.mult, ALU.subtract,
                         reads=[self.b_stsum], writes=[self.b_xn[t]])
                xc = self.hg[:, pp * 1024:(pp + 1) * 1024]
                self.vop("tensor_copy", xc, self.xn_tok[:, t, :],
                         reads=[self.b_xn[t]], writes=[self.b_hg[pp]])
                pt = (self.pT1b, self.pT2b)[pp]
                self.trans([(pt[:, c * 128:(c + 1) * 128], xc[:, c * 128:(c + 1) * 128])
                            for c in range(NC)],
                           reads=(self.b_hg[pp],), writes=[self.b_pT[pp]])
                if "xn" in self.dmp:
                    self.dma(self.dmp["xn"][b, t * 128:(t + 1) * 128, :],
                             self.xn_tok[:, t, :], reads=[self.b_xn[t]], writes=[])
            if pend is not None:
                tp_ = pend
                pp_ = tp_ % 2
                pt_ = (self.pT1b, self.pT2b)[pp_]
                fm = self.hg[:, (2 + pp_) * 1024:(3 + pp_) * 1024]
                self.vop("tensor_copy", fm, pt_[:],
                         reads=[self.b_pT[pp_]], writes=[self.b_hg[2 + pp_]])
                ps = self.pM[:, pp_ * 512:(pp_ + 1) * 512]
                bps = self.b_pM[pp_]
                mms = []
                for i in range(3):
                    for c in range(NC):
                        mms.append((ps[0:64, i * 128:i * 128 + 128],
                                    self.wq_sb(i, c), fm[:, c * 128:(c + 1) * 128]))
                def fnmm(E, mms=mms):
                    last = None
                    for gi in range(3):
                        for ci in range(NC):
                            o, l, r = mms[gi * NC + ci]
                            last = E.matmul(o, l, r, start=(ci == 0), stop=(ci == NC - 1))
                    return last
                self.op("tensor", fnmm,
                        reads=[self.b_hg[2 + pp_]] + self.bB(10240, 12800), writes=[bps])
                self.act(QT[0:64, tp_ * 128:(tp_ + 1) * 128], ps[0:64, 0:128], AF.Identity,
                         scale=0.125, bias=self.bq8[:, 0:1],
                         reads=[bps, self.b_params], writes=bQ)
                self.act(KT[0:64, tp_ * 128:(tp_ + 1) * 128], ps[0:64, 128:256], AF.Identity,
                         bias=self.bkc[:],
                         reads=[bps, self.b_params], writes=bK)
                vt = self.bigB[0:64, 5200 + pp_ * 128:5200 + (pp_ + 1) * 128]
                self.act(vt, ps[0:64, 256:384], AF.Identity, bias=self.bvc[:],
                         reads=[bps, self.b_params], writes=bV)
                vslot = plB[:, pp_ * 2048:pp_ * 2048 + 64]
                self.trans([(vslot, vt)], reads=tuple(bV),
                           writes=[self.b_pL[2 * pp_]])
                self.vop("tensor_copy", VT[:, tp_, 0:64], vslot,
                         reads=[self.b_pL[2 * pp_]], writes=bV)
            pend = t if t < TT else None

    def _stats_collect(self, t, src_ap, bsrc):
        self.vop("reduce_sum", self.statA[:, t, 0:1], src_ap, axis=AX.X,
                 reads=[bsrc], writes=[self.b_stsum])
        self.act(self.pool[:, 2, :], src_ap, AF.Square,
                 accum_out=self.statA[:, t, 1:2],
                 reads=[bsrc], writes=[self.b_pool[2], self.b_stsq])

    def _stats_finalize(self):
        c = lambda i: self.statA[:, :, i:i + 1]
        bb = [self.b_stsum, self.b_stsq]
        self.vop("tensor_scalar", c(0), c(0), 1.0 / D, None, ALU.mult,
                 reads=bb, writes=[self.b_stsum])
        self.vop("tensor_scalar", c(1), c(1), 1.0 / D, None, ALU.mult,
                 reads=bb, writes=[self.b_stsq])
        self.vop("tensor_mul", c(2), c(0), c(0), reads=bb, writes=[self.b_stsum])
        self.vop("tensor_sub", c(1), c(1), c(2), reads=bb, writes=[self.b_stsq])
        self.act(c(3), c(1), AF.Ln, bias=self.epsc[:],
                 reads=bb + [self.b_ones], writes=[self.b_stsum])
        self.act(c(2), c(3), AF.Exp, scale=-0.5, reads=bb, writes=[self.b_stsum])
        self.vop("tensor_mul", c(3), c(0), c(2), reads=bb, writes=[self.b_stsum])

    def phase_B(self, b):
        QT = self.bigB[0:65, 0:2048]
        KT = self.bigB[0:65, 2048:4096]
        VT = self.bigB[:, 4096:5136].rearrange("p (t k) -> p t k", k=65)
        ZF = self.bigB[0:64, 6144:8192]
        QSQ = self.bigB[0:64, 8192:10240]
        bQ, bK = self.bB(0, 2048), self.bB(2048, 4096)
        bV, bZ, bQs = self.bB(4096, 5456), self.bB(6144, 8192), self.bB(8192, 10240)
        row = lambda s, lo, hi: self.poolb[0:1, s, lo:hi]
        for (tile, bt) in ((QT, bQ), (KT, bK)):
            self.vop("tensor_mul", QSQ[:], tile[0:64, :], tile[0:64, :],
                     reads=bt, writes=bQs)
            mms = [(self.pL[0:1, j * 512:(j + 1) * 512], self.onesb[0:64, 0:1],
                    QSQ[:, j * 512:(j + 1) * 512]) for j in range(4)]
            def fnmm(E, mms=mms):
                last = None
                for (o, l, r) in mms:
                    last = E.matmul(o, l, r, start=True, stop=True)
                return last
            self.op("tensor", fnmm, reads=bQs + [self.b_ones],
                    writes=self.pLh(0) + self.pLh(1))
            self.vop("tensor_copy", row(2, 0, 2048), self.pL[0:1, 0:2048],
                     reads=self.pLh(0) + self.pLh(1), writes=[self.b_pool[2]])
            if tile is QT:
                self.act(row(1, 0, 2048), row(2, 0, 2048), AF.Sqrt,
                         reads=[self.b_pool[2]], writes=[self.b_pool[1]])
        self.vop("reduce_max", self.maxsc[0:1, 0:1], row(2, 0, 2048), axis=AX.X,
                 reads=[self.b_pool[2]], writes=[self.b_params])
        self.act(self.maxsc[0:1, 1:2], self.maxsc[0:1, 0:1], AF.Sqrt,
                 reads=[self.b_params], writes=[self.b_params])
        self.vop("tensor_scalar", row(0, 0, 2048), row(1, 0, 2048),
                 self.maxsc[0:1, 1:2], -1.0, ALU.mult, ALU.mult,
                 reads=[self.b_pool[1], self.b_params], writes=[self.b_pool[0]])
        self.dma(QT[64:65, :], row(0, 0, 2048), reads=[self.b_pool[0]], writes=bQ)
        for qb in range(2):
            kmax = (qb + 1) * 8
            zb = [self.b_pM[0], self.b_pM[1]]

            def emit_z(kc, kmax=kmax, zb=zb, bV=bV):
                pt = self.miscb[:, 4096 + (kc % 2) * 1024:4096 + (kc % 2 + 1) * 1024]
                vwide = self.bigB[:, 4096 + kc * 65:4096 + kc * 65 + 128]
                self.mmx([(self.pM[:, j * 512:(j + 1) * 512], vwide,
                           pt[:, j * 512:(j + 1) * 512],
                           kc == 0, kc == kmax - 1) for j in range(2)],
                         reads=bV + self.bB(4096, 5456) + [self.b_ptile[kc % 2]],
                         writes=zb)

            for kc in range(kmax):
                sp = self.pL[:, (kc % 2) * 1024:(kc % 2 + 1) * 1024]
                bsp = self.pLh(kc % 2)
                QT8 = self.bigB[0:128, 0:2048]
                KT8 = self.bigB[0:128, 2048:4096]
                self.mmx([(sp[:, j * 512:(j + 1) * 512],
                           KT8[:, kc * 128:(kc + 1) * 128],
                           QT8[:, qb * 1024 + j * 512:qb * 1024 + (j + 1) * 512],
                           True, True) for j in range(2)],
                         reads=bQ + bK, writes=bsp)
                pt = self.miscb[:, 4096 + (kc % 2) * 1024:4096 + (kc % 2 + 1) * 1024]
                bpt = self.b_ptile[kc % 2]
                d = kc * 128 - qb * 1024
                if d > 0:
                    self.gop("memset", pt[:, 0:d], 0.0, reads=[bpt], writes=[bpt])
                lo = max(d, 0)
                self.act(pt[:, lo:1024], sp[:, lo:1024], AF.Exp,
                         reads=bsp, writes=[bpt])
                if d >= 0:
                    self.vop("tensor_mul", pt[:, d:d + 128], pt[:, d:d + 128], self.ltmask[:],
                             reads=[bpt, self.b_lt], writes=[bpt])
                if kc > 0:
                    emit_z(kc - 1)
            emit_z(kmax - 1)
            self.vop("reciprocal", self.misc[64:65, 3072:4096],
                     self.pM[64:65, 0:1024], reads=zb, writes=[self.b_misc[3]])
            self.dma(self.misc[0:1, 3072:4096], self.misc[64:65, 3072:4096],
                     reads=[self.b_misc[3]], writes=[self.b_misc[3]])
            self.trans([(self.pT1[:, tq:tq + 1],
                         self.misc[0:1, 3072 + tq * 128:3072 + (tq + 1) * 128])
                        for tq in range(8)],
                       reads=(self.b_misc[3],), writes=[self.b_pT[0]], bf=False)
            self.vop("tensor_copy", self.reccol[:, qb * 8:(qb + 1) * 8],
                     self.pT1[:, 0:8], reads=[self.b_pT[0]], writes=[self.b_reccol])
            self.vop("tensor_copy", ZF[:, qb * 1024:(qb + 1) * 1024], self.pM[0:64, 0:1024],
                     reads=zb, writes=bZ)
        self.dma(self.pool[0:1, 2, :], self.bo_d.rearrange("(one d) -> one d", one=1),
                 reads=[self.b_pool[2]], writes=[self.b_pool[2]])
        self.vop("tensor_copy", self.poolb[0:1, 0, 0:1024], self.pool[0:1, 2, :],
                 reads=[self.b_pool[2]], writes=[self.b_pool[0]])
        self.vop("tensor_sub", self.poolb[0:1, 1, 0:1024], self.pool[0:1, 2, :],
                 self.poolb[0:1, 0, 0:1024],
                 reads=[self.b_pool[2], self.b_pool[0]], writes=[self.b_pool[1]])
        mms = []
        for j in range(2):
            for hl in range(2):
                src = self.poolb[0:1, hl, j * 512:(j + 1) * 512]
                mms.append((self.pL[:, j * 512:(j + 1) * 512],
                            self.onesb[0:1, 0:128], src))
        def fnb(E, mms=mms):
            last = None
            for i, (o, l, r) in enumerate(mms):
                last = E.matmul(o, l, r, start=(i % 2 == 0), stop=(i % 2 == 1))
            return last
        self.op("tensor", fnb,
                reads=[self.b_pool[0], self.b_pool[1], self.b_ones],
                writes=self.pLh(0))
        self.act(self.misc[:, 3072:4096], self.pL[:, 0:1024], AF.Copy,
                 reads=self.pLh(0), writes=[self.b_misc[3]])
        for t in range(TT):
            pp = t % 2
            stok = self.misc[:, pp * 1024:(pp + 1) * 1024]
            bst = self.b_misc[pp]
            for half in range(2):
                ps = self.pL[:, (t * 2 + half) % 2 * 1024:((t * 2 + half) % 2) * 1024 + 512]
                bps = self.b_pL[2 * ((t * 2 + half) % 2)]
                self.mm([(ps, ZF[:, t * 128:(t + 1) * 128], self.wo_sb(half))],
                        reads=bZ + self.bB(10240, 12800), writes=[bps])
                self.op("vector",
                        lambda E, o=stok[:, half * 512:(half + 1) * 512], i0=ps,
                               sc=self.reccol[:, t:t + 1],
                               i1=self.misc[:, 3072 + half * 512:3072 + (half + 1) * 512]:
                        E.scalar_tensor_tensor(o, i0, sc, i1, ALU.mult, ALU.add),
                        reads=[bps, self.b_reccol, self.b_misc[3]], writes=[bst])
            self.dma(self.s_spill[b, t * 128:(t + 1) * 128, :], stok,
                     reads=[bst, self.b_sspill], writes=[])
            if "s" in self.dmp:
                self.dma(self.dmp["s"][b, t * 128:(t + 1) * 128, :], stok,
                         reads=[bst], writes=[])

    def _attn_norms(self, QT, KT, QSQ, bQ, bK, bQs):
        W = QT.shape[1]
        nj = W // 512
        row = lambda s: self.poolb[0:1, s, 0:W]
        for (tile, bt, dst_s) in ((QT, bQ, 1), (KT, bK, 2)):
            self.vop("tensor_mul", QSQ[:, 0:W], tile[0:64, :], tile[0:64, :],
                     reads=bt, writes=bQs)
            def fnmm(E, QSQ=QSQ, nj=nj):
                last = None
                for j in range(nj):
                    last = E.matmul(self.pL[0:1, j * 512:(j + 1) * 512],
                                    self.onesb[0:64, 0:1],
                                    QSQ[:, j * 512:(j + 1) * 512],
                                    start=True, stop=True)
                return last
            self.op("tensor", fnmm, reads=tuple(bQs) + (self.b_ones,),
                    writes=self.pLh(0) + self.pLh(1))
            self.vop("tensor_copy", row(2), self.pL[0:1, 0:W],
                     reads=self.pLh(0) + self.pLh(1), writes=[self.b_pool[2]])
            if dst_s == 1:
                self.act(row(1), row(2), AF.Sqrt,
                         reads=[self.b_pool[2]], writes=[self.b_pool[1]])
        self.vop("reduce_max", self.maxsc[0:1, 0:1], row(2), axis=AX.X,
                 reads=[self.b_pool[2]], writes=[self.b_params])
        self.act(self.maxsc[0:1, 1:2], self.maxsc[0:1, 0:1], AF.Sqrt,
                 reads=[self.b_params], writes=[self.b_params])
        self.vop("tensor_scalar", row(0), row(1),
                 self.maxsc[0:1, 1:2], -1.0, ALU.mult, ALU.mult,
                 reads=[self.b_pool[1], self.b_params], writes=[self.b_pool[0]])
        self.dma(QT[64:65, :], row(0), reads=[self.b_pool[0]], writes=bQ)

    def phase_C(self, b):
        for t in range(TT):
            self._stats_collect(t, self.xn_tok[:, t, :], self.b_xn[t])
        self._stats_finalize()
        pend = None
        for t in range(TT + 1):
            if t < TT:
                pp = t % 2
                xt = self.xn_tok[:, t, :]
                u = self.pool[:, pp, :]
                self.vop("tensor_scalar", u, xt,
                         self.statA[:, t, 2:3], self.statA[:, t, 3:4],
                         ALU.mult, ALU.subtract,
                         reads=[self.b_xn[t], self.b_stsum], writes=[self.b_pool[pp]])
                xc = self.hg[:, pp * 1024:(pp + 1) * 1024]
                eng = "vop" if b == 0 else "gop"
                getattr(self, eng)("tensor_copy", xc, u, reads=[self.b_pool[pp]],
                                   writes=[self.b_hg[pp]])
                pt = (self.pT1b, self.pT2b)[pp]
                self.trans([(pt[:, c * 128:(c + 1) * 128], xc[:, c * 128:(c + 1) * 128])
                            for c in range(NC)],
                           reads=(self.b_hg[pp],), writes=[self.b_pT[pp]])
            if pend is not None:
                tp_ = pend
                pp_ = tp_ % 2
                pt_ = (self.pT1b, self.pT2b)[pp_]
                for c in range(NC):
                    dst = self.ln_fm[:, c, tp_ * 128:(tp_ + 1) * 128]
                    src = pt_[:, c * 128:(c + 1) * 128]
                    if c % 2 == 0:
                        self.act(dst, src, AF.Identity,
                                 scale=self.gam1[:, c:c + 1],
                                 bias=self.bet1[:, c:c + 1],
                                 reads=[self.b_pT[pp_], self.b_params],
                                 writes=[self.b_lnfm[tp_]])
                    else:
                        self.vop("tensor_scalar", dst, src,
                                 self.gam1[:, c:c + 1], self.bet1[:, c:c + 1],
                                 ALU.mult, ALU.add,
                                 reads=[self.b_pT[pp_], self.b_params],
                                 writes=[self.b_lnfm[tp_]])
            pend = t if t < TT else None
        QTg = lambda i: self.bigB[0:65, i * 2048:(i + 1) * 2048]
        KTg = lambda i: self.bigB[0:65, 4096 + i * 2048:4096 + (i + 1) * 2048]
        VG = self.bigB[:, 8192:10272].rearrange("p (h t k) -> p h t k", h=2, k=65)
        VTW = self.bigB[:, 10272:10784]
        QSQ = self.bigB[0:64, 10784:12832]
        qkvsl = lambda i: self.bigB[:, 12832 + i * 1024:12832 + (i + 1) * 1024].rearrange(
            "p (c k) -> p c k", k=128)
        bQg = lambda i: self.bB(i * 2048, (i + 1) * 2048)
        bKg = lambda i: self.bB(4096 + i * 2048, 4096 + (i + 1) * 2048)
        bVG = self.bB(8192, 10272)
        bVTW = self.bB(10272, 10784)
        bQs = self.bB(10784, 12832)
        bSL = self.bB(12832, 15904)
        self.vop("memset", self.bigB[64:128, 0:4096], 0.0,
                 reads=bQg(0) + bQg(1), writes=bQg(0) + bQg(1))
        self.vop("memset", self.bigB[64:128, 4096:8192], 0.0,
                 reads=bKg(0) + bKg(1), writes=bKg(0) + bKg(1))
        for i in range(2):
            self.vop("memset", KTg(i)[64:65, :], 1.0, reads=bKg(i), writes=bKg(i))
            self.vop("memset", VG[:, i, :, 64:65], 1.0, reads=bVG, writes=bVG)
        wqkv_r = self.wqkv_bf.rearrange("(c p) n -> p c n", p=128)
        lnfm_all = list(self.b_lnfm)
        for g2 in range(8):
            for i in range(3):
                cs = i * 1024 + g2 * 128
                self.dma(qkvsl(i)[:], wqkv_r[:, :, cs:cs + 128],
                         reads=bSL, writes=bSL + [self.b_wbf])
            for i in (0, 1, "norms", 2):
                if i == "norms":
                    for h01 in range(2):
                        self._attn_norms(QTg(h01), KTg(h01), QSQ,
                                         bQg(h01), bKg(h01), bQs)
                    continue
                for cg in range(4):
                    pp = (i * 4 + cg) % 2
                    ps = (self.pT1, self.pT2)[pp]
                    bps = self.b_pT[pp]
                    self.mm([(ps[:], qkvsl(i)[:, c, :],
                              self.ln_fm[:, c, cg * 512:(cg + 1) * 512])
                             for c in range(NC)],
                            reads=bSL + lnfm_all, writes=[bps])
                    if i == 0:
                        for h01 in range(2):
                            self.act(QTg(h01)[0:64, cg * 512:(cg + 1) * 512],
                                     ps[h01 * 64:(h01 + 1) * 64, :], AF.Copy, scale=0.125,
                                     reads=[bps], writes=bQg(h01))
                    elif i == 1:
                        for h01 in range(2):
                            self.vop("tensor_copy",
                                     KTg(h01)[0:64, cg * 512:(cg + 1) * 512],
                                     ps[h01 * 64:(h01 + 1) * 64, :],
                                     reads=[bps], writes=bKg(h01))
                    else:
                        self.vop("tensor_copy", VTW[:], ps[:],
                                 reads=[bps], writes=bVTW)
                        for h01 in range(2):
                            for t4 in range(4):
                                tg = cg * 4 + t4
                                po = (self.pT1b, self.pT2b)[1 - pp]
                                self.trans([(po[:, 960:1024],
                                             VTW[h01 * 64:(h01 + 1) * 64,
                                                 t4 * 128:(t4 + 1) * 128],
                                             h01 * 64)],
                                           reads=tuple(bVTW), writes=[self.b_pT[1 - pp]])
                                self.vop("tensor_copy", VG[:, h01, tg, 0:64],
                                         po[:, 960:1024],
                                         reads=[self.b_pT[1 - pp]], writes=bVG)
            pending = [None]

            def make_evac(h01, qb, par, ztile, zb, g2=g2):
                def evA():
                    rsl = (h01 + qb) % 2
                    lnrow = self.misc[64:65, rsl * 1024:(rsl + 1) * 1024]
                    if par == 0:
                        self.vop("reciprocal", lnrow, self.pM[64:65, 0:1024],
                                 reads=zb, writes=[self.b_misc[rsl]])
                    else:
                        self.vop("reciprocal", lnrow[:, 0:512],
                                 self.pT1[64:65, 0:512],
                                 reads=zb, writes=[self.b_misc[rsl]])
                        self.vop("reciprocal", lnrow[:, 512:1024],
                                 self.pT2[64:65, 0:512],
                                 reads=zb, writes=[self.b_misc[rsl]])
                    recrow = self.poolb[64:65, 2, rsl * 1024:(rsl + 1) * 1024]
                    self.vop("tensor_copy", recrow, lnrow,
                             reads=[self.b_misc[rsl]],
                             writes=[self.b_pool[2]])

                def ev():
                    rsl = (h01 + qb) % 2
                    recrow = self.poolb[64:65, 2, rsl * 1024:(rsl + 1) * 1024]
                    def fnb(E, recrow=recrow, par=par):
                        last = None
                        for j in range(2):
                            odst = self.pL[0:64, par * 1024 + j * 512:
                                           par * 1024 + (j + 1) * 512]
                            last = E.matmul(
                                odst, self.onesb[64:65, 0:64],
                                recrow[:, j * 512:(j + 1) * 512],
                                start=True, stop=True)
                        return last
                    self.op("tensor", fnb,
                            reads=[self.b_pool[2], self.b_ones],
                            writes=self.pLh(par))
                    rb = self.misc[0:64, rsl * 1024:(rsl + 1) * 1024]
                    self.vop("tensor_copy", rb,
                             self.pL[0:64, par * 1024:par * 1024 + 1024],
                             reads=self.pLh(par), writes=[self.b_misc[rsl]])
                    ydst = self.bigA[h01 * 64:(h01 + 1) * 64,
                                     g2 * 2048 + qb * 1024:g2 * 2048 + (qb + 1) * 1024]
                    if par == 0:
                        self.vop("tensor_mul", ydst, self.pM[0:64, 0:1024], rb,
                                 reads=zb + [self.b_misc[rsl]],
                                 writes=[self.b_bigA[g2]])
                    else:
                        self.vop("tensor_mul", ydst[:, 0:512], self.pT1[0:64, 0:512],
                                 rb[:, 0:512],
                                 reads=zb + [self.b_misc[rsl]],
                                 writes=[self.b_bigA[g2]])
                        self.vop("tensor_mul", ydst[:, 512:1024], self.pT2[0:64, 0:512],
                                 rb[:, 512:1024],
                                 reads=zb + [self.b_misc[rsl]],
                                 writes=[self.b_bigA[g2]])
                return evA, ev

            for li, (h01, qb) in enumerate(((0, 0), (0, 1), (1, 0), (1, 1))):
                QT = self.bigB[0:128, h01 * 2048:(h01 + 1) * 2048]
                KT = self.bigB[0:128, 4096 + h01 * 2048:4096 + (h01 + 1) * 2048]
                bQ, bK = bQg(h01), bKg(h01)
                par = li % 2
                if par == 0:
                    ztile = lambda j: self.pM[:, j * 512:(j + 1) * 512]
                    zb = [self.b_pM[0], self.b_pM[1]]
                else:
                    ztile = lambda j: (self.pT1, self.pT2)[j][:, 0:512]
                    zb = [self.b_pT[0], self.b_pT[1]]

                def emit_z(kc, zb=zb, h01=h01, ztile=ztile):
                    pt = self.miscb[:, 4096 + (kc % 3) * 1024:4096 + (kc % 3 + 1) * 1024]
                    vwide = self.bigB[:, 8192 + h01 * 1040 + kc * 65:
                                      8192 + h01 * 1040 + kc * 65 + 128]
                    self.mmx([(ztile(j),
                               vwide, pt[:, j * 512:(j + 1) * 512],
                               kc == 0, kc == 15) for j in range(2)],
                             reads=bVG + self.bB(8192, 10784) + [self.b_ptile[kc % 3]],
                             writes=zb)

                for kc in range(16):
                    sp = self.pL[:, (kc % 2) * 1024:(kc % 2 + 1) * 1024]
                    bsp = self.pLh(kc % 2)
                    self.mmx([(sp[:, j * 512:(j + 1) * 512],
                               KT[:, kc * 128:(kc + 1) * 128],
                               QT[:, qb * 1024 + j * 512:qb * 1024 + (j + 1) * 512],
                               True, True) for j in range(2)],
                             reads=bQ + bK, writes=bsp)
                    pt = self.miscb[:, 4096 + (kc % 3) * 1024:4096 + (kc % 3 + 1) * 1024]
                    bpt = self.b_ptile[kc % 3]
                    wpt = [bpt, self.b_misc[3]] if kc % 3 == 2 else [bpt]
                    self.act(pt[:], sp[:], AF.Exp, reads=bsp, writes=wpt)
                    if kc > 1:
                        emit_z(kc - 2)
                    if kc == 1 and pending[0] is not None:
                        pending[0][0]()
                    if kc == 3 and pending[0] is not None:
                        pending[0][1]()
                        pending[0] = None
                emit_z(14)
                emit_z(15)
                ev = make_evac(h01, qb, par, ztile, zb)
                if li < 3:
                    pending[0] = ev
                else:
                    ev[0]()
                    ev[1]()
        wps = self.bigB[:, 0:8192].rearrange("p (c n) -> p c n", n=1024)
        bWP = self.bB(0, 8192)
        self.dma(wps[:], self.wproj_bf.rearrange("(c p) n -> p c n", p=128),
                 reads=bWP, writes=bWP + [self.b_wbf])
        yall = list(self.b_bigA)
        for t in range(TT):
            for half in range(2):
                pp = (t * 2 + half) % 2
                ps = (self.pT1, self.pT2)[pp]
                bps = self.b_pT[pp]
                self.mm([(ps[:], self.bigA[:, c * 2048 + t * 128:c * 2048 + (t + 1) * 128],
                          wps[:, c, half * 512:(half + 1) * 512])
                         for c in range(NC)],
                        reads=yall + bWP, writes=[bps])
                self.vop("tensor_add", self.xn_tok[:, t, half * 512:(half + 1) * 512],
                         self.xn_tok[:, t, half * 512:(half + 1) * 512], ps[:],
                         reads=[bps, self.b_xn[t]], writes=[self.b_xn[t]])
            if "xn2" in self.dmp:
                self.dma(self.dmp["xn2"][b, t * 128:(t + 1) * 128, :],
                         self.xn_tok[:, t, :], reads=[self.b_xn[t]], writes=[])

    def phase_D(self, b):
        for t in range(TT):
            self._stats_collect(t, self.xn_tok[:, t, :], self.b_xn[t])
        self._stats_finalize()
        for t in range(TT):
            pp = t % 2
            xt = self.xn_tok[:, t, :]
            u = self.pool[:, pp, :]
            self.vop("tensor_scalar", u, xt,
                     self.statA[:, t, 2:3], self.statA[:, t, 3:4],
                     ALU.mult, ALU.subtract,
                     reads=[self.b_xn[t], self.b_stsum], writes=[self.b_pool[pp]])
            uhi = self.hg[:, 0:1024]
            ulo = self.hg[:, 1024:2048]
            self.gop("tensor_copy", uhi, u, reads=[self.b_pool[pp]], writes=[self.b_hg[0]])
            self.gop("tensor_sub", ulo, u, uhi,
                     reads=[self.b_pool[pp], self.b_hg[0]], writes=[self.b_hg[1]])
            self.trans([(self.pT1b[:, c * 128:(c + 1) * 128], uhi[:, c * 128:(c + 1) * 128])
                        for c in range(NC)],
                       reads=(self.b_hg[0],), writes=[self.b_pT[0]])
            self.trans([(self.pT2b[:, c * 128:(c + 1) * 128], ulo[:, c * 128:(c + 1) * 128])
                        for c in range(NC)],
                       reads=(self.b_hg[1],), writes=[self.b_pT[1]])
            for c in range(NC):
                dst = self.ln_fm[:, c, t * 128:(t + 1) * 128]
                src = self.pT1b[:, c * 128:(c + 1) * 128]
                if c % 2 == 0:
                    self.act(dst, src, AF.Identity,
                             scale=self.gam2[:, c:c + 1], bias=self.bet2[:, c:c + 1],
                             reads=[self.b_pT[0], self.b_params],
                             writes=[self.b_lnfm[t]])
                else:
                    self.vop("tensor_scalar", dst, src,
                             self.gam2[:, c:c + 1], self.bet2[:, c:c + 1],
                             ALU.mult, ALU.add,
                             reads=[self.b_pT[0], self.b_params],
                             writes=[self.b_lnfm[t]])
            uhf = self.hg[:, 2048:3072]
            ulf = self.hg[:, 3072:4096]
            self.vop("tensor_copy", uhf, self.pT1b[:, 0:1024],
                     reads=[self.b_pT[0]], writes=[self.b_hg[2]])
            self.act(ulf, self.pT2b[:, 0:1024], AF.Copy,
                     reads=[self.b_pT[1]], writes=[self.b_hg[3]])
            X = (t % 2) * 1024 + ((t // 2) % 8) * 128
            lg = self.pL[0:NE, X:X + 128]
            blg = self.b_pL[X // 512]
            mms = []
            for c in range(NC):
                mms.append((lg, self.gwg_hi[:, c, :], uhf[:, c * 128:(c + 1) * 128]))
                mms.append((lg, self.gwg_lo[:, c, :], uhf[:, c * 128:(c + 1) * 128]))
                mms.append((lg, self.gwg_hi[:, c, :], ulf[:, c * 128:(c + 1) * 128]))
            self.mm(mms, reads=[self.b_hg[2], self.b_hg[3], self.b_gw], writes=[blg])
            lgt_sl = self.stage[0:NE, 0:128]
            self.act(lgt_sl, lg, AF.Identity, bias=self.gbadd[:],
                     reads=[blg, self.b_params], writes=[self.b_stage[0]])
            po = (self.pT1, self.pT2)[t % 2]
            self.trans([(po[:, 0:NE], lgt_sl)],
                       reads=(self.b_stage[0],), writes=[self.b_pT[t % 2]], bf=False)
            self.vop("tensor_copy", self.lgt_tok[:, t, :], po[:, 0:NE],
                     reads=[self.b_pT[t % 2]], writes=[self.b_masks])
            self.vop("reduce_max", self.statA[:, t, 0:1], self.lgt_tok[:, t, :], axis=AX.X,
                     reads=[self.b_masks], writes=[self.b_stsq])
            self.vop("tensor_scalar", self.masks[:, b, t, :], self.lgt_tok[:, t, :],
                     self.statA[:, t, 0:1], None, ALU.is_ge,
                     reads=[self.b_masks, self.b_stsq], writes=[self.b_masks])
            self.trans([(self.pT1b[:, c * 128:(c + 1) * 128],
                         self.ln_fm[:, c, t * 128:(t + 1) * 128])
                        for c in range(NC)],
                       reads=(self.b_lnfm[t],), writes=[self.b_pT[0]])
            us = self.stageb[:, 2048 + (t % 2) * 1024:3072 + (t % 2) * 1024]
            self.vop("tensor_copy", us, self.pT1b[:, 0:1024],
                     reads=[self.b_pT[0], self.b_ust[t % 2]],
                     writes=[self.b_ust[t % 2], self.b_stage[1]])
            self.dma(self.u_buf[b * T + t * 128:b * T + (t + 1) * 128, :], us,
                     reads=[self.b_ust[t % 2], self.b_stage[1], self.b_ubuf],
                     writes=[])
            if "lgt" in self.dmp:
                self.dma(self.dmp["lgt"][b, t * 128:(t + 1) * 128, 0:NE],
                         self.lgt_tok[:, t, :], reads=[self.b_masks], writes=[])
                self.dma(self.dmp["lgt"][b, t * 128:(t + 1) * 128, NE:2 * NE],
                         self.masks[:, b, t, :], reads=[self.b_masks], writes=[])

    def index_build(self):
        bi = self.b_idx
        ic = self.b_icons
        NT = NB * TT
        allm = [self.b_misc[0], self.b_misc[1], self.b_misc[2], self.b_misc[3]]
        self.dma(self.ltf, self.lt_dram.ap(), reads=[], writes=[bi] + allm)
        self.dma(self.thr16, self.thr16_dram.ap(), reads=[], writes=[bi])
        self.dma(self.thrp, self.thrp_dram.ap(), reads=[], writes=[bi])
        self.dma(self.evalsf, self.evals_dram.ap(), reads=[], writes=[bi])
        self.gop("iota", self.tid32, [[128, NB * TT]], base=0,
                 channel_multiplier=1, reads=[], writes=[bi])
        self.gop("iota", self.pcol, [[0, 1]], base=0, channel_multiplier=1,
                 allow_small_or_imprecise_dtypes=True, reads=[], writes=[bi])
        ntv = self.misc[:, 640:640 + NB * TT].rearrange("p (a b) -> p a b", b=TT)
        self.vop("memset", ntv, 1.0, reads=[bi], writes=[bi])
        for e in range(NE):
            self.vop("tensor_mul", self.masks[:, :, :, e], self.masks[:, :, :, e],
                     ntv, reads=[bi, self.b_masks], writes=[self.b_masks])
            if e < NE - 1:
                self.vop("tensor_sub", ntv, ntv, self.masks[:, :, :, e],
                         reads=[self.b_masks, bi], writes=[bi])
        mms = [(self.pT1[0:1, bt * NE:(bt + 1) * NE], self.onescol,
                self.masks[:, bt // TT, bt % TT, :], True, True)
               for bt in range(NT)]
        self.mmx(mms, reads=[self.b_masks, bi], writes=[self.b_pT[0]])
        self.vop("tensor_copy", self.csum, self.pT1[0:1, 0:NT * NE],
                 reads=[self.b_pT[0]], writes=[bi])
        self.dma(self.iscr.rearrange("(one n) -> one n", one=1), self.csum,
                 reads=[bi], writes=[bi])
        self.dma(self.csum32, self.iscr.rearrange("(a b) -> a b", b=NE),
                 reads=[bi], writes=[bi])
        self.mm([(self.pT2[0:NT, 0:NE], self.ltf[0:NT, 0:NT], self.csum32)],
                reads=[bi, ic], writes=[self.b_pT[1]])
        self.vop("tensor_sub", self.carry32, self.pT2[0:NT, 0:NE], self.csum32,
                 reads=[self.b_pT[1], bi], writes=[bi])
        self.mm([(self.pT2[0:1, 128:128 + NE], self.ltf[0:NT, 127:128],
                  self.csum32)],
                reads=[bi], writes=[self.b_pT[1]])
        self.vop("tensor_copy", self.srow[0:1, 0:NE], self.pT2[0:1, 128:128 + NE],
                 reads=[self.b_pT[1]], writes=[bi])
        self.dma(self.iscr.rearrange("(a b) -> a b", b=NE), self.carry32,
                 reads=[bi], writes=[bi])
        self.dma(self.csum, self.iscr.rearrange("(one n) -> one n", one=1),
                 reads=[bi], writes=[bi])
        self.trans([(self.pT1[0:NE, 0:1], self.srow[0:1, 0:NE])], reads=(bi,),
                   writes=[self.b_pT[0]], bf=False)
        self.vop("tensor_copy", self.icol[:, 0:1], self.pT1[0:NE, 0:1],
                 reads=[self.b_pT[0]], writes=[bi])
        self.vop("tensor_scalar", self.irow[:, 0:16], self.thr16, self.icol[:, 0:1],
                 None, ALU.is_lt, reads=[bi, ic], writes=[bi])
        self.vop("reduce_sum", self.icol[:, 1:2], self.irow[:, 0:16], axis=AX.X,
                 reads=[bi], writes=[bi])
        self.vop("tensor_scalar", self.icol[:, 1:2], self.icol[:, 1:2],
                 float(PASS_TOK), None, ALU.mult, reads=[bi], writes=[bi])
        self.mm([(self.pT1[0:NE, 4:5], self.ltf[0:NE, 0:NE], self.icol[:, 1:2])],
                reads=[bi, ic], writes=[self.b_pT[0]])
        self.vop("tensor_sub", self.icol[:, 2:3], self.pT1[0:NE, 4:5], self.icol[:, 1:2],
                 reads=[self.b_pT[0], bi], writes=[bi])
        self.vop("tensor_add", self.icol[:, 3:4], self.icol[:, 2:3], self.icol[:, 1:2],
                 reads=[bi], writes=[bi])
        self.trans([(self.pT2[0:1, 8:16], self.icol[:, 2:3])], reads=(bi,),
                   writes=[self.b_pT[1]], bf=False)
        self.vop("tensor_copy", self.soffr, self.pT2[0:1, 8:16],
                 reads=[self.b_pT[1]], writes=[bi])
        self.vop("tensor_scalar", self.irow[:, 16:16 + NPASS], self.thrp,
                 self.icol[:, 2:3], None, ALU.is_ge, reads=[bi, ic], writes=[bi])
        self.vop("tensor_scalar", self.irow[:, 40:40 + NPASS], self.thrp,
                 self.icol[:, 3:4], None, ALU.is_lt, reads=[bi, ic], writes=[bi])
        self.vop("tensor_mul", self.irow[:, 16:16 + NPASS], self.irow[:, 16:16 + NPASS],
                 self.irow[:, 40:40 + NPASS], reads=[bi], writes=[bi])
        self.mm([(self.pT1[0:1, 16:16 + NPASS], self.evalsf,
                  self.irow[:, 16:16 + NPASS])],
                reads=[bi, ic], writes=[self.b_pT[0]])
        self.vop("tensor_scalar", self.srow[0:1, 8:8 + NPASS],
                 self.pT1[0:1, 16:16 + NPASS], 512.0, None, ALU.mult,
                 reads=[self.b_pT[0]], writes=[bi])
        self.mm([(self.pM[0:128, 0:NPASS], self.onesrow,
                  self.srow[0:1, 8:8 + NPASS])],
                reads=[bi, ic], writes=[self.b_pM[0]])
        self.vop("tensor_scalar", self.idxf[:, 0:NPASS], self.pM[0:128, 0:NPASS], self.pcol,
                 None, ALU.add, reads=[self.b_pM[0], ic], writes=[bi])
        self.vop("tensor_copy", self.idxs[:, 0:NPASS], self.idxf[:, 0:NPASS],
                 reads=[bi], writes=[self.b_widx])
        self._e_wload(0, 0)
        self._e_wload(0, 1)
        b_it = [Buf(), Buf()]
        b_ip = [self.b_pM[1], self.b_pT[0]]
        for bt in range(NT):
            sl = bt % 2
            m_bt = self.masks[:, bt // TT, bt % TT, :]
            ps = (self.pM[:, 512:512 + NE], self.pT1[:, 96:96 + NE])[sl]
            mms = [(ps, self.ltf, m_bt, True, False),
                   (ps, self.onesrow,
                    self.csum[0:1, bt * NE:(bt + 1) * NE], False, False),
                   (ps, self.onesrow, self.soffr, False, True)]
            self.mmx(mms, reads=[self.b_masks, bi, ic], writes=[b_ip[sl]])
            scr = self.idxf[:, sl * NE:(sl + 1) * NE]
            self.vop("tensor_sub", scr, ps, m_bt,
                     reads=[b_ip[sl], self.b_masks, self.b_widx, b_it[sl]],
                     writes=[b_it[sl]])
            self.vop("tensor_mul", scr, scr, m_bt,
                     reads=[b_it[sl], self.b_masks], writes=[b_it[sl]])
            self.vop("reduce_sum", self.posf[:, sl:sl + 1], scr, axis=AX.X,
                     reads=[b_it[sl]], writes=[b_it[sl]])
            self.vop("tensor_copy", self.idxs[:, 16 + sl:17 + sl],
                     self.posf[:, sl:sl + 1], reads=[b_it[sl]], writes=[b_it[sl]])
            for dst, bdst in ((self.gidx_buf.rearrange("(a b) -> a b", b=1),
                               self.b_gidxb),
                              (self.sidx_buf.rearrange("(a b) -> a b", b=1),
                               self.b_sidxb)):
                def fn(E, o=dst, ix=self.idxs[:, 16 + sl:17 + sl],
                       i=self.tid32[:, bt:bt + 1]):
                    return E.indirect_dma_start(
                        out=o, out_offset=IOA(ap=ix, axis=0), in_=i, in_offset=None)
                self.S.gma(fn, reads=[b_it[sl], ic, bdst], writes=[])
        self.vop("memset", self.misc[0:1, 0:1], 0.0,
                 reads=[], writes=[bi, b_it[0], b_it[1]] + allm)

    def _utok(self):
        return self.ln_fm[:, 0:2, :].rearrange("p a n -> p (a n)")

    def _ufm(self):
        return self.ln_fm[:, 2:4, :].rearrange("p a (b n) -> p (a b) n", n=PASS_TOK)

    def _yacc(self):
        return self.lnfm_f32[:, 4:8, :]

    def _gsl(self, s):
        return self.idxs[:, 18 + 4 * s:22 + 4 * s]

    def _ssl(self, s):
        return self.idxs[:, 26 + 4 * s:30 + 4 * s]

    def _e_gload(self, ps):
        sl = ps % 2
        c0 = ps * PASS_TOK
        for c in range(4):
            if not self._gidx_claimed:
                self._gidx_claimed = True
                wg, rg = [self.b_gidxb], []
                ws, rs_ = [self.b_sidxb], []
            else:
                wg, rg = [], [self.b_gidxb]
                ws, rs_ = [], [self.b_sidxb]
            self.dma(self._gsl(sl)[:, c:c + 1],
                     self.gidx_buf[c0 + c * 128:c0 + (c + 1) * 128].rearrange(
                         "(p one) -> p one", one=1),
                     reads=[self.b_gsb[sl]] + rg, writes=[self.b_gsb[sl]] + wg)
            self.dma(self._ssl(sl)[:, c:c + 1],
                     self.sidx_buf[c0 + c * 128:c0 + (c + 1) * 128].rearrange(
                         "(p one) -> p one", one=1),
                     reads=[self.b_gsb[sl]] + rs_, writes=[self.b_gsb[sl]] + ws)

    def _e_ugather(self, ps):
        sl = ps % 2
        ut = self._utok()
        for c in range(4):
            dst = ut[:, c * 1024:(c + 1) * 1024]
            def fn(E, o=dst, ix=self._gsl(sl)[:, c:c + 1]):
                return E.indirect_dma_start(out=o, out_offset=None, in_=self.u_buf,
                                            in_offset=IOA(ap=ix, axis=0))
            rd = [self.b_gsb[sl], self.b_utok[0]]
            if ps < 1:
                rd += list(self.b_lnfm)
            if not self._ubuf_claimed:
                self._ubuf_claimed = True
                self.S.gma(fn, reads=rd, writes=[self.b_utok[0], self.b_ubuf])
            else:
                self.S.gma(fn, reads=rd + [self.b_ubuf],
                           writes=[self.b_utok[0]])

    def _e_wload(self, ps, hb):
        wsl = hb % 2
        eo = hb * 128 * 8192
        for (big, bufs, src) in ((self.bigA, self.bA, self.w1bf),
                                 (self.bigB, self.bB, self.w2bf)):
            for h2 in range(2):
                dst = big[:, wsl * 8192 + h2 * 4096:wsl * 8192 + (h2 + 1) * 4096]
                def fn(E, o=dst, s=src, ix=self.idxs[:, ps:ps + 1],
                       eo=eo + h2 * 4096):
                    return E.indirect_dma_start(out=o, out_offset=None, in_=s,
                                                in_offset=IOA(ap=ix, axis=0),
                                                element_offset=eo)
                bl = bufs(wsl * 8192, (wsl + 1) * 8192)
                if not self._wcvt_claimed:
                    self._wcvt_claimed = True
                    self.S.gma(fn, reads=[self.b_widx] + bl,
                               writes=bl + [self.b_wcvt])
                else:
                    self.S.gma(fn, reads=[self.b_widx, self.b_wcvt] + bl,
                               writes=bl)

    def phase_E_sparse(self):
        W1q = lambda s: self.bigA[:, s * 8192:(s + 1) * 8192].rearrange(
            "p (c n) -> p c n", n=1024)
        W2q = lambda s: self.bigB[:, s * 8192:(s + 1) * 8192].rearrange(
            "p (c n) -> p c n", n=1024)
        bW1 = lambda s: self.bA(s * 8192, (s + 1) * 8192)
        bW2 = lambda s: self.bB(s * 8192, (s + 1) * 8192)
        hgv = self.hg.rearrange("p (c n) -> p c n", n=PASS_TOK)
        ya = self._yacc()
        self._e_gload(0)
        self._e_ugather(0)
        for ps in range(NPASS):
            sl = ps % 2
            if ps + 1 < NPASS:
                self._e_gload(ps + 1)
            ut = self._utok()
            fence = list(self.b_lnfm) if ps < 1 else []
            uf = self._ufm()
            trs0, trs1, trs2 = [], [], []
            for c2 in range(4):
                for fc in range(8):
                    src_t = ut[:, c2 * 1024 + fc * 128:c2 * 1024 + (fc + 1) * 128]
                    if c2 < 2:
                        trs0.append((self.pMb[:, (c2 * 8 + fc) * 128:
                                              (c2 * 8 + fc + 1) * 128], src_t))
                    elif c2 == 2:
                        trs1.append((self.pT1b[:, fc * 128:(fc + 1) * 128], src_t))
                    else:
                        trs2.append((self.pT2b[:, fc * 128:(fc + 1) * 128], src_t))
            self.trans(trs0, reads=(self.b_utok[0],),
                       writes=[self.b_pM[0], self.b_pM[1]])
            self.trans(trs1, reads=(self.b_utok[0],), writes=[self.b_pT[0]])
            self.trans(trs2, reads=(self.b_utok[0],), writes=[self.b_pT[1]])
            self.vop("tensor_copy", uf[:, :, 0:128],
                     self.pMb[:, 0:1024].rearrange("p (c n) -> p c n", n=128),
                     reads=[self.b_pM[0], self.b_pM[1], self.b_ufm[0]] + fence,
                     writes=[self.b_ufm[0]])
            self.vop("tensor_copy", uf[:, :, 128:256],
                     self.pMb[:, 1024:2048].rearrange("p (c n) -> p c n", n=128),
                     reads=[self.b_pM[0], self.b_pM[1], self.b_ufm[0]],
                     writes=[self.b_ufm[0]])
            self.vop("tensor_copy", uf[:, :, 256:384],
                     self.pT1b[:, 0:1024].rearrange("p (c n) -> p c n", n=128),
                     reads=[self.b_pT[0], self.b_ufm[0]], writes=[self.b_ufm[0]])
            self.vop("tensor_copy", uf[:, :, 384:512],
                     self.pT2b[:, 0:1024].rearrange("p (c n) -> p c n", n=128),
                     reads=[self.b_pT[1], self.b_ufm[0]], writes=[self.b_ufm[0]])
            for hb in range(4):
                wsl = hb % 2
                nps, nhb = (ps, hb + 1) if hb < 3 else (ps + 1, 0)
                if nps < NPASS and not (nps == 0 and nhb <= 1):
                    self._e_wload(nps, nhb)
                if hb == 2 and ps + 1 < NPASS:
                    self._e_ugather(ps + 1)
                bh = list(self.b_hg)
                for rnd in range(4):
                    pr = (rnd % 2) * 1024
                    mms = []
                    for hcq in range(2):
                        hc = rnd * 2 + hcq
                        for fc in range(NC):
                            mms.append((self.pL[:, pr + hcq * 512:
                                                pr + (hcq + 1) * 512],
                                        W1q(wsl)[:, fc, hc * 128:(hc + 1) * 128],
                                        uf[:, fc, :], fc == 0, fc == NC - 1))
                    self.mmx(mms, reads=bW1(wsl) + [self.b_ufm[0]],
                             writes=self.pLh(rnd % 2))
                    self.act(self.hg[:, rnd * 1024:(rnd + 1) * 1024],
                             self.pL[:, pr:pr + 1024],
                             AF.Gelu_apprx_tanh, reads=self.pLh(rnd % 2),
                             writes=[self.b_hg[rnd]])
                for half in range(2):
                    mms = []
                    for t2 in range(4):
                        if half == 0:
                            out = (self.pM[:, 0:512], self.pM[:, 512:1024],
                                   self.pT1[:, 0:512], self.pT2[:, 0:512])[t2]
                        else:
                            out = self.pL[:, t2 * 512:(t2 + 1) * 512]
                        for hc in range(NC):
                            mms.append((out, hgv[:, hc, t2 * 128:(t2 + 1) * 128],
                                        W2q(wsl)[:, hc, half * 512:(half + 1) * 512],
                                        hc == 0, hc == NC - 1))
                    wr = ([self.b_pM[0], self.b_pM[1], self.b_pT[0], self.b_pT[1]]
                          if half == 0 else self.pLh(0) + self.pLh(1))
                    self.mmx(mms, reads=bW2(wsl) + bh, writes=wr)
                    op = "tensor_copy" if hb == 0 else "tensor_add"
                    for t2 in range(4):
                        if half == 0:
                            pin = (self.pM[:, 0:512], self.pM[:, 512:1024],
                                   self.pT1[:, 0:512], self.pT2[:, 0:512])[t2]
                            bp = (self.b_pM[0], self.b_pM[1],
                                  self.b_pT[0], self.b_pT[1])[t2]
                        else:
                            pin = self.pL[:, t2 * 512:(t2 + 1) * 512]
                            bp = self.b_pL[t2]
                        dst = ya[:, t2, half * 512:(half + 1) * 512]
                        args = (dst, pin) if hb == 0 else (dst, dst, pin)
                        rdy = [bp, self.b_yst[0]] + (fence if hb == 0 else [])
                        self.vop(op, *args, reads=rdy, writes=[self.b_yst[0]])
            for c in range(4):
                def fn(E, i=ya[:, c, :], ix=self._ssl(sl)[:, c:c + 1]):
                    return E.indirect_dma_start(out=self.y_buf,
                                                out_offset=IOA(ap=ix, axis=0),
                                                in_=i, in_offset=None)
                self.S.gma(fn, reads=[self.b_yst[0], self.b_gsb[sl]],
                           writes=[self.b_ybuf])

    def phase_F(self, b):
        bigA32 = self.bigA.bitcast(F32)
        bigB32 = self.bigB.bitcast(F32)
        for t in range(TT):
            i4 = t % 4
            xs = bigA32[:, (4 + i4) * 1024:(5 + i4) * 1024]
            bxs = self.b_bigA[4 + i4]
            ygt = bigA32[:, i4 * 1024:(i4 + 1) * 1024]
            byg = [self.b_bigA[i4]]
            ss = bigB32[:, i4 * 1024:(i4 + 1) * 1024]
            bss = self.b_bigB[i4]
            self.dma(xs, self.x[b, t * 128:(t + 1) * 128, :],
                     reads=[bxs], writes=[bxs])
            if not self._ybuf_claimed:
                self._ybuf_claimed = True
                wy, ry = [self.b_ybuf], []
            else:
                wy, ry = [], [self.b_ybuf]
            self.dma(ygt, self.y_buf[b * T + t * 128:b * T + (t + 1) * 128, :],
                     reads=byg + ry, writes=byg + wy)
            if not self._sspill_claimed:
                self._sspill_claimed = True
                wsx, rsx = [self.b_sspill], []
            else:
                wsx, rsx = [], [self.b_sspill]
            self.dma(ss, self.s_spill[b, t * 128:(t + 1) * 128, :],
                     reads=[bss] + rsx, writes=[bss] + wsx)
            if b == 0:
                vs = bigB32[:, (4 + i4) * 1024:(5 + i4) * 1024]
                bvs = self.b_bigB[4 + i4]
                if not self._vit_claimed:
                    self._vit_claimed = True
                    wv, rv = [self.b_vit], []
                else:
                    wv, rv = [], [self.b_vit]
                self.dma(vs, self.vit_buf[t * 128:(t + 1) * 128, :],
                         reads=[bvs] + rv, writes=[bvs] + wv)
                vread = [bvs]
            else:
                vs = self.xn_tok[:, t, :]
                vread = [self.b_xn[t]]
            self.vop("tensor_add", ygt, ygt, vs, reads=byg + vread, writes=byg)
            if "gin" in self.dmp:
                self.dma(self.dmp["gin"][b, t * 128:(t + 1) * 128, :], ygt,
                         reads=byg, writes=[])
            gg = self.lnfm_f32[:, t % 4, :]
            bgg = self.b_lnfm[t % 4]
            wex = ([self.b_utok[0], self.b_ufm[0], self.b_yst[0]]
                   if (b == 0 and t < 4) else [])
            self.act(gg, ygt, AF.Silu, reads=byg, writes=[bgg] + wex)
            self.vop("tensor_mul", gg, gg, ss, reads=[bgg, bss], writes=[bgg])
            self.gop("tensor_add", xs, xs, gg, reads=[bxs, bgg], writes=[bxs])
            self.S.sdma(self.out[b, t * 128:(t + 1) * 128, :], xs,
                        reads=[bxs], writes=[self.b_out])

    def build(self):
        self.b_out = Buf()
        self._wcvt_claimed = False
        self._ubuf_claimed = False
        self._gidx_claimed = False
        self._ybuf_claimed = False
        self._sspill_claimed = False
        self._vit_claimed = False
        self.init_consts()
        self.convert_qkv_weights()
        for b in range(NB):
            self.phase_A_pass1(b)
            self.load_attn_weights()
            self.phase_A_pass2(b)
            self.phase_B(b)
            if b == 0:
                self.convert_moe_weights()
            self.phase_C(b)
            self.phase_D(b)
            if b == 0:
                for t in range(TT):
                    self.dma(self.vit_buf[t * 128:(t + 1) * 128, :],
                             self.xn_tok[:, t, :],
                             reads=[self.b_xn[t], self.b_vit], writes=[])
        self.index_build()
        self.phase_E_sparse()
        for b in range(NB):
            self.phase_F(b)
        it_deps = [e for e in self.S.dma_ring if e is not None]
        it_deps += [e for e in self.S.gma_ring if e is not None]
        eid = self.S._mk("sync", lambda E: None, (self.b_out,), (), False)
        self.S.items[eid]["deps"] = sorted(set(self.S.items[eid]["deps"]) | set(it_deps))
        self.S.finalize(self.nc, self.sems)
        return self.nc


NAME_MAP = {
    "L0_attn_x": "x",
    "L0_attn_wq": "wq", "L0_attn_bq": "bq",
    "L0_attn_wk": "wk", "L0_attn_bk": "bk",
    "L0_attn_wv": "wv", "L0_attn_bv": "bv",
    "L0_attn_wo": "wo", "L0_attn_bo": "bo",
    "L0_ffn_ln1_w": "ln1w", "L0_ffn_ln1_b": "ln1b",
    "L0_ffn_w_qkv": "wqkv", "L0_ffn_w_proj": "wproj",
    "L0_ffn_ln2_w": "ln2w", "L0_ffn_ln2_b": "ln2b",
    "L0_ffn_gate_w": "gatew", "L0_ffn_gate_b": "gateb",
    "L0_ffn_w1": "w1", "L0_ffn_w2": "w2",
}


def build_program(dump=()):
    bld = Builder(dump=dump)
    nc = bld.build()
    return nc


def kernel(**inputs):
    arrs = {NAME_MAP[k]: np.ascontiguousarray(np.asarray(v, dtype=np.float32))
            for k, v in inputs.items()}
    x = arrs.pop("x")
    nc = build_program()
    in_maps = []
    for c in range(8):
        m = dict(arrs)
        m["x"] = np.ascontiguousarray(x[c * NB:(c + 1) * NB])
        in_maps.append(m)
    res = run_bass_kernel_spmd(nc, in_maps, core_ids=list(range(8)))
    return np.concatenate([res.results[c]["out"] for c in range(8)], axis=0)


if __name__ == "__main__":
    pass


# version 23
# speedup vs baseline: 1.0550x; 1.0146x over previous
import numpy as np
from contextlib import ExitStack

import concourse.bass as bass
import concourse.mybir as mybir
from concourse.bass_utils import run_bass_kernel_spmd

F32 = mybir.dt.float32
BF16 = mybir.dt.bfloat16
U32 = mybir.dt.uint32
AF = mybir.ActivationFunctionType
ALU = mybir.AluOpType
AX = mybir.AxisListType
IOA = bass.IndirectOffsetOnAxis

PASS_TOK = 512
NPASS = 16
NSLOT = NPASS * PASS_TOK

ENGS = ("sync", "tensor", "vector", "scalar", "gpsimd")
NDQ = 16
NGQ = 24

NB = 2
TT = 16
T = 2048
D = 1024
NC = 8
NH = 16
DK = 64
NE = 8
HID = 4096
EPS = 1e-5


class Buf:
    __slots__ = ("w", "rs")

    def __init__(self):
        self.w = None
        self.rs = []


PARANOID = False


class Sched:
    def __init__(self):
        self.items = []
        self.per_eng = {e: [] for e in ENGS}
        self.last_per_eng = {e: None for e in ENGS}
        self.ndma = 0
        self.dma_ring = [None] * NDQ
        self.ngma = 0
        self.gma_ring = [None] * NGQ
        self.nsma = 0
        self.sma_ring = [None] * 4

    def _mk(self, eng, fn, reads, writes, dma):
        deps = set()
        for b in reads:
            if b.w is not None:
                deps.add(b.w)
        for b in writes:
            if b.w is not None:
                deps.add(b.w)
            for r in b.rs:
                deps.add(r)
        if PARANOID:
            for e2 in ENGS:
                if e2 != eng and self.last_per_eng[e2] is not None:
                    deps.add(self.last_per_eng[e2])
        eid = len(self.items)
        it = dict(eid=eid, eng=eng, fn=fn, dma=dma)
        if dma and eng == "sync":
            q = self.ndma % NDQ
            it["sem"] = "dq%d" % q
            it["val"] = 16 * (self.ndma // NDQ + 1)
            if self.dma_ring[q] is not None:
                deps.add(self.dma_ring[q])
            self.dma_ring[q] = eid
            self.ndma += 1
        elif dma and eng == "scalar":
            q = self.nsma % 4
            it["sem"] = "sq%d" % q
            it["val"] = 16 * (self.nsma // 4 + 1)
            if self.sma_ring[q] is not None:
                deps.add(self.sma_ring[q])
            self.sma_ring[q] = eid
            self.nsma += 1
        elif dma:
            q = self.ngma % NGQ
            it["sem"] = "gq%d" % q
            it["val"] = 16 * (self.ngma // NGQ + 1)
            if self.gma_ring[q] is not None:
                deps.add(self.gma_ring[q])
            self.gma_ring[q] = eid
            self.ngma += 1
        it["deps"] = sorted(deps)
        self.items.append(it)
        self.per_eng[eng].append(it)
        self.last_per_eng[eng] = eid
        for b in writes:
            b.w = eid
            b.rs = []
        for b in reads:
            b.rs.append(eid)
        return eid

    def op(self, eng, fn, reads=(), writes=()):
        return self._mk(eng, fn, tuple(reads), tuple(writes), False)

    def dma(self, out_ap, in_ap, reads=(), writes=(), slow=False):
        def fn(E, o=out_ap, i=in_ap, s=slow):
            if s:
                return E.dma_start(out=o, in_=i, allow_slow_non_contiguous=True)
            return E.dma_start(out=o, in_=i)
        return self._mk("sync", fn, tuple(reads), tuple(writes), True)

    def gma(self, fn, reads=(), writes=()):
        return self._mk("gpsimd", fn, tuple(reads), tuple(writes), True)

    def sdma(self, out_ap, in_ap, reads=(), writes=()):
        def fn(E, o=out_ap, i=in_ap):
            return E.dma_start(out=o, in_=i)
        return self._mk("scalar", fn, tuple(reads), tuple(writes), True)

    def finalize(self, nc, sems):
        demanded = set()
        for it in self.items:
            for d in it["deps"]:
                demanded.add(d)
        for e in ENGS:
            c = 0
            for it in self.per_eng[e]:
                if it["dma"]:
                    it["inc"] = 16
                elif it["eid"] in demanded:
                    c += 1
                    it["val"] = c
                    it["sem"] = e
                    it["inc"] = 1
                else:
                    it["val"] = None
                    it["sem"] = None
                    it["inc"] = 0

        items_all = self.items

        def mk_runner(engname):
            eng_items = self.per_eng[engname]

            def go(E):
                hi = {}
                for it in eng_items:
                    for d in it["deps"]:
                        p = items_all[d]
                        if engname == "tensor" and p["eng"] == "tensor" and not p["dma"]:
                            continue
                        v = p["val"]
                        if v is not None and hi.get(p["sem"], 0) < v:
                            E.wait_ge(sems[p["sem"]], v)
                            hi[p["sem"]] = v
                    inst = it["fn"](E)
                    if it["inc"]:
                        inst.then_inc(sems[it["sem"]], it["inc"])
            return go

        with nc.Block() as block:
            block.sync(mk_runner("sync"))
            block.tensor(mk_runner("tensor"))
            block.vector(mk_runner("vector"))
            block.scalar(mk_runner("scalar"))
            block.gpsimd(mk_runner("gpsimd"))


class Builder:
    def __init__(self, dump=()):
        self.dump = set(dump)
        nc = bass.Bass("TRN2", target_bir_lowering=False, debug=False)
        self.nc = nc
        self.S = Sched()
        self.ctx = ExitStack()
        self._alloc_dram()
        self._alloc_sbuf()
        self._alloc_psum()
        self.sems = {e: self.ctx.enter_context(nc.semaphore("sem_" + e)) for e in ENGS}
        for q in range(NDQ):
            self.sems["dq%d" % q] = self.ctx.enter_context(nc.semaphore("dq%d" % q))
        for q in range(NGQ):
            self.sems["gq%d" % q] = self.ctx.enter_context(nc.semaphore("gq%d" % q))
        for q in range(4):
            self.sems["sq%d" % q] = self.ctx.enter_context(nc.semaphore("sq%d" % q))
        self._mk_bufs()

    def _alloc_dram(self):
        nc = self.nc
        din = lambda n, s: nc.dram_tensor(n, s, F32, kind="ExternalInput").ap()
        self.x = din("x", [NB, T, D])
        self.wq_d = din("wq", [D, DK]); self.bq_d = din("bq", [DK])
        self.wk_d = din("wk", [D, DK]); self.bk_d = din("bk", [DK])
        self.wv_d = din("wv", [D, DK]); self.bv_d = din("bv", [DK])
        self.wo_d = din("wo", [DK, D]); self.bo_d = din("bo", [D])
        self.ln1w_d = din("ln1w", [D]); self.ln1b_d = din("ln1b", [D])
        self.wqkv_d = din("wqkv", [D, 3 * D])
        self.wproj_d = din("wproj", [D, D])
        self.ln2w_d = din("ln2w", [D]); self.ln2b_d = din("ln2b", [D])
        self.gatew_d = din("gatew", [D, NE]); self.gateb_d = din("gateb", [NE])
        self.w1_d = din("w1", [NE, D, HID])
        self.w2_d = din("w2", [NE, HID, D])
        self.out = nc.dram_tensor("out", [NB, T, D], F32, kind="ExternalOutput").ap()
        self.s_spill = nc.dram_tensor("s_spill", [NB, T, D], F32, kind="Internal").ap()
        self.wqkv_bf = nc.dram_tensor("wqkv_bf", [D, 3 * D], BF16, kind="Internal").ap()
        self.wproj_bf = nc.dram_tensor("wproj_bf", [D, D], BF16, kind="Internal").ap()
        ik = lambda n: "ExternalOutput" if n in self.dump else "Internal"
        self.w1bf = nc.dram_tensor("w1bf", [NE * 4 * 128, 8 * 1024], BF16,
                                   kind="Internal").ap()
        self.w2bf = nc.dram_tensor("w2bf", [NE * 4 * 128, 8 * 1024], BF16,
                                   kind="Internal").ap()
        self.u_buf = nc.dram_tensor("u_buf", [NB * T, D], BF16, kind=ik("u_buf")).ap()
        self.vit_buf = nc.dram_tensor("vit_buf", [T, D], F32, kind="Internal").ap()
        self.y_buf = nc.dram_tensor("y_buf", [NB * T + 128, D], F32,
                                    kind=ik("y_buf")).ap()
        self.gidx_buf = nc.dram_tensor("gidx_buf", [NSLOT], U32, kind=ik("gidx_buf")).ap()
        self.iscr = nc.dram_tensor("iscr", [NB * TT * NE], F32, kind="Internal").ap()
        self.sidx_buf = nc.dram_tensor("sidx_buf", [NSLOT], U32, kind=ik("sidx_buf")).ap()
        self.ident_dram = nc.inline_tensor(np.eye(8, dtype=np.float32), name="identc")
        import ml_dtypes
        self.identb_dram = nc.inline_tensor(
            np.eye(128).astype(ml_dtypes.bfloat16), name="identbc")
        lt = (np.arange(128)[:, None] <= np.arange(128)[None, :]).astype(np.float32)
        self.lt_dram = nc.inline_tensor(lt, name="ltmaskc")
        thr16 = np.tile(PASS_TOK * np.arange(16, dtype=np.float32)[None, :], (NE, 1))
        self.thr16_dram = nc.inline_tensor(thr16, name="thr16c")
        thrp = np.tile(PASS_TOK * np.arange(NPASS, dtype=np.float32)[None, :], (NE, 1))
        self.thrp_dram = nc.inline_tensor(thrp, name="thrpc")
        evals = np.arange(NE, dtype=np.float32)[:, None]
        self.evals_dram = nc.inline_tensor(evals, name="evalsc")
        self.dmp = {}
        for name in self.dump:
            if name in ("u_buf", "y_buf", "gidx_buf", "sidx_buf"):
                continue
            shp = {"xn": [NB, T, D], "xn2": [NB, T, D], "s": [NB, T, D],
                   "lgt": [NB, T, 2 * NE], "gin": [NB, T, D]}[name]
            self.dmp[name] = nc.dram_tensor("dbg_" + name, shp, F32, kind="ExternalOutput").ap()

    def _alloc_sbuf(self):
        nc = self.nc
        E_ = self.ctx.enter_context
        sb = lambda n, s, d: E_(nc.sbuf_tensor(n, s, d))
        self.xn_tok = sb("xn_tok", [128, TT, D], F32)
        self.ln_fm = sb("ln_fm", [128, NC, T], BF16)
        self.bigA = sb("bigA", [128, 16384], BF16)
        self.bigB = sb("bigB", [128, 16384], BF16)
        self.hg = sb("hg", [128, 4096], BF16)
        self.misc = sb("misc", [128, 4096], F32)
        self.stage = sb("stage", [128, 2048], F32)
        self.pool = sb("pool", [128, 3, 1024], F32)
        self.identf = sb("identf", [8, 8], F32)
        self.identb = sb("identb", [128, 128], BF16)
        self.ltmask = sb("ltmask", [128, 128], BF16)
        self.onesb = sb("onesb", [128, 128], BF16)
        self.gam1 = sb("gam1", [128, NC], F32)
        self.bet1 = sb("bet1", [128, NC], F32)
        self.gam2 = sb("gam2", [128, NC], F32)
        self.bet2 = sb("bet2", [128, NC], F32)
        self.bq8 = sb("bq8", [64, 2], F32)
        self.bkc = sb("bkc", [64, 1], F32)
        self.bvc = sb("bvc", [64, 1], F32)
        self.gw_st = sb("gw_st", [128, NC, NE], F32)
        self.gwg_hi = sb("gwg_hi", [128, NC, NE], BF16)
        self.gwg_lo = sb("gwg_lo", [128, NC, NE], BF16)
        self.gw_bf = sb("gw_bf", [128, NC, NE], BF16)
        self.b2bf = sb("b2bf", [128, NC], BF16)
        self.gb_row = sb("gb_row", [1, NE], F32)
        self.gbadd = sb("gbadd", [NE, 1], F32)
        self.masks = sb("masksb", [128, NB, TT, NE], F32)
        self.lgt_tok = sb("lgt_tok", [128, TT, NE], F32)
        self.statA = sb("statA", [128, TT, 4], F32)
        self.ntcol = sb("ntcol", [128, 1], F32)
        self.maxsc = sb("maxsc", [1, 2], F32)
        self.reccol = sb("reccol", [128, TT], F32)
        self.epsc = sb("epsc", [128, 1], F32)
        self.idxs = sb("idxs", [128, 40], U32)
        self.miscu = self.misc.bitcast(U32)
        self.ltf = self.misc[:, 0:128]
        self.csum = self.misc[0:1, 128:384]
        self.csum32 = self.misc[0:32, 384:392]
        self.carry32 = self.misc[0:32, 392:400]
        self.irow = self.misc[0:8, 400:464]
        self.icol = self.misc[0:8, 464:472]
        self.srow = self.misc[0:1, 472:504]
        self.soffr = self.misc[0:1, 504:512]
        self.thr16 = self.misc[0:8, 512:528]
        self.thrp = self.misc[0:8, 528:528 + NPASS]
        self.evalsf = self.misc[0:8, 552:553]
        self.idxf = self.misc[:, 560:584]
        self.pcol = self.misc[:, 584:585]
        self.posf = self.misc[:, 585:587]
        self.tid32 = self.miscu[:, 588:588 + NB * TT]
        self.iniz = self.miscu[:, 0:NSLOT // 128]
        self.onescol = self.ltf[:, 127:128]
        self.onesrow = self.ltf[0:1, 0:128]
        self.miscb = self.misc.bitcast(BF16)
        self.poolb = self.pool.bitcast(BF16)
        self.hgf = self.hg.bitcast(F32)
        self.stageb = self.stage.bitcast(BF16)
        self.lnfm_f32 = self.ln_fm.bitcast(F32)
        rem = nc.sbuf_bytes_remaining
        assert rem >= 0, f"SBUF over budget by {-rem} bytes"

    def _alloc_psum(self):
        nc = self.nc
        E_ = self.ctx.enter_context
        self.pL = E_(nc.psum_tensor("pL", [128, 2048], F32))
        self.pM = E_(nc.psum_tensor("pM", [128, 1024], F32))
        self.pT1 = E_(nc.psum_tensor("pT1", [128, 512], F32))
        self.pT2 = E_(nc.psum_tensor("pT2", [128, 512], F32))
        self.pT1b = self.pT1.bitcast(BF16)
        self.pT2b = self.pT2.bitcast(BF16)
        self.pMb = self.pM.bitcast(BF16)

    def _mk_bufs(self):
        self.b_ident = Buf(); self.b_lt = Buf(); self.b_ones = Buf()
        self.b_params = Buf(); self.b_gw = Buf()
        self.b_pool = [Buf() for _ in range(3)]
        self.b_stage = [Buf(), Buf()]
        self.b_hg = [Buf() for _ in range(4)]
        self.b_misc = [Buf() for _ in range(4)]
        self.b_xn = [Buf() for _ in range(TT)]
        self.b_lnfm = [Buf() for _ in range(TT)]
        self.b_bigA = [Buf() for _ in range(8)]
        self.b_bigB = [Buf() for _ in range(8)]
        self.b_stat = [Buf(), Buf()]
        self.b_masks = Buf()
        self.b_pL = [Buf(), Buf(), Buf(), Buf()]
        self.b_pM = [Buf(), Buf()]
        self.b_pT = [Buf(), Buf()]
        self.b_sspill = Buf()
        self.b_wbf = Buf()
        self.b_reccol = Buf()
        self.b_ptile = [Buf(), Buf(), Buf()]
        self.b_stsum = Buf()
        self.b_stsq = Buf()
        self.b_ust = [Buf(), Buf()]
        self.b_wcvt = Buf()
        self.b_ubuf = Buf()
        self.b_vit = Buf()
        self.b_ybuf = Buf()
        self.b_gidxb = Buf()
        self.b_sidxb = Buf()
        self.b_idx = Buf()
        self.b_icons = Buf()
        self.b_utok = [Buf(), Buf()]
        self.b_ufm = [Buf(), Buf()]
        self.b_yst = [Buf(), Buf()]
        self.b_gsb = [Buf(), Buf()]
        self.b_widx = Buf()

    def pLh(self, i):
        return [self.b_pL[2 * i], self.b_pL[2 * i + 1]]

    def bA(self, lo, hi):
        return [self.b_bigA[i] for i in range(lo // 2048, (hi + 2047) // 2048)]

    def bB(self, lo, hi):
        return [self.b_bigB[i] for i in range(lo // 2048, (hi + 2047) // 2048)]

    def op(self, eng, fn, reads=(), writes=()):
        return self.S.op(eng, fn, reads, writes)

    def dma(self, out_ap, in_ap, reads=(), writes=(), slow=False):
        return self.S.dma(out_ap, in_ap, reads, writes, slow=slow)

    def mm(self, mms, reads=(), writes=()):
        def fn(E, mms=mms):
            n = len(mms)
            last = None
            for i, (o, l, r) in enumerate(mms):
                last = E.matmul(o, l, r, start=(i == 0), stop=(i == n - 1))
            return last
        return self.op("tensor", fn, reads, writes)

    def mmx(self, mms, reads=(), writes=()):
        def fn(E, mms=mms):
            last = None
            for (o, l, r, st, sp) in mms:
                last = E.matmul(o, l, r, start=st, stop=sp)
            return last
        return self.op("tensor", fn, reads, writes)

    def trans(self, trs, reads=(), writes=(), bf=True):
        ident = self.identb if bf else self.identf

        def fn(E, trs=trs, ident=ident):
            last = None
            for item in trs:
                (o, i), bp = item[:2], (item[2] if len(item) > 2 else 0)
                p = i.shape[0]
                last = E.transpose(o, i, ident[bp:bp + p, bp:bp + p])
            return last
        return self.op("tensor", fn, tuple(reads) + (self.b_ident,), writes)

    def act(self, out, in_, func, reads=(), writes=(), **kw):
        def fn(E, o=out, i=in_, f=func, kw=kw):
            return E.activation(o, i, f, **kw)
        return self.op("scalar", fn, reads, writes)

    def vop(self, name, *args, reads=(), writes=(), **kw):
        def fn(E, n=name, a=args, kw=kw):
            return getattr(E, n)(*a, **kw)
        return self.op("vector", fn, reads, writes)

    def gop(self, name, *args, reads=(), writes=(), **kw):
        def fn(E, n=name, a=args, kw=kw):
            return getattr(E, n)(*a, **kw)
        return self.op("gpsimd", fn, reads, writes)

    def init_consts(self):
        pb = self.b_params
        self.dma(self.identf[:], self.ident_dram.ap(), writes=[self.b_ident])
        self.dma(self.identb[:], self.identb_dram.ap(), writes=[self.b_ident])
        self.dma(self.pool[:, 2, :][:, 0:128], self.lt_dram.ap()[:, 0:128],
                 writes=[self.b_pool[2]])
        self.vop("tensor_copy", self.ltmask[:], self.pool[:, 2, 0:128],
                 reads=[self.b_pool[2]], writes=[self.b_lt])
        self.vop("memset", self.onesb[:], 1.0, writes=[self.b_ones])
        self.vop("memset", self.epsc[:], EPS, writes=[self.b_ones])
        rr = lambda ap: ap.rearrange("(c p) -> p c", p=128)
        cc = lambda ap: ap.rearrange("(p one) -> p one", one=1)
        self.dma(self.gam1[:], rr(self.ln1w_d), writes=[pb], slow=True)
        self.dma(self.bet1[:], rr(self.ln1b_d), writes=[pb], slow=True)
        self.dma(self.gam2[:], rr(self.ln2w_d), writes=[pb], slow=True)
        self.dma(self.bet2[:], rr(self.ln2b_d), writes=[pb], slow=True)
        self.dma(self.bq8[:, 1:2], cc(self.bq_d), writes=[pb])
        self.dma(self.bkc[:], cc(self.bk_d), writes=[pb])
        self.dma(self.bvc[:], cc(self.bv_d), writes=[pb])
        self.dma(self.gb_row[:], self.gateb_d.rearrange("(one d) -> one d", one=1), writes=[pb])
        self.vop("tensor_scalar", self.bq8[:, 0:1], self.bq8[:, 1:2], 0.125, None, ALU.mult,
                 reads=[pb], writes=[pb])
        gwr = self.gatew_d.rearrange("(c p) e -> p c e", p=128)
        self.dma(self.gw_st[:], gwr, writes=[self.b_gw])
        for c in range(NC):
            self.vop("tensor_scalar", self.gw_st[:, c, :], self.gw_st[:, c, :],
                     self.gam2[:, c:c + 1], None, ALU.mult,
                     reads=[self.b_gw, pb], writes=[self.b_gw])
        self.vop("tensor_copy", self.gwg_hi[:], self.gw_st[:],
                 reads=[self.b_gw], writes=[self.b_gw])
        self.vop("tensor_sub", self.gwg_lo[:], self.gw_st[:], self.gwg_hi[:],
                 reads=[self.b_gw], writes=[self.b_gw])
        self.dma(self.gw_st[:], gwr, reads=[self.b_gw], writes=[self.b_gw])
        self.vop("tensor_copy", self.gw_bf[:], self.gw_st[:],
                 reads=[self.b_gw], writes=[self.b_gw])
        self.vop("tensor_copy", self.b2bf[:], self.bet2[:], reads=[pb], writes=[pb])
        mms = [(self.pT1[0:1, 0:NE], self.b2bf[:, c:c + 1], self.gw_bf[:, c, :])
               for c in range(NC)]
        self.mm(mms, reads=[self.b_gw, pb], writes=[self.b_pT[0]])
        self.vop("tensor_add", self.misc[0:1, 0:NE], self.pT1[0:1, 0:NE], self.gb_row[:],
                 reads=[self.b_pT[0], pb], writes=[self.b_misc[0]])
        self.vop("tensor_copy", self.miscb[0:1, 2048:2048 + NE], self.misc[0:1, 0:NE],
                 reads=[self.b_misc[0]], writes=[self.b_misc[1]])
        self.trans([(self.pT2b[0:NE, 0:1], self.miscb[0:1, 2048:2048 + NE])],
                   reads=(self.b_misc[1],), writes=[self.b_pT[1]])
        self.vop("tensor_copy", self.gbadd[:], self.pT2b[0:NE, 0:1],
                 reads=[self.b_pT[1]], writes=[pb])
        iniz2 = self.miscu[0:NSLOT // 128, 0:128]
        self.gop("iota", iniz2, [[0, 128]], base=0,
                 channel_multiplier=0, reads=[], writes=[self.b_idx, self.b_misc[0]])
        self.dma(self.gidx_buf.rearrange("(c p) -> c p", p=128), iniz2,
                 reads=[self.b_idx, self.b_misc[0]], writes=[self.b_gidxb])
        self.gop("iota", iniz2, [[1, 128]], base=NB * T,
                 channel_multiplier=0, reads=[self.b_gidxb],
                 writes=[self.b_idx, self.b_misc[0]])
        self.dma(self.sidx_buf.rearrange("(c p) -> c p", p=128), iniz2,
                 reads=[self.b_idx, self.b_misc[0]], writes=[self.b_sidxb])

    def convert_qkv_weights(self):
        def cast_dma(out_ap, in_ap, reads, writes):
            def fn(E, o=out_ap, i=in_ap):
                return E.dma_start(out=o, in_=i, max_dma_last_dim=8192)
            return self.S.gma(fn, reads, writes)
        cast_dma(self.wqkv_bf, self.wqkv_d, [self.b_wbf], [])
        cast_dma(self.wproj_bf, self.wproj_d, [self.b_wbf], [])

    def convert_moe_weights(self):
        def cast_dma(out_ap, in_ap, reads, writes):
            def fn(E, o=out_ap, i=in_ap):
                return E.dma_start(out=o, in_=i, max_dma_last_dim=8192)
            return self.S.gma(fn, reads, writes)
        w1v = self.w1bf.rearrange("(s p) (c n) -> s p c n", p=128, n=1024)
        w2v = self.w2bf.rearrange("(s p) (c n) -> s p c n", p=128, n=1024)
        for e in range(NE):
            for hb in range(4):
                s = e * 4 + hb
                cast_dma(w1v[s],
                         self.w1_d[e, :, hb * 1024:(hb + 1) * 1024].rearrange(
                             "(c p) n -> p c n", p=128),
                         [self.b_wcvt], [])
                cast_dma(w2v[s],
                         self.w2_d[e, hb * 1024:(hb + 1) * 1024, :].rearrange(
                             "(c p) n -> p c n", p=128),
                         [self.b_wcvt], [])

    def load_attn_weights(self):
        w_bufs = self.bB(10240, 12800)
        for i, w in enumerate((self.wq_d, self.wk_d, self.wv_d)):
            sl = i % 2
            self.dma(self.stage[:, sl * 1024:sl * 1024 + 512].rearrange(
                         "p (c k) -> p c k", k=DK),
                     w.rearrange("(c p) k -> p c k", p=128),
                     reads=[self.b_stage[sl]], writes=[self.b_stage[sl]])
            self.gop("tensor_copy", self.bigB[:, 10240 + i * 512:10240 + (i + 1) * 512],
                     self.stage[:, sl * 1024:sl * 1024 + 512],
                     reads=[self.b_stage[sl]], writes=w_bufs)
        self.dma(self.stage[0:64, 1024:2048], self.wo_d,
                 reads=[self.b_stage[1]], writes=[self.b_stage[1]])
        self.gop("tensor_copy", self.bigB[0:64, 11776:12800], self.stage[0:64, 1024:2048],
                 reads=[self.b_stage[1]], writes=w_bufs)

    def wq_sb(self, i, c):
        off = 10240 + i * 512 + c * 64
        return self.bigB[:, off:off + 64]

    def wo_sb(self, half):
        return self.bigB[0:64, 11776 + half * 512:11776 + (half + 1) * 512]

    def phase_A_pass1(self, b):
        for t in range(TT):
            xt = self.xn_tok[:, t, :]
            bx = self.b_xn[t]
            self.dma(xt, self.x[b, t * 128:(t + 1) * 128, :], reads=[bx], writes=[bx])
            self._stats_collect(t, xt, bx)
        self._stats_finalize()

    def phase_A_pass2(self, b):
        QT = self.bigB[0:65, 0:2048]
        KT = self.bigB[0:65, 2048:4096]
        VT = self.bigB[:, 4096:5136].rearrange("p (t k) -> p t k", k=65)
        bQ, bK = self.bB(0, 2048), self.bB(2048, 4096)
        bV = self.bB(4096, 5456)
        plB = self.pL.bitcast(BF16)
        self.vop("memset", self.bigB[64:128, 0:2048], 0.0, reads=bQ, writes=bQ)
        self.vop("memset", self.bigB[64:128, 2048:4096], 0.0, reads=bK, writes=bK)
        self.vop("memset", KT[64:65, :], 1.0, reads=bK, writes=bK)
        self.vop("memset", VT[:, :, 64:65], 1.0, reads=bV, writes=bV)
        self.vop("memset", self.bigB[:, 5136:5200], 0.0,
                 reads=self.bB(5136, 5200), writes=self.bB(5136, 5200))
        pend = None
        for t in range(TT + 1):
            if t < TT:
                pp = t % 2
                self.vop("tensor_scalar", self.xn_tok[:, t, :], self.xn_tok[:, t, :],
                         self.statA[:, t, 2:3], self.statA[:, t, 3:4],
                         ALU.mult, ALU.subtract,
                         reads=[self.b_stsum], writes=[self.b_xn[t]])
                xc = self.hg[:, pp * 1024:(pp + 1) * 1024]
                self.vop("tensor_copy", xc, self.xn_tok[:, t, :],
                         reads=[self.b_xn[t]], writes=[self.b_hg[pp]])
                pt = (self.pT1b, self.pT2b)[pp]
                self.trans([(pt[:, c * 128:(c + 1) * 128], xc[:, c * 128:(c + 1) * 128])
                            for c in range(NC)],
                           reads=(self.b_hg[pp],), writes=[self.b_pT[pp]])
                if "xn" in self.dmp:
                    self.dma(self.dmp["xn"][b, t * 128:(t + 1) * 128, :],
                             self.xn_tok[:, t, :], reads=[self.b_xn[t]], writes=[])
            if pend is not None:
                tp_ = pend
                pp_ = tp_ % 2
                pt_ = (self.pT1b, self.pT2b)[pp_]
                fm = self.hg[:, (2 + pp_) * 1024:(3 + pp_) * 1024]
                self.vop("tensor_copy", fm, pt_[:],
                         reads=[self.b_pT[pp_]], writes=[self.b_hg[2 + pp_]])
                ps = self.pM[:, pp_ * 512:(pp_ + 1) * 512]
                bps = self.b_pM[pp_]
                mms = []
                for i in range(3):
                    for c in range(NC):
                        mms.append((ps[0:64, i * 128:i * 128 + 128],
                                    self.wq_sb(i, c), fm[:, c * 128:(c + 1) * 128]))
                def fnmm(E, mms=mms):
                    last = None
                    for gi in range(3):
                        for ci in range(NC):
                            o, l, r = mms[gi * NC + ci]
                            last = E.matmul(o, l, r, start=(ci == 0), stop=(ci == NC - 1))
                    return last
                self.op("tensor", fnmm,
                        reads=[self.b_hg[2 + pp_]] + self.bB(10240, 12800), writes=[bps])
                self.act(QT[0:64, tp_ * 128:(tp_ + 1) * 128], ps[0:64, 0:128], AF.Identity,
                         scale=0.125, bias=self.bq8[:, 0:1],
                         reads=[bps, self.b_params], writes=bQ)
                self.act(KT[0:64, tp_ * 128:(tp_ + 1) * 128], ps[0:64, 128:256], AF.Identity,
                         bias=self.bkc[:],
                         reads=[bps, self.b_params], writes=bK)
                vt = self.bigB[0:64, 5200 + pp_ * 128:5200 + (pp_ + 1) * 128]
                self.act(vt, ps[0:64, 256:384], AF.Identity, bias=self.bvc[:],
                         reads=[bps, self.b_params], writes=bV)
                vslot = plB[:, pp_ * 2048:pp_ * 2048 + 64]
                self.trans([(vslot, vt)], reads=tuple(bV),
                           writes=[self.b_pL[2 * pp_]])
                self.vop("tensor_copy", VT[:, tp_, 0:64], vslot,
                         reads=[self.b_pL[2 * pp_]], writes=bV)
            pend = t if t < TT else None

    def _stats_collect(self, t, src_ap, bsrc):
        self.vop("reduce_sum", self.statA[:, t, 0:1], src_ap, axis=AX.X,
                 reads=[bsrc], writes=[self.b_stsum])
        self.act(self.pool[:, 2, :], src_ap, AF.Square,
                 accum_out=self.statA[:, t, 1:2],
                 reads=[bsrc], writes=[self.b_pool[2], self.b_stsq])

    def _stats_finalize(self):
        c = lambda i: self.statA[:, :, i:i + 1]
        bb = [self.b_stsum, self.b_stsq]
        self.vop("tensor_scalar", c(0), c(0), 1.0 / D, None, ALU.mult,
                 reads=bb, writes=[self.b_stsum])
        self.vop("tensor_scalar", c(1), c(1), 1.0 / D, None, ALU.mult,
                 reads=bb, writes=[self.b_stsq])
        self.vop("tensor_mul", c(2), c(0), c(0), reads=bb, writes=[self.b_stsum])
        self.vop("tensor_sub", c(1), c(1), c(2), reads=bb, writes=[self.b_stsq])
        self.act(c(3), c(1), AF.Ln, bias=self.epsc[:],
                 reads=bb + [self.b_ones], writes=[self.b_stsum])
        self.act(c(2), c(3), AF.Exp, scale=-0.5, reads=bb, writes=[self.b_stsum])
        self.vop("tensor_mul", c(3), c(0), c(2), reads=bb, writes=[self.b_stsum])

    def phase_B(self, b):
        QT = self.bigB[0:65, 0:2048]
        KT = self.bigB[0:65, 2048:4096]
        VT = self.bigB[:, 4096:5136].rearrange("p (t k) -> p t k", k=65)
        ZF = self.bigB[0:64, 6144:8192]
        QSQ = self.bigB[0:64, 8192:10240]
        bQ, bK = self.bB(0, 2048), self.bB(2048, 4096)
        bV, bZ, bQs = self.bB(4096, 5456), self.bB(6144, 8192), self.bB(8192, 10240)
        row = lambda s, lo, hi: self.poolb[0:1, s, lo:hi]
        for (tile, bt) in ((QT, bQ), (KT, bK)):
            self.gop("tensor_mul", QSQ[:], tile[0:64, :], tile[0:64, :],
                     reads=bt, writes=bQs)
            mms = [(self.pL[0:1, j * 512:(j + 1) * 512], self.onesb[0:64, 0:1],
                    QSQ[:, j * 512:(j + 1) * 512]) for j in range(4)]
            def fnmm(E, mms=mms):
                last = None
                for (o, l, r) in mms:
                    last = E.matmul(o, l, r, start=True, stop=True)
                return last
            self.op("tensor", fnmm, reads=bQs + [self.b_ones],
                    writes=self.pLh(0) + self.pLh(1))
            if tile is QT:
                self.act(row(1, 0, 2048), self.pL[0:1, 0:2048], AF.Sqrt,
                         reads=self.pLh(0) + self.pLh(1), writes=[self.b_pool[1]])
        self.vop("reduce_max", self.maxsc[0:1, 0:1], self.pL[0:1, 0:2048], axis=AX.X,
                 reads=self.pLh(0) + self.pLh(1), writes=[self.b_params])
        self.act(self.maxsc[0:1, 1:2], self.maxsc[0:1, 0:1], AF.Sqrt,
                 reads=[self.b_params], writes=[self.b_params])
        self.vop("tensor_scalar", row(0, 0, 2048), row(1, 0, 2048),
                 self.maxsc[0:1, 1:2], -1.0, ALU.mult, ALU.mult,
                 reads=[self.b_pool[1], self.b_params], writes=[self.b_pool[0]])
        self.dma(QT[64:65, :], row(0, 0, 2048), reads=[self.b_pool[0]], writes=bQ)
        for qb in range(2):
            kmax = (qb + 1) * 8
            zb = [self.b_pM[0], self.b_pM[1]]

            def emit_z(kc, kmax=kmax, zb=zb, bV=bV):
                pt = self.miscb[:, 4096 + (kc % 2) * 1024:4096 + (kc % 2 + 1) * 1024]
                vwide = self.bigB[:, 4096 + kc * 65:4096 + kc * 65 + 128]
                self.mmx([(self.pM[:, j * 512:(j + 1) * 512], vwide,
                           pt[:, j * 512:(j + 1) * 512],
                           kc == 0, kc == kmax - 1) for j in range(2)],
                         reads=bV + self.bB(4096, 5456) + [self.b_ptile[kc % 2]],
                         writes=zb)

            for kc in range(kmax):
                sp = self.pL[:, (kc % 2) * 1024:(kc % 2 + 1) * 1024]
                bsp = self.pLh(kc % 2)
                QT8 = self.bigB[0:128, 0:2048]
                KT8 = self.bigB[0:128, 2048:4096]
                self.mmx([(sp[:, j * 512:(j + 1) * 512],
                           KT8[:, kc * 128:(kc + 1) * 128],
                           QT8[:, qb * 1024 + j * 512:qb * 1024 + (j + 1) * 512],
                           True, True) for j in range(2)],
                         reads=bQ + bK, writes=bsp)
                pt = self.miscb[:, 4096 + (kc % 2) * 1024:4096 + (kc % 2 + 1) * 1024]
                bpt = self.b_ptile[kc % 2]
                d = kc * 128 - qb * 1024
                if d > 0:
                    self.gop("memset", pt[:, 0:d], 0.0, reads=[bpt], writes=[bpt])
                lo = max(d, 0)
                self.act(pt[:, lo:1024], sp[:, lo:1024], AF.Exp,
                         reads=bsp, writes=[bpt])
                if d >= 0:
                    self.vop("tensor_mul", pt[:, d:d + 128], pt[:, d:d + 128], self.ltmask[:],
                             reads=[bpt, self.b_lt], writes=[bpt])
                if kc > 0:
                    emit_z(kc - 1)
            emit_z(kmax - 1)
            self.vop("reciprocal", self.misc[64:65, 3072:4096],
                     self.pM[64:65, 0:1024], reads=zb, writes=[self.b_misc[3]])
            self.dma(self.misc[0:1, 3072:4096], self.misc[64:65, 3072:4096],
                     reads=[self.b_misc[3]], writes=[self.b_misc[3]])
            self.trans([(self.pT1[:, tq:tq + 1],
                         self.misc[0:1, 3072 + tq * 128:3072 + (tq + 1) * 128])
                        for tq in range(8)],
                       reads=(self.b_misc[3],), writes=[self.b_pT[0]], bf=False)
            self.vop("tensor_copy", self.reccol[:, qb * 8:(qb + 1) * 8],
                     self.pT1[:, 0:8], reads=[self.b_pT[0]], writes=[self.b_reccol])
            self.vop("tensor_copy", ZF[:, qb * 1024:(qb + 1) * 1024], self.pM[0:64, 0:1024],
                     reads=zb, writes=bZ)
        self.dma(self.pool[0:1, 2, :], self.bo_d.rearrange("(one d) -> one d", one=1),
                 reads=[self.b_pool[2]], writes=[self.b_pool[2]])
        self.vop("tensor_copy", self.poolb[0:1, 0, 0:1024], self.pool[0:1, 2, :],
                 reads=[self.b_pool[2]], writes=[self.b_pool[0]])
        self.vop("tensor_sub", self.poolb[0:1, 1, 0:1024], self.pool[0:1, 2, :],
                 self.poolb[0:1, 0, 0:1024],
                 reads=[self.b_pool[2], self.b_pool[0]], writes=[self.b_pool[1]])
        mms = []
        for j in range(2):
            for hl in range(2):
                src = self.poolb[0:1, hl, j * 512:(j + 1) * 512]
                mms.append((self.pL[:, j * 512:(j + 1) * 512],
                            self.onesb[0:1, 0:128], src))
        def fnb(E, mms=mms):
            last = None
            for i, (o, l, r) in enumerate(mms):
                last = E.matmul(o, l, r, start=(i % 2 == 0), stop=(i % 2 == 1))
            return last
        self.op("tensor", fnb,
                reads=[self.b_pool[0], self.b_pool[1], self.b_ones],
                writes=self.pLh(0))
        self.act(self.misc[:, 3072:4096], self.pL[:, 0:1024], AF.Copy,
                 reads=self.pLh(0), writes=[self.b_misc[3]])
        for t in range(TT):
            pp = t % 2
            stok = self.misc[:, pp * 1024:(pp + 1) * 1024]
            bst = self.b_misc[pp]
            for half in range(2):
                ps = self.pL[:, (t * 2 + half) % 2 * 1024:((t * 2 + half) % 2) * 1024 + 512]
                bps = self.b_pL[2 * ((t * 2 + half) % 2)]
                self.mm([(ps, ZF[:, t * 128:(t + 1) * 128], self.wo_sb(half))],
                        reads=bZ + self.bB(10240, 12800), writes=[bps])
                self.op("vector",
                        lambda E, o=stok[:, half * 512:(half + 1) * 512], i0=ps,
                               sc=self.reccol[:, t:t + 1],
                               i1=self.misc[:, 3072 + half * 512:3072 + (half + 1) * 512]:
                        E.scalar_tensor_tensor(o, i0, sc, i1, ALU.mult, ALU.add),
                        reads=[bps, self.b_reccol, self.b_misc[3]], writes=[bst])
            self.dma(self.s_spill[b, t * 128:(t + 1) * 128, :], stok,
                     reads=[bst, self.b_sspill], writes=[])
            if "s" in self.dmp:
                self.dma(self.dmp["s"][b, t * 128:(t + 1) * 128, :], stok,
                         reads=[bst], writes=[])

    def _attn_norms(self, QT, KT, QSQ, bQ, bK, bQs, b):
        W = QT.shape[1]
        nj = W // 512
        row = lambda s: self.poolb[0:1, s, 0:W]
        sq_eng = "vop" if b == 0 else "gop"
        for (tile, bt, dst_s) in ((QT, bQ, 1), (KT, bK, 2)):
            getattr(self, sq_eng)("tensor_mul", QSQ[:, 0:W],
                                  tile[0:64, :], tile[0:64, :],
                                  reads=bt, writes=bQs)
            def fnmm(E, QSQ=QSQ, nj=nj):
                last = None
                for j in range(nj):
                    last = E.matmul(self.pL[0:1, j * 512:(j + 1) * 512],
                                    self.onesb[0:64, 0:1],
                                    QSQ[:, j * 512:(j + 1) * 512],
                                    start=True, stop=True)
                return last
            self.op("tensor", fnmm, reads=tuple(bQs) + (self.b_ones,),
                    writes=self.pLh(0) + self.pLh(1))
            if dst_s == 1:
                self.act(row(1), self.pL[0:1, 0:W], AF.Sqrt,
                         reads=self.pLh(0) + self.pLh(1), writes=[self.b_pool[1]])
        self.vop("reduce_max", self.maxsc[0:1, 0:1], self.pL[0:1, 0:W], axis=AX.X,
                 reads=self.pLh(0) + self.pLh(1), writes=[self.b_params])
        self.act(self.maxsc[0:1, 1:2], self.maxsc[0:1, 0:1], AF.Sqrt,
                 reads=[self.b_params], writes=[self.b_params])
        self.vop("tensor_scalar", row(0), row(1),
                 self.maxsc[0:1, 1:2], -1.0, ALU.mult, ALU.mult,
                 reads=[self.b_pool[1], self.b_params], writes=[self.b_pool[0]])
        self.dma(QT[64:65, :], row(0), reads=[self.b_pool[0]], writes=bQ)

    def phase_C(self, b):
        for t in range(TT):
            self._stats_collect(t, self.xn_tok[:, t, :], self.b_xn[t])
        self._stats_finalize()
        pend = None
        for t in range(TT + 1):
            if t < TT:
                pp = t % 2
                xt = self.xn_tok[:, t, :]
                u = self.pool[:, pp, :]
                self.vop("tensor_scalar", u, xt,
                         self.statA[:, t, 2:3], self.statA[:, t, 3:4],
                         ALU.mult, ALU.subtract,
                         reads=[self.b_xn[t], self.b_stsum], writes=[self.b_pool[pp]])
                xc = self.hg[:, pp * 1024:(pp + 1) * 1024]
                eng = "vop" if b == 0 else "gop"
                getattr(self, eng)("tensor_copy", xc, u, reads=[self.b_pool[pp]],
                                   writes=[self.b_hg[pp]])
                pt = (self.pT1b, self.pT2b)[pp]
                self.trans([(pt[:, c * 128:(c + 1) * 128], xc[:, c * 128:(c + 1) * 128])
                            for c in range(NC)],
                           reads=(self.b_hg[pp],), writes=[self.b_pT[pp]])
            if pend is not None:
                tp_ = pend
                pp_ = tp_ % 2
                pt_ = (self.pT1b, self.pT2b)[pp_]
                for c in range(NC):
                    dst = self.ln_fm[:, c, tp_ * 128:(tp_ + 1) * 128]
                    src = pt_[:, c * 128:(c + 1) * 128]
                    if c % 2 == 0:
                        self.act(dst, src, AF.Identity,
                                 scale=self.gam1[:, c:c + 1],
                                 bias=self.bet1[:, c:c + 1],
                                 reads=[self.b_pT[pp_], self.b_params],
                                 writes=[self.b_lnfm[tp_]])
                    else:
                        self.vop("tensor_scalar", dst, src,
                                 self.gam1[:, c:c + 1], self.bet1[:, c:c + 1],
                                 ALU.mult, ALU.add,
                                 reads=[self.b_pT[pp_], self.b_params],
                                 writes=[self.b_lnfm[tp_]])
            pend = t if t < TT else None
        QTg = lambda i: self.bigB[0:65, i * 2048:(i + 1) * 2048]
        KTg = lambda i: self.bigB[0:65, 4096 + i * 2048:4096 + (i + 1) * 2048]
        VG = self.bigB[:, 8192:10272].rearrange("p (h t k) -> p h t k", h=2, k=65)
        VTW = self.bigB[:, 10272:10784]
        QSQ = self.bigB[0:64, 10784:12832]
        qkvsl = lambda i: self.bigB[:, 12832 + i * 1024:12832 + (i + 1) * 1024].rearrange(
            "p (c k) -> p c k", k=128)
        bQg = lambda i: self.bB(i * 2048, (i + 1) * 2048)
        bKg = lambda i: self.bB(4096 + i * 2048, 4096 + (i + 1) * 2048)
        bVG = self.bB(8192, 10272)
        bVTW = self.bB(10272, 10784)
        bQs = self.bB(10784, 12832)
        bSL = self.bB(12832, 15904)
        self.vop("memset", self.bigB[64:128, 0:4096], 0.0,
                 reads=bQg(0) + bQg(1), writes=bQg(0) + bQg(1))
        self.vop("memset", self.bigB[64:128, 4096:8192], 0.0,
                 reads=bKg(0) + bKg(1), writes=bKg(0) + bKg(1))
        for i in range(2):
            self.vop("memset", KTg(i)[64:65, :], 1.0, reads=bKg(i), writes=bKg(i))
            self.vop("memset", VG[:, i, :, 64:65], 1.0, reads=bVG, writes=bVG)
        wqkv_r = self.wqkv_bf.rearrange("(c p) n -> p c n", p=128)
        lnfm_all = list(self.b_lnfm)
        for g2 in range(8):
            for i in range(3):
                cs = i * 1024 + g2 * 128
                self.dma(qkvsl(i)[:], wqkv_r[:, :, cs:cs + 128],
                         reads=bSL, writes=bSL + [self.b_wbf])
            for i in (0, 1, 2, "norms"):
                if i == "norms":
                    for h01 in range(2):
                        self._attn_norms(QTg(h01), KTg(h01), QSQ,
                                         bQg(h01), bKg(h01), bQs, b)
                    continue
                for cg in range(4):
                    pp = (i * 4 + cg) % 2
                    ps = (self.pT1, self.pT2)[pp]
                    bps = self.b_pT[pp]
                    self.mm([(ps[:], qkvsl(i)[:, c, :],
                              self.ln_fm[:, c, cg * 512:(cg + 1) * 512])
                             for c in range(NC)],
                            reads=bSL + lnfm_all, writes=[bps])
                    if i == 0:
                        for h01 in range(2):
                            self.act(QTg(h01)[0:64, cg * 512:(cg + 1) * 512],
                                     ps[h01 * 64:(h01 + 1) * 64, :], AF.Copy, scale=0.125,
                                     reads=[bps], writes=bQg(h01))
                    elif i == 1:
                        for h01 in range(2):
                            self.vop("tensor_copy",
                                     KTg(h01)[0:64, cg * 512:(cg + 1) * 512],
                                     ps[h01 * 64:(h01 + 1) * 64, :],
                                     reads=[bps], writes=bKg(h01))
                    else:
                        self.vop("tensor_copy", VTW[:], ps[:],
                                 reads=[bps], writes=bVTW)
                        for h01 in range(2):
                            for t4 in range(4):
                                tg = cg * 4 + t4
                                po = (self.pT1b, self.pT2b)[1 - pp]
                                self.trans([(po[:, 960:1024],
                                             VTW[h01 * 64:(h01 + 1) * 64,
                                                 t4 * 128:(t4 + 1) * 128],
                                             h01 * 64)],
                                           reads=tuple(bVTW), writes=[self.b_pT[1 - pp]])
                                self.vop("tensor_copy", VG[:, h01, tg, 0:64],
                                         po[:, 960:1024],
                                         reads=[self.b_pT[1 - pp]], writes=bVG)
            pending = [None]

            def make_evac(h01, qb, par, ztile, zb, g2=g2):
                def evA():
                    rsl = (h01 + qb) % 2
                    lnrow = self.misc[64:65, rsl * 1024:(rsl + 1) * 1024]
                    if par == 0:
                        self.vop("reciprocal", lnrow, self.pM[64:65, 0:1024],
                                 reads=zb, writes=[self.b_misc[rsl]])
                    else:
                        self.vop("reciprocal", lnrow[:, 0:512],
                                 self.pT1[64:65, 0:512],
                                 reads=zb, writes=[self.b_misc[rsl]])
                        self.vop("reciprocal", lnrow[:, 512:1024],
                                 self.pT2[64:65, 0:512],
                                 reads=zb, writes=[self.b_misc[rsl]])
                    recrow = self.poolb[64:65, 2, rsl * 1024:(rsl + 1) * 1024]
                    self.vop("tensor_copy", recrow, lnrow,
                             reads=[self.b_misc[rsl]],
                             writes=[self.b_pool[2]])

                def ev():
                    rsl = (h01 + qb) % 2
                    recrow = self.poolb[64:65, 2, rsl * 1024:(rsl + 1) * 1024]
                    def fnb(E, recrow=recrow, par=par):
                        last = None
                        for j in range(2):
                            odst = self.pL[0:64, par * 1024 + j * 512:
                                           par * 1024 + (j + 1) * 512]
                            last = E.matmul(
                                odst, self.onesb[64:65, 0:64],
                                recrow[:, j * 512:(j + 1) * 512],
                                start=True, stop=True)
                        return last
                    self.op("tensor", fnb,
                            reads=[self.b_pool[2], self.b_ones],
                            writes=self.pLh(par))
                    rb = self.misc[0:64, rsl * 1024:(rsl + 1) * 1024]
                    self.vop("tensor_copy", rb,
                             self.pL[0:64, par * 1024:par * 1024 + 1024],
                             reads=self.pLh(par), writes=[self.b_misc[rsl]])
                    ydst = self.bigA[h01 * 64:(h01 + 1) * 64,
                                     g2 * 2048 + qb * 1024:g2 * 2048 + (qb + 1) * 1024]
                    if par == 0:
                        self.vop("tensor_mul", ydst, self.pM[0:64, 0:1024], rb,
                                 reads=zb + [self.b_misc[rsl]],
                                 writes=[self.b_bigA[g2]])
                    else:
                        self.vop("tensor_mul", ydst[:, 0:512], self.pT1[0:64, 0:512],
                                 rb[:, 0:512],
                                 reads=zb + [self.b_misc[rsl]],
                                 writes=[self.b_bigA[g2]])
                        self.vop("tensor_mul", ydst[:, 512:1024], self.pT2[0:64, 0:512],
                                 rb[:, 512:1024],
                                 reads=zb + [self.b_misc[rsl]],
                                 writes=[self.b_bigA[g2]])
                return evA, ev

            for li, (h01, qb) in enumerate(((0, 0), (0, 1), (1, 0), (1, 1))):
                QT = self.bigB[0:128, h01 * 2048:(h01 + 1) * 2048]
                KT = self.bigB[0:128, 4096 + h01 * 2048:4096 + (h01 + 1) * 2048]
                bQ, bK = bQg(h01), bKg(h01)
                par = li % 2
                if par == 0:
                    ztile = lambda j: self.pM[:, j * 512:(j + 1) * 512]
                    zb = [self.b_pM[0], self.b_pM[1]]
                else:
                    ztile = lambda j: (self.pT1, self.pT2)[j][:, 0:512]
                    zb = [self.b_pT[0], self.b_pT[1]]

                def emit_z(kc, zb=zb, h01=h01, ztile=ztile):
                    pt = self.miscb[:, 4096 + (kc % 3) * 1024:4096 + (kc % 3 + 1) * 1024]
                    vwide = self.bigB[:, 8192 + h01 * 1040 + kc * 65:
                                      8192 + h01 * 1040 + kc * 65 + 128]
                    self.mmx([(ztile(j),
                               vwide, pt[:, j * 512:(j + 1) * 512],
                               kc == 0, kc == 15) for j in range(2)],
                             reads=bVG + self.bB(8192, 10784) + [self.b_ptile[kc % 3]],
                             writes=zb)

                for kc in range(16):
                    sp = self.pL[:, (kc % 2) * 1024:(kc % 2 + 1) * 1024]
                    bsp = self.pLh(kc % 2)
                    self.mmx([(sp[:, j * 512:(j + 1) * 512],
                               KT[:, kc * 128:(kc + 1) * 128],
                               QT[:, qb * 1024 + j * 512:qb * 1024 + (j + 1) * 512],
                               True, True) for j in range(2)],
                             reads=bQ + bK, writes=bsp)
                    pt = self.miscb[:, 4096 + (kc % 3) * 1024:4096 + (kc % 3 + 1) * 1024]
                    bpt = self.b_ptile[kc % 3]
                    wpt = [bpt, self.b_misc[3]] if kc % 3 == 2 else [bpt]
                    self.act(pt[:], sp[:], AF.Exp, reads=bsp, writes=wpt)
                    if kc > 1:
                        emit_z(kc - 2)
                    if kc == 1 and pending[0] is not None:
                        pending[0][0]()
                    if kc == 3 and pending[0] is not None:
                        pending[0][1]()
                        pending[0] = None
                emit_z(14)
                emit_z(15)
                ev = make_evac(h01, qb, par, ztile, zb)
                if li < 3:
                    pending[0] = ev
                else:
                    ev[0]()
                    ev[1]()
        wps = self.bigB[:, 0:8192].rearrange("p (c n) -> p c n", n=1024)
        bWP = self.bB(0, 8192)
        self.dma(wps[:], self.wproj_bf.rearrange("(c p) n -> p c n", p=128),
                 reads=bWP, writes=bWP + [self.b_wbf])
        yall = list(self.b_bigA)
        for t in range(TT):
            for half in range(2):
                pp = (t * 2 + half) % 2
                ps = (self.pT1, self.pT2)[pp]
                bps = self.b_pT[pp]
                self.mm([(ps[:], self.bigA[:, c * 2048 + t * 128:c * 2048 + (t + 1) * 128],
                          wps[:, c, half * 512:(half + 1) * 512])
                         for c in range(NC)],
                        reads=yall + bWP, writes=[bps])
                self.vop("tensor_add", self.xn_tok[:, t, half * 512:(half + 1) * 512],
                         self.xn_tok[:, t, half * 512:(half + 1) * 512], ps[:],
                         reads=[bps, self.b_xn[t]], writes=[self.b_xn[t]])
            if "xn2" in self.dmp:
                self.dma(self.dmp["xn2"][b, t * 128:(t + 1) * 128, :],
                         self.xn_tok[:, t, :], reads=[self.b_xn[t]], writes=[])

    def phase_D(self, b):
        for t in range(TT):
            self._stats_collect(t, self.xn_tok[:, t, :], self.b_xn[t])
        self._stats_finalize()
        for t in range(TT):
            pp = t % 2
            xt = self.xn_tok[:, t, :]
            u = self.pool[:, pp, :]
            self.vop("tensor_scalar", u, xt,
                     self.statA[:, t, 2:3], self.statA[:, t, 3:4],
                     ALU.mult, ALU.subtract,
                     reads=[self.b_xn[t], self.b_stsum], writes=[self.b_pool[pp]])
            uhi = self.hg[:, 0:1024]
            ulo = self.hg[:, 1024:2048]
            self.gop("tensor_copy", uhi, u, reads=[self.b_pool[pp]], writes=[self.b_hg[0]])
            self.gop("tensor_sub", ulo, u, uhi,
                     reads=[self.b_pool[pp], self.b_hg[0]], writes=[self.b_hg[1]])
            self.trans([(self.pT1b[:, c * 128:(c + 1) * 128], uhi[:, c * 128:(c + 1) * 128])
                        for c in range(NC)],
                       reads=(self.b_hg[0],), writes=[self.b_pT[0]])
            self.trans([(self.pT2b[:, c * 128:(c + 1) * 128], ulo[:, c * 128:(c + 1) * 128])
                        for c in range(NC)],
                       reads=(self.b_hg[1],), writes=[self.b_pT[1]])
            for c in range(NC):
                dst = self.ln_fm[:, c, t * 128:(t + 1) * 128]
                src = self.pT1b[:, c * 128:(c + 1) * 128]
                if c % 2 == 0:
                    self.act(dst, src, AF.Identity,
                             scale=self.gam2[:, c:c + 1], bias=self.bet2[:, c:c + 1],
                             reads=[self.b_pT[0], self.b_params],
                             writes=[self.b_lnfm[t]])
                else:
                    self.vop("tensor_scalar", dst, src,
                             self.gam2[:, c:c + 1], self.bet2[:, c:c + 1],
                             ALU.mult, ALU.add,
                             reads=[self.b_pT[0], self.b_params],
                             writes=[self.b_lnfm[t]])
            uhf = self.hg[:, 2048:3072]
            ulf = self.hg[:, 3072:4096]
            self.vop("tensor_copy", uhf, self.pT1b[:, 0:1024],
                     reads=[self.b_pT[0]], writes=[self.b_hg[2]])
            self.act(ulf, self.pT2b[:, 0:1024], AF.Copy,
                     reads=[self.b_pT[1]], writes=[self.b_hg[3]])
            X = (t % 2) * 1024 + ((t // 2) % 8) * 128
            lg = self.pL[0:NE, X:X + 128]
            blg = self.b_pL[X // 512]
            mms = []
            for c in range(NC):
                mms.append((lg, self.gwg_hi[:, c, :], uhf[:, c * 128:(c + 1) * 128]))
                mms.append((lg, self.gwg_lo[:, c, :], uhf[:, c * 128:(c + 1) * 128]))
                mms.append((lg, self.gwg_hi[:, c, :], ulf[:, c * 128:(c + 1) * 128]))
            self.mm(mms, reads=[self.b_hg[2], self.b_hg[3], self.b_gw], writes=[blg])
            lgt_sl = self.stage[0:NE, 0:128]
            self.act(lgt_sl, lg, AF.Identity, bias=self.gbadd[:],
                     reads=[blg, self.b_params], writes=[self.b_stage[0]])
            po = (self.pT1, self.pT2)[t % 2]
            self.trans([(po[:, 0:NE], lgt_sl)],
                       reads=(self.b_stage[0],), writes=[self.b_pT[t % 2]], bf=False)
            self.vop("tensor_copy", self.lgt_tok[:, t, :], po[:, 0:NE],
                     reads=[self.b_pT[t % 2]], writes=[self.b_masks])
            self.vop("reduce_max", self.statA[:, t, 0:1], self.lgt_tok[:, t, :], axis=AX.X,
                     reads=[self.b_masks], writes=[self.b_stsq])
            self.vop("tensor_scalar", self.masks[:, b, t, :], self.lgt_tok[:, t, :],
                     self.statA[:, t, 0:1], None, ALU.is_ge,
                     reads=[self.b_masks, self.b_stsq], writes=[self.b_masks])
            self.trans([(self.pT1b[:, c * 128:(c + 1) * 128],
                         self.ln_fm[:, c, t * 128:(t + 1) * 128])
                        for c in range(NC)],
                       reads=(self.b_lnfm[t],), writes=[self.b_pT[0]])
            us = self.stageb[:, 2048 + (t % 2) * 1024:3072 + (t % 2) * 1024]
            self.vop("tensor_copy", us, self.pT1b[:, 0:1024],
                     reads=[self.b_pT[0], self.b_ust[t % 2]],
                     writes=[self.b_ust[t % 2], self.b_stage[1]])
            self.dma(self.u_buf[b * T + t * 128:b * T + (t + 1) * 128, :], us,
                     reads=[self.b_ust[t % 2], self.b_stage[1], self.b_ubuf],
                     writes=[])
            if "lgt" in self.dmp:
                self.dma(self.dmp["lgt"][b, t * 128:(t + 1) * 128, 0:NE],
                         self.lgt_tok[:, t, :], reads=[self.b_masks], writes=[])
                self.dma(self.dmp["lgt"][b, t * 128:(t + 1) * 128, NE:2 * NE],
                         self.masks[:, b, t, :], reads=[self.b_masks], writes=[])

    def index_build(self):
        bi = self.b_idx
        ic = self.b_icons
        NT = NB * TT
        allm = [self.b_misc[0], self.b_misc[1], self.b_misc[2], self.b_misc[3]]
        self.dma(self.ltf, self.lt_dram.ap(), reads=[], writes=[bi] + allm)
        self.dma(self.thr16, self.thr16_dram.ap(), reads=[], writes=[bi])
        self.dma(self.thrp, self.thrp_dram.ap(), reads=[], writes=[bi])
        self.dma(self.evalsf, self.evals_dram.ap(), reads=[], writes=[bi])
        self.gop("iota", self.tid32, [[128, NB * TT]], base=0,
                 channel_multiplier=1, reads=[], writes=[bi])
        self.gop("iota", self.pcol, [[0, 1]], base=0, channel_multiplier=1,
                 allow_small_or_imprecise_dtypes=True, reads=[], writes=[bi])
        ntv = self.misc[:, 640:640 + NB * TT].rearrange("p (a b) -> p a b", b=TT)
        self.vop("memset", ntv, 1.0, reads=[bi], writes=[bi])
        for e in range(NE):
            self.vop("tensor_mul", self.masks[:, :, :, e], self.masks[:, :, :, e],
                     ntv, reads=[bi, self.b_masks], writes=[self.b_masks])
            if e < NE - 1:
                self.vop("tensor_sub", ntv, ntv, self.masks[:, :, :, e],
                         reads=[self.b_masks, bi], writes=[bi])
        mms = [(self.pT1[0:1, bt * NE:(bt + 1) * NE], self.onescol,
                self.masks[:, bt // TT, bt % TT, :], True, True)
               for bt in range(NT)]
        self.mmx(mms, reads=[self.b_masks, bi], writes=[self.b_pT[0]])
        self.vop("tensor_copy", self.csum, self.pT1[0:1, 0:NT * NE],
                 reads=[self.b_pT[0]], writes=[bi])
        self.dma(self.iscr.rearrange("(one n) -> one n", one=1), self.csum,
                 reads=[bi], writes=[bi])
        self.dma(self.csum32, self.iscr.rearrange("(a b) -> a b", b=NE),
                 reads=[bi], writes=[bi])
        self.mm([(self.pT2[0:NT, 0:NE], self.ltf[0:NT, 0:NT], self.csum32)],
                reads=[bi, ic], writes=[self.b_pT[1]])
        self.vop("tensor_sub", self.carry32, self.pT2[0:NT, 0:NE], self.csum32,
                 reads=[self.b_pT[1], bi], writes=[bi])
        self.mm([(self.pT2[0:1, 128:128 + NE], self.ltf[0:NT, 127:128],
                  self.csum32)],
                reads=[bi], writes=[self.b_pT[1]])
        self.vop("tensor_copy", self.srow[0:1, 0:NE], self.pT2[0:1, 128:128 + NE],
                 reads=[self.b_pT[1]], writes=[bi])
        self.dma(self.iscr.rearrange("(a b) -> a b", b=NE), self.carry32,
                 reads=[bi], writes=[bi])
        self.dma(self.csum, self.iscr.rearrange("(one n) -> one n", one=1),
                 reads=[bi], writes=[bi])
        self.trans([(self.pT1[0:NE, 0:1], self.srow[0:1, 0:NE])], reads=(bi,),
                   writes=[self.b_pT[0]], bf=False)
        self.vop("tensor_copy", self.icol[:, 0:1], self.pT1[0:NE, 0:1],
                 reads=[self.b_pT[0]], writes=[bi])
        self.vop("tensor_scalar", self.irow[:, 0:16], self.thr16, self.icol[:, 0:1],
                 None, ALU.is_lt, reads=[bi, ic], writes=[bi])
        self.vop("reduce_sum", self.icol[:, 1:2], self.irow[:, 0:16], axis=AX.X,
                 reads=[bi], writes=[bi])
        self.vop("tensor_scalar", self.icol[:, 1:2], self.icol[:, 1:2],
                 float(PASS_TOK), None, ALU.mult, reads=[bi], writes=[bi])
        self.mm([(self.pT1[0:NE, 4:5], self.ltf[0:NE, 0:NE], self.icol[:, 1:2])],
                reads=[bi, ic], writes=[self.b_pT[0]])
        self.vop("tensor_sub", self.icol[:, 2:3], self.pT1[0:NE, 4:5], self.icol[:, 1:2],
                 reads=[self.b_pT[0], bi], writes=[bi])
        self.vop("tensor_add", self.icol[:, 3:4], self.icol[:, 2:3], self.icol[:, 1:2],
                 reads=[bi], writes=[bi])
        self.trans([(self.pT2[0:1, 8:16], self.icol[:, 2:3])], reads=(bi,),
                   writes=[self.b_pT[1]], bf=False)
        self.vop("tensor_copy", self.soffr, self.pT2[0:1, 8:16],
                 reads=[self.b_pT[1]], writes=[bi])
        self.vop("tensor_scalar", self.irow[:, 16:16 + NPASS], self.thrp,
                 self.icol[:, 2:3], None, ALU.is_ge, reads=[bi, ic], writes=[bi])
        self.vop("tensor_scalar", self.irow[:, 40:40 + NPASS], self.thrp,
                 self.icol[:, 3:4], None, ALU.is_lt, reads=[bi, ic], writes=[bi])
        self.vop("tensor_mul", self.irow[:, 16:16 + NPASS], self.irow[:, 16:16 + NPASS],
                 self.irow[:, 40:40 + NPASS], reads=[bi], writes=[bi])
        self.mm([(self.pT1[0:1, 16:16 + NPASS], self.evalsf,
                  self.irow[:, 16:16 + NPASS])],
                reads=[bi, ic], writes=[self.b_pT[0]])
        self.vop("tensor_scalar", self.srow[0:1, 8:8 + NPASS],
                 self.pT1[0:1, 16:16 + NPASS], 512.0, None, ALU.mult,
                 reads=[self.b_pT[0]], writes=[bi])
        self.mm([(self.pM[0:128, 0:NPASS], self.onesrow,
                  self.srow[0:1, 8:8 + NPASS])],
                reads=[bi, ic], writes=[self.b_pM[0]])
        self.vop("tensor_scalar", self.idxf[:, 0:NPASS], self.pM[0:128, 0:NPASS], self.pcol,
                 None, ALU.add, reads=[self.b_pM[0], ic], writes=[bi])
        self.vop("tensor_copy", self.idxs[:, 0:NPASS], self.idxf[:, 0:NPASS],
                 reads=[bi], writes=[self.b_widx])
        self._e_wload(0, 0)
        self._e_wload(0, 1)
        b_it = [Buf(), Buf()]
        b_ip = [self.b_pM[1], self.b_pT[0]]
        for bt in range(NT):
            sl = bt % 2
            m_bt = self.masks[:, bt // TT, bt % TT, :]
            ps = (self.pM[:, 512:512 + NE], self.pT1[:, 96:96 + NE])[sl]
            mms = [(ps, self.ltf, m_bt, True, False),
                   (ps, self.onesrow,
                    self.csum[0:1, bt * NE:(bt + 1) * NE], False, False),
                   (ps, self.onesrow, self.soffr, False, True)]
            self.mmx(mms, reads=[self.b_masks, bi, ic], writes=[b_ip[sl]])
            scr = self.idxf[:, sl * NE:(sl + 1) * NE]
            self.vop("tensor_sub", scr, ps, m_bt,
                     reads=[b_ip[sl], self.b_masks, self.b_widx, b_it[sl]],
                     writes=[b_it[sl]])
            self.vop("tensor_mul", scr, scr, m_bt,
                     reads=[b_it[sl], self.b_masks], writes=[b_it[sl]])
            self.vop("reduce_sum", self.posf[:, sl:sl + 1], scr, axis=AX.X,
                     reads=[b_it[sl]], writes=[b_it[sl]])
            self.vop("tensor_copy", self.idxs[:, 16 + sl:17 + sl],
                     self.posf[:, sl:sl + 1], reads=[b_it[sl]], writes=[b_it[sl]])
            for dst, bdst in ((self.gidx_buf.rearrange("(a b) -> a b", b=1),
                               self.b_gidxb),
                              (self.sidx_buf.rearrange("(a b) -> a b", b=1),
                               self.b_sidxb)):
                def fn(E, o=dst, ix=self.idxs[:, 16 + sl:17 + sl],
                       i=self.tid32[:, bt:bt + 1]):
                    return E.indirect_dma_start(
                        out=o, out_offset=IOA(ap=ix, axis=0), in_=i, in_offset=None)
                self.S.gma(fn, reads=[b_it[sl], ic, bdst], writes=[])
        self.vop("memset", self.misc[0:1, 0:1], 0.0,
                 reads=[], writes=[bi, b_it[0], b_it[1]] + allm)

    def _utok(self):
        return self.ln_fm[:, 0:2, :].rearrange("p a n -> p (a n)")

    def _ufm(self):
        return self.ln_fm[:, 2:4, :].rearrange("p a (b n) -> p (a b) n", n=PASS_TOK)

    def _yacc(self):
        return self.lnfm_f32[:, 4:8, :]

    def _gsl(self, s):
        return self.idxs[:, 18 + 4 * s:22 + 4 * s]

    def _ssl(self, s):
        return self.idxs[:, 26 + 4 * s:30 + 4 * s]

    def _e_gload(self, ps):
        sl = ps % 2
        c0 = ps * PASS_TOK
        for c in range(4):
            if not self._gidx_claimed:
                self._gidx_claimed = True
                wg, rg = [self.b_gidxb], []
                ws, rs_ = [self.b_sidxb], []
            else:
                wg, rg = [], [self.b_gidxb]
                ws, rs_ = [], [self.b_sidxb]
            self.dma(self._gsl(sl)[:, c:c + 1],
                     self.gidx_buf[c0 + c * 128:c0 + (c + 1) * 128].rearrange(
                         "(p one) -> p one", one=1),
                     reads=[self.b_gsb[sl]] + rg, writes=[self.b_gsb[sl]] + wg)
            self.dma(self._ssl(sl)[:, c:c + 1],
                     self.sidx_buf[c0 + c * 128:c0 + (c + 1) * 128].rearrange(
                         "(p one) -> p one", one=1),
                     reads=[self.b_gsb[sl]] + rs_, writes=[self.b_gsb[sl]] + ws)

    def _e_ugather(self, ps):
        sl = ps % 2
        ut = self._utok()
        for c in range(4):
            dst = ut[:, c * 1024:(c + 1) * 1024]
            def fn(E, o=dst, ix=self._gsl(sl)[:, c:c + 1]):
                return E.indirect_dma_start(out=o, out_offset=None, in_=self.u_buf,
                                            in_offset=IOA(ap=ix, axis=0))
            rd = [self.b_gsb[sl], self.b_utok[0]]
            if ps < 1:
                rd += list(self.b_lnfm)
            if not self._ubuf_claimed:
                self._ubuf_claimed = True
                self.S.gma(fn, reads=rd, writes=[self.b_utok[0], self.b_ubuf])
            else:
                self.S.gma(fn, reads=rd + [self.b_ubuf],
                           writes=[self.b_utok[0]])

    def _e_wload(self, ps, hb):
        wsl = hb % 2
        eo = hb * 128 * 8192
        for (big, bufs, src) in ((self.bigA, self.bA, self.w1bf),
                                 (self.bigB, self.bB, self.w2bf)):
            for h2 in range(2):
                dst = big[:, wsl * 8192 + h2 * 4096:wsl * 8192 + (h2 + 1) * 4096]
                def fn(E, o=dst, s=src, ix=self.idxs[:, ps:ps + 1],
                       eo=eo + h2 * 4096):
                    return E.indirect_dma_start(out=o, out_offset=None, in_=s,
                                                in_offset=IOA(ap=ix, axis=0),
                                                element_offset=eo)
                bl = bufs(wsl * 8192, (wsl + 1) * 8192)
                if not self._wcvt_claimed:
                    self._wcvt_claimed = True
                    self.S.gma(fn, reads=[self.b_widx] + bl,
                               writes=bl + [self.b_wcvt])
                else:
                    self.S.gma(fn, reads=[self.b_widx, self.b_wcvt] + bl,
                               writes=bl)

    def phase_E_sparse(self):
        W1q = lambda s: self.bigA[:, s * 8192:(s + 1) * 8192].rearrange(
            "p (c n) -> p c n", n=1024)
        W2q = lambda s: self.bigB[:, s * 8192:(s + 1) * 8192].rearrange(
            "p (c n) -> p c n", n=1024)
        bW1 = lambda s: self.bA(s * 8192, (s + 1) * 8192)
        bW2 = lambda s: self.bB(s * 8192, (s + 1) * 8192)
        hgv = self.hg.rearrange("p (c n) -> p c n", n=PASS_TOK)
        ya = self._yacc()
        self._e_gload(0)
        self._e_ugather(0)
        for ps in range(NPASS):
            sl = ps % 2
            if ps + 1 < NPASS:
                self._e_gload(ps + 1)
            ut = self._utok()
            fence = list(self.b_lnfm) if ps < 1 else []
            uf = self._ufm()
            trs0, trs1, trs2 = [], [], []
            for c2 in range(4):
                for fc in range(8):
                    src_t = ut[:, c2 * 1024 + fc * 128:c2 * 1024 + (fc + 1) * 128]
                    if c2 < 2:
                        trs0.append((self.pMb[:, (c2 * 8 + fc) * 128:
                                              (c2 * 8 + fc + 1) * 128], src_t))
                    elif c2 == 2:
                        trs1.append((self.pT1b[:, fc * 128:(fc + 1) * 128], src_t))
                    else:
                        trs2.append((self.pT2b[:, fc * 128:(fc + 1) * 128], src_t))
            self.trans(trs0, reads=(self.b_utok[0],),
                       writes=[self.b_pM[0], self.b_pM[1]])
            self.trans(trs1, reads=(self.b_utok[0],), writes=[self.b_pT[0]])
            self.trans(trs2, reads=(self.b_utok[0],), writes=[self.b_pT[1]])
            self.vop("tensor_copy", uf[:, :, 0:128],
                     self.pMb[:, 0:1024].rearrange("p (c n) -> p c n", n=128),
                     reads=[self.b_pM[0], self.b_pM[1], self.b_ufm[0]] + fence,
                     writes=[self.b_ufm[0]])
            self.vop("tensor_copy", uf[:, :, 128:256],
                     self.pMb[:, 1024:2048].rearrange("p (c n) -> p c n", n=128),
                     reads=[self.b_pM[0], self.b_pM[1], self.b_ufm[0]],
                     writes=[self.b_ufm[0]])
            self.vop("tensor_copy", uf[:, :, 256:384],
                     self.pT1b[:, 0:1024].rearrange("p (c n) -> p c n", n=128),
                     reads=[self.b_pT[0], self.b_ufm[0]], writes=[self.b_ufm[0]])
            self.vop("tensor_copy", uf[:, :, 384:512],
                     self.pT2b[:, 0:1024].rearrange("p (c n) -> p c n", n=128),
                     reads=[self.b_pT[1], self.b_ufm[0]], writes=[self.b_ufm[0]])
            for hb in range(4):
                wsl = hb % 2
                nps, nhb = (ps, hb + 1) if hb < 3 else (ps + 1, 0)
                if nps < NPASS and not (nps == 0 and nhb <= 1):
                    self._e_wload(nps, nhb)
                if hb == 2 and ps + 1 < NPASS:
                    self._e_ugather(ps + 1)
                bh = list(self.b_hg)
                for rnd in range(4):
                    pr = (rnd % 2) * 1024
                    mms = []
                    for hcq in range(2):
                        hc = rnd * 2 + hcq
                        for fc in range(NC):
                            mms.append((self.pL[:, pr + hcq * 512:
                                                pr + (hcq + 1) * 512],
                                        W1q(wsl)[:, fc, hc * 128:(hc + 1) * 128],
                                        uf[:, fc, :], fc == 0, fc == NC - 1))
                    self.mmx(mms, reads=bW1(wsl) + [self.b_ufm[0]],
                             writes=self.pLh(rnd % 2))
                    self.act(self.hg[:, rnd * 1024:(rnd + 1) * 1024],
                             self.pL[:, pr:pr + 1024],
                             AF.Gelu_apprx_tanh, reads=self.pLh(rnd % 2),
                             writes=[self.b_hg[rnd]])
                for half in range(2):
                    mms = []
                    for t2 in range(4):
                        if half == 0:
                            out = (self.pM[:, 0:512], self.pM[:, 512:1024],
                                   self.pT1[:, 0:512], self.pT2[:, 0:512])[t2]
                        else:
                            out = self.pL[:, t2 * 512:(t2 + 1) * 512]
                        for hc in range(NC):
                            mms.append((out, hgv[:, hc, t2 * 128:(t2 + 1) * 128],
                                        W2q(wsl)[:, hc, half * 512:(half + 1) * 512],
                                        hc == 0, hc == NC - 1))
                    wr = ([self.b_pM[0], self.b_pM[1], self.b_pT[0], self.b_pT[1]]
                          if half == 0 else self.pLh(0) + self.pLh(1))
                    self.mmx(mms, reads=bW2(wsl) + bh, writes=wr)
                    op = "tensor_copy" if hb == 0 else "tensor_add"
                    for t2 in range(4):
                        if half == 0:
                            pin = (self.pM[:, 0:512], self.pM[:, 512:1024],
                                   self.pT1[:, 0:512], self.pT2[:, 0:512])[t2]
                            bp = (self.b_pM[0], self.b_pM[1],
                                  self.b_pT[0], self.b_pT[1])[t2]
                        else:
                            pin = self.pL[:, t2 * 512:(t2 + 1) * 512]
                            bp = self.b_pL[t2]
                        dst = ya[:, t2, half * 512:(half + 1) * 512]
                        args = (dst, pin) if hb == 0 else (dst, dst, pin)
                        rdy = [bp, self.b_yst[0]] + (fence if hb == 0 else [])
                        self.vop(op, *args, reads=rdy, writes=[self.b_yst[0]])
            for c in range(4):
                def fn(E, i=ya[:, c, :], ix=self._ssl(sl)[:, c:c + 1]):
                    return E.indirect_dma_start(out=self.y_buf,
                                                out_offset=IOA(ap=ix, axis=0),
                                                in_=i, in_offset=None)
                self.S.gma(fn, reads=[self.b_yst[0], self.b_gsb[sl]],
                           writes=[self.b_ybuf])

    def phase_F(self, b):
        bigA32 = self.bigA.bitcast(F32)
        bigB32 = self.bigB.bitcast(F32)
        for t in range(TT):
            i4 = t % 4
            xs = bigA32[:, (4 + i4) * 1024:(5 + i4) * 1024]
            bxs = self.b_bigA[4 + i4]
            ygt = bigA32[:, i4 * 1024:(i4 + 1) * 1024]
            byg = [self.b_bigA[i4]]
            ss = bigB32[:, i4 * 1024:(i4 + 1) * 1024]
            bss = self.b_bigB[i4]
            self.dma(xs, self.x[b, t * 128:(t + 1) * 128, :],
                     reads=[bxs], writes=[bxs])
            if not self._ybuf_claimed:
                self._ybuf_claimed = True
                wy, ry = [self.b_ybuf], []
            else:
                wy, ry = [], [self.b_ybuf]
            self.dma(ygt, self.y_buf[b * T + t * 128:b * T + (t + 1) * 128, :],
                     reads=byg + ry, writes=byg + wy)
            if not self._sspill_claimed:
                self._sspill_claimed = True
                wsx, rsx = [self.b_sspill], []
            else:
                wsx, rsx = [], [self.b_sspill]
            self.dma(ss, self.s_spill[b, t * 128:(t + 1) * 128, :],
                     reads=[bss] + rsx, writes=[bss] + wsx)
            if b == 0:
                vs = bigB32[:, (4 + i4) * 1024:(5 + i4) * 1024]
                bvs = self.b_bigB[4 + i4]
                if not self._vit_claimed:
                    self._vit_claimed = True
                    wv, rv = [self.b_vit], []
                else:
                    wv, rv = [], [self.b_vit]
                self.dma(vs, self.vit_buf[t * 128:(t + 1) * 128, :],
                         reads=[bvs] + rv, writes=[bvs] + wv)
                vread = [bvs]
            else:
                vs = self.xn_tok[:, t, :]
                vread = [self.b_xn[t]]
            self.vop("tensor_add", ygt, ygt, vs, reads=byg + vread, writes=byg)
            if "gin" in self.dmp:
                self.dma(self.dmp["gin"][b, t * 128:(t + 1) * 128, :], ygt,
                         reads=byg, writes=[])
            gg = self.lnfm_f32[:, t % 4, :]
            bgg = self.b_lnfm[t % 4]
            wex = ([self.b_utok[0], self.b_ufm[0], self.b_yst[0]]
                   if (b == 0 and t < 4) else [])
            self.act(gg, ygt, AF.Silu, reads=byg, writes=[bgg] + wex)
            self.vop("tensor_mul", gg, gg, ss, reads=[bgg, bss], writes=[bgg])
            self.gop("tensor_add", xs, xs, gg, reads=[bxs, bgg], writes=[bxs])
            self.S.sdma(self.out[b, t * 128:(t + 1) * 128, :], xs,
                        reads=[bxs], writes=[self.b_out])

    def build(self):
        self.b_out = Buf()
        self._wcvt_claimed = False
        self._ubuf_claimed = False
        self._gidx_claimed = False
        self._ybuf_claimed = False
        self._sspill_claimed = False
        self._vit_claimed = False
        self.init_consts()
        self.convert_qkv_weights()
        for b in range(NB):
            self.phase_A_pass1(b)
            self.load_attn_weights()
            self.phase_A_pass2(b)
            self.phase_B(b)
            if b == 0:
                self.convert_moe_weights()
            self.phase_C(b)
            self.phase_D(b)
            if b == 0:
                for t in range(TT):
                    self.dma(self.vit_buf[t * 128:(t + 1) * 128, :],
                             self.xn_tok[:, t, :],
                             reads=[self.b_xn[t], self.b_vit], writes=[])
        self.index_build()
        self.phase_E_sparse()
        for b in range(NB):
            self.phase_F(b)
        it_deps = [e for e in self.S.dma_ring if e is not None]
        it_deps += [e for e in self.S.gma_ring if e is not None]
        eid = self.S._mk("sync", lambda E: None, (self.b_out,), (), False)
        self.S.items[eid]["deps"] = sorted(set(self.S.items[eid]["deps"]) | set(it_deps))
        self.S.finalize(self.nc, self.sems)
        return self.nc


NAME_MAP = {
    "L0_attn_x": "x",
    "L0_attn_wq": "wq", "L0_attn_bq": "bq",
    "L0_attn_wk": "wk", "L0_attn_bk": "bk",
    "L0_attn_wv": "wv", "L0_attn_bv": "bv",
    "L0_attn_wo": "wo", "L0_attn_bo": "bo",
    "L0_ffn_ln1_w": "ln1w", "L0_ffn_ln1_b": "ln1b",
    "L0_ffn_w_qkv": "wqkv", "L0_ffn_w_proj": "wproj",
    "L0_ffn_ln2_w": "ln2w", "L0_ffn_ln2_b": "ln2b",
    "L0_ffn_gate_w": "gatew", "L0_ffn_gate_b": "gateb",
    "L0_ffn_w1": "w1", "L0_ffn_w2": "w2",
}


def build_program(dump=()):
    bld = Builder(dump=dump)
    nc = bld.build()
    return nc


def kernel(**inputs):
    arrs = {NAME_MAP[k]: np.ascontiguousarray(np.asarray(v, dtype=np.float32))
            for k, v in inputs.items()}
    x = arrs.pop("x")
    nc = build_program()
    in_maps = []
    for c in range(8):
        m = dict(arrs)
        m["x"] = np.ascontiguousarray(x[c * NB:(c + 1) * NB])
        in_maps.append(m)
    res = run_bass_kernel_spmd(nc, in_maps, core_ids=list(range(8)))
    return np.concatenate([res.results[c]["out"] for c in range(8)], axis=0)


if __name__ == "__main__":
    pass


# version 24
# speedup vs baseline: 1.0666x; 1.0110x over previous
import numpy as np
from contextlib import ExitStack

import concourse.bass as bass
import concourse.mybir as mybir
from concourse.bass_utils import run_bass_kernel_spmd

F32 = mybir.dt.float32
BF16 = mybir.dt.bfloat16
U32 = mybir.dt.uint32
AF = mybir.ActivationFunctionType
ALU = mybir.AluOpType
AX = mybir.AxisListType
IOA = bass.IndirectOffsetOnAxis

PASS_TOK = 512
NPASS = 16
NSLOT = NPASS * PASS_TOK

ENGS = ("sync", "tensor", "vector", "scalar", "gpsimd")
NDQ = 16
NGQ = 24

NB = 2
TT = 16
T = 2048
D = 1024
NC = 8
NH = 16
DK = 64
NE = 8
HID = 4096
EPS = 1e-5


class Buf:
    __slots__ = ("w", "rs")

    def __init__(self):
        self.w = None
        self.rs = []


PARANOID = False


class Sched:
    def __init__(self):
        self.items = []
        self.per_eng = {e: [] for e in ENGS}
        self.last_per_eng = {e: None for e in ENGS}
        self.ndma = 0
        self.dma_ring = [None] * NDQ
        self.ngma = 0
        self.gma_ring = [None] * NGQ
        self.nsma = 0
        self.sma_ring = [None] * 4

    def _mk(self, eng, fn, reads, writes, dma):
        deps = set()
        for b in reads:
            if b.w is not None:
                deps.add(b.w)
        for b in writes:
            if b.w is not None:
                deps.add(b.w)
            for r in b.rs:
                deps.add(r)
        if PARANOID:
            for e2 in ENGS:
                if e2 != eng and self.last_per_eng[e2] is not None:
                    deps.add(self.last_per_eng[e2])
        eid = len(self.items)
        it = dict(eid=eid, eng=eng, fn=fn, dma=dma)
        if dma and eng == "sync":
            q = self.ndma % NDQ
            it["sem"] = "dq%d" % q
            it["val"] = 16 * (self.ndma // NDQ + 1)
            if self.dma_ring[q] is not None:
                deps.add(self.dma_ring[q])
            self.dma_ring[q] = eid
            self.ndma += 1
        elif dma and eng == "scalar":
            q = self.nsma % 4
            it["sem"] = "sq%d" % q
            it["val"] = 16 * (self.nsma // 4 + 1)
            if self.sma_ring[q] is not None:
                deps.add(self.sma_ring[q])
            self.sma_ring[q] = eid
            self.nsma += 1
        elif dma:
            q = self.ngma % NGQ
            it["sem"] = "gq%d" % q
            it["val"] = 16 * (self.ngma // NGQ + 1)
            if self.gma_ring[q] is not None:
                deps.add(self.gma_ring[q])
            self.gma_ring[q] = eid
            self.ngma += 1
        it["deps"] = sorted(deps)
        self.items.append(it)
        self.per_eng[eng].append(it)
        self.last_per_eng[eng] = eid
        for b in writes:
            b.w = eid
            b.rs = []
        for b in reads:
            b.rs.append(eid)
        return eid

    def op(self, eng, fn, reads=(), writes=()):
        return self._mk(eng, fn, tuple(reads), tuple(writes), False)

    def dma(self, out_ap, in_ap, reads=(), writes=(), slow=False):
        def fn(E, o=out_ap, i=in_ap, s=slow):
            if s:
                return E.dma_start(out=o, in_=i, allow_slow_non_contiguous=True)
            return E.dma_start(out=o, in_=i)
        return self._mk("sync", fn, tuple(reads), tuple(writes), True)

    def gma(self, fn, reads=(), writes=()):
        return self._mk("gpsimd", fn, tuple(reads), tuple(writes), True)

    def sdma(self, out_ap, in_ap, reads=(), writes=()):
        def fn(E, o=out_ap, i=in_ap):
            return E.dma_start(out=o, in_=i)
        return self._mk("scalar", fn, tuple(reads), tuple(writes), True)

    def finalize(self, nc, sems):
        demanded = set()
        for it in self.items:
            for d in it["deps"]:
                demanded.add(d)
        for e in ENGS:
            c = 0
            for it in self.per_eng[e]:
                if it["dma"]:
                    it["inc"] = 16
                elif it["eid"] in demanded:
                    c += 1
                    it["val"] = c
                    it["sem"] = e
                    it["inc"] = 1
                else:
                    it["val"] = None
                    it["sem"] = None
                    it["inc"] = 0

        items_all = self.items

        def mk_runner(engname):
            eng_items = self.per_eng[engname]

            def go(E):
                hi = {}
                for it in eng_items:
                    for d in it["deps"]:
                        p = items_all[d]
                        if engname == "tensor" and p["eng"] == "tensor" and not p["dma"]:
                            continue
                        v = p["val"]
                        if v is not None and hi.get(p["sem"], 0) < v:
                            E.wait_ge(sems[p["sem"]], v)
                            hi[p["sem"]] = v
                    inst = it["fn"](E)
                    if it["inc"]:
                        inst.then_inc(sems[it["sem"]], it["inc"])
            return go

        with nc.Block() as block:
            block.sync(mk_runner("sync"))
            block.tensor(mk_runner("tensor"))
            block.vector(mk_runner("vector"))
            block.scalar(mk_runner("scalar"))
            block.gpsimd(mk_runner("gpsimd"))


class Builder:
    def __init__(self, dump=()):
        self.dump = set(dump)
        nc = bass.Bass("TRN2", target_bir_lowering=False, debug=False)
        self.nc = nc
        self.S = Sched()
        self.ctx = ExitStack()
        self._alloc_dram()
        self._alloc_sbuf()
        self._alloc_psum()
        self.sems = {e: self.ctx.enter_context(nc.semaphore("sem_" + e)) for e in ENGS}
        for q in range(NDQ):
            self.sems["dq%d" % q] = self.ctx.enter_context(nc.semaphore("dq%d" % q))
        for q in range(NGQ):
            self.sems["gq%d" % q] = self.ctx.enter_context(nc.semaphore("gq%d" % q))
        for q in range(4):
            self.sems["sq%d" % q] = self.ctx.enter_context(nc.semaphore("sq%d" % q))
        self._mk_bufs()

    def _alloc_dram(self):
        nc = self.nc
        din = lambda n, s: nc.dram_tensor(n, s, F32, kind="ExternalInput").ap()
        self.x = din("x", [NB, T, D])
        self.wq_d = din("wq", [D, DK]); self.bq_d = din("bq", [DK])
        self.wk_d = din("wk", [D, DK]); self.bk_d = din("bk", [DK])
        self.wv_d = din("wv", [D, DK]); self.bv_d = din("bv", [DK])
        self.wo_d = din("wo", [DK, D]); self.bo_d = din("bo", [D])
        self.ln1w_d = din("ln1w", [D]); self.ln1b_d = din("ln1b", [D])
        self.wqkv_d = din("wqkv", [D, 3 * D])
        self.wproj_d = din("wproj", [D, D])
        self.ln2w_d = din("ln2w", [D]); self.ln2b_d = din("ln2b", [D])
        self.gatew_d = din("gatew", [D, NE]); self.gateb_d = din("gateb", [NE])
        self.w1_d = din("w1", [NE, D, HID])
        self.w2_d = din("w2", [NE, HID, D])
        self.out = nc.dram_tensor("out", [NB, T, D], F32, kind="ExternalOutput").ap()
        self.s_spill = nc.dram_tensor("s_spill", [NB, T, D], F32, kind="Internal").ap()
        self.wqkv_bf = nc.dram_tensor("wqkv_bf", [D, 3 * D], BF16, kind="Internal").ap()
        self.wproj_bf = nc.dram_tensor("wproj_bf", [D, D], BF16, kind="Internal").ap()
        ik = lambda n: "ExternalOutput" if n in self.dump else "Internal"
        self.w1bf = nc.dram_tensor("w1bf", [NE * 4 * 128, 8 * 1024], BF16,
                                   kind="Internal").ap()
        self.w2bf = nc.dram_tensor("w2bf", [NE * 4 * 128, 8 * 1024], BF16,
                                   kind="Internal").ap()
        self.u_buf = nc.dram_tensor("u_buf", [NB * T, D], BF16, kind=ik("u_buf")).ap()
        self.vit_buf = nc.dram_tensor("vit_buf", [T, D], F32, kind="Internal").ap()
        self.y_buf = nc.dram_tensor("y_buf", [NB * T + 128, D], F32,
                                    kind=ik("y_buf")).ap()
        self.gidx_buf = nc.dram_tensor("gidx_buf", [NSLOT], U32, kind=ik("gidx_buf")).ap()
        self.iscr = nc.dram_tensor("iscr", [NB * TT * NE], F32, kind="Internal").ap()
        self.sidx_buf = nc.dram_tensor("sidx_buf", [NSLOT], U32, kind=ik("sidx_buf")).ap()
        self.ident_dram = nc.inline_tensor(np.eye(8, dtype=np.float32), name="identc")
        import ml_dtypes
        self.identb_dram = nc.inline_tensor(
            np.eye(128).astype(ml_dtypes.bfloat16), name="identbc")
        lt = (np.arange(128)[:, None] <= np.arange(128)[None, :]).astype(np.float32)
        self.lt_dram = nc.inline_tensor(lt, name="ltmaskc")
        thr16 = np.tile(PASS_TOK * np.arange(16, dtype=np.float32)[None, :], (NE, 1))
        self.thr16_dram = nc.inline_tensor(thr16, name="thr16c")
        thrp = np.tile(PASS_TOK * np.arange(NPASS, dtype=np.float32)[None, :], (NE, 1))
        self.thrp_dram = nc.inline_tensor(thrp, name="thrpc")
        evals = np.arange(NE, dtype=np.float32)[:, None]
        self.evals_dram = nc.inline_tensor(evals, name="evalsc")
        self.dmp = {}
        for name in self.dump:
            if name in ("u_buf", "y_buf", "gidx_buf", "sidx_buf"):
                continue
            shp = {"xn": [NB, T, D], "xn2": [NB, T, D], "s": [NB, T, D],
                   "lgt": [NB, T, 2 * NE], "gin": [NB, T, D]}[name]
            self.dmp[name] = nc.dram_tensor("dbg_" + name, shp, F32, kind="ExternalOutput").ap()

    def _alloc_sbuf(self):
        nc = self.nc
        E_ = self.ctx.enter_context
        sb = lambda n, s, d: E_(nc.sbuf_tensor(n, s, d))
        self.xn_tok = sb("xn_tok", [128, TT, D], F32)
        self.ln_fm = sb("ln_fm", [128, NC, T], BF16)
        self.bigA = sb("bigA", [128, 16384], BF16)
        self.bigB = sb("bigB", [128, 16384], BF16)
        self.hg = sb("hg", [128, 4096], BF16)
        self.misc = sb("misc", [128, 4096], F32)
        self.stage = sb("stage", [128, 2048], F32)
        self.pool = sb("pool", [128, 3, 1024], F32)
        self.identf = sb("identf", [8, 8], F32)
        self.identb = sb("identb", [128, 128], BF16)
        self.ltmask = sb("ltmask", [128, 128], BF16)
        self.onesb = sb("onesb", [128, 128], BF16)
        self.gam1 = sb("gam1", [128, NC], F32)
        self.bet1 = sb("bet1", [128, NC], F32)
        self.gam2 = sb("gam2", [128, NC], F32)
        self.bet2 = sb("bet2", [128, NC], F32)
        self.bq8 = sb("bq8", [64, 2], F32)
        self.bkc = sb("bkc", [64, 1], F32)
        self.bvc = sb("bvc", [64, 1], F32)
        self.gw_st = sb("gw_st", [128, NC, NE], F32)
        self.gwg_hi = sb("gwg_hi", [128, NC, NE], BF16)
        self.gwg_lo = sb("gwg_lo", [128, NC, NE], BF16)
        self.gw_bf = sb("gw_bf", [128, NC, NE], BF16)
        self.b2bf = sb("b2bf", [128, NC], BF16)
        self.gb_row = sb("gb_row", [1, NE], F32)
        self.gbadd = sb("gbadd", [NE, 1], F32)
        self.masks = sb("masksb", [128, NB, TT, NE], F32)
        self.lgt_tok = sb("lgt_tok", [128, TT, NE], F32)
        self.statA = sb("statA", [128, TT, 4], F32)
        self.ntcol = sb("ntcol", [128, 1], F32)
        self.maxsc = sb("maxsc", [1, 2], F32)
        self.reccol = sb("reccol", [128, TT], F32)
        self.epsc = sb("epsc", [128, 1], F32)
        self.idxs = sb("idxs", [128, 40], U32)
        self.miscu = self.misc.bitcast(U32)
        self.ltf = self.misc[:, 0:128]
        self.csum = self.misc[0:1, 128:384]
        self.csum32 = self.misc[0:32, 384:392]
        self.carry32 = self.misc[0:32, 392:400]
        self.irow = self.misc[0:8, 400:464]
        self.icol = self.misc[0:8, 464:472]
        self.srow = self.misc[0:1, 472:504]
        self.soffr = self.misc[0:1, 504:512]
        self.thr16 = self.misc[0:8, 512:528]
        self.thrp = self.misc[0:8, 528:528 + NPASS]
        self.evalsf = self.misc[0:8, 552:553]
        self.idxf = self.misc[:, 560:584]
        self.pcol = self.misc[:, 584:585]
        self.posf = self.misc[:, 585:587]
        self.tid32 = self.miscu[:, 588:588 + NB * TT]
        self.iniz = self.miscu[:, 0:NSLOT // 128]
        self.onescol = self.ltf[:, 127:128]
        self.onesrow = self.ltf[0:1, 0:128]
        self.miscb = self.misc.bitcast(BF16)
        self.poolb = self.pool.bitcast(BF16)
        self.hgf = self.hg.bitcast(F32)
        self.stageb = self.stage.bitcast(BF16)
        self.lnfm_f32 = self.ln_fm.bitcast(F32)
        rem = nc.sbuf_bytes_remaining
        assert rem >= 0, f"SBUF over budget by {-rem} bytes"

    def _alloc_psum(self):
        nc = self.nc
        E_ = self.ctx.enter_context
        self.pL = E_(nc.psum_tensor("pL", [128, 2048], F32))
        self.pM = E_(nc.psum_tensor("pM", [128, 1024], F32))
        self.pT1 = E_(nc.psum_tensor("pT1", [128, 512], F32))
        self.pT2 = E_(nc.psum_tensor("pT2", [128, 512], F32))
        self.pT1b = self.pT1.bitcast(BF16)
        self.pT2b = self.pT2.bitcast(BF16)
        self.pMb = self.pM.bitcast(BF16)

    def _mk_bufs(self):
        self.b_ident = Buf(); self.b_lt = Buf(); self.b_ones = Buf()
        self.b_params = Buf(); self.b_gw = Buf()
        self.b_pool = [Buf() for _ in range(3)]
        self.b_stage = [Buf(), Buf()]
        self.b_hg = [Buf() for _ in range(4)]
        self.b_misc = [Buf() for _ in range(4)]
        self.b_xn = [Buf() for _ in range(TT)]
        self.b_lnfm = [Buf() for _ in range(TT)]
        self.b_bigA = [Buf() for _ in range(8)]
        self.b_bigB = [Buf() for _ in range(8)]
        self.b_stat = [Buf(), Buf()]
        self.b_masks = Buf()
        self.b_pL = [Buf(), Buf(), Buf(), Buf()]
        self.b_pM = [Buf(), Buf()]
        self.b_pT = [Buf(), Buf()]
        self.b_sspill = Buf()
        self.b_wbf = Buf()
        self.b_reccol = Buf()
        self.b_ptile = [Buf(), Buf(), Buf()]
        self.b_stsum = Buf()
        self.b_stsq = Buf()
        self.b_ust = [Buf(), Buf()]
        self.b_wcvt = Buf()
        self.b_ubuf = Buf()
        self.b_vit = Buf()
        self.b_ybuf = Buf()
        self.b_gidxb = Buf()
        self.b_sidxb = Buf()
        self.b_idx = Buf()
        self.b_icons = Buf()
        self.b_utok = [Buf(), Buf()]
        self.b_ufm = [Buf(), Buf()]
        self.b_yst = [Buf(), Buf()]
        self.b_gsb = [Buf(), Buf()]
        self.b_widx = Buf()

    def pLh(self, i):
        return [self.b_pL[2 * i], self.b_pL[2 * i + 1]]

    def bA(self, lo, hi):
        return [self.b_bigA[i] for i in range(lo // 2048, (hi + 2047) // 2048)]

    def bB(self, lo, hi):
        return [self.b_bigB[i] for i in range(lo // 2048, (hi + 2047) // 2048)]

    def op(self, eng, fn, reads=(), writes=()):
        return self.S.op(eng, fn, reads, writes)

    def dma(self, out_ap, in_ap, reads=(), writes=(), slow=False):
        return self.S.dma(out_ap, in_ap, reads, writes, slow=slow)

    def mm(self, mms, reads=(), writes=()):
        def fn(E, mms=mms):
            n = len(mms)
            last = None
            for i, (o, l, r) in enumerate(mms):
                last = E.matmul(o, l, r, start=(i == 0), stop=(i == n - 1))
            return last
        return self.op("tensor", fn, reads, writes)

    def mmx(self, mms, reads=(), writes=()):
        def fn(E, mms=mms):
            last = None
            for (o, l, r, st, sp) in mms:
                last = E.matmul(o, l, r, start=st, stop=sp)
            return last
        return self.op("tensor", fn, reads, writes)

    def trans(self, trs, reads=(), writes=(), bf=True):
        ident = self.identb if bf else self.identf

        def fn(E, trs=trs, ident=ident):
            last = None
            for item in trs:
                (o, i), bp = item[:2], (item[2] if len(item) > 2 else 0)
                p = i.shape[0]
                last = E.transpose(o, i, ident[bp:bp + p, bp:bp + p])
            return last
        return self.op("tensor", fn, tuple(reads) + (self.b_ident,), writes)

    def act(self, out, in_, func, reads=(), writes=(), **kw):
        def fn(E, o=out, i=in_, f=func, kw=kw):
            return E.activation(o, i, f, **kw)
        return self.op("scalar", fn, reads, writes)

    def vop(self, name, *args, reads=(), writes=(), **kw):
        def fn(E, n=name, a=args, kw=kw):
            return getattr(E, n)(*a, **kw)
        return self.op("vector", fn, reads, writes)

    def gop(self, name, *args, reads=(), writes=(), **kw):
        def fn(E, n=name, a=args, kw=kw):
            return getattr(E, n)(*a, **kw)
        return self.op("gpsimd", fn, reads, writes)

    def init_consts(self):
        pb = self.b_params
        self.dma(self.identf[:], self.ident_dram.ap(), writes=[self.b_ident])
        self.dma(self.identb[:], self.identb_dram.ap(), writes=[self.b_ident])
        self.dma(self.pool[:, 2, :][:, 0:128], self.lt_dram.ap()[:, 0:128],
                 writes=[self.b_pool[2]])
        self.vop("tensor_copy", self.ltmask[:], self.pool[:, 2, 0:128],
                 reads=[self.b_pool[2]], writes=[self.b_lt])
        self.vop("memset", self.onesb[:], 1.0, writes=[self.b_ones])
        self.vop("memset", self.epsc[:], EPS, writes=[self.b_ones])
        rr = lambda ap: ap.rearrange("(c p) -> p c", p=128)
        cc = lambda ap: ap.rearrange("(p one) -> p one", one=1)
        self.dma(self.gam1[:], rr(self.ln1w_d), writes=[pb], slow=True)
        self.dma(self.bet1[:], rr(self.ln1b_d), writes=[pb], slow=True)
        self.dma(self.gam2[:], rr(self.ln2w_d), writes=[pb], slow=True)
        self.dma(self.bet2[:], rr(self.ln2b_d), writes=[pb], slow=True)
        self.dma(self.bq8[:, 1:2], cc(self.bq_d), writes=[pb])
        self.dma(self.bkc[:], cc(self.bk_d), writes=[pb])
        self.dma(self.bvc[:], cc(self.bv_d), writes=[pb])
        self.dma(self.gb_row[:], self.gateb_d.rearrange("(one d) -> one d", one=1), writes=[pb])
        self.vop("tensor_scalar", self.bq8[:, 0:1], self.bq8[:, 1:2], 0.125, None, ALU.mult,
                 reads=[pb], writes=[pb])
        gwr = self.gatew_d.rearrange("(c p) e -> p c e", p=128)
        self.dma(self.gw_st[:], gwr, writes=[self.b_gw])
        for c in range(NC):
            self.vop("tensor_scalar", self.gw_st[:, c, :], self.gw_st[:, c, :],
                     self.gam2[:, c:c + 1], None, ALU.mult,
                     reads=[self.b_gw, pb], writes=[self.b_gw])
        self.vop("tensor_copy", self.gwg_hi[:], self.gw_st[:],
                 reads=[self.b_gw], writes=[self.b_gw])
        self.vop("tensor_sub", self.gwg_lo[:], self.gw_st[:], self.gwg_hi[:],
                 reads=[self.b_gw], writes=[self.b_gw])
        self.dma(self.gw_st[:], gwr, reads=[self.b_gw], writes=[self.b_gw])
        self.vop("tensor_copy", self.gw_bf[:], self.gw_st[:],
                 reads=[self.b_gw], writes=[self.b_gw])
        self.vop("tensor_copy", self.b2bf[:], self.bet2[:], reads=[pb], writes=[pb])
        mms = [(self.pT1[0:1, 0:NE], self.b2bf[:, c:c + 1], self.gw_bf[:, c, :])
               for c in range(NC)]
        self.mm(mms, reads=[self.b_gw, pb], writes=[self.b_pT[0]])
        self.vop("tensor_add", self.misc[0:1, 0:NE], self.pT1[0:1, 0:NE], self.gb_row[:],
                 reads=[self.b_pT[0], pb], writes=[self.b_misc[0]])
        self.vop("tensor_copy", self.miscb[0:1, 2048:2048 + NE], self.misc[0:1, 0:NE],
                 reads=[self.b_misc[0]], writes=[self.b_misc[1]])
        self.trans([(self.pT2b[0:NE, 0:1], self.miscb[0:1, 2048:2048 + NE])],
                   reads=(self.b_misc[1],), writes=[self.b_pT[1]])
        self.vop("tensor_copy", self.gbadd[:], self.pT2b[0:NE, 0:1],
                 reads=[self.b_pT[1]], writes=[pb])
        iniz2 = self.miscu[0:NSLOT // 128, 0:128]
        self.gop("iota", iniz2, [[0, 128]], base=0,
                 channel_multiplier=0, reads=[], writes=[self.b_idx, self.b_misc[0]])
        self.dma(self.gidx_buf.rearrange("(c p) -> c p", p=128), iniz2,
                 reads=[self.b_idx, self.b_misc[0]], writes=[self.b_gidxb])
        self.gop("iota", iniz2, [[1, 128]], base=NB * T,
                 channel_multiplier=0, reads=[self.b_gidxb],
                 writes=[self.b_idx, self.b_misc[0]])
        self.dma(self.sidx_buf.rearrange("(c p) -> c p", p=128), iniz2,
                 reads=[self.b_idx, self.b_misc[0]], writes=[self.b_sidxb])

    def convert_qkv_weights(self):
        def cast_dma(out_ap, in_ap, reads, writes):
            def fn(E, o=out_ap, i=in_ap):
                return E.dma_start(out=o, in_=i, max_dma_last_dim=8192)
            return self.S.gma(fn, reads, writes)
        cast_dma(self.wqkv_bf, self.wqkv_d, [self.b_wbf], [])
        cast_dma(self.wproj_bf, self.wproj_d, [self.b_wbf], [])

    def convert_moe_weights(self):
        def cast_dma(out_ap, in_ap, reads, writes):
            def fn(E, o=out_ap, i=in_ap):
                return E.dma_start(out=o, in_=i, max_dma_last_dim=8192)
            return self.S.gma(fn, reads, writes)
        w1v = self.w1bf.rearrange("(s p) (c n) -> s p c n", p=128, n=1024)
        w2v = self.w2bf.rearrange("(s p) (c n) -> s p c n", p=128, n=1024)
        for e in range(NE):
            for hb in range(4):
                s = e * 4 + hb
                cast_dma(w1v[s],
                         self.w1_d[e, :, hb * 1024:(hb + 1) * 1024].rearrange(
                             "(c p) n -> p c n", p=128),
                         [self.b_wcvt], [])
                cast_dma(w2v[s],
                         self.w2_d[e, hb * 1024:(hb + 1) * 1024, :].rearrange(
                             "(c p) n -> p c n", p=128),
                         [self.b_wcvt], [])

    def load_attn_weights(self):
        w_bufs = self.bB(10240, 12800)
        for i, w in enumerate((self.wq_d, self.wk_d, self.wv_d)):
            sl = i % 2
            self.dma(self.stage[:, sl * 1024:sl * 1024 + 512].rearrange(
                         "p (c k) -> p c k", k=DK),
                     w.rearrange("(c p) k -> p c k", p=128),
                     reads=[self.b_stage[sl]], writes=[self.b_stage[sl]])
            self.gop("tensor_copy", self.bigB[:, 10240 + i * 512:10240 + (i + 1) * 512],
                     self.stage[:, sl * 1024:sl * 1024 + 512],
                     reads=[self.b_stage[sl]], writes=w_bufs)
        self.dma(self.stage[0:64, 1024:2048], self.wo_d,
                 reads=[self.b_stage[1]], writes=[self.b_stage[1]])
        self.gop("tensor_copy", self.bigB[0:64, 11776:12800], self.stage[0:64, 1024:2048],
                 reads=[self.b_stage[1]], writes=w_bufs)

    def wq_sb(self, i, c):
        off = 10240 + i * 512 + c * 64
        return self.bigB[:, off:off + 64]

    def wo_sb(self, half):
        return self.bigB[0:64, 11776 + half * 512:11776 + (half + 1) * 512]

    def phase_A_pass1(self, b):
        for t in range(TT):
            xt = self.xn_tok[:, t, :]
            bx = self.b_xn[t]
            self.dma(xt, self.x[b, t * 128:(t + 1) * 128, :], reads=[bx], writes=[bx])
            self._stats_collect(t, xt, bx)
        self._stats_finalize()

    def phase_A_pass2(self, b):
        QT = self.bigB[0:65, 0:2048]
        KT = self.bigB[0:65, 2048:4096]
        VT = self.bigB[:, 4096:5136].rearrange("p (t k) -> p t k", k=65)
        bQ, bK = self.bB(0, 2048), self.bB(2048, 4096)
        bV = self.bB(4096, 5456)
        plB = self.pL.bitcast(BF16)
        self.vop("memset", self.bigB[64:128, 0:2048], 0.0, reads=bQ, writes=bQ)
        self.vop("memset", self.bigB[64:128, 2048:4096], 0.0, reads=bK, writes=bK)
        self.vop("memset", KT[64:65, :], 1.0, reads=bK, writes=bK)
        self.vop("memset", VT[:, :, 64:65], 1.0, reads=bV, writes=bV)
        self.vop("memset", self.bigB[:, 5136:5200], 0.0,
                 reads=self.bB(5136, 5200), writes=self.bB(5136, 5200))
        pend = None
        for t in range(TT + 1):
            if t < TT:
                pp = t % 2
                self.vop("tensor_scalar", self.xn_tok[:, t, :], self.xn_tok[:, t, :],
                         self.statA[:, t, 2:3], self.statA[:, t, 3:4],
                         ALU.mult, ALU.subtract,
                         reads=[self.b_stsum], writes=[self.b_xn[t]])
                xc = self.hg[:, pp * 1024:(pp + 1) * 1024]
                self.vop("tensor_copy", xc, self.xn_tok[:, t, :],
                         reads=[self.b_xn[t]], writes=[self.b_hg[pp]])
                pt = (self.pT1b, self.pT2b)[pp]
                self.trans([(pt[:, c * 128:(c + 1) * 128], xc[:, c * 128:(c + 1) * 128])
                            for c in range(NC)],
                           reads=(self.b_hg[pp],), writes=[self.b_pT[pp]])
                if "xn" in self.dmp:
                    self.dma(self.dmp["xn"][b, t * 128:(t + 1) * 128, :],
                             self.xn_tok[:, t, :], reads=[self.b_xn[t]], writes=[])
            if pend is not None:
                tp_ = pend
                pp_ = tp_ % 2
                pt_ = (self.pT1b, self.pT2b)[pp_]
                fm = self.hg[:, (2 + pp_) * 1024:(3 + pp_) * 1024]
                self.vop("tensor_copy", fm, pt_[:],
                         reads=[self.b_pT[pp_]], writes=[self.b_hg[2 + pp_]])
                ps = self.pM[:, pp_ * 512:(pp_ + 1) * 512]
                bps = self.b_pM[pp_]
                mms = []
                for i in range(3):
                    for c in range(NC):
                        mms.append((ps[0:64, i * 128:i * 128 + 128],
                                    self.wq_sb(i, c), fm[:, c * 128:(c + 1) * 128]))
                def fnmm(E, mms=mms):
                    last = None
                    for gi in range(3):
                        for ci in range(NC):
                            o, l, r = mms[gi * NC + ci]
                            last = E.matmul(o, l, r, start=(ci == 0), stop=(ci == NC - 1))
                    return last
                self.op("tensor", fnmm,
                        reads=[self.b_hg[2 + pp_]] + self.bB(10240, 12800), writes=[bps])
                self.act(QT[0:64, tp_ * 128:(tp_ + 1) * 128], ps[0:64, 0:128], AF.Identity,
                         scale=0.125, bias=self.bq8[:, 0:1],
                         reads=[bps, self.b_params], writes=bQ)
                self.act(KT[0:64, tp_ * 128:(tp_ + 1) * 128], ps[0:64, 128:256], AF.Identity,
                         bias=self.bkc[:],
                         reads=[bps, self.b_params], writes=bK)
                vt = self.bigB[0:64, 5200 + pp_ * 128:5200 + (pp_ + 1) * 128]
                self.act(vt, ps[0:64, 256:384], AF.Identity, bias=self.bvc[:],
                         reads=[bps, self.b_params], writes=bV)
                vslot = plB[:, pp_ * 2048:pp_ * 2048 + 64]
                self.trans([(vslot, vt)], reads=tuple(bV),
                           writes=[self.b_pL[2 * pp_]])
                self.vop("tensor_copy", VT[:, tp_, 0:64], vslot,
                         reads=[self.b_pL[2 * pp_]], writes=bV)
            pend = t if t < TT else None

    def _stats_collect(self, t, src_ap, bsrc):
        self.vop("reduce_sum", self.statA[:, t, 0:1], src_ap, axis=AX.X,
                 reads=[bsrc], writes=[self.b_stsum])
        self.act(self.pool[:, 2, :], src_ap, AF.Square,
                 accum_out=self.statA[:, t, 1:2],
                 reads=[bsrc], writes=[self.b_pool[2], self.b_stsq])

    def _stats_finalize(self):
        c = lambda i: self.statA[:, :, i:i + 1]
        bb = [self.b_stsum, self.b_stsq]
        self.vop("tensor_scalar", c(0), c(0), 1.0 / D, None, ALU.mult,
                 reads=bb, writes=[self.b_stsum])
        self.vop("tensor_scalar", c(1), c(1), 1.0 / D, None, ALU.mult,
                 reads=bb, writes=[self.b_stsq])
        self.vop("tensor_mul", c(2), c(0), c(0), reads=bb, writes=[self.b_stsum])
        self.vop("tensor_sub", c(1), c(1), c(2), reads=bb, writes=[self.b_stsq])
        self.act(c(3), c(1), AF.Ln, bias=self.epsc[:],
                 reads=bb + [self.b_ones], writes=[self.b_stsum])
        self.act(c(2), c(3), AF.Exp, scale=-0.5, reads=bb, writes=[self.b_stsum])
        self.vop("tensor_mul", c(3), c(0), c(2), reads=bb, writes=[self.b_stsum])

    def phase_B(self, b):
        QT = self.bigB[0:65, 0:2048]
        KT = self.bigB[0:65, 2048:4096]
        VT = self.bigB[:, 4096:5136].rearrange("p (t k) -> p t k", k=65)
        ZF = self.bigB[0:64, 6144:8192]
        QSQ = self.bigB[0:64, 8192:10240]
        bQ, bK = self.bB(0, 2048), self.bB(2048, 4096)
        bV, bZ, bQs = self.bB(4096, 5456), self.bB(6144, 8192), self.bB(8192, 10240)
        row = lambda s, lo, hi: self.poolb[0:1, s, lo:hi]
        for (tile, bt) in ((QT, bQ), (KT, bK)):
            self.gop("tensor_mul", QSQ[:], tile[0:64, :], tile[0:64, :],
                     reads=bt, writes=bQs)
            mms = [(self.pL[0:1, j * 512:(j + 1) * 512], self.onesb[0:64, 0:1],
                    QSQ[:, j * 512:(j + 1) * 512]) for j in range(4)]
            def fnmm(E, mms=mms):
                last = None
                for (o, l, r) in mms:
                    last = E.matmul(o, l, r, start=True, stop=True)
                return last
            self.op("tensor", fnmm, reads=bQs + [self.b_ones],
                    writes=self.pLh(0) + self.pLh(1))
            if tile is QT:
                self.act(row(1, 0, 2048), self.pL[0:1, 0:2048], AF.Sqrt,
                         reads=self.pLh(0) + self.pLh(1), writes=[self.b_pool[1]])
        self.vop("reduce_max", self.maxsc[0:1, 0:1], self.pL[0:1, 0:2048], axis=AX.X,
                 reads=self.pLh(0) + self.pLh(1), writes=[self.b_params])
        self.act(self.maxsc[0:1, 1:2], self.maxsc[0:1, 0:1], AF.Sqrt,
                 reads=[self.b_params], writes=[self.b_params])
        self.vop("tensor_scalar", row(0, 0, 2048), row(1, 0, 2048),
                 self.maxsc[0:1, 1:2], -1.0, ALU.mult, ALU.mult,
                 reads=[self.b_pool[1], self.b_params], writes=[self.b_pool[0]])
        self.dma(QT[64:65, :], row(0, 0, 2048), reads=[self.b_pool[0]], writes=bQ)
        for qb in range(2):
            kmax = (qb + 1) * 8
            zb = [self.b_pM[0], self.b_pM[1]]

            def emit_z(kc, kmax=kmax, zb=zb, bV=bV):
                pt = self.miscb[:, 4096 + (kc % 2) * 1024:4096 + (kc % 2 + 1) * 1024]
                vwide = self.bigB[:, 4096 + kc * 65:4096 + kc * 65 + 128]
                self.mmx([(self.pM[:, j * 512:(j + 1) * 512], vwide,
                           pt[:, j * 512:(j + 1) * 512],
                           kc == 0, kc == kmax - 1) for j in range(2)],
                         reads=bV + self.bB(4096, 5456) + [self.b_ptile[kc % 2]],
                         writes=zb)

            for kc in range(kmax):
                sp = self.pL[:, (kc % 2) * 1024:(kc % 2 + 1) * 1024]
                bsp = self.pLh(kc % 2)
                QT8 = self.bigB[0:128, 0:2048]
                KT8 = self.bigB[0:128, 2048:4096]
                self.mmx([(sp[:, j * 512:(j + 1) * 512],
                           KT8[:, kc * 128:(kc + 1) * 128],
                           QT8[:, qb * 1024 + j * 512:qb * 1024 + (j + 1) * 512],
                           True, True) for j in range(2)],
                         reads=bQ + bK, writes=bsp)
                pt = self.miscb[:, 4096 + (kc % 2) * 1024:4096 + (kc % 2 + 1) * 1024]
                bpt = self.b_ptile[kc % 2]
                d = kc * 128 - qb * 1024
                if d > 0:
                    self.gop("memset", pt[:, 0:d], 0.0, reads=[bpt], writes=[bpt])
                lo = max(d, 0)
                self.act(pt[:, lo:1024], sp[:, lo:1024], AF.Exp,
                         reads=bsp, writes=[bpt])
                if d >= 0:
                    self.vop("tensor_mul", pt[:, d:d + 128], pt[:, d:d + 128], self.ltmask[:],
                             reads=[bpt, self.b_lt], writes=[bpt])
                if kc > 0:
                    emit_z(kc - 1)
            emit_z(kmax - 1)
            self.vop("reciprocal", self.misc[64:65, 3072:4096],
                     self.pM[64:65, 0:1024], reads=zb, writes=[self.b_misc[3]])
            self.dma(self.misc[0:1, 3072:4096], self.misc[64:65, 3072:4096],
                     reads=[self.b_misc[3]], writes=[self.b_misc[3]])
            self.trans([(self.pT1[:, tq:tq + 1],
                         self.misc[0:1, 3072 + tq * 128:3072 + (tq + 1) * 128])
                        for tq in range(8)],
                       reads=(self.b_misc[3],), writes=[self.b_pT[0]], bf=False)
            self.vop("tensor_copy", self.reccol[:, qb * 8:(qb + 1) * 8],
                     self.pT1[:, 0:8], reads=[self.b_pT[0]], writes=[self.b_reccol])
            self.vop("tensor_copy", ZF[:, qb * 1024:(qb + 1) * 1024], self.pM[0:64, 0:1024],
                     reads=zb, writes=bZ)
        self.dma(self.pool[0:1, 2, :], self.bo_d.rearrange("(one d) -> one d", one=1),
                 reads=[self.b_pool[2]], writes=[self.b_pool[2]])
        self.vop("tensor_copy", self.poolb[0:1, 0, 0:1024], self.pool[0:1, 2, :],
                 reads=[self.b_pool[2]], writes=[self.b_pool[0]])
        self.vop("tensor_sub", self.poolb[0:1, 1, 0:1024], self.pool[0:1, 2, :],
                 self.poolb[0:1, 0, 0:1024],
                 reads=[self.b_pool[2], self.b_pool[0]], writes=[self.b_pool[1]])
        mms = []
        for j in range(2):
            for hl in range(2):
                src = self.poolb[0:1, hl, j * 512:(j + 1) * 512]
                mms.append((self.pL[:, j * 512:(j + 1) * 512],
                            self.onesb[0:1, 0:128], src))
        def fnb(E, mms=mms):
            last = None
            for i, (o, l, r) in enumerate(mms):
                last = E.matmul(o, l, r, start=(i % 2 == 0), stop=(i % 2 == 1))
            return last
        self.op("tensor", fnb,
                reads=[self.b_pool[0], self.b_pool[1], self.b_ones],
                writes=self.pLh(0))
        self.act(self.misc[:, 3072:4096], self.pL[:, 0:1024], AF.Copy,
                 reads=self.pLh(0), writes=[self.b_misc[3]])
        for t in range(TT):
            pp = t % 2
            stok = self.misc[:, pp * 1024:(pp + 1) * 1024]
            bst = self.b_misc[pp]
            for half in range(2):
                ps = self.pL[:, (t * 2 + half) % 2 * 1024:((t * 2 + half) % 2) * 1024 + 512]
                bps = self.b_pL[2 * ((t * 2 + half) % 2)]
                self.mm([(ps, ZF[:, t * 128:(t + 1) * 128], self.wo_sb(half))],
                        reads=bZ + self.bB(10240, 12800), writes=[bps])
                self.op("vector",
                        lambda E, o=stok[:, half * 512:(half + 1) * 512], i0=ps,
                               sc=self.reccol[:, t:t + 1],
                               i1=self.misc[:, 3072 + half * 512:3072 + (half + 1) * 512]:
                        E.scalar_tensor_tensor(o, i0, sc, i1, ALU.mult, ALU.add),
                        reads=[bps, self.b_reccol, self.b_misc[3]], writes=[bst])
            self.dma(self.s_spill[b, t * 128:(t + 1) * 128, :], stok,
                     reads=[bst, self.b_sspill], writes=[])
            if "s" in self.dmp:
                self.dma(self.dmp["s"][b, t * 128:(t + 1) * 128, :], stok,
                         reads=[bst], writes=[])

    def _attn_norms(self, QT, KT, QSQ, bQ, bK, bQs, b):
        W = QT.shape[1]
        nj = W // 512
        row = lambda s: self.poolb[0:1, s, 0:W]
        sq_eng = "vop" if b == 0 else "gop"
        for (tile, bt, dst_s) in ((QT, bQ, 1), (KT, bK, 2)):
            getattr(self, sq_eng)("tensor_mul", QSQ[:, 0:W],
                                  tile[0:64, :], tile[0:64, :],
                                  reads=bt, writes=bQs)
            def fnmm(E, QSQ=QSQ, nj=nj):
                last = None
                for j in range(nj):
                    last = E.matmul(self.pL[0:1, j * 512:(j + 1) * 512],
                                    self.onesb[0:64, 0:1],
                                    QSQ[:, j * 512:(j + 1) * 512],
                                    start=True, stop=True)
                return last
            self.op("tensor", fnmm, reads=tuple(bQs) + (self.b_ones,),
                    writes=self.pLh(0) + self.pLh(1))
            if dst_s == 1:
                self.act(row(1), self.pL[0:1, 0:W], AF.Sqrt,
                         reads=self.pLh(0) + self.pLh(1), writes=[self.b_pool[1]])
        self.vop("reduce_max", self.maxsc[0:1, 0:1], self.pL[0:1, 0:W], axis=AX.X,
                 reads=self.pLh(0) + self.pLh(1), writes=[self.b_params])
        self.act(self.maxsc[0:1, 1:2], self.maxsc[0:1, 0:1], AF.Sqrt,
                 reads=[self.b_params], writes=[self.b_params])
        self.vop("tensor_scalar", row(0), row(1),
                 self.maxsc[0:1, 1:2], -1.0, ALU.mult, ALU.mult,
                 reads=[self.b_pool[1], self.b_params], writes=[self.b_pool[0]])
        self.dma(QT[64:65, :], row(0), reads=[self.b_pool[0]], writes=bQ)

    def phase_C(self, b):
        for t in range(TT):
            self._stats_collect(t, self.xn_tok[:, t, :], self.b_xn[t])
        self._stats_finalize()
        pend = None
        for t in range(TT + 1):
            if t < TT:
                pp = t % 2
                xt = self.xn_tok[:, t, :]
                u = self.pool[:, pp, :]
                self.vop("tensor_scalar", u, xt,
                         self.statA[:, t, 2:3], self.statA[:, t, 3:4],
                         ALU.mult, ALU.subtract,
                         reads=[self.b_xn[t], self.b_stsum], writes=[self.b_pool[pp]])
                xc = self.hg[:, pp * 1024:(pp + 1) * 1024]
                eng = "vop" if b == 0 else "gop"
                getattr(self, eng)("tensor_copy", xc, u, reads=[self.b_pool[pp]],
                                   writes=[self.b_hg[pp]])
                pt = (self.pT1b, self.pT2b)[pp]
                self.trans([(pt[:, c * 128:(c + 1) * 128], xc[:, c * 128:(c + 1) * 128])
                            for c in range(NC)],
                           reads=(self.b_hg[pp],), writes=[self.b_pT[pp]])
            if pend is not None:
                tp_ = pend
                pp_ = tp_ % 2
                pt_ = (self.pT1b, self.pT2b)[pp_]
                for c in range(NC):
                    dst = self.ln_fm[:, c, tp_ * 128:(tp_ + 1) * 128]
                    src = pt_[:, c * 128:(c + 1) * 128]
                    if c % 2 == 0:
                        self.act(dst, src, AF.Identity,
                                 scale=self.gam1[:, c:c + 1],
                                 bias=self.bet1[:, c:c + 1],
                                 reads=[self.b_pT[pp_], self.b_params],
                                 writes=[self.b_lnfm[tp_]])
                    else:
                        self.vop("tensor_scalar", dst, src,
                                 self.gam1[:, c:c + 1], self.bet1[:, c:c + 1],
                                 ALU.mult, ALU.add,
                                 reads=[self.b_pT[pp_], self.b_params],
                                 writes=[self.b_lnfm[tp_]])
            pend = t if t < TT else None
        QTg = lambda i: self.bigB[0:65, i * 2048:(i + 1) * 2048]
        KTg = lambda i: self.bigB[0:65, 4096 + i * 2048:4096 + (i + 1) * 2048]
        VG = self.bigB[:, 8192:10272].rearrange("p (h t k) -> p h t k", h=2, k=65)
        VTW = self.bigB[:, 10272:10784]
        QSQ = self.bigB[0:64, 10784:12832]
        qkvsl = lambda i: self.bigB[:, 12832 + i * 1024:12832 + (i + 1) * 1024].rearrange(
            "p (c k) -> p c k", k=128)
        bQg = lambda i: self.bB(i * 2048, (i + 1) * 2048)
        bKg = lambda i: self.bB(4096 + i * 2048, 4096 + (i + 1) * 2048)
        bVG = self.bB(8192, 10272)
        bVTW = self.bB(10272, 10784)
        bQs = self.bB(10784, 12832)
        bSL = self.bB(12832, 15904)
        self.vop("memset", self.bigB[64:128, 0:4096], 0.0,
                 reads=bQg(0) + bQg(1), writes=bQg(0) + bQg(1))
        self.vop("memset", self.bigB[64:128, 4096:8192], 0.0,
                 reads=bKg(0) + bKg(1), writes=bKg(0) + bKg(1))
        for i in range(2):
            self.vop("memset", KTg(i)[64:65, :], 1.0, reads=bKg(i), writes=bKg(i))
            self.vop("memset", VG[:, i, :, 64:65], 1.0, reads=bVG, writes=bVG)
        wqkv_r = self.wqkv_bf.rearrange("(c p) n -> p c n", p=128)
        lnfm_all = list(self.b_lnfm)
        for g2 in range(8):
            for i in range(3):
                cs = i * 1024 + g2 * 128
                self.dma(qkvsl(i)[:], wqkv_r[:, :, cs:cs + 128],
                         reads=bSL, writes=bSL + [self.b_wbf])
            for i in (0, 1, "norms", 2):
                if i == "norms":
                    for h01 in range(2):
                        self._attn_norms(QTg(h01), KTg(h01), QSQ,
                                         bQg(h01), bKg(h01), bQs, b)
                    continue
                for cg in range(4):
                    pp = (i * 4 + cg) % 2
                    ps = (self.pT1, self.pT2)[pp]
                    bps = self.b_pT[pp]
                    self.mm([(ps[:], qkvsl(i)[:, c, :],
                              self.ln_fm[:, c, cg * 512:(cg + 1) * 512])
                             for c in range(NC)],
                            reads=bSL + lnfm_all, writes=[bps])
                    if i == 0:
                        for h01 in range(2):
                            self.act(QTg(h01)[0:64, cg * 512:(cg + 1) * 512],
                                     ps[h01 * 64:(h01 + 1) * 64, :], AF.Copy, scale=0.125,
                                     reads=[bps], writes=bQg(h01))
                    elif i == 1:
                        for h01 in range(2):
                            self.vop("tensor_copy",
                                     KTg(h01)[0:64, cg * 512:(cg + 1) * 512],
                                     ps[h01 * 64:(h01 + 1) * 64, :],
                                     reads=[bps], writes=bKg(h01))
                    else:
                        self.vop("tensor_copy", VTW[:], ps[:],
                                 reads=[bps], writes=bVTW)
                        for h01 in range(2):
                            for t4 in range(4):
                                tg = cg * 4 + t4
                                po = (self.pT1b, self.pT2b)[1 - pp]
                                self.trans([(po[:, 960:1024],
                                             VTW[h01 * 64:(h01 + 1) * 64,
                                                 t4 * 128:(t4 + 1) * 128],
                                             h01 * 64)],
                                           reads=tuple(bVTW), writes=[self.b_pT[1 - pp]])
                                self.vop("tensor_copy", VG[:, h01, tg, 0:64],
                                         po[:, 960:1024],
                                         reads=[self.b_pT[1 - pp]], writes=bVG)
            pending = [None]

            def make_evac(h01, qb, par, ztile, zb, g2=g2):
                def evA():
                    rsl = (h01 + qb) % 2
                    lnrow = self.misc[64:65, rsl * 1024:(rsl + 1) * 1024]
                    if par == 0:
                        self.vop("reciprocal", lnrow, self.pM[64:65, 0:1024],
                                 reads=zb, writes=[self.b_misc[rsl]])
                    else:
                        self.vop("reciprocal", lnrow[:, 0:512],
                                 self.pT1[64:65, 0:512],
                                 reads=zb, writes=[self.b_misc[rsl]])
                        self.vop("reciprocal", lnrow[:, 512:1024],
                                 self.pT2[64:65, 0:512],
                                 reads=zb, writes=[self.b_misc[rsl]])
                    recrow = self.poolb[64:65, 2, rsl * 1024:(rsl + 1) * 1024]
                    self.vop("tensor_copy", recrow, lnrow,
                             reads=[self.b_misc[rsl]],
                             writes=[self.b_pool[2]])

                def ev():
                    rsl = (h01 + qb) % 2
                    recrow = self.poolb[64:65, 2, rsl * 1024:(rsl + 1) * 1024]
                    def fnb(E, recrow=recrow, par=par):
                        last = None
                        for j in range(2):
                            odst = self.pL[0:64, par * 1024 + j * 512:
                                           par * 1024 + (j + 1) * 512]
                            last = E.matmul(
                                odst, self.onesb[64:65, 0:64],
                                recrow[:, j * 512:(j + 1) * 512],
                                start=True, stop=True)
                        return last
                    self.op("tensor", fnb,
                            reads=[self.b_pool[2], self.b_ones],
                            writes=self.pLh(par))
                    rb = self.misc[0:64, rsl * 1024:(rsl + 1) * 1024]
                    self.vop("tensor_copy", rb,
                             self.pL[0:64, par * 1024:par * 1024 + 1024],
                             reads=self.pLh(par), writes=[self.b_misc[rsl]])
                    ydst = self.bigA[h01 * 64:(h01 + 1) * 64,
                                     g2 * 2048 + qb * 1024:g2 * 2048 + (qb + 1) * 1024]
                    if par == 0:
                        self.vop("tensor_mul", ydst, self.pM[0:64, 0:1024], rb,
                                 reads=zb + [self.b_misc[rsl]],
                                 writes=[self.b_bigA[g2]])
                    else:
                        self.vop("tensor_mul", ydst[:, 0:512], self.pT1[0:64, 0:512],
                                 rb[:, 0:512],
                                 reads=zb + [self.b_misc[rsl]],
                                 writes=[self.b_bigA[g2]])
                        self.vop("tensor_mul", ydst[:, 512:1024], self.pT2[0:64, 0:512],
                                 rb[:, 512:1024],
                                 reads=zb + [self.b_misc[rsl]],
                                 writes=[self.b_bigA[g2]])
                return evA, ev

            for li, (h01, qb) in enumerate(((0, 0), (0, 1), (1, 0), (1, 1))):
                QT = self.bigB[0:128, h01 * 2048:(h01 + 1) * 2048]
                KT = self.bigB[0:128, 4096 + h01 * 2048:4096 + (h01 + 1) * 2048]
                bQ, bK = bQg(h01), bKg(h01)
                par = li % 2
                if par == 0:
                    ztile = lambda j: self.pM[:, j * 512:(j + 1) * 512]
                    zb = [self.b_pM[0], self.b_pM[1]]
                else:
                    ztile = lambda j: (self.pT1, self.pT2)[j][:, 0:512]
                    zb = [self.b_pT[0], self.b_pT[1]]

                def emit_z(kc, zb=zb, h01=h01, ztile=ztile):
                    pt = self.miscb[:, 4096 + (kc % 3) * 1024:4096 + (kc % 3 + 1) * 1024]
                    vwide = self.bigB[:, 8192 + h01 * 1040 + kc * 65:
                                      8192 + h01 * 1040 + kc * 65 + 128]
                    self.mmx([(ztile(j),
                               vwide, pt[:, j * 512:(j + 1) * 512],
                               kc == 0, kc == 15) for j in range(2)],
                             reads=bVG + self.bB(8192, 10784) + [self.b_ptile[kc % 3]],
                             writes=zb)

                for kc in range(16):
                    sp = self.pL[:, (kc % 2) * 1024:(kc % 2 + 1) * 1024]
                    bsp = self.pLh(kc % 2)
                    self.mmx([(sp[:, j * 512:(j + 1) * 512],
                               KT[:, kc * 128:(kc + 1) * 128],
                               QT[:, qb * 1024 + j * 512:qb * 1024 + (j + 1) * 512],
                               True, True) for j in range(2)],
                             reads=bQ + bK, writes=bsp)
                    pt = self.miscb[:, 4096 + (kc % 3) * 1024:4096 + (kc % 3 + 1) * 1024]
                    bpt = self.b_ptile[kc % 3]
                    wpt = [bpt, self.b_misc[3]] if kc % 3 == 2 else [bpt]
                    self.act(pt[:], sp[:], AF.Exp, reads=bsp, writes=wpt)
                    if kc > 1:
                        emit_z(kc - 2)
                    if kc == 1 and pending[0] is not None:
                        pending[0][0]()
                    if kc == 3 and pending[0] is not None:
                        pending[0][1]()
                        pending[0] = None
                emit_z(14)
                emit_z(15)
                ev = make_evac(h01, qb, par, ztile, zb)
                if li < 3:
                    pending[0] = ev
                else:
                    ev[0]()
                    ev[1]()
        wps = self.bigB[:, 0:8192].rearrange("p (c n) -> p c n", n=1024)
        bWP = self.bB(0, 8192)
        self.dma(wps[:], self.wproj_bf.rearrange("(c p) n -> p c n", p=128),
                 reads=bWP, writes=bWP + [self.b_wbf])
        yall = list(self.b_bigA)
        for t in range(TT):
            for half in range(2):
                pp = (t * 2 + half) % 2
                ps = (self.pT1, self.pT2)[pp]
                bps = self.b_pT[pp]
                self.mm([(ps[:], self.bigA[:, c * 2048 + t * 128:c * 2048 + (t + 1) * 128],
                          wps[:, c, half * 512:(half + 1) * 512])
                         for c in range(NC)],
                        reads=yall + bWP, writes=[bps])
                self.vop("tensor_add", self.xn_tok[:, t, half * 512:(half + 1) * 512],
                         self.xn_tok[:, t, half * 512:(half + 1) * 512], ps[:],
                         reads=[bps, self.b_xn[t]], writes=[self.b_xn[t]])
            if "xn2" in self.dmp:
                self.dma(self.dmp["xn2"][b, t * 128:(t + 1) * 128, :],
                         self.xn_tok[:, t, :], reads=[self.b_xn[t]], writes=[])

    def phase_D(self, b):
        for t in range(TT):
            self._stats_collect(t, self.xn_tok[:, t, :], self.b_xn[t])
        self._stats_finalize()
        for t in range(TT):
            pp = t % 2
            xt = self.xn_tok[:, t, :]
            u = self.pool[:, pp, :]
            self.vop("tensor_scalar", u, xt,
                     self.statA[:, t, 2:3], self.statA[:, t, 3:4],
                     ALU.mult, ALU.subtract,
                     reads=[self.b_xn[t], self.b_stsum], writes=[self.b_pool[pp]])
            uhi = self.hg[:, 0:1024]
            ulo = self.hg[:, 1024:2048]
            self.gop("tensor_copy", uhi, u, reads=[self.b_pool[pp]], writes=[self.b_hg[0]])
            self.gop("tensor_sub", ulo, u, uhi,
                     reads=[self.b_pool[pp], self.b_hg[0]], writes=[self.b_hg[1]])
            self.trans([(self.pT1b[:, c * 128:(c + 1) * 128], uhi[:, c * 128:(c + 1) * 128])
                        for c in range(NC)],
                       reads=(self.b_hg[0],), writes=[self.b_pT[0]])
            self.trans([(self.pT2b[:, c * 128:(c + 1) * 128], ulo[:, c * 128:(c + 1) * 128])
                        for c in range(NC)],
                       reads=(self.b_hg[1],), writes=[self.b_pT[1]])
            for c in range(NC):
                dst = self.ln_fm[:, c, t * 128:(t + 1) * 128]
                src = self.pT1b[:, c * 128:(c + 1) * 128]
                if c % 2 == 0:
                    self.act(dst, src, AF.Identity,
                             scale=self.gam2[:, c:c + 1], bias=self.bet2[:, c:c + 1],
                             reads=[self.b_pT[0], self.b_params],
                             writes=[self.b_lnfm[t]])
                else:
                    self.vop("tensor_scalar", dst, src,
                             self.gam2[:, c:c + 1], self.bet2[:, c:c + 1],
                             ALU.mult, ALU.add,
                             reads=[self.b_pT[0], self.b_params],
                             writes=[self.b_lnfm[t]])
            uhf = self.hg[:, 2048:3072]
            ulf = self.hg[:, 3072:4096]
            self.vop("tensor_copy", uhf, self.pT1b[:, 0:1024],
                     reads=[self.b_pT[0]], writes=[self.b_hg[2]])
            self.act(ulf, self.pT2b[:, 0:1024], AF.Copy,
                     reads=[self.b_pT[1]], writes=[self.b_hg[3]])
            X = (t % 2) * 1024 + ((t // 2) % 8) * 128
            lg = self.pL[0:NE, X:X + 128]
            blg = self.b_pL[X // 512]
            mms = []
            for c in range(NC):
                mms.append((lg, self.gwg_hi[:, c, :], uhf[:, c * 128:(c + 1) * 128]))
                mms.append((lg, self.gwg_lo[:, c, :], uhf[:, c * 128:(c + 1) * 128]))
                mms.append((lg, self.gwg_hi[:, c, :], ulf[:, c * 128:(c + 1) * 128]))
            self.mm(mms, reads=[self.b_hg[2], self.b_hg[3], self.b_gw], writes=[blg])
            lgt_sl = self.stage[0:NE, 0:128]
            self.act(lgt_sl, lg, AF.Identity, bias=self.gbadd[:],
                     reads=[blg, self.b_params], writes=[self.b_stage[0]])
            po = (self.pT1, self.pT2)[t % 2]
            self.trans([(po[:, 0:NE], lgt_sl)],
                       reads=(self.b_stage[0],), writes=[self.b_pT[t % 2]], bf=False)
            self.vop("tensor_copy", self.lgt_tok[:, t, :], po[:, 0:NE],
                     reads=[self.b_pT[t % 2]], writes=[self.b_masks])
            self.vop("reduce_max", self.statA[:, t, 0:1], self.lgt_tok[:, t, :], axis=AX.X,
                     reads=[self.b_masks], writes=[self.b_stsq])
            self.vop("tensor_scalar", self.masks[:, b, t, :], self.lgt_tok[:, t, :],
                     self.statA[:, t, 0:1], None, ALU.is_ge,
                     reads=[self.b_masks, self.b_stsq], writes=[self.b_masks])
            self.trans([(self.pT1b[:, c * 128:(c + 1) * 128],
                         self.ln_fm[:, c, t * 128:(t + 1) * 128])
                        for c in range(NC)],
                       reads=(self.b_lnfm[t],), writes=[self.b_pT[0]])
            us = self.stageb[:, 2048 + (t % 2) * 1024:3072 + (t % 2) * 1024]
            self.vop("tensor_copy", us, self.pT1b[:, 0:1024],
                     reads=[self.b_pT[0], self.b_ust[t % 2]],
                     writes=[self.b_ust[t % 2], self.b_stage[1]])
            self.dma(self.u_buf[b * T + t * 128:b * T + (t + 1) * 128, :], us,
                     reads=[self.b_ust[t % 2], self.b_stage[1], self.b_ubuf],
                     writes=[])
            if "lgt" in self.dmp:
                self.dma(self.dmp["lgt"][b, t * 128:(t + 1) * 128, 0:NE],
                         self.lgt_tok[:, t, :], reads=[self.b_masks], writes=[])
                self.dma(self.dmp["lgt"][b, t * 128:(t + 1) * 128, NE:2 * NE],
                         self.masks[:, b, t, :], reads=[self.b_masks], writes=[])

    def index_build(self):
        bi = self.b_idx
        ic = self.b_icons
        NT = NB * TT
        allm = [self.b_misc[0], self.b_misc[1], self.b_misc[2], self.b_misc[3]]
        self.dma(self.ltf, self.lt_dram.ap(), reads=[], writes=[bi] + allm)
        self.dma(self.thr16, self.thr16_dram.ap(), reads=[], writes=[bi])
        self.dma(self.thrp, self.thrp_dram.ap(), reads=[], writes=[bi])
        self.dma(self.evalsf, self.evals_dram.ap(), reads=[], writes=[bi])
        self.gop("iota", self.tid32, [[128, NB * TT]], base=0,
                 channel_multiplier=1, reads=[], writes=[bi])
        self.gop("iota", self.pcol, [[0, 1]], base=0, channel_multiplier=1,
                 allow_small_or_imprecise_dtypes=True, reads=[], writes=[bi])
        ntv = self.misc[:, 640:640 + NB * TT].rearrange("p (a b) -> p a b", b=TT)
        self.vop("memset", ntv, 1.0, reads=[bi], writes=[bi])
        for e in range(NE):
            self.vop("tensor_mul", self.masks[:, :, :, e], self.masks[:, :, :, e],
                     ntv, reads=[bi, self.b_masks], writes=[self.b_masks])
            if e < NE - 1:
                self.vop("tensor_sub", ntv, ntv, self.masks[:, :, :, e],
                         reads=[self.b_masks, bi], writes=[bi])
        mms = [(self.pT1[0:1, bt * NE:(bt + 1) * NE], self.onescol,
                self.masks[:, bt // TT, bt % TT, :], True, True)
               for bt in range(NT)]
        self.mmx(mms, reads=[self.b_masks, bi], writes=[self.b_pT[0]])
        self.vop("tensor_copy", self.csum, self.pT1[0:1, 0:NT * NE],
                 reads=[self.b_pT[0]], writes=[bi])
        self.dma(self.iscr.rearrange("(one n) -> one n", one=1), self.csum,
                 reads=[bi], writes=[bi])
        self.dma(self.csum32, self.iscr.rearrange("(a b) -> a b", b=NE),
                 reads=[bi], writes=[bi])
        self.mm([(self.pT2[0:NT, 0:NE], self.ltf[0:NT, 0:NT], self.csum32)],
                reads=[bi, ic], writes=[self.b_pT[1]])
        self.vop("tensor_sub", self.carry32, self.pT2[0:NT, 0:NE], self.csum32,
                 reads=[self.b_pT[1], bi], writes=[bi])
        self.mm([(self.pT2[0:1, 128:128 + NE], self.ltf[0:NT, 127:128],
                  self.csum32)],
                reads=[bi], writes=[self.b_pT[1]])
        self.vop("tensor_copy", self.srow[0:1, 0:NE], self.pT2[0:1, 128:128 + NE],
                 reads=[self.b_pT[1]], writes=[bi])
        self.dma(self.iscr.rearrange("(a b) -> a b", b=NE), self.carry32,
                 reads=[bi], writes=[bi])
        self.dma(self.csum, self.iscr.rearrange("(one n) -> one n", one=1),
                 reads=[bi], writes=[bi])
        self.trans([(self.pT1[0:NE, 0:1], self.srow[0:1, 0:NE])], reads=(bi,),
                   writes=[self.b_pT[0]], bf=False)
        self.vop("tensor_copy", self.icol[:, 0:1], self.pT1[0:NE, 0:1],
                 reads=[self.b_pT[0]], writes=[bi])
        self.vop("tensor_scalar", self.irow[:, 0:16], self.thr16, self.icol[:, 0:1],
                 None, ALU.is_lt, reads=[bi, ic], writes=[bi])
        self.vop("reduce_sum", self.icol[:, 1:2], self.irow[:, 0:16], axis=AX.X,
                 reads=[bi], writes=[bi])
        self.vop("tensor_scalar", self.icol[:, 1:2], self.icol[:, 1:2],
                 float(PASS_TOK), None, ALU.mult, reads=[bi], writes=[bi])
        self.mm([(self.pT1[0:NE, 4:5], self.ltf[0:NE, 0:NE], self.icol[:, 1:2])],
                reads=[bi, ic], writes=[self.b_pT[0]])
        self.vop("tensor_sub", self.icol[:, 2:3], self.pT1[0:NE, 4:5], self.icol[:, 1:2],
                 reads=[self.b_pT[0], bi], writes=[bi])
        self.vop("tensor_add", self.icol[:, 3:4], self.icol[:, 2:3], self.icol[:, 1:2],
                 reads=[bi], writes=[bi])
        self.trans([(self.pT2[0:1, 8:16], self.icol[:, 2:3])], reads=(bi,),
                   writes=[self.b_pT[1]], bf=False)
        self.vop("tensor_copy", self.soffr, self.pT2[0:1, 8:16],
                 reads=[self.b_pT[1]], writes=[bi])
        self.vop("tensor_scalar", self.irow[:, 16:16 + NPASS], self.thrp,
                 self.icol[:, 2:3], None, ALU.is_ge, reads=[bi, ic], writes=[bi])
        self.vop("tensor_scalar", self.irow[:, 40:40 + NPASS], self.thrp,
                 self.icol[:, 3:4], None, ALU.is_lt, reads=[bi, ic], writes=[bi])
        self.vop("tensor_mul", self.irow[:, 16:16 + NPASS], self.irow[:, 16:16 + NPASS],
                 self.irow[:, 40:40 + NPASS], reads=[bi], writes=[bi])
        self.mm([(self.pT1[0:1, 16:16 + NPASS], self.evalsf,
                  self.irow[:, 16:16 + NPASS])],
                reads=[bi, ic], writes=[self.b_pT[0]])
        self.vop("tensor_scalar", self.srow[0:1, 8:8 + NPASS],
                 self.pT1[0:1, 16:16 + NPASS], 512.0, None, ALU.mult,
                 reads=[self.b_pT[0]], writes=[bi])
        self.mm([(self.pM[0:128, 0:NPASS], self.onesrow,
                  self.srow[0:1, 8:8 + NPASS])],
                reads=[bi, ic], writes=[self.b_pM[0]])
        self.vop("tensor_scalar", self.idxf[:, 0:NPASS], self.pM[0:128, 0:NPASS], self.pcol,
                 None, ALU.add, reads=[self.b_pM[0], ic], writes=[bi])
        self.vop("tensor_copy", self.idxs[:, 0:NPASS], self.idxf[:, 0:NPASS],
                 reads=[bi], writes=[self.b_widx])
        self._e_wload(0, 0)
        self._e_wload(0, 1)
        b_it = [Buf(), Buf()]
        b_ip = [self.b_pM[1], self.b_pT[0]]
        for bt in range(NT):
            sl = bt % 2
            m_bt = self.masks[:, bt // TT, bt % TT, :]
            ps = (self.pM[:, 512:512 + NE], self.pT1[:, 96:96 + NE])[sl]
            mms = [(ps, self.ltf, m_bt, True, False),
                   (ps, self.onesrow,
                    self.csum[0:1, bt * NE:(bt + 1) * NE], False, False),
                   (ps, self.onesrow, self.soffr, False, True)]
            self.mmx(mms, reads=[self.b_masks, bi, ic], writes=[b_ip[sl]])
            scr = self.idxf[:, sl * NE:(sl + 1) * NE]
            self.vop("tensor_sub", scr, ps, m_bt,
                     reads=[b_ip[sl], self.b_masks, self.b_widx, b_it[sl]],
                     writes=[b_it[sl]])
            self.vop("tensor_mul", scr, scr, m_bt,
                     reads=[b_it[sl], self.b_masks], writes=[b_it[sl]])
            self.vop("reduce_sum", self.posf[:, sl:sl + 1], scr, axis=AX.X,
                     reads=[b_it[sl]], writes=[b_it[sl]])
            self.vop("tensor_copy", self.idxs[:, 16 + sl:17 + sl],
                     self.posf[:, sl:sl + 1], reads=[b_it[sl]], writes=[b_it[sl]])
            for dst, bdst in ((self.gidx_buf.rearrange("(a b) -> a b", b=1),
                               self.b_gidxb),
                              (self.sidx_buf.rearrange("(a b) -> a b", b=1),
                               self.b_sidxb)):
                def fn(E, o=dst, ix=self.idxs[:, 16 + sl:17 + sl],
                       i=self.tid32[:, bt:bt + 1]):
                    return E.indirect_dma_start(
                        out=o, out_offset=IOA(ap=ix, axis=0), in_=i, in_offset=None)
                self.S.gma(fn, reads=[b_it[sl], ic, bdst], writes=[])
        self.vop("memset", self.misc[0:1, 0:1], 0.0,
                 reads=[], writes=[bi, b_it[0], b_it[1]] + allm)

    def _utok(self):
        return self.ln_fm[:, 0:2, :].rearrange("p a n -> p (a n)")

    def _ufm(self):
        return self.ln_fm[:, 2:4, :].rearrange("p a (b n) -> p (a b) n", n=PASS_TOK)

    def _yacc(self):
        return self.lnfm_f32[:, 4:8, :]

    def _gsl(self, s):
        return self.idxs[:, 18 + 4 * s:22 + 4 * s]

    def _ssl(self, s):
        return self.idxs[:, 26 + 4 * s:30 + 4 * s]

    def _e_gload(self, ps):
        sl = ps % 2
        c0 = ps * PASS_TOK
        for c in range(4):
            if not self._gidx_claimed:
                self._gidx_claimed = True
                wg, rg = [self.b_gidxb], []
                ws, rs_ = [self.b_sidxb], []
            else:
                wg, rg = [], [self.b_gidxb]
                ws, rs_ = [], [self.b_sidxb]
            self.dma(self._gsl(sl)[:, c:c + 1],
                     self.gidx_buf[c0 + c * 128:c0 + (c + 1) * 128].rearrange(
                         "(p one) -> p one", one=1),
                     reads=[self.b_gsb[sl]] + rg, writes=[self.b_gsb[sl]] + wg)
            self.dma(self._ssl(sl)[:, c:c + 1],
                     self.sidx_buf[c0 + c * 128:c0 + (c + 1) * 128].rearrange(
                         "(p one) -> p one", one=1),
                     reads=[self.b_gsb[sl]] + rs_, writes=[self.b_gsb[sl]] + ws)

    def _e_ugather(self, ps):
        sl = ps % 2
        ut = self._utok()
        for c in range(4):
            dst = ut[:, c * 1024:(c + 1) * 1024]
            def fn(E, o=dst, ix=self._gsl(sl)[:, c:c + 1]):
                return E.indirect_dma_start(out=o, out_offset=None, in_=self.u_buf,
                                            in_offset=IOA(ap=ix, axis=0))
            rd = [self.b_gsb[sl], self.b_utok[0]]
            if ps < 1:
                rd += list(self.b_lnfm)
            if not self._ubuf_claimed:
                self._ubuf_claimed = True
                self.S.gma(fn, reads=rd, writes=[self.b_utok[0], self.b_ubuf])
            else:
                self.S.gma(fn, reads=rd + [self.b_ubuf],
                           writes=[self.b_utok[0]])

    def _e_wload(self, ps, hb):
        wsl = hb % 2
        eo = hb * 128 * 8192
        for (big, bufs, src) in ((self.bigA, self.bA, self.w1bf),
                                 (self.bigB, self.bB, self.w2bf)):
            for h2 in range(2):
                dst = big[:, wsl * 8192 + h2 * 4096:wsl * 8192 + (h2 + 1) * 4096]
                def fn(E, o=dst, s=src, ix=self.idxs[:, ps:ps + 1],
                       eo=eo + h2 * 4096):
                    return E.indirect_dma_start(out=o, out_offset=None, in_=s,
                                                in_offset=IOA(ap=ix, axis=0),
                                                element_offset=eo)
                bl = bufs(wsl * 8192, (wsl + 1) * 8192)
                if not self._wcvt_claimed:
                    self._wcvt_claimed = True
                    self.S.gma(fn, reads=[self.b_widx] + bl,
                               writes=bl + [self.b_wcvt])
                else:
                    self.S.gma(fn, reads=[self.b_widx, self.b_wcvt] + bl,
                               writes=bl)

    def phase_E_sparse(self):
        W1q = lambda s: self.bigA[:, s * 8192:(s + 1) * 8192].rearrange(
            "p (c n) -> p c n", n=1024)
        W2q = lambda s: self.bigB[:, s * 8192:(s + 1) * 8192].rearrange(
            "p (c n) -> p c n", n=1024)
        bW1 = lambda s: self.bA(s * 8192, (s + 1) * 8192)
        bW2 = lambda s: self.bB(s * 8192, (s + 1) * 8192)
        hgv = self.hg.rearrange("p (c n) -> p c n", n=PASS_TOK)
        ya = self._yacc()
        self._e_gload(0)
        self._e_ugather(0)
        for ps in range(NPASS):
            sl = ps % 2
            if ps + 1 < NPASS:
                self._e_gload(ps + 1)
            ut = self._utok()
            fence = list(self.b_lnfm) if ps < 1 else []
            uf = self._ufm()
            trs0, trs1, trs2 = [], [], []
            for c2 in range(4):
                for fc in range(8):
                    src_t = ut[:, c2 * 1024 + fc * 128:c2 * 1024 + (fc + 1) * 128]
                    if c2 < 2:
                        trs0.append((self.pMb[:, (c2 * 8 + fc) * 128:
                                              (c2 * 8 + fc + 1) * 128], src_t))
                    elif c2 == 2:
                        trs1.append((self.pT1b[:, fc * 128:(fc + 1) * 128], src_t))
                    else:
                        trs2.append((self.pT2b[:, fc * 128:(fc + 1) * 128], src_t))
            self.trans(trs0, reads=(self.b_utok[0],),
                       writes=[self.b_pM[0], self.b_pM[1]])
            self.trans(trs1, reads=(self.b_utok[0],), writes=[self.b_pT[0]])
            self.trans(trs2, reads=(self.b_utok[0],), writes=[self.b_pT[1]])
            self.vop("tensor_copy", uf[:, :, 0:128],
                     self.pMb[:, 0:1024].rearrange("p (c n) -> p c n", n=128),
                     reads=[self.b_pM[0], self.b_pM[1], self.b_ufm[0]] + fence,
                     writes=[self.b_ufm[0]])
            self.vop("tensor_copy", uf[:, :, 128:256],
                     self.pMb[:, 1024:2048].rearrange("p (c n) -> p c n", n=128),
                     reads=[self.b_pM[0], self.b_pM[1], self.b_ufm[0]],
                     writes=[self.b_ufm[0]])
            self.vop("tensor_copy", uf[:, :, 256:384],
                     self.pT1b[:, 0:1024].rearrange("p (c n) -> p c n", n=128),
                     reads=[self.b_pT[0], self.b_ufm[0]], writes=[self.b_ufm[0]])
            self.vop("tensor_copy", uf[:, :, 384:512],
                     self.pT2b[:, 0:1024].rearrange("p (c n) -> p c n", n=128),
                     reads=[self.b_pT[1], self.b_ufm[0]], writes=[self.b_ufm[0]])
            for hb in range(4):
                wsl = hb % 2
                nps, nhb = (ps, hb + 1) if hb < 3 else (ps + 1, 0)
                if nps < NPASS and not (nps == 0 and nhb <= 1):
                    self._e_wload(nps, nhb)
                if hb == 2 and ps + 1 < NPASS:
                    self._e_ugather(ps + 1)
                bh = list(self.b_hg)
                for rnd in range(4):
                    pr = (rnd % 2) * 1024
                    mms = []
                    for hcq in range(2):
                        hc = rnd * 2 + hcq
                        for fc in range(NC):
                            mms.append((self.pL[:, pr + hcq * 512:
                                                pr + (hcq + 1) * 512],
                                        W1q(wsl)[:, fc, hc * 128:(hc + 1) * 128],
                                        uf[:, fc, :], fc == 0, fc == NC - 1))
                    self.mmx(mms, reads=bW1(wsl) + [self.b_ufm[0]],
                             writes=self.pLh(rnd % 2))
                    self.act(self.hg[:, rnd * 1024:(rnd + 1) * 1024],
                             self.pL[:, pr:pr + 1024],
                             AF.Gelu_apprx_tanh, reads=self.pLh(rnd % 2),
                             writes=[self.b_hg[rnd]])
                for half in range(2):
                    mms = []
                    for t2 in range(4):
                        if half == 0:
                            out = (self.pM[:, 0:512], self.pM[:, 512:1024],
                                   self.pT1[:, 0:512], self.pT2[:, 0:512])[t2]
                        else:
                            out = self.pL[:, t2 * 512:(t2 + 1) * 512]
                        for hc in range(NC):
                            mms.append((out, hgv[:, hc, t2 * 128:(t2 + 1) * 128],
                                        W2q(wsl)[:, hc, half * 512:(half + 1) * 512],
                                        hc == 0, hc == NC - 1))
                    wr = ([self.b_pM[0], self.b_pM[1], self.b_pT[0], self.b_pT[1]]
                          if half == 0 else self.pLh(0) + self.pLh(1))
                    self.mmx(mms, reads=bW2(wsl) + bh, writes=wr)
                    op = "tensor_copy" if hb == 0 else "tensor_add"
                    for t2 in range(4):
                        if half == 0:
                            pin = (self.pM[:, 0:512], self.pM[:, 512:1024],
                                   self.pT1[:, 0:512], self.pT2[:, 0:512])[t2]
                            bp = (self.b_pM[0], self.b_pM[1],
                                  self.b_pT[0], self.b_pT[1])[t2]
                        else:
                            pin = self.pL[:, t2 * 512:(t2 + 1) * 512]
                            bp = self.b_pL[t2]
                        dst = ya[:, t2, half * 512:(half + 1) * 512]
                        args = (dst, pin) if hb == 0 else (dst, dst, pin)
                        rdy = [bp, self.b_yst[0]] + (fence if hb == 0 else [])
                        self.vop(op, *args, reads=rdy, writes=[self.b_yst[0]])
            for c in range(4):
                def fn(E, i=ya[:, c, :], ix=self._ssl(sl)[:, c:c + 1]):
                    return E.indirect_dma_start(out=self.y_buf,
                                                out_offset=IOA(ap=ix, axis=0),
                                                in_=i, in_offset=None)
                self.S.gma(fn, reads=[self.b_yst[0], self.b_gsb[sl]],
                           writes=[self.b_ybuf])

    def phase_F(self, b):
        bigA32 = self.bigA.bitcast(F32)
        bigB32 = self.bigB.bitcast(F32)
        for t in range(TT):
            i4 = t % 4
            xs = bigA32[:, (4 + i4) * 1024:(5 + i4) * 1024]
            bxs = self.b_bigA[4 + i4]
            ygt = bigA32[:, i4 * 1024:(i4 + 1) * 1024]
            byg = [self.b_bigA[i4]]
            ss = bigB32[:, i4 * 1024:(i4 + 1) * 1024]
            bss = self.b_bigB[i4]
            self.dma(xs, self.x[b, t * 128:(t + 1) * 128, :],
                     reads=[bxs], writes=[bxs])
            if not self._ybuf_claimed:
                self._ybuf_claimed = True
                wy, ry = [self.b_ybuf], []
            else:
                wy, ry = [], [self.b_ybuf]
            self.dma(ygt, self.y_buf[b * T + t * 128:b * T + (t + 1) * 128, :],
                     reads=byg + ry, writes=byg + wy)
            if not self._sspill_claimed:
                self._sspill_claimed = True
                wsx, rsx = [self.b_sspill], []
            else:
                wsx, rsx = [], [self.b_sspill]
            self.dma(ss, self.s_spill[b, t * 128:(t + 1) * 128, :],
                     reads=[bss] + rsx, writes=[bss] + wsx)
            if b == 0:
                vs = bigB32[:, (4 + i4) * 1024:(5 + i4) * 1024]
                bvs = self.b_bigB[4 + i4]
                if not self._vit_claimed:
                    self._vit_claimed = True
                    wv, rv = [self.b_vit], []
                else:
                    wv, rv = [], [self.b_vit]
                self.dma(vs, self.vit_buf[t * 128:(t + 1) * 128, :],
                         reads=[bvs] + rv, writes=[bvs] + wv)
                vread = [bvs]
            else:
                vs = self.xn_tok[:, t, :]
                vread = [self.b_xn[t]]
            self.vop("tensor_add", ygt, ygt, vs, reads=byg + vread, writes=byg)
            if "gin" in self.dmp:
                self.dma(self.dmp["gin"][b, t * 128:(t + 1) * 128, :], ygt,
                         reads=byg, writes=[])
            gg = self.lnfm_f32[:, t % 4, :]
            bgg = self.b_lnfm[t % 4]
            wex = ([self.b_utok[0], self.b_ufm[0], self.b_yst[0]]
                   if (b == 0 and t < 4) else [])
            self.act(gg, ygt, AF.Silu, reads=byg, writes=[bgg] + wex)
            self.vop("tensor_mul", gg, gg, ss, reads=[bgg, bss], writes=[bgg])
            self.gop("tensor_add", xs, xs, gg, reads=[bxs, bgg], writes=[bxs])
            self.S.sdma(self.out[b, t * 128:(t + 1) * 128, :], xs,
                        reads=[bxs], writes=[self.b_out])

    def build(self):
        self.b_out = Buf()
        self._wcvt_claimed = False
        self._ubuf_claimed = False
        self._gidx_claimed = False
        self._ybuf_claimed = False
        self._sspill_claimed = False
        self._vit_claimed = False
        self.init_consts()
        self.convert_qkv_weights()
        for b in range(NB):
            self.phase_A_pass1(b)
            self.load_attn_weights()
            self.phase_A_pass2(b)
            self.phase_B(b)
            if b == 0:
                self.convert_moe_weights()
            self.phase_C(b)
            self.phase_D(b)
            if b == 0:
                for t in range(TT):
                    self.dma(self.vit_buf[t * 128:(t + 1) * 128, :],
                             self.xn_tok[:, t, :],
                             reads=[self.b_xn[t], self.b_vit], writes=[])
        self.index_build()
        self.phase_E_sparse()
        for b in range(NB):
            self.phase_F(b)
        it_deps = [e for e in self.S.dma_ring if e is not None]
        it_deps += [e for e in self.S.gma_ring if e is not None]
        eid = self.S._mk("sync", lambda E: None, (self.b_out,), (), False)
        self.S.items[eid]["deps"] = sorted(set(self.S.items[eid]["deps"]) | set(it_deps))
        self.S.finalize(self.nc, self.sems)
        return self.nc


NAME_MAP = {
    "L0_attn_x": "x",
    "L0_attn_wq": "wq", "L0_attn_bq": "bq",
    "L0_attn_wk": "wk", "L0_attn_bk": "bk",
    "L0_attn_wv": "wv", "L0_attn_bv": "bv",
    "L0_attn_wo": "wo", "L0_attn_bo": "bo",
    "L0_ffn_ln1_w": "ln1w", "L0_ffn_ln1_b": "ln1b",
    "L0_ffn_w_qkv": "wqkv", "L0_ffn_w_proj": "wproj",
    "L0_ffn_ln2_w": "ln2w", "L0_ffn_ln2_b": "ln2b",
    "L0_ffn_gate_w": "gatew", "L0_ffn_gate_b": "gateb",
    "L0_ffn_w1": "w1", "L0_ffn_w2": "w2",
}


def build_program(dump=()):
    bld = Builder(dump=dump)
    nc = bld.build()
    return nc


def kernel(**inputs):
    arrs = {NAME_MAP[k]: np.ascontiguousarray(np.asarray(v, dtype=np.float32))
            for k, v in inputs.items()}
    x = arrs.pop("x")
    nc = build_program()
    in_maps = []
    for c in range(8):
        m = dict(arrs)
        m["x"] = np.ascontiguousarray(x[c * NB:(c + 1) * NB])
        in_maps.append(m)
    res = run_bass_kernel_spmd(nc, in_maps, core_ids=list(range(8)))
    return np.concatenate([res.results[c]["out"] for c in range(8)], axis=0)


if __name__ == "__main__":
    pass


# version 26 (final)
# speedup vs baseline: 1.0766x; 1.0094x over previous
import numpy as np
from contextlib import ExitStack

import concourse.bass as bass
import concourse.mybir as mybir
from concourse.bass_utils import run_bass_kernel_spmd

F32 = mybir.dt.float32
BF16 = mybir.dt.bfloat16
U32 = mybir.dt.uint32
AF = mybir.ActivationFunctionType
ALU = mybir.AluOpType
AX = mybir.AxisListType
IOA = bass.IndirectOffsetOnAxis

PASS_TOK = 512
NPASS = 16
NSLOT = NPASS * PASS_TOK

ENGS = ("sync", "tensor", "vector", "scalar", "gpsimd")
NDQ = 16
NGQ = 24

NB = 2
TT = 16
T = 2048
D = 1024
NC = 8
NH = 16
DK = 64
NE = 8
HID = 4096
EPS = 1e-5


class Buf:
    __slots__ = ("w", "rs")

    def __init__(self):
        self.w = None
        self.rs = []


PARANOID = False


class Sched:
    def __init__(self):
        self.items = []
        self.per_eng = {e: [] for e in ENGS}
        self.last_per_eng = {e: None for e in ENGS}
        self.ndma = 0
        self.dma_ring = [None] * NDQ
        self.ngma = 0
        self.gma_ring = [None] * NGQ
        self.nsma = 0
        self.sma_ring = [None] * 4

    def _mk(self, eng, fn, reads, writes, dma):
        deps = set()
        for b in reads:
            if b.w is not None:
                deps.add(b.w)
        for b in writes:
            if b.w is not None:
                deps.add(b.w)
            for r in b.rs:
                deps.add(r)
        if PARANOID:
            for e2 in ENGS:
                if e2 != eng and self.last_per_eng[e2] is not None:
                    deps.add(self.last_per_eng[e2])
        eid = len(self.items)
        it = dict(eid=eid, eng=eng, fn=fn, dma=dma)
        if dma and eng == "sync":
            q = self.ndma % NDQ
            it["sem"] = "dq%d" % q
            it["val"] = 16 * (self.ndma // NDQ + 1)
            if self.dma_ring[q] is not None:
                deps.add(self.dma_ring[q])
            self.dma_ring[q] = eid
            self.ndma += 1
        elif dma and eng == "scalar":
            q = self.nsma % 4
            it["sem"] = "sq%d" % q
            it["val"] = 16 * (self.nsma // 4 + 1)
            if self.sma_ring[q] is not None:
                deps.add(self.sma_ring[q])
            self.sma_ring[q] = eid
            self.nsma += 1
        elif dma:
            q = self.ngma % NGQ
            it["sem"] = "gq%d" % q
            it["val"] = 16 * (self.ngma // NGQ + 1)
            if self.gma_ring[q] is not None:
                deps.add(self.gma_ring[q])
            self.gma_ring[q] = eid
            self.ngma += 1
        it["deps"] = sorted(deps)
        self.items.append(it)
        self.per_eng[eng].append(it)
        self.last_per_eng[eng] = eid
        for b in writes:
            b.w = eid
            b.rs = []
        for b in reads:
            b.rs.append(eid)
        return eid

    def op(self, eng, fn, reads=(), writes=()):
        return self._mk(eng, fn, tuple(reads), tuple(writes), False)

    def dma(self, out_ap, in_ap, reads=(), writes=(), slow=False):
        def fn(E, o=out_ap, i=in_ap, s=slow):
            if s:
                return E.dma_start(out=o, in_=i, allow_slow_non_contiguous=True)
            return E.dma_start(out=o, in_=i)
        return self._mk("sync", fn, tuple(reads), tuple(writes), True)

    def gma(self, fn, reads=(), writes=()):
        return self._mk("gpsimd", fn, tuple(reads), tuple(writes), True)

    def sdma(self, out_ap, in_ap, reads=(), writes=()):
        def fn(E, o=out_ap, i=in_ap):
            return E.dma_start(out=o, in_=i)
        return self._mk("scalar", fn, tuple(reads), tuple(writes), True)

    def finalize(self, nc, sems):
        demanded = set()
        for it in self.items:
            for d in it["deps"]:
                demanded.add(d)
        for e in ENGS:
            c = 0
            for it in self.per_eng[e]:
                if it["dma"]:
                    it["inc"] = 16
                elif it["eid"] in demanded:
                    c += 1
                    it["val"] = c
                    it["sem"] = e
                    it["inc"] = 1
                else:
                    it["val"] = None
                    it["sem"] = None
                    it["inc"] = 0

        items_all = self.items

        def mk_runner(engname):
            eng_items = self.per_eng[engname]

            def go(E):
                hi = {}
                for it in eng_items:
                    for d in it["deps"]:
                        p = items_all[d]
                        if engname == "tensor" and p["eng"] == "tensor" and not p["dma"]:
                            continue
                        v = p["val"]
                        if v is not None and hi.get(p["sem"], 0) < v:
                            E.wait_ge(sems[p["sem"]], v)
                            hi[p["sem"]] = v
                    inst = it["fn"](E)
                    if it["inc"]:
                        inst.then_inc(sems[it["sem"]], it["inc"])
            return go

        with nc.Block() as block:
            block.sync(mk_runner("sync"))
            block.tensor(mk_runner("tensor"))
            block.vector(mk_runner("vector"))
            block.scalar(mk_runner("scalar"))
            block.gpsimd(mk_runner("gpsimd"))


class Builder:
    def __init__(self, dump=()):
        self.dump = set(dump)
        nc = bass.Bass("TRN2", target_bir_lowering=False, debug=False)
        self.nc = nc
        self.S = Sched()
        self.ctx = ExitStack()
        self._alloc_dram()
        self._alloc_sbuf()
        self._alloc_psum()
        self.sems = {e: self.ctx.enter_context(nc.semaphore("sem_" + e)) for e in ENGS}
        for q in range(NDQ):
            self.sems["dq%d" % q] = self.ctx.enter_context(nc.semaphore("dq%d" % q))
        for q in range(NGQ):
            self.sems["gq%d" % q] = self.ctx.enter_context(nc.semaphore("gq%d" % q))
        for q in range(4):
            self.sems["sq%d" % q] = self.ctx.enter_context(nc.semaphore("sq%d" % q))
        self._mk_bufs()

    def _alloc_dram(self):
        nc = self.nc
        din = lambda n, s: nc.dram_tensor(n, s, F32, kind="ExternalInput").ap()
        self.x = din("x", [NB, T, D])
        self.wq_d = din("wq", [D, DK]); self.bq_d = din("bq", [DK])
        self.wk_d = din("wk", [D, DK]); self.bk_d = din("bk", [DK])
        self.wv_d = din("wv", [D, DK]); self.bv_d = din("bv", [DK])
        self.wo_d = din("wo", [DK, D]); self.bo_d = din("bo", [D])
        self.ln1w_d = din("ln1w", [D]); self.ln1b_d = din("ln1b", [D])
        self.wqkv_d = din("wqkv", [D, 3 * D])
        self.wproj_d = din("wproj", [D, D])
        self.ln2w_d = din("ln2w", [D]); self.ln2b_d = din("ln2b", [D])
        self.gatew_d = din("gatew", [D, NE]); self.gateb_d = din("gateb", [NE])
        self.w1_d = din("w1", [NE, D, HID])
        self.w2_d = din("w2", [NE, HID, D])
        self.out = nc.dram_tensor("out", [NB, T, D], F32, kind="ExternalOutput").ap()
        self.s_spill = nc.dram_tensor("s_spill", [NB, T, D], F32, kind="Internal").ap()
        self.wqkv_bf = nc.dram_tensor("wqkv_bf", [D, 3 * D], BF16, kind="Internal").ap()
        self.wproj_bf = nc.dram_tensor("wproj_bf", [D, D], BF16, kind="Internal").ap()
        ik = lambda n: "ExternalOutput" if n in self.dump else "Internal"
        self.w1bf = nc.dram_tensor("w1bf", [NE * 4 * 128, 8 * 1024], BF16,
                                   kind="Internal").ap()
        self.w2bf = nc.dram_tensor("w2bf", [NE * 4 * 128, 8 * 1024], BF16,
                                   kind="Internal").ap()
        self.u_buf = nc.dram_tensor("u_buf", [NB * T, D], BF16, kind=ik("u_buf")).ap()
        self.vit_buf = nc.dram_tensor("vit_buf", [T, D], F32, kind="Internal").ap()
        self.y_buf = nc.dram_tensor("y_buf", [NB * T + 128, D], F32,
                                    kind=ik("y_buf")).ap()
        self.gidx_buf = nc.dram_tensor("gidx_buf", [NSLOT], U32, kind=ik("gidx_buf")).ap()
        self.iscr = nc.dram_tensor("iscr", [NB * TT * NE], F32, kind="Internal").ap()
        self.sidx_buf = nc.dram_tensor("sidx_buf", [NSLOT], U32, kind=ik("sidx_buf")).ap()
        self.ident_dram = nc.inline_tensor(np.eye(8, dtype=np.float32), name="identc")
        import ml_dtypes
        self.identb_dram = nc.inline_tensor(
            np.eye(128).astype(ml_dtypes.bfloat16), name="identbc")
        lt = (np.arange(128)[:, None] <= np.arange(128)[None, :]).astype(np.float32)
        self.lt_dram = nc.inline_tensor(lt, name="ltmaskc")
        thr16 = np.tile(PASS_TOK * np.arange(16, dtype=np.float32)[None, :], (NE, 1))
        self.thr16_dram = nc.inline_tensor(thr16, name="thr16c")
        thrp = np.tile(PASS_TOK * np.arange(NPASS, dtype=np.float32)[None, :], (NE, 1))
        self.thrp_dram = nc.inline_tensor(thrp, name="thrpc")
        evals = np.arange(NE, dtype=np.float32)[:, None]
        self.evals_dram = nc.inline_tensor(evals, name="evalsc")
        self.dmp = {}
        for name in self.dump:
            if name in ("u_buf", "y_buf", "gidx_buf", "sidx_buf"):
                continue
            shp = {"xn": [NB, T, D], "xn2": [NB, T, D], "s": [NB, T, D],
                   "lgt": [NB, T, 2 * NE], "gin": [NB, T, D]}[name]
            self.dmp[name] = nc.dram_tensor("dbg_" + name, shp, F32, kind="ExternalOutput").ap()

    def _alloc_sbuf(self):
        nc = self.nc
        E_ = self.ctx.enter_context
        sb = lambda n, s, d: E_(nc.sbuf_tensor(n, s, d))
        self.xn_tok = sb("xn_tok", [128, TT, D], F32)
        self.ln_fm = sb("ln_fm", [128, NC, T], BF16)
        self.bigA = sb("bigA", [128, 16384], BF16)
        self.bigB = sb("bigB", [128, 16384], BF16)
        self.hg = sb("hg", [128, 4096], BF16)
        self.misc = sb("misc", [128, 4096], F32)
        self.stage = sb("stage", [128, 2048], F32)
        self.pool = sb("pool", [128, 3, 1024], F32)
        self.identf = sb("identf", [8, 8], F32)
        self.identb = sb("identb", [128, 128], BF16)
        self.ltmask = sb("ltmask", [128, 128], BF16)
        self.onesb = sb("onesb", [128, 128], BF16)
        self.gam1 = sb("gam1", [128, NC], F32)
        self.bet1 = sb("bet1", [128, NC], F32)
        self.gam2 = sb("gam2", [128, NC], F32)
        self.bet2 = sb("bet2", [128, NC], F32)
        self.bq8 = sb("bq8", [64, 2], F32)
        self.bkc = sb("bkc", [64, 1], F32)
        self.bvc = sb("bvc", [64, 1], F32)
        self.gw_st = sb("gw_st", [128, NC, NE], F32)
        self.gwg_hi = sb("gwg_hi", [128, NC, NE], BF16)
        self.gwg_lo = sb("gwg_lo", [128, NC, NE], BF16)
        self.gw_bf = sb("gw_bf", [128, NC, NE], BF16)
        self.b2bf = sb("b2bf", [128, NC], BF16)
        self.gb_row = sb("gb_row", [1, NE], F32)
        self.gbadd = sb("gbadd", [NE, 1], F32)
        self.masks = sb("masksb", [128, NB, TT, NE], F32)
        self.lgt_tok = sb("lgt_tok", [128, TT, NE], F32)
        self.statA = sb("statA", [128, TT, 4], F32)
        self.ntcol = sb("ntcol", [128, 1], F32)
        self.maxsc = sb("maxsc", [1, 2], F32)
        self.reccol = sb("reccol", [128, TT], F32)
        self.epsc = sb("epsc", [128, 1], F32)
        self.idxs = sb("idxs", [128, 40], U32)
        self.miscu = self.misc.bitcast(U32)
        self.ltf = self.misc[:, 0:128]
        self.csum = self.misc[0:1, 128:384]
        self.csum32 = self.misc[0:32, 384:392]
        self.carry32 = self.misc[0:32, 392:400]
        self.irow = self.misc[0:8, 400:464]
        self.icol = self.misc[0:8, 464:472]
        self.srow = self.misc[0:1, 472:504]
        self.soffr = self.misc[0:1, 504:512]
        self.thr16 = self.misc[0:8, 512:528]
        self.thrp = self.misc[0:8, 528:528 + NPASS]
        self.evalsf = self.misc[0:8, 552:553]
        self.idxf = self.misc[:, 560:584]
        self.pcol = self.misc[:, 584:585]
        self.posf = self.misc[:, 585:587]
        self.tid32 = self.miscu[:, 588:588 + NB * TT]
        self.iniz = self.miscu[:, 0:NSLOT // 128]
        self.onescol = self.ltf[:, 127:128]
        self.onesrow = self.ltf[0:1, 0:128]
        self.miscb = self.misc.bitcast(BF16)
        self.poolb = self.pool.bitcast(BF16)
        self.hgf = self.hg.bitcast(F32)
        self.stageb = self.stage.bitcast(BF16)
        self.lnfm_f32 = self.ln_fm.bitcast(F32)
        rem = nc.sbuf_bytes_remaining
        assert rem >= 0, f"SBUF over budget by {-rem} bytes"

    def _alloc_psum(self):
        nc = self.nc
        E_ = self.ctx.enter_context
        self.pL = E_(nc.psum_tensor("pL", [128, 2048], F32))
        self.pM = E_(nc.psum_tensor("pM", [128, 1024], F32))
        self.pT1 = E_(nc.psum_tensor("pT1", [128, 512], F32))
        self.pT2 = E_(nc.psum_tensor("pT2", [128, 512], F32))
        self.pT1b = self.pT1.bitcast(BF16)
        self.pT2b = self.pT2.bitcast(BF16)
        self.pMb = self.pM.bitcast(BF16)

    def _mk_bufs(self):
        self.b_ident = Buf(); self.b_lt = Buf(); self.b_ones = Buf()
        self.b_params = Buf(); self.b_gw = Buf()
        self.b_pool = [Buf() for _ in range(3)]
        self.b_stage = [Buf(), Buf()]
        self.b_hg = [Buf() for _ in range(4)]
        self.b_misc = [Buf() for _ in range(4)]
        self.b_xn = [Buf() for _ in range(TT)]
        self.b_lnfm = [Buf() for _ in range(TT)]
        self.b_bigA = [Buf() for _ in range(8)]
        self.b_bigB = [Buf() for _ in range(8)]
        self.b_stat = [Buf(), Buf()]
        self.b_masks = Buf()
        self.b_pL = [Buf(), Buf(), Buf(), Buf()]
        self.b_pM = [Buf(), Buf()]
        self.b_pT = [Buf(), Buf()]
        self.b_sspill = Buf()
        self.b_wbf = Buf()
        self.b_reccol = Buf()
        self.b_ptile = [Buf(), Buf(), Buf()]
        self.b_stsum = Buf()
        self.b_stsq = Buf()
        self.b_ust = [Buf(), Buf()]
        self.b_wcvt = Buf()
        self.b_ubuf = Buf()
        self.b_vit = Buf()
        self.b_ybuf = Buf()
        self.b_gidxb = Buf()
        self.b_sidxb = Buf()
        self.b_idx = Buf()
        self.b_icons = Buf()
        self.b_utok = [Buf(), Buf()]
        self.b_ufm = [Buf(), Buf()]
        self.b_yst = [Buf(), Buf()]
        self.b_gsb = [Buf(), Buf()]
        self.b_widx = Buf()

    def pLh(self, i):
        return [self.b_pL[2 * i], self.b_pL[2 * i + 1]]

    def bA(self, lo, hi):
        return [self.b_bigA[i] for i in range(lo // 2048, (hi + 2047) // 2048)]

    def bB(self, lo, hi):
        return [self.b_bigB[i] for i in range(lo // 2048, (hi + 2047) // 2048)]

    def op(self, eng, fn, reads=(), writes=()):
        return self.S.op(eng, fn, reads, writes)

    def dma(self, out_ap, in_ap, reads=(), writes=(), slow=False):
        return self.S.dma(out_ap, in_ap, reads, writes, slow=slow)

    def mm(self, mms, reads=(), writes=()):
        def fn(E, mms=mms):
            n = len(mms)
            last = None
            for i, (o, l, r) in enumerate(mms):
                last = E.matmul(o, l, r, start=(i == 0), stop=(i == n - 1))
            return last
        return self.op("tensor", fn, reads, writes)

    def mmx(self, mms, reads=(), writes=()):
        def fn(E, mms=mms):
            last = None
            for (o, l, r, st, sp) in mms:
                last = E.matmul(o, l, r, start=st, stop=sp)
            return last
        return self.op("tensor", fn, reads, writes)

    def trans(self, trs, reads=(), writes=(), bf=True):
        ident = self.identb if bf else self.identf

        def fn(E, trs=trs, ident=ident):
            last = None
            for item in trs:
                (o, i), bp = item[:2], (item[2] if len(item) > 2 else 0)
                p = i.shape[0]
                last = E.transpose(o, i, ident[bp:bp + p, bp:bp + p])
            return last
        return self.op("tensor", fn, tuple(reads) + (self.b_ident,), writes)

    def act(self, out, in_, func, reads=(), writes=(), **kw):
        def fn(E, o=out, i=in_, f=func, kw=kw):
            return E.activation(o, i, f, **kw)
        return self.op("scalar", fn, reads, writes)

    def vop(self, name, *args, reads=(), writes=(), **kw):
        def fn(E, n=name, a=args, kw=kw):
            return getattr(E, n)(*a, **kw)
        return self.op("vector", fn, reads, writes)

    def gop(self, name, *args, reads=(), writes=(), **kw):
        def fn(E, n=name, a=args, kw=kw):
            return getattr(E, n)(*a, **kw)
        return self.op("gpsimd", fn, reads, writes)

    def init_consts(self):
        pb = self.b_params
        self.dma(self.identf[:], self.ident_dram.ap(), writes=[self.b_ident])
        self.dma(self.identb[:], self.identb_dram.ap(), writes=[self.b_ident])
        self.dma(self.pool[:, 2, :][:, 0:128], self.lt_dram.ap()[:, 0:128],
                 writes=[self.b_pool[2]])
        self.vop("tensor_copy", self.ltmask[:], self.pool[:, 2, 0:128],
                 reads=[self.b_pool[2]], writes=[self.b_lt])
        self.vop("memset", self.onesb[:], 1.0, writes=[self.b_ones])
        self.vop("memset", self.epsc[:], EPS, writes=[self.b_ones])
        rr = lambda ap: ap.rearrange("(c p) -> p c", p=128)
        cc = lambda ap: ap.rearrange("(p one) -> p one", one=1)
        self.dma(self.gam1[:], rr(self.ln1w_d), writes=[pb], slow=True)
        self.dma(self.bet1[:], rr(self.ln1b_d), writes=[pb], slow=True)
        self.dma(self.gam2[:], rr(self.ln2w_d), writes=[pb], slow=True)
        self.dma(self.bet2[:], rr(self.ln2b_d), writes=[pb], slow=True)
        self.dma(self.bq8[:, 1:2], cc(self.bq_d), writes=[pb])
        self.dma(self.bkc[:], cc(self.bk_d), writes=[pb])
        self.dma(self.bvc[:], cc(self.bv_d), writes=[pb])
        self.dma(self.gb_row[:], self.gateb_d.rearrange("(one d) -> one d", one=1), writes=[pb])
        self.vop("tensor_scalar", self.bq8[:, 0:1], self.bq8[:, 1:2], 0.125, None, ALU.mult,
                 reads=[pb], writes=[pb])
        gwr = self.gatew_d.rearrange("(c p) e -> p c e", p=128)
        self.dma(self.gw_st[:], gwr, writes=[self.b_gw])
        for c in range(NC):
            self.vop("tensor_scalar", self.gw_st[:, c, :], self.gw_st[:, c, :],
                     self.gam2[:, c:c + 1], None, ALU.mult,
                     reads=[self.b_gw, pb], writes=[self.b_gw])
        self.vop("tensor_copy", self.gwg_hi[:], self.gw_st[:],
                 reads=[self.b_gw], writes=[self.b_gw])
        self.vop("tensor_sub", self.gwg_lo[:], self.gw_st[:], self.gwg_hi[:],
                 reads=[self.b_gw], writes=[self.b_gw])
        self.dma(self.gw_st[:], gwr, reads=[self.b_gw], writes=[self.b_gw])
        self.vop("tensor_copy", self.gw_bf[:], self.gw_st[:],
                 reads=[self.b_gw], writes=[self.b_gw])
        self.vop("tensor_copy", self.b2bf[:], self.bet2[:], reads=[pb], writes=[pb])
        mms = [(self.pT1[0:1, 0:NE], self.b2bf[:, c:c + 1], self.gw_bf[:, c, :])
               for c in range(NC)]
        self.mm(mms, reads=[self.b_gw, pb], writes=[self.b_pT[0]])
        self.vop("tensor_add", self.misc[0:1, 0:NE], self.pT1[0:1, 0:NE], self.gb_row[:],
                 reads=[self.b_pT[0], pb], writes=[self.b_misc[0]])
        self.vop("tensor_copy", self.miscb[0:1, 2048:2048 + NE], self.misc[0:1, 0:NE],
                 reads=[self.b_misc[0]], writes=[self.b_misc[1]])
        self.trans([(self.pT2b[0:NE, 0:1], self.miscb[0:1, 2048:2048 + NE])],
                   reads=(self.b_misc[1],), writes=[self.b_pT[1]])
        self.vop("tensor_copy", self.gbadd[:], self.pT2b[0:NE, 0:1],
                 reads=[self.b_pT[1]], writes=[pb])
        iniz2 = self.miscu[0:NSLOT // 128, 0:128]
        self.gop("iota", iniz2, [[0, 128]], base=0,
                 channel_multiplier=0, reads=[], writes=[self.b_idx, self.b_misc[0]])
        self.dma(self.gidx_buf.rearrange("(c p) -> c p", p=128), iniz2,
                 reads=[self.b_idx, self.b_misc[0]], writes=[self.b_gidxb])
        self.gop("iota", iniz2, [[1, 128]], base=NB * T,
                 channel_multiplier=0, reads=[self.b_gidxb],
                 writes=[self.b_idx, self.b_misc[0]])
        self.dma(self.sidx_buf.rearrange("(c p) -> c p", p=128), iniz2,
                 reads=[self.b_idx, self.b_misc[0]], writes=[self.b_sidxb])

    def convert_qkv_weights(self):
        def cast_dma(out_ap, in_ap, reads, writes):
            def fn(E, o=out_ap, i=in_ap):
                return E.dma_start(out=o, in_=i, max_dma_last_dim=8192)
            return self.S.gma(fn, reads, writes)
        cast_dma(self.wqkv_bf, self.wqkv_d, [self.b_wbf], [])
        cast_dma(self.wproj_bf, self.wproj_d, [self.b_wbf], [])

    def convert_moe_weights(self):
        def cast_dma(out_ap, in_ap, reads, writes):
            def fn(E, o=out_ap, i=in_ap):
                return E.dma_start(out=o, in_=i, max_dma_last_dim=8192)
            return self.S.gma(fn, reads, writes)
        w1v = self.w1bf.rearrange("(s p) (c n) -> s p c n", p=128, n=1024)
        w2v = self.w2bf.rearrange("(s p) (c n) -> s p c n", p=128, n=1024)
        for e in range(NE):
            for hb in range(4):
                s = e * 4 + hb
                cast_dma(w1v[s],
                         self.w1_d[e, :, hb * 1024:(hb + 1) * 1024].rearrange(
                             "(c p) n -> p c n", p=128),
                         [self.b_wcvt], [])
                cast_dma(w2v[s],
                         self.w2_d[e, hb * 1024:(hb + 1) * 1024, :].rearrange(
                             "(c p) n -> p c n", p=128),
                         [self.b_wcvt], [])

    def load_attn_weights(self):
        w_bufs = self.bB(10240, 12800)
        for i, w in enumerate((self.wq_d, self.wk_d, self.wv_d)):
            sl = i % 2
            self.dma(self.stage[:, sl * 1024:sl * 1024 + 512].rearrange(
                         "p (c k) -> p c k", k=DK),
                     w.rearrange("(c p) k -> p c k", p=128),
                     reads=[self.b_stage[sl]], writes=[self.b_stage[sl]])
            self.gop("tensor_copy", self.bigB[:, 10240 + i * 512:10240 + (i + 1) * 512],
                     self.stage[:, sl * 1024:sl * 1024 + 512],
                     reads=[self.b_stage[sl]], writes=w_bufs)
        self.dma(self.stage[0:64, 1024:2048], self.wo_d,
                 reads=[self.b_stage[1]], writes=[self.b_stage[1]])
        self.gop("tensor_copy", self.bigB[0:64, 11776:12800], self.stage[0:64, 1024:2048],
                 reads=[self.b_stage[1]], writes=w_bufs)

    def wq_sb(self, i, c):
        off = 10240 + i * 512 + c * 64
        return self.bigB[:, off:off + 64]

    def wo_sb(self, half):
        return self.bigB[0:64, 11776 + half * 512:11776 + (half + 1) * 512]

    def phase_A_pass1(self, b):
        for t in range(TT):
            xt = self.xn_tok[:, t, :]
            bx = self.b_xn[t]
            self.dma(xt, self.x[b, t * 128:(t + 1) * 128, :], reads=[bx], writes=[bx])
            self._stats_collect(t, xt, bx)
        self._stats_finalize()

    def phase_A_pass2(self, b):
        QT = self.bigB[0:65, 0:2048]
        KT = self.bigB[0:65, 2048:4096]
        VT = self.bigB[:, 4096:5136].rearrange("p (t k) -> p t k", k=65)
        bQ, bK = self.bB(0, 2048), self.bB(2048, 4096)
        bV = self.bB(4096, 5456)
        plB = self.pL.bitcast(BF16)
        self.vop("memset", self.bigB[64:128, 0:2048], 0.0, reads=bQ, writes=bQ)
        self.vop("memset", self.bigB[64:128, 2048:4096], 0.0, reads=bK, writes=bK)
        self.vop("memset", KT[64:65, :], 1.0, reads=bK, writes=bK)
        self.vop("memset", VT[:, :, 64:65], 1.0, reads=bV, writes=bV)
        self.vop("memset", self.bigB[:, 5136:5200], 0.0,
                 reads=self.bB(5136, 5200), writes=self.bB(5136, 5200))
        pend = None
        for t in range(TT + 1):
            if t < TT:
                pp = t % 2
                self.vop("tensor_scalar", self.xn_tok[:, t, :], self.xn_tok[:, t, :],
                         self.statA[:, t, 2:3], self.statA[:, t, 3:4],
                         ALU.mult, ALU.subtract,
                         reads=[self.b_stsum], writes=[self.b_xn[t]])
                xc = self.hg[:, pp * 1024:(pp + 1) * 1024]
                self.vop("tensor_copy", xc, self.xn_tok[:, t, :],
                         reads=[self.b_xn[t]], writes=[self.b_hg[pp]])
                pt = (self.pT1b, self.pT2b)[pp]
                self.trans([(pt[:, c * 128:(c + 1) * 128], xc[:, c * 128:(c + 1) * 128])
                            for c in range(NC)],
                           reads=(self.b_hg[pp],), writes=[self.b_pT[pp]])
                if "xn" in self.dmp:
                    self.dma(self.dmp["xn"][b, t * 128:(t + 1) * 128, :],
                             self.xn_tok[:, t, :], reads=[self.b_xn[t]], writes=[])
            if pend is not None:
                tp_ = pend
                pp_ = tp_ % 2
                pt_ = (self.pT1b, self.pT2b)[pp_]
                fm = self.hg[:, (2 + pp_) * 1024:(3 + pp_) * 1024]
                self.vop("tensor_copy", fm, pt_[:],
                         reads=[self.b_pT[pp_]], writes=[self.b_hg[2 + pp_]])
                ps = self.pM[:, pp_ * 512:(pp_ + 1) * 512]
                bps = self.b_pM[pp_]
                mms = []
                for i in range(3):
                    for c in range(NC):
                        mms.append((ps[0:64, i * 128:i * 128 + 128],
                                    self.wq_sb(i, c), fm[:, c * 128:(c + 1) * 128]))
                def fnmm(E, mms=mms):
                    last = None
                    for gi in range(3):
                        for ci in range(NC):
                            o, l, r = mms[gi * NC + ci]
                            last = E.matmul(o, l, r, start=(ci == 0), stop=(ci == NC - 1))
                    return last
                self.op("tensor", fnmm,
                        reads=[self.b_hg[2 + pp_]] + self.bB(10240, 12800), writes=[bps])
                self.act(QT[0:64, tp_ * 128:(tp_ + 1) * 128], ps[0:64, 0:128], AF.Identity,
                         scale=0.125, bias=self.bq8[:, 0:1],
                         reads=[bps, self.b_params], writes=bQ)
                self.act(KT[0:64, tp_ * 128:(tp_ + 1) * 128], ps[0:64, 128:256], AF.Identity,
                         bias=self.bkc[:],
                         reads=[bps, self.b_params], writes=bK)
                vt = self.bigB[0:64, 5200 + pp_ * 128:5200 + (pp_ + 1) * 128]
                self.act(vt, ps[0:64, 256:384], AF.Identity, bias=self.bvc[:],
                         reads=[bps, self.b_params], writes=bV)
                vslot = plB[:, pp_ * 2048:pp_ * 2048 + 64]
                self.trans([(vslot, vt)], reads=tuple(bV),
                           writes=[self.b_pL[2 * pp_]])
                self.vop("tensor_copy", VT[:, tp_, 0:64], vslot,
                         reads=[self.b_pL[2 * pp_]], writes=bV)
            pend = t if t < TT else None

    def _stats_collect(self, t, src_ap, bsrc):
        self.vop("reduce_sum", self.statA[:, t, 0:1], src_ap, axis=AX.X,
                 reads=[bsrc], writes=[self.b_stsum])
        self.act(self.pool[:, 2, :], src_ap, AF.Square,
                 accum_out=self.statA[:, t, 1:2],
                 reads=[bsrc], writes=[self.b_pool[2], self.b_stsq])

    def _stats_finalize(self):
        c = lambda i: self.statA[:, :, i:i + 1]
        bb = [self.b_stsum, self.b_stsq]
        self.vop("tensor_scalar", c(0), c(0), 1.0 / D, None, ALU.mult,
                 reads=bb, writes=[self.b_stsum])
        self.vop("tensor_scalar", c(1), c(1), 1.0 / D, None, ALU.mult,
                 reads=bb, writes=[self.b_stsq])
        self.vop("tensor_mul", c(2), c(0), c(0), reads=bb, writes=[self.b_stsum])
        self.vop("tensor_sub", c(1), c(1), c(2), reads=bb, writes=[self.b_stsq])
        self.act(c(3), c(1), AF.Ln, bias=self.epsc[:],
                 reads=bb + [self.b_ones], writes=[self.b_stsum])
        self.act(c(2), c(3), AF.Exp, scale=-0.5, reads=bb, writes=[self.b_stsum])
        self.vop("tensor_mul", c(3), c(0), c(2), reads=bb, writes=[self.b_stsum])

    def phase_B(self, b):
        QT = self.bigB[0:65, 0:2048]
        KT = self.bigB[0:65, 2048:4096]
        VT = self.bigB[:, 4096:5136].rearrange("p (t k) -> p t k", k=65)
        ZF = self.bigB[0:64, 6144:8192]
        QSQ = self.bigB[0:64, 8192:10240]
        bQ, bK = self.bB(0, 2048), self.bB(2048, 4096)
        bV, bZ, bQs = self.bB(4096, 5456), self.bB(6144, 8192), self.bB(8192, 10240)
        row = lambda s, lo, hi: self.poolb[0:1, s, lo:hi]
        for (tile, bt) in ((QT, bQ), (KT, bK)):
            self.gop("tensor_mul", QSQ[:], tile[0:64, :], tile[0:64, :],
                     reads=bt, writes=bQs)
            mms = [(self.pL[0:1, j * 512:(j + 1) * 512], self.onesb[0:64, 0:1],
                    QSQ[:, j * 512:(j + 1) * 512]) for j in range(4)]
            def fnmm(E, mms=mms):
                last = None
                for (o, l, r) in mms:
                    last = E.matmul(o, l, r, start=True, stop=True)
                return last
            self.op("tensor", fnmm, reads=bQs + [self.b_ones],
                    writes=self.pLh(0) + self.pLh(1))
            if tile is QT:
                self.act(row(1, 0, 2048), self.pL[0:1, 0:2048], AF.Sqrt,
                         reads=self.pLh(0) + self.pLh(1), writes=[self.b_pool[1]])
        self.vop("reduce_max", self.maxsc[0:1, 0:1], self.pL[0:1, 0:2048], axis=AX.X,
                 reads=self.pLh(0) + self.pLh(1), writes=[self.b_params])
        self.act(self.maxsc[0:1, 1:2], self.maxsc[0:1, 0:1], AF.Sqrt,
                 reads=[self.b_params], writes=[self.b_params])
        self.vop("tensor_scalar", row(0, 0, 2048), row(1, 0, 2048),
                 self.maxsc[0:1, 1:2], -1.0, ALU.mult, ALU.mult,
                 reads=[self.b_pool[1], self.b_params], writes=[self.b_pool[0]])
        self.dma(QT[64:65, :], row(0, 0, 2048), reads=[self.b_pool[0]], writes=bQ)
        for qb in range(2):
            kmax = (qb + 1) * 8
            zb = [self.b_pM[0], self.b_pM[1]]

            def emit_z(kc, kmax=kmax, zb=zb, bV=bV):
                pt = self.miscb[:, 4096 + (kc % 2) * 1024:4096 + (kc % 2 + 1) * 1024]
                vwide = self.bigB[:, 4096 + kc * 65:4096 + kc * 65 + 128]
                self.mmx([(self.pM[:, j * 512:(j + 1) * 512], vwide,
                           pt[:, j * 512:(j + 1) * 512],
                           kc == 0, kc == kmax - 1) for j in range(2)],
                         reads=bV + self.bB(4096, 5456) + [self.b_ptile[kc % 2]],
                         writes=zb)

            for kc in range(kmax):
                sp = self.pL[:, (kc % 2) * 1024:(kc % 2 + 1) * 1024]
                bsp = self.pLh(kc % 2)
                QT8 = self.bigB[0:128, 0:2048]
                KT8 = self.bigB[0:128, 2048:4096]
                self.mmx([(sp[:, j * 512:(j + 1) * 512],
                           KT8[:, kc * 128:(kc + 1) * 128],
                           QT8[:, qb * 1024 + j * 512:qb * 1024 + (j + 1) * 512],
                           True, True) for j in range(2)],
                         reads=bQ + bK, writes=bsp)
                pt = self.miscb[:, 4096 + (kc % 2) * 1024:4096 + (kc % 2 + 1) * 1024]
                bpt = self.b_ptile[kc % 2]
                d = kc * 128 - qb * 1024
                if d > 0:
                    self.gop("memset", pt[:, 0:d], 0.0, reads=[bpt], writes=[bpt])
                lo = max(d, 0)
                self.act(pt[:, lo:1024], sp[:, lo:1024], AF.Exp,
                         reads=bsp, writes=[bpt])
                if d >= 0:
                    self.vop("tensor_mul", pt[:, d:d + 128], pt[:, d:d + 128], self.ltmask[:],
                             reads=[bpt, self.b_lt], writes=[bpt])
                if kc > 0:
                    emit_z(kc - 1)
            emit_z(kmax - 1)
            self.vop("reciprocal", self.misc[64:65, 3072:4096],
                     self.pM[64:65, 0:1024], reads=zb, writes=[self.b_misc[3]])
            self.dma(self.misc[0:1, 3072:4096], self.misc[64:65, 3072:4096],
                     reads=[self.b_misc[3]], writes=[self.b_misc[3]])
            self.trans([(self.pT1[:, tq:tq + 1],
                         self.misc[0:1, 3072 + tq * 128:3072 + (tq + 1) * 128])
                        for tq in range(8)],
                       reads=(self.b_misc[3],), writes=[self.b_pT[0]], bf=False)
            self.vop("tensor_copy", self.reccol[:, qb * 8:(qb + 1) * 8],
                     self.pT1[:, 0:8], reads=[self.b_pT[0]], writes=[self.b_reccol])
            self.vop("tensor_copy", ZF[:, qb * 1024:(qb + 1) * 1024], self.pM[0:64, 0:1024],
                     reads=zb, writes=bZ)
        self.dma(self.pool[0:1, 2, :], self.bo_d.rearrange("(one d) -> one d", one=1),
                 reads=[self.b_pool[2]], writes=[self.b_pool[2]])
        self.vop("tensor_copy", self.poolb[0:1, 0, 0:1024], self.pool[0:1, 2, :],
                 reads=[self.b_pool[2]], writes=[self.b_pool[0]])
        self.vop("tensor_sub", self.poolb[0:1, 1, 0:1024], self.pool[0:1, 2, :],
                 self.poolb[0:1, 0, 0:1024],
                 reads=[self.b_pool[2], self.b_pool[0]], writes=[self.b_pool[1]])
        mms = []
        for j in range(2):
            for hl in range(2):
                src = self.poolb[0:1, hl, j * 512:(j + 1) * 512]
                mms.append((self.pL[:, j * 512:(j + 1) * 512],
                            self.onesb[0:1, 0:128], src))
        def fnb(E, mms=mms):
            last = None
            for i, (o, l, r) in enumerate(mms):
                last = E.matmul(o, l, r, start=(i % 2 == 0), stop=(i % 2 == 1))
            return last
        self.op("tensor", fnb,
                reads=[self.b_pool[0], self.b_pool[1], self.b_ones],
                writes=self.pLh(0))
        self.act(self.misc[:, 3072:4096], self.pL[:, 0:1024], AF.Copy,
                 reads=self.pLh(0), writes=[self.b_misc[3]])
        for t in range(TT):
            pp = t % 2
            stok = self.misc[:, pp * 1024:(pp + 1) * 1024]
            bst = self.b_misc[pp]
            for half in range(2):
                ps = self.pL[:, (t * 2 + half) % 2 * 1024:((t * 2 + half) % 2) * 1024 + 512]
                bps = self.b_pL[2 * ((t * 2 + half) % 2)]
                self.mm([(ps, ZF[:, t * 128:(t + 1) * 128], self.wo_sb(half))],
                        reads=bZ + self.bB(10240, 12800), writes=[bps])
                self.op("vector",
                        lambda E, o=stok[:, half * 512:(half + 1) * 512], i0=ps,
                               sc=self.reccol[:, t:t + 1],
                               i1=self.misc[:, 3072 + half * 512:3072 + (half + 1) * 512]:
                        E.scalar_tensor_tensor(o, i0, sc, i1, ALU.mult, ALU.add),
                        reads=[bps, self.b_reccol, self.b_misc[3]], writes=[bst])
            self.dma(self.s_spill[b, t * 128:(t + 1) * 128, :], stok,
                     reads=[bst, self.b_sspill], writes=[])
            if "s" in self.dmp:
                self.dma(self.dmp["s"][b, t * 128:(t + 1) * 128, :], stok,
                         reads=[bst], writes=[])

    def _attn_norms(self, QT, KT, QSQ, bQ, bK, bQs, b):
        W = QT.shape[1]
        nj = W // 512
        row = lambda s: self.poolb[0:1, s, 0:W]
        sq_eng = "vop" if b == 0 else "gop"
        for (tile, bt, dst_s) in ((QT, bQ, 1), (KT, bK, 2)):
            getattr(self, sq_eng)("tensor_mul", QSQ[:, 0:W],
                                  tile[0:64, :], tile[0:64, :],
                                  reads=bt, writes=bQs)
            def fnmm(E, QSQ=QSQ, nj=nj):
                last = None
                for j in range(nj):
                    last = E.matmul(self.pL[0:1, j * 512:(j + 1) * 512],
                                    self.onesb[0:64, 0:1],
                                    QSQ[:, j * 512:(j + 1) * 512],
                                    start=True, stop=True)
                return last
            self.op("tensor", fnmm, reads=tuple(bQs) + (self.b_ones,),
                    writes=self.pLh(0) + self.pLh(1))
            if dst_s == 1:
                self.act(row(1), self.pL[0:1, 0:W], AF.Sqrt,
                         reads=self.pLh(0) + self.pLh(1), writes=[self.b_pool[1]])
        self.vop("reduce_max", self.maxsc[0:1, 0:1], self.pL[0:1, 0:W], axis=AX.X,
                 reads=self.pLh(0) + self.pLh(1), writes=[self.b_params])
        self.act(self.maxsc[0:1, 1:2], self.maxsc[0:1, 0:1], AF.Sqrt,
                 reads=[self.b_params], writes=[self.b_params])
        self.vop("tensor_scalar", row(0), row(1),
                 self.maxsc[0:1, 1:2], -1.0, ALU.mult, ALU.mult,
                 reads=[self.b_pool[1], self.b_params], writes=[self.b_pool[0]])
        self.dma(QT[64:65, :], row(0), reads=[self.b_pool[0]], writes=bQ)

    def phase_C(self, b):
        for t in range(TT):
            self._stats_collect(t, self.xn_tok[:, t, :], self.b_xn[t])
        self._stats_finalize()
        pend = None
        for t in range(TT + 1):
            if t < TT:
                pp = t % 2
                xt = self.xn_tok[:, t, :]
                u = self.pool[:, pp, :]
                self.vop("tensor_scalar", u, xt,
                         self.statA[:, t, 2:3], self.statA[:, t, 3:4],
                         ALU.mult, ALU.subtract,
                         reads=[self.b_xn[t], self.b_stsum], writes=[self.b_pool[pp]])
                xc = self.hg[:, pp * 1024:(pp + 1) * 1024]
                eng = "vop" if b == 0 else "gop"
                getattr(self, eng)("tensor_copy", xc, u, reads=[self.b_pool[pp]],
                                   writes=[self.b_hg[pp]])
                pt = (self.pT1b, self.pT2b)[pp]
                self.trans([(pt[:, c * 128:(c + 1) * 128], xc[:, c * 128:(c + 1) * 128])
                            for c in range(NC)],
                           reads=(self.b_hg[pp],), writes=[self.b_pT[pp]])
            if pend is not None:
                tp_ = pend
                pp_ = tp_ % 2
                pt_ = (self.pT1b, self.pT2b)[pp_]
                for c in range(NC):
                    dst = self.ln_fm[:, c, tp_ * 128:(tp_ + 1) * 128]
                    src = pt_[:, c * 128:(c + 1) * 128]
                    if c % 2 == 0:
                        self.act(dst, src, AF.Identity,
                                 scale=self.gam1[:, c:c + 1],
                                 bias=self.bet1[:, c:c + 1],
                                 reads=[self.b_pT[pp_], self.b_params],
                                 writes=[self.b_lnfm[tp_]])
                    else:
                        self.vop("tensor_scalar", dst, src,
                                 self.gam1[:, c:c + 1], self.bet1[:, c:c + 1],
                                 ALU.mult, ALU.add,
                                 reads=[self.b_pT[pp_], self.b_params],
                                 writes=[self.b_lnfm[tp_]])
            pend = t if t < TT else None
        QTg = lambda i: self.bigB[0:65, i * 2048:(i + 1) * 2048]
        KTg = lambda i: self.bigB[0:65, 4096 + i * 2048:4096 + (i + 1) * 2048]
        VG = self.bigB[:, 8192:10272].rearrange("p (h t k) -> p h t k", h=2, k=65)
        VTW = self.bigB[:, 10272:10784]
        QSQ = self.bigB[0:64, 10784:12832]
        qkvsl = lambda i: self.bigB[:, 12832 + i * 1024:12832 + (i + 1) * 1024].rearrange(
            "p (c k) -> p c k", k=128)
        bQg = lambda i: self.bB(i * 2048, (i + 1) * 2048)
        bKg = lambda i: self.bB(4096 + i * 2048, 4096 + (i + 1) * 2048)
        bVG = self.bB(8192, 10272)
        bVTW = self.bB(10272, 10784)
        bQs = self.bB(10784, 12832)
        bSL = self.bB(12832, 15904)
        self.vop("memset", self.bigB[64:128, 0:4096], 0.0,
                 reads=bQg(0) + bQg(1), writes=bQg(0) + bQg(1))
        self.vop("memset", self.bigB[64:128, 4096:8192], 0.0,
                 reads=bKg(0) + bKg(1), writes=bKg(0) + bKg(1))
        for i in range(2):
            self.vop("memset", KTg(i)[64:65, :], 1.0, reads=bKg(i), writes=bKg(i))
            self.vop("memset", VG[:, i, :, 64:65], 1.0, reads=bVG, writes=bVG)
        wqkv_r = self.wqkv_bf.rearrange("(c p) n -> p c n", p=128)
        lnfm_all = list(self.b_lnfm)
        for g2 in range(8):
            for i in range(3):
                cs = i * 1024 + g2 * 128
                self.dma(qkvsl(i)[:], wqkv_r[:, :, cs:cs + 128],
                         reads=bSL, writes=bSL + [self.b_wbf])
            for i in (0, 1, "norms", 2):
                if i == "norms":
                    for h01 in range(2):
                        self._attn_norms(QTg(h01), KTg(h01), QSQ,
                                         bQg(h01), bKg(h01), bQs, b)
                    continue
                for cg in range(4):
                    pp = (i * 4 + cg) % 2
                    ps = (self.pT1, self.pT2)[pp]
                    bps = self.b_pT[pp]
                    self.mm([(ps[:], qkvsl(i)[:, c, :],
                              self.ln_fm[:, c, cg * 512:(cg + 1) * 512])
                             for c in range(NC)],
                            reads=bSL + lnfm_all, writes=[bps])
                    if i == 0:
                        for h01 in range(2):
                            self.vop("tensor_scalar",
                                     QTg(h01)[0:64, cg * 512:(cg + 1) * 512],
                                     ps[h01 * 64:(h01 + 1) * 64, :],
                                     0.125, None, ALU.mult,
                                     reads=[bps], writes=bQg(h01))
                    elif i == 1:
                        for h01 in range(2):
                            self.vop("tensor_copy",
                                     KTg(h01)[0:64, cg * 512:(cg + 1) * 512],
                                     ps[h01 * 64:(h01 + 1) * 64, :],
                                     reads=[bps], writes=bKg(h01))
                    else:
                        self.vop("tensor_copy", VTW[:], ps[:],
                                 reads=[bps], writes=bVTW)
                        for h01 in range(2):
                            for t4 in range(4):
                                tg = cg * 4 + t4
                                po = (self.pT1b, self.pT2b)[1 - pp]
                                self.trans([(po[:, 960:1024],
                                             VTW[h01 * 64:(h01 + 1) * 64,
                                                 t4 * 128:(t4 + 1) * 128],
                                             h01 * 64)],
                                           reads=tuple(bVTW), writes=[self.b_pT[1 - pp]])
                                self.vop("tensor_copy", VG[:, h01, tg, 0:64],
                                         po[:, 960:1024],
                                         reads=[self.b_pT[1 - pp]], writes=bVG)
            pending = [None]

            def make_evac(h01, qb, par, ztile, zb, g2=g2):
                def evA():
                    rsl = (h01 + qb) % 2
                    lnrow = self.misc[64:65, rsl * 1024:(rsl + 1) * 1024]
                    if par == 0:
                        self.vop("reciprocal", lnrow, self.pM[64:65, 0:1024],
                                 reads=zb, writes=[self.b_misc[rsl]])
                    else:
                        self.vop("reciprocal", lnrow[:, 0:512],
                                 self.pT1[64:65, 0:512],
                                 reads=zb, writes=[self.b_misc[rsl]])
                        self.vop("reciprocal", lnrow[:, 512:1024],
                                 self.pT2[64:65, 0:512],
                                 reads=zb, writes=[self.b_misc[rsl]])
                    recrow = self.poolb[64:65, 2, rsl * 1024:(rsl + 1) * 1024]
                    self.vop("tensor_copy", recrow, lnrow,
                             reads=[self.b_misc[rsl]],
                             writes=[self.b_pool[2]])

                def ev():
                    rsl = (h01 + qb) % 2
                    recrow = self.poolb[64:65, 2, rsl * 1024:(rsl + 1) * 1024]
                    def fnb(E, recrow=recrow, par=par):
                        last = None
                        for j in range(2):
                            odst = self.pL[0:64, par * 1024 + j * 512:
                                           par * 1024 + (j + 1) * 512]
                            last = E.matmul(
                                odst, self.onesb[64:65, 0:64],
                                recrow[:, j * 512:(j + 1) * 512],
                                start=True, stop=True)
                        return last
                    self.op("tensor", fnb,
                            reads=[self.b_pool[2], self.b_ones],
                            writes=self.pLh(par))
                    rb = self.misc[0:64, rsl * 1024:(rsl + 1) * 1024]
                    self.vop("tensor_copy", rb,
                             self.pL[0:64, par * 1024:par * 1024 + 1024],
                             reads=self.pLh(par), writes=[self.b_misc[rsl]])
                    ydst = self.bigA[h01 * 64:(h01 + 1) * 64,
                                     g2 * 2048 + qb * 1024:g2 * 2048 + (qb + 1) * 1024]
                    if par == 0:
                        self.vop("tensor_mul", ydst, self.pM[0:64, 0:1024], rb,
                                 reads=zb + [self.b_misc[rsl]],
                                 writes=[self.b_bigA[g2]])
                    else:
                        self.vop("tensor_mul", ydst[:, 0:512], self.pT1[0:64, 0:512],
                                 rb[:, 0:512],
                                 reads=zb + [self.b_misc[rsl]],
                                 writes=[self.b_bigA[g2]])
                        self.vop("tensor_mul", ydst[:, 512:1024], self.pT2[0:64, 0:512],
                                 rb[:, 512:1024],
                                 reads=zb + [self.b_misc[rsl]],
                                 writes=[self.b_bigA[g2]])
                return evA, ev

            for li, (h01, qb) in enumerate(((0, 0), (0, 1), (1, 0), (1, 1))):
                QT = self.bigB[0:128, h01 * 2048:(h01 + 1) * 2048]
                KT = self.bigB[0:128, 4096 + h01 * 2048:4096 + (h01 + 1) * 2048]
                bQ, bK = bQg(h01), bKg(h01)
                par = li % 2
                if par == 0:
                    ztile = lambda j: self.pM[:, j * 512:(j + 1) * 512]
                    zb = [self.b_pM[0], self.b_pM[1]]
                else:
                    ztile = lambda j: (self.pT1, self.pT2)[j][:, 0:512]
                    zb = [self.b_pT[0], self.b_pT[1]]

                def emit_z(kc, zb=zb, h01=h01, ztile=ztile):
                    pt = self.miscb[:, 4096 + (kc % 3) * 1024:4096 + (kc % 3 + 1) * 1024]
                    vwide = self.bigB[:, 8192 + h01 * 1040 + kc * 65:
                                      8192 + h01 * 1040 + kc * 65 + 128]
                    self.mmx([(ztile(j),
                               vwide, pt[:, j * 512:(j + 1) * 512],
                               kc == 0, kc == 15) for j in range(2)],
                             reads=bVG + self.bB(8192, 10784) + [self.b_ptile[kc % 3]],
                             writes=zb)

                for kc in range(16):
                    sp = self.pL[:, (kc % 2) * 1024:(kc % 2 + 1) * 1024]
                    bsp = self.pLh(kc % 2)
                    self.mmx([(sp[:, j * 512:(j + 1) * 512],
                               KT[:, kc * 128:(kc + 1) * 128],
                               QT[:, qb * 1024 + j * 512:qb * 1024 + (j + 1) * 512],
                               True, True) for j in range(2)],
                             reads=bQ + bK, writes=bsp)
                    pt = self.miscb[:, 4096 + (kc % 3) * 1024:4096 + (kc % 3 + 1) * 1024]
                    bpt = self.b_ptile[kc % 3]
                    wpt = [bpt, self.b_misc[3]] if kc % 3 == 2 else [bpt]
                    self.act(pt[:], sp[:], AF.Exp, reads=bsp, writes=wpt)
                    if kc > 1:
                        emit_z(kc - 2)
                    if kc == 1 and pending[0] is not None:
                        pending[0][0]()
                    if kc == 3 and pending[0] is not None:
                        pending[0][1]()
                        pending[0] = None
                emit_z(14)
                emit_z(15)
                ev = make_evac(h01, qb, par, ztile, zb)
                if li < 3:
                    pending[0] = ev
                else:
                    ev[0]()
                    ev[1]()
        wps = self.bigB[:, 0:8192].rearrange("p (c n) -> p c n", n=1024)
        bWP = self.bB(0, 8192)
        self.dma(wps[:], self.wproj_bf.rearrange("(c p) n -> p c n", p=128),
                 reads=bWP, writes=bWP + [self.b_wbf])
        yall = list(self.b_bigA)
        for t in range(TT):
            for half in range(2):
                pp = (t * 2 + half) % 2
                ps = (self.pT1, self.pT2)[pp]
                bps = self.b_pT[pp]
                self.mm([(ps[:], self.bigA[:, c * 2048 + t * 128:c * 2048 + (t + 1) * 128],
                          wps[:, c, half * 512:(half + 1) * 512])
                         for c in range(NC)],
                        reads=yall + bWP, writes=[bps])
                self.vop("tensor_add", self.xn_tok[:, t, half * 512:(half + 1) * 512],
                         self.xn_tok[:, t, half * 512:(half + 1) * 512], ps[:],
                         reads=[bps, self.b_xn[t]], writes=[self.b_xn[t]])
            if "xn2" in self.dmp:
                self.dma(self.dmp["xn2"][b, t * 128:(t + 1) * 128, :],
                         self.xn_tok[:, t, :], reads=[self.b_xn[t]], writes=[])

    def phase_D(self, b):
        for t in range(TT):
            self._stats_collect(t, self.xn_tok[:, t, :], self.b_xn[t])
        self._stats_finalize()
        for t in range(TT):
            pp = t % 2
            xt = self.xn_tok[:, t, :]
            u = self.pool[:, pp, :]
            self.vop("tensor_scalar", u, xt,
                     self.statA[:, t, 2:3], self.statA[:, t, 3:4],
                     ALU.mult, ALU.subtract,
                     reads=[self.b_xn[t], self.b_stsum], writes=[self.b_pool[pp]])
            uhi = self.hg[:, 0:1024]
            ulo = self.hg[:, 1024:2048]
            self.vop("tensor_copy", uhi, u, reads=[self.b_pool[pp]], writes=[self.b_hg[0]])
            self.gop("tensor_sub", ulo, u, uhi,
                     reads=[self.b_pool[pp], self.b_hg[0]], writes=[self.b_hg[1]])
            self.trans([(self.pT1b[:, c * 128:(c + 1) * 128], uhi[:, c * 128:(c + 1) * 128])
                        for c in range(NC)],
                       reads=(self.b_hg[0],), writes=[self.b_pT[0]])
            self.trans([(self.pT2b[:, c * 128:(c + 1) * 128], ulo[:, c * 128:(c + 1) * 128])
                        for c in range(NC)],
                       reads=(self.b_hg[1],), writes=[self.b_pT[1]])
            for c in range(NC):
                dst = self.ln_fm[:, c, t * 128:(t + 1) * 128]
                src = self.pT1b[:, c * 128:(c + 1) * 128]
                if c % 2 == 0:
                    self.act(dst, src, AF.Identity,
                             scale=self.gam2[:, c:c + 1], bias=self.bet2[:, c:c + 1],
                             reads=[self.b_pT[0], self.b_params],
                             writes=[self.b_lnfm[t]])
                else:
                    self.vop("tensor_scalar", dst, src,
                             self.gam2[:, c:c + 1], self.bet2[:, c:c + 1],
                             ALU.mult, ALU.add,
                             reads=[self.b_pT[0], self.b_params],
                             writes=[self.b_lnfm[t]])
            uhf = self.hg[:, 2048:3072]
            ulf = self.hg[:, 3072:4096]
            self.vop("tensor_copy", uhf, self.pT1b[:, 0:1024],
                     reads=[self.b_pT[0]], writes=[self.b_hg[2]])
            self.act(ulf, self.pT2b[:, 0:1024], AF.Copy,
                     reads=[self.b_pT[1]], writes=[self.b_hg[3]])
            X = (t % 2) * 1024 + ((t // 2) % 8) * 128
            lg = self.pL[0:NE, X:X + 128]
            blg = self.b_pL[X // 512]
            mms = []
            for c in range(NC):
                mms.append((lg, self.gwg_hi[:, c, :], uhf[:, c * 128:(c + 1) * 128]))
                mms.append((lg, self.gwg_lo[:, c, :], uhf[:, c * 128:(c + 1) * 128]))
                mms.append((lg, self.gwg_hi[:, c, :], ulf[:, c * 128:(c + 1) * 128]))
            self.mm(mms, reads=[self.b_hg[2], self.b_hg[3], self.b_gw], writes=[blg])
            lgt_sl = self.stage[0:NE, 0:128]
            self.act(lgt_sl, lg, AF.Identity, bias=self.gbadd[:],
                     reads=[blg, self.b_params], writes=[self.b_stage[0]])
            po = (self.pT1, self.pT2)[t % 2]
            self.trans([(po[:, 0:NE], lgt_sl)],
                       reads=(self.b_stage[0],), writes=[self.b_pT[t % 2]], bf=False)
            self.vop("tensor_copy", self.lgt_tok[:, t, :], po[:, 0:NE],
                     reads=[self.b_pT[t % 2]], writes=[self.b_masks])
            self.vop("reduce_max", self.statA[:, t, 0:1], self.lgt_tok[:, t, :], axis=AX.X,
                     reads=[self.b_masks], writes=[self.b_stsq])
            self.vop("tensor_scalar", self.masks[:, b, t, :], self.lgt_tok[:, t, :],
                     self.statA[:, t, 0:1], None, ALU.is_ge,
                     reads=[self.b_masks, self.b_stsq], writes=[self.b_masks])
            self.trans([(self.pT1b[:, c * 128:(c + 1) * 128],
                         self.ln_fm[:, c, t * 128:(t + 1) * 128])
                        for c in range(NC)],
                       reads=(self.b_lnfm[t],), writes=[self.b_pT[0]])
            us = self.stageb[:, 2048 + (t % 2) * 1024:3072 + (t % 2) * 1024]
            self.vop("tensor_copy", us, self.pT1b[:, 0:1024],
                     reads=[self.b_pT[0], self.b_ust[t % 2]],
                     writes=[self.b_ust[t % 2], self.b_stage[1]])
            self.dma(self.u_buf[b * T + t * 128:b * T + (t + 1) * 128, :], us,
                     reads=[self.b_ust[t % 2], self.b_stage[1], self.b_ubuf],
                     writes=[])
            if "lgt" in self.dmp:
                self.dma(self.dmp["lgt"][b, t * 128:(t + 1) * 128, 0:NE],
                         self.lgt_tok[:, t, :], reads=[self.b_masks], writes=[])
                self.dma(self.dmp["lgt"][b, t * 128:(t + 1) * 128, NE:2 * NE],
                         self.masks[:, b, t, :], reads=[self.b_masks], writes=[])

    def index_build(self):
        bi = self.b_idx
        ic = self.b_icons
        NT = NB * TT
        allm = [self.b_misc[0], self.b_misc[1], self.b_misc[2], self.b_misc[3]]
        self.dma(self.ltf, self.lt_dram.ap(), reads=[], writes=[bi] + allm)
        self.dma(self.thr16, self.thr16_dram.ap(), reads=[], writes=[bi])
        self.dma(self.thrp, self.thrp_dram.ap(), reads=[], writes=[bi])
        self.dma(self.evalsf, self.evals_dram.ap(), reads=[], writes=[bi])
        self.gop("iota", self.tid32, [[128, NB * TT]], base=0,
                 channel_multiplier=1, reads=[], writes=[bi])
        self.gop("iota", self.pcol, [[0, 1]], base=0, channel_multiplier=1,
                 allow_small_or_imprecise_dtypes=True, reads=[], writes=[bi])
        ntv = self.misc[:, 640:640 + NB * TT].rearrange("p (a b) -> p a b", b=TT)
        self.vop("memset", ntv, 1.0, reads=[bi], writes=[bi])
        for e in range(NE):
            self.vop("tensor_mul", self.masks[:, :, :, e], self.masks[:, :, :, e],
                     ntv, reads=[bi, self.b_masks], writes=[self.b_masks])
            if e < NE - 1:
                self.vop("tensor_sub", ntv, ntv, self.masks[:, :, :, e],
                         reads=[self.b_masks, bi], writes=[bi])
        mms = [(self.pT1[0:1, bt * NE:(bt + 1) * NE], self.onescol,
                self.masks[:, bt // TT, bt % TT, :], True, True)
               for bt in range(NT)]
        self.mmx(mms, reads=[self.b_masks, bi], writes=[self.b_pT[0]])
        self.vop("tensor_copy", self.csum, self.pT1[0:1, 0:NT * NE],
                 reads=[self.b_pT[0]], writes=[bi])
        self.dma(self.iscr.rearrange("(one n) -> one n", one=1), self.csum,
                 reads=[bi], writes=[bi])
        self.dma(self.csum32, self.iscr.rearrange("(a b) -> a b", b=NE),
                 reads=[bi], writes=[bi])
        self.mm([(self.pT2[0:NT, 0:NE], self.ltf[0:NT, 0:NT], self.csum32)],
                reads=[bi, ic], writes=[self.b_pT[1]])
        self.vop("tensor_sub", self.carry32, self.pT2[0:NT, 0:NE], self.csum32,
                 reads=[self.b_pT[1], bi], writes=[bi])
        self.mm([(self.pT2[0:1, 128:128 + NE], self.ltf[0:NT, 127:128],
                  self.csum32)],
                reads=[bi], writes=[self.b_pT[1]])
        self.vop("tensor_copy", self.srow[0:1, 0:NE], self.pT2[0:1, 128:128 + NE],
                 reads=[self.b_pT[1]], writes=[bi])
        self.dma(self.iscr.rearrange("(a b) -> a b", b=NE), self.carry32,
                 reads=[bi], writes=[bi])
        self.dma(self.csum, self.iscr.rearrange("(one n) -> one n", one=1),
                 reads=[bi], writes=[bi])
        self.trans([(self.pT1[0:NE, 0:1], self.srow[0:1, 0:NE])], reads=(bi,),
                   writes=[self.b_pT[0]], bf=False)
        self.vop("tensor_copy", self.icol[:, 0:1], self.pT1[0:NE, 0:1],
                 reads=[self.b_pT[0]], writes=[bi])
        self.vop("tensor_scalar", self.irow[:, 0:16], self.thr16, self.icol[:, 0:1],
                 None, ALU.is_lt, reads=[bi, ic], writes=[bi])
        self.vop("reduce_sum", self.icol[:, 1:2], self.irow[:, 0:16], axis=AX.X,
                 reads=[bi], writes=[bi])
        self.vop("tensor_scalar", self.icol[:, 1:2], self.icol[:, 1:2],
                 float(PASS_TOK), None, ALU.mult, reads=[bi], writes=[bi])
        self.mm([(self.pT1[0:NE, 4:5], self.ltf[0:NE, 0:NE], self.icol[:, 1:2])],
                reads=[bi, ic], writes=[self.b_pT[0]])
        self.vop("tensor_sub", self.icol[:, 2:3], self.pT1[0:NE, 4:5], self.icol[:, 1:2],
                 reads=[self.b_pT[0], bi], writes=[bi])
        self.vop("tensor_add", self.icol[:, 3:4], self.icol[:, 2:3], self.icol[:, 1:2],
                 reads=[bi], writes=[bi])
        self.trans([(self.pT2[0:1, 8:16], self.icol[:, 2:3])], reads=(bi,),
                   writes=[self.b_pT[1]], bf=False)
        self.vop("tensor_copy", self.soffr, self.pT2[0:1, 8:16],
                 reads=[self.b_pT[1]], writes=[bi])
        self.vop("tensor_scalar", self.irow[:, 16:16 + NPASS], self.thrp,
                 self.icol[:, 2:3], None, ALU.is_ge, reads=[bi, ic], writes=[bi])
        self.vop("tensor_scalar", self.irow[:, 40:40 + NPASS], self.thrp,
                 self.icol[:, 3:4], None, ALU.is_lt, reads=[bi, ic], writes=[bi])
        self.vop("tensor_mul", self.irow[:, 16:16 + NPASS], self.irow[:, 16:16 + NPASS],
                 self.irow[:, 40:40 + NPASS], reads=[bi], writes=[bi])
        self.mm([(self.pT1[0:1, 16:16 + NPASS], self.evalsf,
                  self.irow[:, 16:16 + NPASS])],
                reads=[bi, ic], writes=[self.b_pT[0]])
        self.vop("tensor_scalar", self.srow[0:1, 8:8 + NPASS],
                 self.pT1[0:1, 16:16 + NPASS], 512.0, None, ALU.mult,
                 reads=[self.b_pT[0]], writes=[bi])
        self.mm([(self.pM[0:128, 0:NPASS], self.onesrow,
                  self.srow[0:1, 8:8 + NPASS])],
                reads=[bi, ic], writes=[self.b_pM[0]])
        self.vop("tensor_scalar", self.idxf[:, 0:NPASS], self.pM[0:128, 0:NPASS], self.pcol,
                 None, ALU.add, reads=[self.b_pM[0], ic], writes=[bi])
        self.vop("tensor_copy", self.idxs[:, 0:NPASS], self.idxf[:, 0:NPASS],
                 reads=[bi], writes=[self.b_widx])
        self._e_wload(0, 0)
        self._e_wload(0, 1)
        b_it = [Buf(), Buf()]
        b_ip = [self.b_pM[1], self.b_pT[0]]
        for bt in range(NT):
            sl = bt % 2
            m_bt = self.masks[:, bt // TT, bt % TT, :]
            ps = (self.pM[:, 512:512 + NE], self.pT1[:, 96:96 + NE])[sl]
            mms = [(ps, self.ltf, m_bt, True, False),
                   (ps, self.onesrow,
                    self.csum[0:1, bt * NE:(bt + 1) * NE], False, False),
                   (ps, self.onesrow, self.soffr, False, True)]
            self.mmx(mms, reads=[self.b_masks, bi, ic], writes=[b_ip[sl]])
            scr = self.idxf[:, sl * NE:(sl + 1) * NE]
            self.vop("tensor_sub", scr, ps, m_bt,
                     reads=[b_ip[sl], self.b_masks, self.b_widx, b_it[sl]],
                     writes=[b_it[sl]])
            self.vop("tensor_mul", scr, scr, m_bt,
                     reads=[b_it[sl], self.b_masks], writes=[b_it[sl]])
            self.vop("reduce_sum", self.posf[:, sl:sl + 1], scr, axis=AX.X,
                     reads=[b_it[sl]], writes=[b_it[sl]])
            self.vop("tensor_copy", self.idxs[:, 16 + sl:17 + sl],
                     self.posf[:, sl:sl + 1], reads=[b_it[sl]], writes=[b_it[sl]])
            for dst, bdst in ((self.gidx_buf.rearrange("(a b) -> a b", b=1),
                               self.b_gidxb),
                              (self.sidx_buf.rearrange("(a b) -> a b", b=1),
                               self.b_sidxb)):
                def fn(E, o=dst, ix=self.idxs[:, 16 + sl:17 + sl],
                       i=self.tid32[:, bt:bt + 1]):
                    return E.indirect_dma_start(
                        out=o, out_offset=IOA(ap=ix, axis=0), in_=i, in_offset=None)
                self.S.gma(fn, reads=[b_it[sl], ic, bdst], writes=[])
        self.vop("memset", self.misc[0:1, 0:1], 0.0,
                 reads=[], writes=[bi, b_it[0], b_it[1]] + allm)

    def _utok(self):
        return self.ln_fm[:, 0:2, :].rearrange("p a n -> p (a n)")

    def _ufm(self):
        return self.ln_fm[:, 2:4, :].rearrange("p a (b n) -> p (a b) n", n=PASS_TOK)

    def _yacc(self):
        return self.lnfm_f32[:, 4:8, :]

    def _gsl(self, s):
        return self.idxs[:, 18 + 4 * s:22 + 4 * s]

    def _ssl(self, s):
        return self.idxs[:, 26 + 4 * s:30 + 4 * s]

    def _e_gload(self, ps):
        sl = ps % 2
        c0 = ps * PASS_TOK
        for c in range(4):
            if not self._gidx_claimed:
                self._gidx_claimed = True
                wg, rg = [self.b_gidxb], []
                ws, rs_ = [self.b_sidxb], []
            else:
                wg, rg = [], [self.b_gidxb]
                ws, rs_ = [], [self.b_sidxb]
            self.dma(self._gsl(sl)[:, c:c + 1],
                     self.gidx_buf[c0 + c * 128:c0 + (c + 1) * 128].rearrange(
                         "(p one) -> p one", one=1),
                     reads=[self.b_gsb[sl]] + rg, writes=[self.b_gsb[sl]] + wg)
            self.dma(self._ssl(sl)[:, c:c + 1],
                     self.sidx_buf[c0 + c * 128:c0 + (c + 1) * 128].rearrange(
                         "(p one) -> p one", one=1),
                     reads=[self.b_gsb[sl]] + rs_, writes=[self.b_gsb[sl]] + ws)

    def _e_ugather(self, ps):
        sl = ps % 2
        ut = self._utok()
        for c in range(4):
            dst = ut[:, c * 1024:(c + 1) * 1024]
            def fn(E, o=dst, ix=self._gsl(sl)[:, c:c + 1]):
                return E.indirect_dma_start(out=o, out_offset=None, in_=self.u_buf,
                                            in_offset=IOA(ap=ix, axis=0))
            rd = [self.b_gsb[sl], self.b_utok[0]]
            if ps < 1:
                rd += list(self.b_lnfm)
            if not self._ubuf_claimed:
                self._ubuf_claimed = True
                self.S.gma(fn, reads=rd, writes=[self.b_utok[0], self.b_ubuf])
            else:
                self.S.gma(fn, reads=rd + [self.b_ubuf],
                           writes=[self.b_utok[0]])

    def _e_wload(self, ps, hb):
        wsl = hb % 2
        eo = hb * 128 * 8192
        for (big, bufs, src) in ((self.bigA, self.bA, self.w1bf),
                                 (self.bigB, self.bB, self.w2bf)):
            for h2 in range(2):
                dst = big[:, wsl * 8192 + h2 * 4096:wsl * 8192 + (h2 + 1) * 4096]
                def fn(E, o=dst, s=src, ix=self.idxs[:, ps:ps + 1],
                       eo=eo + h2 * 4096):
                    return E.indirect_dma_start(out=o, out_offset=None, in_=s,
                                                in_offset=IOA(ap=ix, axis=0),
                                                element_offset=eo)
                bl = bufs(wsl * 8192, (wsl + 1) * 8192)
                if not self._wcvt_claimed:
                    self._wcvt_claimed = True
                    self.S.gma(fn, reads=[self.b_widx] + bl,
                               writes=bl + [self.b_wcvt])
                else:
                    self.S.gma(fn, reads=[self.b_widx, self.b_wcvt] + bl,
                               writes=bl)

    def phase_E_sparse(self):
        W1q = lambda s: self.bigA[:, s * 8192:(s + 1) * 8192].rearrange(
            "p (c n) -> p c n", n=1024)
        W2q = lambda s: self.bigB[:, s * 8192:(s + 1) * 8192].rearrange(
            "p (c n) -> p c n", n=1024)
        bW1 = lambda s: self.bA(s * 8192, (s + 1) * 8192)
        bW2 = lambda s: self.bB(s * 8192, (s + 1) * 8192)
        hgv = self.hg.rearrange("p (c n) -> p c n", n=PASS_TOK)
        ya = self._yacc()
        self._e_gload(0)
        self._e_ugather(0)
        for ps in range(NPASS):
            sl = ps % 2
            if ps + 1 < NPASS:
                self._e_gload(ps + 1)
            ut = self._utok()
            fence = list(self.b_lnfm) if ps < 1 else []
            uf = self._ufm()
            trs0, trs1, trs2 = [], [], []
            for c2 in range(4):
                for fc in range(8):
                    src_t = ut[:, c2 * 1024 + fc * 128:c2 * 1024 + (fc + 1) * 128]
                    if c2 < 2:
                        trs0.append((self.pMb[:, (c2 * 8 + fc) * 128:
                                              (c2 * 8 + fc + 1) * 128], src_t))
                    elif c2 == 2:
                        trs1.append((self.pT1b[:, fc * 128:(fc + 1) * 128], src_t))
                    else:
                        trs2.append((self.pT2b[:, fc * 128:(fc + 1) * 128], src_t))
            self.trans(trs0, reads=(self.b_utok[0],),
                       writes=[self.b_pM[0], self.b_pM[1]])
            self.trans(trs1, reads=(self.b_utok[0],), writes=[self.b_pT[0]])
            self.trans(trs2, reads=(self.b_utok[0],), writes=[self.b_pT[1]])
            self.vop("tensor_copy", uf[:, :, 0:128],
                     self.pMb[:, 0:1024].rearrange("p (c n) -> p c n", n=128),
                     reads=[self.b_pM[0], self.b_pM[1], self.b_ufm[0]] + fence,
                     writes=[self.b_ufm[0]])
            self.vop("tensor_copy", uf[:, :, 128:256],
                     self.pMb[:, 1024:2048].rearrange("p (c n) -> p c n", n=128),
                     reads=[self.b_pM[0], self.b_pM[1], self.b_ufm[0]],
                     writes=[self.b_ufm[0]])
            self.vop("tensor_copy", uf[:, :, 256:384],
                     self.pT1b[:, 0:1024].rearrange("p (c n) -> p c n", n=128),
                     reads=[self.b_pT[0], self.b_ufm[0]], writes=[self.b_ufm[0]])
            self.vop("tensor_copy", uf[:, :, 384:512],
                     self.pT2b[:, 0:1024].rearrange("p (c n) -> p c n", n=128),
                     reads=[self.b_pT[1], self.b_ufm[0]], writes=[self.b_ufm[0]])
            for hb in range(4):
                wsl = hb % 2
                nps, nhb = (ps, hb + 1) if hb < 3 else (ps + 1, 0)
                if nps < NPASS and not (nps == 0 and nhb <= 1):
                    self._e_wload(nps, nhb)
                if hb == 2 and ps + 1 < NPASS:
                    self._e_ugather(ps + 1)
                bh = list(self.b_hg)
                for rnd in range(4):
                    pr = (rnd % 2) * 1024
                    mms = []
                    for hcq in range(2):
                        hc = rnd * 2 + hcq
                        for fc in range(NC):
                            mms.append((self.pL[:, pr + hcq * 512:
                                                pr + (hcq + 1) * 512],
                                        W1q(wsl)[:, fc, hc * 128:(hc + 1) * 128],
                                        uf[:, fc, :], fc == 0, fc == NC - 1))
                    self.mmx(mms, reads=bW1(wsl) + [self.b_ufm[0]],
                             writes=self.pLh(rnd % 2))
                    self.act(self.hg[:, rnd * 1024:(rnd + 1) * 1024],
                             self.pL[:, pr:pr + 1024],
                             AF.Gelu_apprx_tanh, reads=self.pLh(rnd % 2),
                             writes=[self.b_hg[rnd]])
                for half in range(2):
                    mms = []
                    for t2 in range(4):
                        if half == 0:
                            out = (self.pM[:, 0:512], self.pM[:, 512:1024],
                                   self.pT1[:, 0:512], self.pT2[:, 0:512])[t2]
                        else:
                            out = self.pL[:, t2 * 512:(t2 + 1) * 512]
                        for hc in range(NC):
                            mms.append((out, hgv[:, hc, t2 * 128:(t2 + 1) * 128],
                                        W2q(wsl)[:, hc, half * 512:(half + 1) * 512],
                                        hc == 0, hc == NC - 1))
                    wr = ([self.b_pM[0], self.b_pM[1], self.b_pT[0], self.b_pT[1]]
                          if half == 0 else self.pLh(0) + self.pLh(1))
                    self.mmx(mms, reads=bW2(wsl) + bh, writes=wr)
                    op = "tensor_copy" if hb == 0 else "tensor_add"
                    for t2 in range(4):
                        if half == 0:
                            pin = (self.pM[:, 0:512], self.pM[:, 512:1024],
                                   self.pT1[:, 0:512], self.pT2[:, 0:512])[t2]
                            bp = (self.b_pM[0], self.b_pM[1],
                                  self.b_pT[0], self.b_pT[1])[t2]
                        else:
                            pin = self.pL[:, t2 * 512:(t2 + 1) * 512]
                            bp = self.b_pL[t2]
                        dst = ya[:, t2, half * 512:(half + 1) * 512]
                        args = (dst, pin) if hb == 0 else (dst, dst, pin)
                        rdy = [bp, self.b_yst[0]] + (fence if hb == 0 else [])
                        self.vop(op, *args, reads=rdy, writes=[self.b_yst[0]])
            for c in range(4):
                def fn(E, i=ya[:, c, :], ix=self._ssl(sl)[:, c:c + 1]):
                    return E.indirect_dma_start(out=self.y_buf,
                                                out_offset=IOA(ap=ix, axis=0),
                                                in_=i, in_offset=None)
                self.S.gma(fn, reads=[self.b_yst[0], self.b_gsb[sl]],
                           writes=[self.b_ybuf])

    def phase_F(self, b):
        bigA32 = self.bigA.bitcast(F32)
        bigB32 = self.bigB.bitcast(F32)
        for t in range(TT):
            i4 = t % 4
            xs = bigA32[:, (4 + i4) * 1024:(5 + i4) * 1024]
            bxs = self.b_bigA[4 + i4]
            ygt = bigA32[:, i4 * 1024:(i4 + 1) * 1024]
            byg = [self.b_bigA[i4]]
            ss = bigB32[:, i4 * 1024:(i4 + 1) * 1024]
            bss = self.b_bigB[i4]
            self.dma(xs, self.x[b, t * 128:(t + 1) * 128, :],
                     reads=[bxs], writes=[bxs])
            if not self._ybuf_claimed:
                self._ybuf_claimed = True
                wy, ry = [self.b_ybuf], []
            else:
                wy, ry = [], [self.b_ybuf]
            self.dma(ygt, self.y_buf[b * T + t * 128:b * T + (t + 1) * 128, :],
                     reads=byg + ry, writes=byg + wy)
            if not self._sspill_claimed:
                self._sspill_claimed = True
                wsx, rsx = [self.b_sspill], []
            else:
                wsx, rsx = [], [self.b_sspill]
            self.dma(ss, self.s_spill[b, t * 128:(t + 1) * 128, :],
                     reads=[bss] + rsx, writes=[bss] + wsx)
            if b == 0:
                vs = bigB32[:, (4 + i4) * 1024:(5 + i4) * 1024]
                bvs = self.b_bigB[4 + i4]
                if not self._vit_claimed:
                    self._vit_claimed = True
                    wv, rv = [self.b_vit], []
                else:
                    wv, rv = [], [self.b_vit]
                self.dma(vs, self.vit_buf[t * 128:(t + 1) * 128, :],
                         reads=[bvs] + rv, writes=[bvs] + wv)
                vread = [bvs]
            else:
                vs = self.xn_tok[:, t, :]
                vread = [self.b_xn[t]]
            self.vop("tensor_add", ygt, ygt, vs, reads=byg + vread, writes=byg)
            if "gin" in self.dmp:
                self.dma(self.dmp["gin"][b, t * 128:(t + 1) * 128, :], ygt,
                         reads=byg, writes=[])
            gg = self.lnfm_f32[:, t % 4, :]
            bgg = self.b_lnfm[t % 4]
            wex = ([self.b_utok[0], self.b_ufm[0], self.b_yst[0]]
                   if (b == 0 and t < 4) else [])
            self.act(gg, ygt, AF.Silu, reads=byg, writes=[bgg] + wex)
            self.vop("tensor_mul", gg, gg, ss, reads=[bgg, bss], writes=[bgg])
            self.vop("tensor_add", xs, xs, gg, reads=[bxs, bgg], writes=[bxs])
            self.S.sdma(self.out[b, t * 128:(t + 1) * 128, :], xs,
                        reads=[bxs], writes=[self.b_out])

    def build(self):
        self.b_out = Buf()
        self._wcvt_claimed = False
        self._ubuf_claimed = False
        self._gidx_claimed = False
        self._ybuf_claimed = False
        self._sspill_claimed = False
        self._vit_claimed = False
        self.init_consts()
        self.convert_qkv_weights()
        for b in range(NB):
            self.phase_A_pass1(b)
            self.load_attn_weights()
            self.phase_A_pass2(b)
            self.phase_B(b)
            if b == 0:
                self.convert_moe_weights()
            self.phase_C(b)
            self.phase_D(b)
            if b == 0:
                for t in range(TT):
                    self.dma(self.vit_buf[t * 128:(t + 1) * 128, :],
                             self.xn_tok[:, t, :],
                             reads=[self.b_xn[t], self.b_vit], writes=[])
        self.index_build()
        self.phase_E_sparse()
        for b in range(NB):
            self.phase_F(b)
        it_deps = [e for e in self.S.dma_ring if e is not None]
        it_deps += [e for e in self.S.gma_ring if e is not None]
        eid = self.S._mk("sync", lambda E: None, (self.b_out,), (), False)
        self.S.items[eid]["deps"] = sorted(set(self.S.items[eid]["deps"]) | set(it_deps))
        self.S.finalize(self.nc, self.sems)
        return self.nc


NAME_MAP = {
    "L0_attn_x": "x",
    "L0_attn_wq": "wq", "L0_attn_bq": "bq",
    "L0_attn_wk": "wk", "L0_attn_bk": "bk",
    "L0_attn_wv": "wv", "L0_attn_bv": "bv",
    "L0_attn_wo": "wo", "L0_attn_bo": "bo",
    "L0_ffn_ln1_w": "ln1w", "L0_ffn_ln1_b": "ln1b",
    "L0_ffn_w_qkv": "wqkv", "L0_ffn_w_proj": "wproj",
    "L0_ffn_ln2_w": "ln2w", "L0_ffn_ln2_b": "ln2b",
    "L0_ffn_gate_w": "gatew", "L0_ffn_gate_b": "gateb",
    "L0_ffn_w1": "w1", "L0_ffn_w2": "w2",
}


def build_program(dump=()):
    bld = Builder(dump=dump)
    nc = bld.build()
    return nc


def kernel(**inputs):
    arrs = {NAME_MAP[k]: np.ascontiguousarray(np.asarray(v, dtype=np.float32))
            for k, v in inputs.items()}
    x = arrs.pop("x")
    nc = build_program()
    in_maps = []
    for c in range(8):
        m = dict(arrs)
        m["x"] = np.ascontiguousarray(x[c * NB:(c + 1) * NB])
        in_maps.append(m)
    res = run_bass_kernel_spmd(nc, in_maps, core_ids=list(range(8)))
    return np.concatenate([res.results[c]["out"] for c in range(8)], axis=0)


if __name__ == "__main__":
    pass
